# version 03
# baseline (speedup 1.0000x reference)
import os
import numpy as np
import ml_dtypes
from contextlib import ExitStack

import concourse.bass as bass
import concourse.mybir as mybir
from concourse.bass_utils import run_bass_kernel_spmd

F32 = mybir.dt.float32
BF16 = mybir.dt.bfloat16
I32 = mybir.dt.int32
U32 = mybir.dt.uint32
AF = mybir.ActivationFunctionType
ALU = mybir.AluOpType
NCORES = 8
BLOC = 2
T = 4096
D = 1024
NTOK = BLOC * T
NT = NTOK // 128

C0 = 8.0
C1 = 12.0
SC0 = 1.0 / float(np.sqrt(160.0))
SC1 = 1.0 / 8.0

bf16 = ml_dtypes.bfloat16


class Sched:
    STREAMS = ("tensor", "scalar", "vector", "gpsimd", "sync")

    def __init__(self, nc):
        self.nc = nc
        self.ops = []
        self.lw = {}
        self.rd = {}
        self.es = ExitStack()
        self.scopes = []
        self._names = set()
        self._nuniq = 0

    def push_scope(self):
        self.scopes.append(ExitStack())

    def pop_scope(self):
        self.scopes.pop().close()

    def barrier(self):
        last_by_ch = {}
        for op in self.ops:
            last_by_ch[op["ch"]] = op["i"]
        deps = sorted(set(last_by_ch.values()))
        for stream in self.STREAMS:
            i = len(self.ops)
            self.ops.append(dict(i=i, stream=stream, ch=stream, emit=lambda e: e.nop(),
                                 deps=deps, dma=False, tag="barrier", need_inc=False))

    def _stack(self):
        return self.scopes[-1] if self.scopes else self.es

    def sbuf(self, name, shape, dtype):
        if name in self._names:
            self._nuniq += 1
            name = f"{name}_u{self._nuniq}"
        self._names.add(name)
        return self._stack().enter_context(self.nc.sbuf_tensor(name, shape, dtype))

    def psum(self, name, shape, dtype):
        if name in self._names:
            self._nuniq += 1
            name = f"{name}_u{self._nuniq}"
        self._names.add(name)
        return self._stack().enter_context(self.nc.psum_tensor(name, shape, dtype))

    def add(self, stream, emit, reads=(), writes=(), dma=False, tag=""):
        i = len(self.ops)
        deps = set()
        for k in reads:
            j = self.lw.get(k)
            if j is not None:
                deps.add(j)
        for k in writes:
            j = self.lw.get(k)
            if j is not None:
                deps.add(j)
            deps.update(self.rd.get(k, ()))
        deps.discard(i)
        ch = stream + ("_q" if dma else "")
        self.ops.append(dict(i=i, stream=stream, ch=ch, emit=emit, deps=sorted(deps),
                             dma=dma, tag=tag, need_inc=False))
        for k in writes:
            self.lw[k] = i
            self.rd[k] = []
        for k in reads:
            self.rd.setdefault(k, []).append(i)
        return i

    def te(self, emit, reads=(), writes=(), tag=""):
        return self.add("tensor", emit, reads, writes, tag=tag)

    def act(self, emit, reads=(), writes=(), tag=""):
        return self.add("scalar", emit, reads, writes, tag=tag)

    def dve(self, emit, reads=(), writes=(), tag=""):
        return self.add("vector", emit, reads, writes, tag=tag)

    def pool(self, emit, reads=(), writes=(), tag=""):
        return self.add("gpsimd", emit, reads, writes, tag=tag)

    def dma(self, queue, out, in_, reads=(), writes=(), tag="", semkey=None, **kw):
        if semkey is None:
            if writes:
                semkey = "dq:" + writes[0]
            elif reads:
                semkey = "dq:st:" + reads[0]
            else:
                semkey = "dq:anon"

        def emit(eng):
            return eng.dma_start(out=out, in_=in_, **kw)
        i = self.add(queue, emit, reads, writes, dma=True, tag=tag)
        self.ops[i]["ch"] = semkey
        return i

    def emit_all(self, check=False):
        nc = self.nc
        for op in self.ops:
            kept = []
            for j in op["deps"]:
                dj = self.ops[j]
                if (dj["stream"] == "tensor" and op["stream"] == "tensor"
                        and op["tag"] != "barrier"):
                    continue
                kept.append(j)
                dj["need_inc"] = True
            op["kdeps"] = kept
        chcnt = {}
        for op in self.ops:
            if op["dma"]:
                op["need_inc"] = True
            if op["need_inc"]:
                ch = op["ch"]
                chcnt[ch] = chcnt.get(ch, 0) + (16 if op["dma"] else 1)
                op["cnt"] = chcnt[ch]
            else:
                op["cnt"] = None
        know = {st: {} for st in self.STREAMS}
        know_at = {}
        for op in self.ops:
            k = know[op["stream"]]
            waits = []
            for j in op["kdeps"]:
                dj = self.ops[j]
                if dj["cnt"] is None:
                    continue
                if k.get(dj["ch"], 0) >= dj["cnt"]:
                    continue
                waits.append((dj["ch"], dj["cnt"]))
                ka = know_at.get(j)
                if ka:
                    for c2, v2 in ka.items():
                        if k.get(c2, 0) < v2:
                            k[c2] = v2
                if k.get(dj["ch"], 0) < dj["cnt"]:
                    k[dj["ch"]] = dj["cnt"]
            op["waits"] = waits
            if op["need_inc"]:
                ka = dict(k)
                ka[op["ch"]] = op["cnt"]
                know_at[op["i"]] = ka
        self.channels = sorted(chcnt)
        self.chcnt = chcnt
        self.n_waits = sum(len(op["waits"]) for op in self.ops)
        with ExitStack() as es:
            sems = {ch: es.enter_context(nc.semaphore("sem_" + ch)) for ch in self.channels}
            block = es.enter_context(nc.Block())

            def run_stream(eng, stream):
                for op in self.ops:
                    if op["stream"] != stream:
                        continue
                    for ch, v in op["waits"]:
                        eng.wait_ge(sems[ch], v)
                    inst = op["emit"](eng)
                    if op["need_inc"]:
                        if inst is None:
                            raise RuntimeError(f"op {op['i']} ({op['tag']}) returned None but needs inc")
                        inst.then_inc(sems[op["ch"]], 16 if op["dma"] else 1)

            @block.tensor
            def _(eng):
                run_stream(eng, "tensor")

            @block.scalar
            def _(eng):
                run_stream(eng, "scalar")

            @block.vector
            def _(eng):
                run_stream(eng, "vector")

            @block.gpsimd
            def _(eng):
                run_stream(eng, "gpsimd")

            @block.sync
            def _(eng):
                run_stream(eng, "sync")
                for ch in self.channels:
                    eng.wait_ge(sems[ch], chcnt[ch])

        self.es.close()


def MM(out, lhsT, rhs, start, stop):
    def f(eng):
        return eng.matmul(out, lhsT, rhs, start=start, stop=stop)
    return f


def TRANS(out, in_, ident):
    def f(eng):
        return eng.transpose(out, in_, ident)
    return f


def COPY(out, in_):
    def f(eng):
        return eng.tensor_copy(out, in_)
    return f


def SCOPY(out, in_):
    def f(eng):
        return eng.copy(out, in_)
    return f


def ACTV(out, in_, func, bias=None, scale=None, accum_out=None):
    kw = {}
    if bias is not None:
        kw["bias"] = bias
    if scale is not None:
        kw["scale"] = scale
    if accum_out is not None:
        kw["accum_out"] = accum_out

    def f(eng):
        return eng.activation(out, in_, func, **kw)
    return f


def TT(out, a, b, op):
    def f(eng):
        return eng.tensor_tensor(out, a, b, op)
    return f


def TS(out, in0, s1, s2, op0, op1=None, accum_out=None):
    kw = dict(op0=op0)
    if op1 is not None:
        kw["op1"] = op1
    if accum_out is not None:
        kw["accum_out"] = accum_out

    def f(eng):
        return eng.tensor_scalar(out, in0, s1, s2, **kw)
    return f


def MEMSET(ap, v):
    def f(eng):
        return eng.memset(ap, v)
    return f




def host_consts():
    pos = np.arange(4096, dtype=np.float64)
    inv = 1.0 / (10000.0 ** (np.arange(0, 64, 2, dtype=np.float64) / 64.0))
    fr = pos[:, None] * inv[None, :]
    cos = np.cos(fr).astype(np.float32)
    sin = np.sin(fr).astype(np.float32)
    cos2 = np.concatenate([cos, cos], 1)
    sinpm = np.concatenate([-sin, sin], 1)
    cosrep = np.tile(cos2, (1, 16)).astype(np.float32)
    sinrep = np.tile(sinpm, (1, 16)).astype(np.float32)
    mask = np.kron(np.eye(8, dtype=np.float32), np.ones((16, 16), np.float32))
    mask4 = np.tile(mask, (1, 4)).astype(bf16)
    return cosrep, sinrep, mask4


def build_counts(s, erow_all, prow, tag="cnt"):
    onesr = s.sbuf(tag + "_onesr", [1, 8], BF16)
    s.pool(MEMSET(onesr[:, :], 1.0), writes=[tag + "onesr"])
    iota8 = s.sbuf(tag + "_iota8", [8, 1], F32)
    s.pool(lambda e: e.iota(iota8[:, :], [[1, 1]], base=0, channel_multiplier=1,
                            allow_small_or_imprecise_dtypes=True), writes=[tag + "iota8"])
    rowb = s.sbuf(tag + "_rowb", [1, 512], BF16)
    ohb = s.sbuf(tag + "_ohb", [8, 512], F32)
    red = s.sbuf(tag + "_red", [8, 1], F32)
    cmat = [s.sbuf(tag + f"_cm{j}", [8, 8], F32) for j in range(2)]
    for cp in range(8):
        for j in range(2):
            r = cp * 2 + j
            dstk = tag + f"cm{j}"
            for ch in range(16):
                s.dma("sync", rowb[0:1, :], erow_all[r:r + 1, ch * 512:(ch + 1) * 512],
                      writes=[tag + "rowb"], tag=f"{tag} ld {r},{ch}")
                s.te(MM(prow[0:8, 0:512], onesr[0:1, :], rowb[0:1, 0:512], True, True),
                     reads=[tag + "rowb", tag + "onesr", "prow_r"], writes=["prow"],
                     tag=f"{tag} rep {r},{ch}")
                s.act(SCOPY(ohb[:, :], prow[0:8, 0:512]), reads=["prow"],
                      writes=[tag + "ohbr", "prow_r"], tag=f"{tag} ev {r},{ch}")
                s.dve(TS(ohb[:, :], ohb[:, :], iota8[:, 0:1], None, ALU.is_equal),
                      reads=[tag + "ohbr", tag + "iota8"], writes=[tag + "ohb"],
                      tag=f"{tag} oh {r},{ch}")
                s.dve(lambda e, a=red[:, 0:1], b_=ohb[:, :]:
                      e.reduce_sum(a, b_, axis=mybir.AxisListType.X),
                      reads=[tag + "ohb"], writes=[tag + "red"], tag=f"{tag} rs {r},{ch}")
                if ch == 0:
                    s.dve(COPY(cmat[j][:, cp:cp + 1], red[:, 0:1]), reads=[tag + "red"],
                          writes=[dstk + f"_{cp}"], tag=f"{tag} st {r},{ch}")
                else:
                    s.dve(TT(cmat[j][:, cp:cp + 1], cmat[j][:, cp:cp + 1], red[:, 0:1],
                             ALU.add), reads=[tag + "red", dstk + f"_{cp}"],
                          writes=[dstk + f"_{cp}"], tag=f"{tag} ac {r},{ch}")
    CK = [tag + f"cm{j}_{cp}" for j in range(2) for cp in range(8)]
    return cmat[0], cmat[1], CK


def emit_prep(s, q1, q2, sc1, sc2, rw, identf, w1b, w2b, rwT):
    sid = s.sbuf("sid", [128, 128], F32)
    s.dma("sync", sid[:, :], identf[:, :], writes=["sid"])

    win = [s.sbuf(f"win{i}", [128, 128], F32) for i in range(4)]
    pt = [s.psum(f"pt{i}", [128, 512], F32) for i in range(4)]
    obuf = [s.sbuf(f"obuf{i}", [128, 4096], BF16) for i in range(2)]

    def transpose_mat(src, dst, R, C, tagp):
        nr, ncol = R // 128, C // 128
        for cj in range(ncol):
            ob = obuf[cj % 2]
            for ri in range(nr):
                ib = win[ri % 4]
                ibk = f"win{ri % 4}"
                pb = pt[ri % 4]
                pbk = f"pt{ri % 4}"
                s.dma("sync", ib[:, :], src[ri * 128:(ri + 1) * 128, cj * 128:(cj + 1) * 128],
                      writes=[ibk], tag=f"{tagp} ld {ri},{cj}")
                s.te(TRANS(pb[:, 0:128], ib[:, :], sid[:, :]),
                     reads=[ibk, "sid"], writes=[pbk], tag=f"{tagp} tr {ri},{cj}")
                s.dve(COPY(ob[:, ri * 128:(ri + 1) * 128], pb[:, 0:128]),
                      reads=[pbk], writes=[f"obuf{cj % 2}_{ri}"], tag=f"{tagp} ev {ri},{cj}")
            s.dma("gpsimd", dst[cj * 128:(cj + 1) * 128, :], ob[:, 0:R],
                  reads=[f"obuf{cj % 2}_{ri}" for ri in range(nr)],
                  writes=[f"{tagp}st"], semkey=f"dq:st:obuf{cj % 2}", tag=f"{tagp} st {cj}")


    rws = s.sbuf("rws", [8, 1024], F32)
    rwo = s.sbuf("rwo", [128, 8 * 8], F32)
    s.dma("sync", rws[:, :], rw[:, :], writes=["rws"])
    for c in range(8):
        pb = pt[c % 4]
        s.te(TRANS(pb[0:128, 0:8], rws[0:8, c * 128:(c + 1) * 128], sid[0:8, 0:8]),
             reads=["rws", "sid"], writes=[f"pt{c % 4}"], tag=f"rwT {c}")
        s.dve(COPY(rwo[:, c * 8:(c + 1) * 8], pb[0:128, 0:8]),
              reads=[f"pt{c % 4}"], writes=[f"rwo{c}"], tag=f"rwT ev {c}")
    for c in range(8):
        s.dma("gpsimd", rwT[c * 128:(c + 1) * 128, :], rwo[:, c * 8:(c + 1) * 8],
              reads=[f"rwo{c}"], writes=["rwT_st"], tag=f"rwT st {c}")

    ones1 = s.sbuf("ones1", [1, 128], BF16)
    s.pool(MEMSET(ones1[:, :], 1.0), writes=["ones1"])
    qin = [s.sbuf(f"qin{i}", [128, 4096], I32) for i in range(2)]
    qf = [s.sbuf(f"qf{i}", [128, 4096], F32) for i in range(2)]
    screp = [s.sbuf(f"screp{i}", [128, 4096], F32) for i in range(2)]
    psc = [s.psum(f"psc{i}", [128, 512], F32) for i in range(2)]
    wob = [s.sbuf(f"wob{i}", [128, 4096], BF16) for i in range(2)]
    scb = [s.sbuf(f"scb{i}", [1, 4096], BF16) for i in range(2)]

    def dequant(qsrc, scsrc, dst, K, N, tagp):
        ng = K // 128
        for g in range(ng):
            qi, qik = qin[g % 2], f"qin{g % 2}"
            qo, qok = qf[g % 2], f"qf{g % 2}"
            sr, srk = screp[g % 2], f"screp{g % 2}"
            wo, wok = wob[g % 2], f"wob{g % 2}"
            sb, sbk = scb[g % 2], f"scb{g % 2}"
            s.dma("sync", qi[:, 0:N], qsrc[g * 128:(g + 1) * 128, :], writes=[qik],
                  tag=f"{tagp} qld {g}")
            s.dma("gpsimd", sb[0:1, 0:N], scsrc[g:g + 1, :], writes=[sbk],
                  tag=f"{tagp} scld {g}")
            for nb in range((N + 511) // 512):
                n0, n1 = nb * 512, min(N, (nb + 1) * 512)
                pb = psc[nb % 2]
                s.te(MM(pb[:, 0:n1 - n0], ones1[0:1, :], sb[0:1, n0:n1], True, True),
                     reads=["ones1", sbk], writes=[f"psc{nb % 2}"],
                     tag=f"{tagp} screp {g},{nb}")
                s.act(SCOPY(sr[:, n0:n1], pb[:, 0:n1 - n0]),
                      reads=[f"psc{nb % 2}"], writes=[srk + f"_{nb}"],
                      tag=f"{tagp} screv {g},{nb}")
            s.dve(TS(qo[:, 0:N], qi[:, 0:N], -128.0, None, ALU.add),
                  reads=[qik], writes=[qok], tag=f"{tagp} qcvt {g}")
            s.dve(TT(wo[:, 0:N], qo[:, 0:N], sr[:, 0:N], ALU.mult),
                  reads=[qok] + [srk + f"_{nb}" for nb in range((N + 511) // 512)],
                  writes=[wok], tag=f"{tagp} mul {g}")
            s.dma("gpsimd", dst[g * 128:(g + 1) * 128, :], wo[:, 0:N],
                  reads=[wok], writes=[f"{tagp}st"], tag=f"{tagp} st {g}")

    dequant(q1, sc1, w1b, 1024, 4096, "d1")
    dequant(q2, sc2, w2b, 4096, 1024, "d2")



def emit_l0pre(s, x_in, wdq, wdkv, wkr, bcat, wuq, buq, wuk, buk, wuv, buv, wqr, bqr, identb_in, xn_o, musig_o, QT1, QT2, KT1, KT2, V_o):
    identb = s.sbuf("identb_sb", [128, 128], BF16)
    s.dma("sync", identb[:, :], identb_in[:, :], writes=["identb"])
    onesb = s.sbuf("onesb", [1, 128], BF16)
    s.pool(MEMSET(onesb[:, :], 1.0), writes=["onesb"])
    epsc = s.sbuf("epsc", [128, 1], F32)
    s.pool(MEMSET(epsc[:, :], 1e-5), writes=["epsc"])
    wcat = s.sbuf("wcat", [128, 8 * 96], BF16)
    for c in range(8):
        s.dma("gpsimd", wcat[:, c * 96:c * 96 + 32], wdq[c * 128:(c + 1) * 128, :],
              writes=[f"wcat{c}a"], semkey="dq:wini", tag=f"wdq {c}")
        s.dma("gpsimd", wcat[:, c * 96 + 32:c * 96 + 64], wdkv[c * 128:(c + 1) * 128, :],
              writes=[f"wcat{c}b"], semkey="dq:wini", tag=f"wdkv {c}")
        s.dma("gpsimd", wcat[:, c * 96 + 64:c * 96 + 96], wkr[c * 128:(c + 1) * 128, :],
              writes=[f"wcat{c}c"], semkey="dq:wini", tag=f"wkr {c}")
    wcb = s.sbuf("wcb", [1, 96], BF16)
    s.dma("gpsimd", wcb[:, :], bcat[:, :], writes=["wcb"], semkey="dq:wini")
    WUQp = s.sbuf("WUQp", [33, 1024], BF16)
    WUKp = s.sbuf("WUKp", [33, 1024], BF16)
    WUVp = s.sbuf("WUVp", [33, 1024], BF16)
    WQRp = s.sbuf("WQRp", [33, 256], BF16)
    for nm, wt, bt, ww in (("WUQp", wuq, buq, WUQp), ("WUKp", wuk, buk, WUKp),
                           ("WUVp", wuv, buv, WUVp), ("WQRp", wqr, bqr, WQRp)):
        s.dma("gpsimd", ww[0:32, :], wt[:, :], writes=[nm], semkey="dq:wini", tag=nm)
        s.dma("gpsimd", ww[32:33, :], bt[:, :], writes=[nm + "b"], semkey="dq:wini", tag=nm + "b")
    WK = ["wcb", "WUQp", "WUQpb", "WUKp", "WUKpb", "WUVp", "WUVpb", "WQRp", "WQRpb",
          "onesb", "identb"] + [f"wcat{c}{x}" for c in range(8) for x in "abc"]

    xin = [s.sbuf(f"xin{i}", [128, D], F32) for i in range(2)]
    xnf = [s.sbuf(f"xnf{i}", [128, D], F32) for i in range(2)]
    xnb = [s.sbuf(f"xnb{i}", [128, D], BF16) for i in range(2)]
    xnT = [s.sbuf(f"xnT{i}", [128, D], BF16) for i in range(2)]
    stt = s.sbuf("stt", [128, 12], F32)
    ag = s.sbuf("ag", [128, 2], F32)
    rstd = s.sbuf("rstd", [128, 1], F32)
    st2 = s.sbuf("st2", [128, 12], F32)
    ag2 = s.sbuf("ag2", [128, 2], F32)
    mus = [s.sbuf(f"mus{i}", [128, 2], F32) for i in range(2)]
    cqb = [s.sbuf(f"cqb{i}", [128, 96], BF16) for i in range(2)]
    cqT = [s.sbuf(f"cqT{i}", [33, 128], BF16) for i in range(2)]
    ckvT = [s.sbuf(f"ckvT{i}", [33, 128], BF16) for i in range(2)]
    krT = [s.sbuf(f"krT{i}", [32, 128], BF16) for i in range(2)]
    qstage = [s.sbuf(f"qstage{i}", [128, 1024], BF16) for i in range(2)]
    kstage = [s.sbuf(f"kstage{i}", [128, 1024], BF16) for i in range(2)]
    qrstage = [s.sbuf(f"qrstage{i}", [32, 1024], BF16) for i in range(2)]
    vstage = [s.sbuf(f"vstage{i}", [128, 1024], BF16) for i in range(2)]
    ptb = [s.psum(f"ptb{i}", [128, 1024], BF16) for i in range(2)]
    p96 = s.psum("p96", [128, 512], F32)
    pq = [s.psum(f"pq{i}", [128, 512], F32) for i in range(2)]
    pqr = s.psum("pqr", [128, 512], F32)
    pv = [s.psum(f"pv{i}", [128, 512], F32) for i in range(2)]

    for i in range(2):
        s.pool(MEMSET(cqT[i][32:33, :], 1.0), writes=[f"cqT{i}_ones"])
        s.pool(MEMSET(ckvT[i][32:33, :], 1.0), writes=[f"ckvT{i}_ones"])

    for t in range(NT):
        p = t % 2
        b, tc = t // 32, t % 32
        xi, xik = xin[p], f"xin{p}"
        xf, xfk = xnf[p], f"xnf{p}"
        xb, xbk = xnb[p], f"xnb{p}"
        xT, xTk = xnT[p], "xnTA"
        r0 = t * 128
        s.dma("sync", xi[:, :], x_in[r0:r0 + 128, :], writes=[xik], tag=f"x ld {t}")
        s.dve(lambda e, a=stt[:, 0:6], b_=xi[:, 0:512]: e.bn_stats(a, b_),
              reads=[xik], writes=["stt_a"], tag=f"bn0a {t}")
        s.dve(lambda e, a=stt[:, 6:12], b_=xi[:, 512:1024]: e.bn_stats(a, b_),
              reads=[xik], writes=["stt_b"], tag=f"bn0b {t}")
        s.dve(lambda e, a=ag[:, :], b_=stt[:, 0:12]: e.bn_aggr(a, b_),
              reads=["stt_a", "stt_b"], writes=["ag"], tag=f"aggr0 {t}")
        s.act(ACTV(rstd[:, 0:1], ag[:, 1:2], AF.Ln, bias=epsc[:, 0:1]), reads=["ag", "epsc"],
              writes=["rstd_ln"], tag=f"ln0 {t}")
        s.act(ACTV(rstd[:, 0:1], rstd[:, 0:1], AF.Exp, scale=-0.5), reads=["rstd_ln"],
              writes=["rstd"], tag=f"exp0 {t}")
        s.dve(TS(xf[:, :], xi[:, :], ag[:, 0:1], rstd[:, 0:1], ALU.subtract, ALU.mult),
              reads=[xik, "ag", "rstd"], writes=[xfk], tag=f"xn {t}")
        s.dma("gpsimd", xn_o[r0:r0 + 128, :], xf[:, :], reads=[xfk],
              semkey="dq:st:xn", tag=f"xn st {t}")
        s.dve(lambda e, a=st2[:, 0:6], b_=xf[:, 0:512]: e.bn_stats(a, b_),
              reads=[xfk], writes=["st2_a"], tag=f"bn2a {t}")
        s.dve(lambda e, a=st2[:, 6:12], b_=xf[:, 512:1024]: e.bn_stats(a, b_),
              reads=[xfk], writes=["st2_b"], tag=f"bn2b {t}")
        s.dve(lambda e, a=ag2[:, :], b_=st2[:, 0:12]: e.bn_aggr(a, b_),
              reads=["st2_a", "st2_b"], writes=["ag2"], tag=f"aggr2 {t}")
        mu, muk = mus[p], f"mus{p}"
        s.act(ACTV(mu[:, 1:2], ag2[:, 1:2], AF.Ln, bias=epsc[:, 0:1]), reads=["ag2", "epsc"],
              writes=[muk + "ln"], tag=f"ln2 {t}")
        s.act(ACTV(mu[:, 1:2], mu[:, 1:2], AF.Exp, scale=-0.5), reads=[muk + "ln"],
              writes=[muk + "r"], tag=f"exp2 {t}")
        s.act(SCOPY(mu[:, 0:1], ag2[:, 0:1]), reads=["ag2"], writes=[muk + "m"],
              tag=f"mu2 {t}")
        s.dma("gpsimd", musig_o[r0:r0 + 128, :], mu[:, :], reads=[muk + "r", muk + "m"],
              semkey="dq:st:mus", tag=f"mus st {t}")
        s.dve(COPY(xb[:, :], xf[:, :]), reads=[xfk], writes=[xbk], tag=f"cast {t}")
        for c in range(8):
            pt_ = ptb[c % 2]
            ptk = f"ptb{c % 2}"
            s.te(TRANS(pt_[:, 0:128], xb[:, c * 128:(c + 1) * 128], identb[:, :]),
                 reads=[xbk, "identb", ptk + "_r"], writes=[ptk], tag=f"trx {t},{c}")
            s.act(SCOPY(xT[:, c * 128:(c + 1) * 128], pt_[:, 0:128]),
                  reads=[ptk], writes=[xTk + f"_{c}", ptk + "_r"], tag=f"evx {t},{c}")
        xTkeys = [xTk + f"_{c}" for c in range(8)]
        for c in range(8):
            s.te(MM(p96[:, 0:96], xT[:, c * 128:(c + 1) * 128], wcat[:, c * 96:(c + 1) * 96],
                    c == 0, False), reads=xTkeys + WK, writes=["p96"], tag=f"p1 {t},{c}")
        s.te(MM(p96[:, 0:96], onesb[0:1, :], wcb[0:1, :], False, True),
             reads=WK, writes=["p96"], tag=f"p1b {t}")
        cq, cqk = cqb[p], f"cqb{p}"
        s.dve(COPY(cq[:, :], p96[:, 0:96]), reads=["p96"], writes=[cqk], tag=f"cq ev {t}")
        c3 = [(cqT[p], f"cqT{p}", 0), (ckvT[p], f"ckvT{p}", 32), (krT[p], f"krT{p}", 64)]
        for (dstT, dstk, off) in c3:
            pt_ = ptb[0]
            s.te(TRANS(pt_[0:32, 0:128], cq[:, off:off + 32], identb[:, :]),
                 reads=[cqk, "identb", "ptb0_r"], writes=["ptb0"], tag=f"trc {t},{off}")
            s.act(SCOPY(dstT[0:32, :], pt_[0:32, 0:128]), reads=["ptb0"],
                  writes=[dstk, "ptb0_r"], tag=f"evc {t},{off}")
        qs, qsk = qstage[p], f"qstage{p}"
        ks, ksk = kstage[p], f"kstage{p}"
        qr_, qrk = qrstage[p], f"qrstage{p}"
        vs, vsk = vstage[p], f"vstage{p}"
        for h in range(8):
            pqh = pq[h % 2]
            pqhk = f"pq{h % 2}"
            s.te(MM(pqh[:, 0:128], WUQp[:, h * 128:(h + 1) * 128], cqT[p][:, :], True, True),
                 reads=[f"cqT{p}", f"cqT{p}_ones", pqhk + "_r"] + WK, writes=[pqhk],
                 tag=f"qc {t},{h}")
            s.act(SCOPY(qs[:, h * 128:(h + 1) * 128], pqh[:, 0:128]), reads=[pqhk],
                  writes=[qsk + f"_{h}", pqhk + "_r"], tag=f"qc ev {t},{h}")
        for h in range(8):
            pqh = pq[h % 2]
            pqhk = f"pq{h % 2}"
            s.te(MM(pqh[:, 0:128], WUKp[:, h * 128:(h + 1) * 128], ckvT[p][:, :], True, True),
                 reads=[f"ckvT{p}", f"ckvT{p}_ones", pqhk + "_r"] + WK, writes=[pqhk],
                 tag=f"kc {t},{h}")
            s.dve(COPY(ks[:, h * 128:(h + 1) * 128], pqh[:, 0:128]), reads=[pqhk],
                  writes=[ksk + f"_{h}", pqhk + "_r"], tag=f"kc ev {t},{h}")
        for h in range(8):
            s.te(MM(pqr[0:32, 0:128], WQRp[:, h * 32:(h + 1) * 32], cqT[p][:, :], True, True),
                 reads=[f"cqT{p}", f"cqT{p}_ones", "pqr_r"] + WK, writes=["pqr"],
                 tag=f"qr {t},{h}")
            s.dve(COPY(qr_[0:32, h * 128:(h + 1) * 128], pqr[0:32, 0:128]), reads=["pqr"],
                  writes=[qrk + f"_{h}", "pqr_r"], tag=f"qr ev {t},{h}")
        for nb in range(2):
            pvb = pv[nb]
            pvbk = f"pv{nb}"
            s.te(MM(pvb[:, :], ckvT[p][:, :], WUVp[:, nb * 512:(nb + 1) * 512], True, True),
                 reads=[f"ckvT{p}", f"ckvT{p}_ones", pvbk + "_r"] + WK, writes=[pvbk],
                 tag=f"v {t},{nb}")
            s.dve(COPY(vs[:, nb * 512:(nb + 1) * 512], pvb[:, :]), reads=[pvbk],
                  writes=[vsk + f"_{nb}", pvbk + "_r"], tag=f"v ev {t},{nb}")
        qkeys = [qsk + f"_{h}" for h in range(8)]
        kkeys = [ksk + f"_{h}" for h in range(8)]
        qrkeys = [qrk + f"_{h}" for h in range(8)]
        vkeys = [vsk + f"_{nb}" for nb in range(2)]
        boff = b * 8 * 128 * 4096
        s.dma("sync",
              bass.AP(QT1, boff + tc * 128, [[4096, 128], [128 * 4096, 8], [1, 128]]),
              bass.AP(qs, 0, [[1024, 128], [128, 8], [1, 128]]),
              reads=qkeys, semkey="dq:st:q", tag=f"QT1 st {t}")
        s.dma("sync",
              bass.AP(KT1, boff + tc * 128, [[4096, 128], [128 * 4096, 8], [1, 128]]),
              bass.AP(ks, 0, [[1024, 128], [128, 8], [1, 128]]),
              reads=kkeys, semkey="dq:st:k", tag=f"KT1 st {t}")
        s.dma("sync",
              bass.AP(QT2, b * 8 * 32 * 4096 + tc * 128, [[4096, 32], [32 * 4096, 8], [1, 128]]),
              bass.AP(qr_, 0, [[1024, 32], [128, 8], [1, 128]]),
              reads=qrkeys, semkey="dq:st:qr", tag=f"QT2 st {t}")
        s.dma("sync",
              bass.AP(KT2, b * 32 * 4096 + tc * 128, [[4096, 32], [1, 128]]),
              bass.AP(krT[p], 0, [[128, 32], [1, 128]]),
              reads=[f"krT{p}"], semkey="dq:st:kr", tag=f"KT2 st {t}")
        s.dma("sync", V_o[r0:r0 + 128, :], vs[:, :], reads=vkeys,
              semkey="dq:st:v", tag=f"V st {t}")



def emit_l0attn(s, QT1, QT2, KT1, KT2, V_i, OT):
    onesc = s.sbuf("onesc", [128, 1], BF16)
    s.pool(MEMSET(onesc[:, :], 1.0), writes=["onesc"])
    onesf = s.sbuf("onesf", [1, 128], F32)
    s.pool(MEMSET(onesf[:, :], 1.0), writes=["onesf"])
    nC0 = s.sbuf("nC0", [128, 1], F32)
    s.pool(MEMSET(nC0[:, :], -C0), writes=["nC0"])

    kt1 = [s.sbuf(f"kt1_{i}", [128, 4096], BF16) for i in range(2)]
    kt2 = [s.sbuf(f"kt2_{i}", [32, 4096], BF16) for i in range(2)]
    vt = [s.sbuf(f"vt{i}", [128, 4096], BF16) for i in range(2)]
    qt1 = [s.sbuf(f"qt1_{i}", [128, 512], BF16) for i in range(2)]
    qt2 = [s.sbuf(f"qt2_{i}", [32, 512], BF16) for i in range(2)]
    pexp = [s.sbuf(f"pexp{i}", [128, 512], BF16) for i in range(2)]
    rden = s.sbuf("rden", [1, 512], F32)
    prs = s.sbuf("prs", [128, 512], F32)
    obuf = [s.sbuf(f"obuf{i}", [128, 512], BF16) for i in range(2)]

    ps = [s.psum(f"ps{i}", [128, 512], F32) for i in range(2)]
    po = s.psum("po", [128, 512], F32)
    pd = s.psum("pd", [128, 512], F32)
    pr = s.psum("pr", [128, 512], F32)

    for u in range(16):
        b, h = u // 8, u % 8
        dp = u % 2
        k1, k1k = kt1[dp], f"kt1_{dp}"
        k2, k2k = kt2[dp], f"kt2_{dp}"
        v_, vk = vt[dp], f"vt{dp}"
        row = (b * 8 + h) * 128
        s.dma("sync", k1[:, :], KT1[row:row + 128, :], writes=[k1k], tag=f"kt1 {u}")
        s.dma("sync", k2[:, :], KT2[b * 32:(b + 1) * 32, :], writes=[k2k], tag=f"kt2 {u}")
        s.dma("sync",
              bass.AP(vt[dp], 0, [[4096, 128], [128, 32], [1, 128]]),
              bass.AP(V_i, b * 4096 * 1024 + h * 128, [[1024, 128], [128 * 1024, 32], [1, 128]]),
              writes=[vk], tag=f"vt {u}")
        for g in range(8):
            qp = g % 2
            q1, q1k = qt1[qp], f"qt1_{qp}"
            q2, q2k = qt2[qp], f"qt2_{qp}"
            qrow = (b * 8 + h) * 128
            s.dma("sync", q1[:, :], QT1[qrow:qrow + 128, g * 512:(g + 1) * 512],
                  writes=[q1k], tag=f"qt1 {u},{g}")
            s.dma("sync", q2[:, :], QT2[(b * 8 + h) * 32:(b * 8 + h) * 32 + 32, g * 512:(g + 1) * 512],
                  writes=[q2k], tag=f"qt2 {u},{g}")
            for j in range(32):
                sp = j % 2
                psb, psk = ps[sp], f"ps{sp}"
                pe, pek = pexp[sp], f"pexp{sp}"
                s.te(MM(psb[:, :], k1[:, j * 128:(j + 1) * 128], q1[:, :], True, False),
                     reads=[k1k, q1k, psk + "_r"], writes=[psk], tag=f"s1 {u},{g},{j}")
                s.te(MM(psb[:, :], k2[:, j * 128:(j + 1) * 128], q2[:, :], False, True),
                     reads=[k2k, q2k], writes=[psk], tag=f"s2 {u},{g},{j}")
                s.act(ACTV(pe[:, :], psb[:, :], AF.Exp, bias=nC0[:, 0:1], scale=SC0),
                      reads=[psk, "nC0"], writes=[pek, psk + "_r"], tag=f"exp {u},{g},{j}")
                s.te(MM(po[:, :], v_[:, j * 128:(j + 1) * 128], pe[:, :], j == 0, j == 31),
                     reads=[vk, pek, "po_r"] if j == 0 else [vk, pek], writes=["po"],
                     tag=f"pv {u},{g},{j}")
                s.te(MM(pd[0:1, :], onesc[:, 0:1], pe[:, :], j == 0, j == 31),
                     reads=["onesc", pek, "pd_r"] if j == 0 else ["onesc", pek],
                     writes=["pd"], tag=f"den {u},{g},{j}")
            s.dve(lambda e, a=rden[0:1, :], b_=pd[0:1, :]: e.reciprocal(a, b_),
                  reads=["pd"], writes=["rden", "pd_r"], tag=f"rcp {u},{g}")
            s.te(MM(pr[:, :], onesf[0:1, :], rden[0:1, :], True, True),
                 reads=["onesf", "rden", "pr_r"], writes=["pr"], tag=f"rep {u},{g}")
            s.act(SCOPY(prs[:, :], pr[:, :]), reads=["pr"], writes=["prs", "pr_r"],
                  tag=f"rev {u},{g}")
            ob, obk = obuf[g % 2], f"obuf{g % 2}"
            s.dve(TT(ob[:, :], po[:, :], prs[:, :], ALU.mult),
                  reads=["po", "prs"], writes=[obk, "po_r"], tag=f"omul {u},{g}")
            s.dma("gpsimd", OT[row:row + 128, g * 512:(g + 1) * 512], ob[:, :],
                  reads=[obk], semkey=f"dq:st:ot{g % 2}", tag=f"ot st {u},{g}")



def emit_l0rest(s, x_in, xn_i, mus_i, OT, WO, bO, gam, bet, w1d, w2d, fb1, fb2, rwT_i, rb_i, identb_in, x1_o, xn1_o, rt_o):
    identb = s.sbuf("identb_sb", [128, 128], BF16)
    s.dma("sync", identb[:, :], identb_in[:, :], writes=["identb"])
    onesb = s.sbuf("onesb", [1, 128], BF16)
    s.pool(MEMSET(onesb[:, :], 1.0), writes=["onesb"])
    epsc = s.sbuf("epsc", [128, 1], F32)
    s.pool(MEMSET(epsc[:, :], 1e-5), writes=["epsc"])
    w1sb = s.sbuf("w1sb", [128, 8 * 4096], BF16)
    for c in range(8):
        s.dma("sync", w1sb[:, c * 4096:(c + 1) * 4096], w1d[c * 128:(c + 1) * 128, :],
              writes=[f"w1sb{c}"], semkey="dq:w1", tag=f"w1 {c}")
    W1K = [f"w1sb{c}" for c in range(8)]
    WOsb = s.sbuf("WOsb", [128, 8 * 1024], BF16)
    for c in range(8):
        s.dma("gpsimd", WOsb[:, c * 1024:(c + 1) * 1024], WO[c * 128:(c + 1) * 128, :],
              writes=[f"WOsb{c}"], semkey="dq:wini", tag=f"WO {c}")
    WOK = [f"WOsb{c}" for c in range(8)]
    bOsb = s.sbuf("bOsb", [1, 1024], BF16)
    s.dma("gpsimd", bOsb[:, :], bO[:, :], writes=["bOsb"], semkey="dq:wini")
    fb2sb = s.sbuf("fb2sb", [1, 1024], BF16)
    s.dma("gpsimd", fb2sb[:, :], fb2[:, :], writes=["fb2sb"], semkey="dq:wini")
    rbsb = s.sbuf("rbsb", [1, 8], BF16)
    s.dma("gpsimd", rbsb[:, :], rb_i[:, :], writes=["rbsb"], semkey="dq:wini")
    rwsb = s.sbuf("rwsb", [128, 64], BF16)
    for c in range(8):
        s.dma("gpsimd", rwsb[:, c * 8:(c + 1) * 8], rwT_i[c * 128:(c + 1) * 128, :],
              writes=[f"rwsb{c}"], semkey="dq:wini", tag=f"rw {c}")
    RWK = [f"rwsb{c}" for c in range(8)]
    fb1r = s.sbuf("fb1r", [32, 128], F32)
    s.dma("sync", fb1r[:, :], fb1[:, :], writes=["fb1r"])
    fb1c = s.sbuf("fb1c", [128, 32], F32)
    identf2 = s.sbuf("identf2", [128, 128], F32)
    s.pool(lambda e: e.memset(identf2[:, :], 0.0), writes=["identf2z"])
    s.dve(COPY(identf2[:, :], identb[:, :]), reads=["identb", "identf2z"], writes=["identf2"])
    grow = s.sbuf("grow", [1, 1024], F32)
    brow = s.sbuf("brow", [1, 1024], F32)
    s.dma("sync", grow[:, :], gam[:, :], writes=["grow"])
    s.dma("sync", brow[:, :], bet[:, :], writes=["brow"])
    onesf = s.sbuf("onesf", [1, 128], F32)
    s.pool(MEMSET(onesf[:, :], 1.0), writes=["onesf"])
    grep = s.sbuf("grep", [128, 1024], F32)
    brep = s.sbuf("brep", [128, 1024], F32)

    pso = [s.psum(f"pso{i}", [128, 512], F32) for i in range(2)]
    ptb = [s.psum(f"ptb{i}", [128, 1024], BF16) for i in range(2)]
    pyt = [s.psum(f"pyt{i}", [128, 512], F32) for i in range(2)]
    pz = [s.psum(f"pz{i}", [128, 512], F32) for i in range(2)]

    for nb in range(2):
        s.te(MM(pso[nb][:, :], onesf[0:1, :], grow[0:1, nb * 512:(nb + 1) * 512], True, True),
             reads=["onesf", "grow", f"pso{nb}_r"], writes=[f"pso{nb}"], tag=f"grep {nb}")
        s.act(SCOPY(grep[:, nb * 512:(nb + 1) * 512], pso[nb][:, :]), reads=[f"pso{nb}"],
              writes=[f"grep{nb}", f"pso{nb}_r"], tag=f"grep ev {nb}")
        s.te(MM(pso[nb][:, :], onesf[0:1, :], brow[0:1, nb * 512:(nb + 1) * 512], True, True),
             reads=["onesf", "brow", f"pso{nb}_r"], writes=[f"pso{nb}"], tag=f"brep {nb}")
        s.act(SCOPY(brep[:, nb * 512:(nb + 1) * 512], pso[nb][:, :]), reads=[f"pso{nb}"],
              writes=[f"brep{nb}", f"pso{nb}_r"], tag=f"brep ev {nb}")
    GBK = ["grep0", "grep1", "brep0", "brep1"]
    s.te(TRANS(pyt[0][:, 0:32], fb1r[:, :], identf2[0:32, 0:32]),
         reads=["fb1r", "identf2", "pyt0_r"], writes=["pyt0"], tag="fb1 tr")
    s.act(SCOPY(fb1c[:, :], pyt[0][:, 0:32]), reads=["pyt0"], writes=["fb1c", "pyt0_r"],
          tag="fb1 ev")

    oin = [s.sbuf(f"oin{i}", [128, 1024], BF16) for i in range(2)]
    sA = s.sbuf("sA", [128, 4 * 1024], BF16)
    xng = s.sbuf("xng", [128, 4 * 1024], F32)
    xg = [s.sbuf(f"xg{i}", [128, 1024], F32) for i in range(2)]
    msg = s.sbuf("msg", [128, 4 * 2], F32)
    hb = [s.sbuf(f"hb{i}", [128, 1024], BF16) for i in range(2)]
    hfp = [s.sbuf("hfp0", [128, 1024], F32)] * 2
    hTg = s.sbuf("hTg", [128, 8 * 512], BF16)
    ybT = s.sbuf("ybT", [128, 32 * 512], BF16)
    w2s = [s.sbuf(f"w2s{i}", [128, 1024], BF16) for i in range(2)]
    gp = [s.sbuf("gp0", [128, 1024], F32)] * 2
    x1t = [s.sbuf("x1t0", [128, 1024], F32)] * 2
    xn1t = [s.sbuf("xn1t0", [128, 1024], F32)] * 2
    xn1b = s.sbuf("xn1b", [128, 1024], BF16)
    xnT1 = s.sbuf("xnT1", [128, 1024], BF16)
    stt = s.sbuf("stt", [128, 12], F32)
    ag = s.sbuf("ag", [128, 2], F32)
    rstd = s.sbuf("rstd", [128, 1], F32)
    pe8 = s.sbuf("pe8", [128, 8], F32)
    nmx = s.sbuf("nmx", [128, 1], F32)
    sden = s.sbuf("sden", [128, 1], F32)
    probs = s.sbuf("probs", [128, 8], F32)
    tv8 = s.sbuf("tv8", [128, 8], F32)
    ti8 = s.sbuf("ti8", [128, 8], U32)
    r2 = s.sbuf("r2", [128, 1], F32)
    rteb = [s.sbuf(f"rteb{i}", [128, 4], F32) for i in range(2)]

    eps9 = s.sbuf("eps9", [128, 1], F32)
    s.pool(MEMSET(eps9[:, :], 1e-9), writes=["eps9"])

    for grp in range(16):
        hTkeys = []
        for ti in range(4):
            t = grp * 4 + ti
            p = t % 2
            b, tc = t // 32, t % 32
            r0 = t * 128
            oi, oik = oin[p], f"oin{p}"
            s.dma("sync",
                  bass.AP(oi, 0, [[1024, 128], [128, 8], [1, 128]]),
                  bass.AP(OT, b * 8 * 128 * 4096 + tc * 128, [[4096, 128], [128 * 4096, 8], [1, 128]]),
                  writes=[oik], tag=f"ot ld {t}")
            for nb in range(2):
                pp, ppk = pso[nb], f"pso{nb}"
                for h in range(8):
                    s.te(MM(pp[:, :], oi[:, h * 128:(h + 1) * 128],
                            WOsb[:, h * 1024 + nb * 512:h * 1024 + (nb + 1) * 512],
                            h == 0, False),
                         reads=[oik, ppk + "_r"] + WOK, writes=[ppk], tag=f"wo {t},{nb},{h}")
                s.te(MM(pp[:, :], onesb[0:1, :], bOsb[0:1, nb * 512:(nb + 1) * 512], False, True),
                     reads=["onesb", "bOsb"], writes=[ppk], tag=f"wob {t},{nb}")
                s.act(SCOPY(sA[:, ti * 1024 + nb * 512: ti * 1024 + (nb + 1) * 512], pp[:, :]),
                      reads=[ppk], writes=[f"sA{ti}_{nb}", ppk + "_r"], tag=f"wo ev {t},{nb}")
            s.dma("sync", xng[:, ti * 1024:(ti + 1) * 1024], xn_i[r0:r0 + 128, :],
                  writes=[f"xng{ti}"], semkey="dq:xng", tag=f"xn ld {t}")
            s.dma("sync", msg[:, ti * 2:(ti + 1) * 2], mus_i[r0:r0 + 128, :],
                  writes=[f"msg{ti}"], semkey="dq:msg", tag=f"mus ld {t}")
            hf, hfk = hfp[p], "hfp0"
            s.dve(TS(hf[:, :], xng[:, ti * 1024:(ti + 1) * 1024], msg[:, ti * 2:ti * 2 + 1],
                     msg[:, ti * 2 + 1:ti * 2 + 2], ALU.subtract, ALU.mult),
                  reads=[f"xng{ti}", f"msg{ti}"], writes=[hfk], tag=f"hln {t}")
            s.dve(TT(hf[:, :], hf[:, :], grep[:, :], ALU.mult), reads=[hfk] + GBK,
                  writes=[hfk], tag=f"hg {t}")
            hbb, hbk = hb[p], f"hb{p}"
            s.dve(TT(hbb[:, :], hf[:, :], brep[:, :], ALU.add), reads=[hfk] + GBK,
                  writes=[hbk], tag=f"hbeta {t}")
            for c in range(8):
                pt_, ptk = ptb[c % 2], f"ptb{c % 2}"
                s.te(TRANS(pt_[:, 0:128], hbb[:, c * 128:(c + 1) * 128], identb[:, :]),
                     reads=[hbk, "identb", ptk + "_r"], writes=[ptk], tag=f"trh {t},{c}")
                s.act(SCOPY(hTg[:, c * 512 + ti * 128: c * 512 + (ti + 1) * 128], pt_[:, 0:128]),
                      reads=[ptk], writes=[f"hTg{c}_{ti}", ptk + "_r"], tag=f"evh {t},{c}")
                hTkeys.append(f"hTg{c}_{ti}")
        for yt in range(32):
            pp, ppk = pyt[yt % 2], f"pyt{yt % 2}"
            for c in range(8):
                s.te(MM(pp[:, :], w1sb[:, c * 4096 + yt * 128: c * 4096 + (yt + 1) * 128],
                        hTg[:, c * 512:(c + 1) * 512], c == 0, c == 7),
                     reads=hTkeys + W1K + [ppk + "_r"], writes=[ppk], tag=f"fc1 {grp},{yt},{c}")
            s.act(ACTV(ybT[:, yt * 512:(yt + 1) * 512], pp[:, :], AF.Silu,
                       bias=fb1c[:, yt:yt + 1]),
                  reads=[ppk, "fb1c"], writes=[f"ybT{yt}", ppk + "_r"], tag=f"silu {grp},{yt}")
        for half in range(2):
          zbank = [pz[0], pz[1], pyt[0], pyt[1]]
          zbk = ["pz0", "pz1", "pyt0", "pyt1"]
          for yt in range(32):
            wsl, wslk = w2s[yt % 2], f"w2s{yt % 2}"
            s.dma("sync", wsl[:, :], w2d[yt * 128:(yt + 1) * 128, :],
                  writes=[wslk], semkey=f"dq:{wslk}", tag=f"w2 ld {grp},{half},{yt}")
            for q in range(4):
                ti = half * 2 + q // 2
                nb = q % 2
                s.te(MM(zbank[q][:, :], ybT[:, yt * 512 + ti * 128: yt * 512 + (ti + 1) * 128],
                        wsl[:, nb * 512:(nb + 1) * 512], yt == 0, False),
                     reads=[f"ybT{yt}", wslk], writes=[zbk[q]],
                     tag=f"fc2 {grp},{half},{yt},{q}")
          for q in range(4):
            nb = q % 2
            s.te(MM(zbank[q][:, :], onesb[0:1, :], fb2sb[0:1, nb * 512:(nb + 1) * 512],
                    False, True), reads=["onesb", "fb2sb"], writes=[zbk[q]],
                 tag=f"fc2b {grp},{half},{q}")
          for tq in range(2):
            ti = half * 2 + tq
            t = grp * 4 + ti
            p = t % 2
            r0 = t * 128
            xgt, xgk = xg[p], f"xg{p}"
            s.dma("sync", xgt[:, :], x_in[r0:r0 + 128, :], writes=[xgk], tag=f"x ld {t}")
            gpt, gpk = gp[p], "gp0"
            for nb in range(2):
                q = tq * 2 + nb
                s.dve(lambda e, o=gpt[:, nb * 512:(nb + 1) * 512], z=zbank[q][:, :],
                      xn_=xng[:, ti * 1024 + nb * 512:ti * 1024 + (nb + 1) * 512]:
                      e.scalar_tensor_tensor(o, z, 0.5, xn_, ALU.mult, ALU.add),
                      reads=[zbk[q], f"xng{ti}"], writes=[gpk + f"_{nb}"],
                      tag=f"gp {t},{nb}")
            s.act(ACTV(gpt[:, :], gpt[:, :], AF.Silu), reads=[gpk + "_0", gpk + "_1"],
                  writes=[gpk], tag=f"g {t}")
            s.dve(TT(gpt[:, :], gpt[:, :], sA[:, ti * 1024:(ti + 1) * 1024], ALU.mult),
                  reads=[gpk, f"sA{ti}_0", f"sA{ti}_1"], writes=[gpk], tag=f"sg {t}")
            x1b, x1k = x1t[p], "x1t0"
            s.dve(TT(x1b[:, :], gpt[:, :], xgt[:, :], ALU.add), reads=[gpk, xgk],
                  writes=[x1k], tag=f"x1 {t}")
            s.dma("gpsimd", x1_o[r0:r0 + 128, :], x1b[:, :], reads=[x1k],
                  semkey="dq:st:x1", tag=f"x1 st {t}")
            s.dve(lambda e, a=stt[:, 0:6], b_=x1b[:, 0:512]: e.bn_stats(a, b_),
                  reads=[x1k], writes=["stt_a"], tag=f"bn1a {t}")
            s.dve(lambda e, a=stt[:, 6:12], b_=x1b[:, 512:1024]: e.bn_stats(a, b_),
                  reads=[x1k], writes=["stt_b"], tag=f"bn1b {t}")
            s.dve(lambda e, a=ag[:, :], b_=stt[:, 0:12]: e.bn_aggr(a, b_),
                  reads=["stt_a", "stt_b"], writes=["ag"], tag=f"aggr1 {t}")
            s.act(ACTV(rstd[:, 0:1], ag[:, 1:2], AF.Ln, bias=epsc[:, 0:1]),
                  reads=["ag", "epsc"], writes=["rstd_ln"], tag=f"ln1 {t}")
            s.act(ACTV(rstd[:, 0:1], rstd[:, 0:1], AF.Exp, scale=-0.5), reads=["rstd_ln"],
                  writes=["rstd"], tag=f"exp1 {t}")
            xnt, xntk = xn1t[p], "xn1t0"
            s.dve(TS(xnt[:, :], x1b[:, :], ag[:, 0:1], rstd[:, 0:1], ALU.subtract, ALU.mult),
                  reads=[x1k, "ag", "rstd"], writes=[xntk], tag=f"xn1 {t}")
            s.dma("gpsimd", xn1_o[r0:r0 + 128, :], xnt[:, :], reads=[xntk],
                  semkey="dq:st:xn1", tag=f"xn1 st {t}")
            s.dve(COPY(xn1b[:, :], xnt[:, :]), reads=[xntk], writes=["xn1b"], tag=f"cast1 {t}")
            for c in range(8):
                pt_, ptk = ptb[c % 2], f"ptb{c % 2}"
                s.te(TRANS(pt_[:, 0:128], xn1b[:, c * 128:(c + 1) * 128], identb[:, :]),
                     reads=["xn1b", "identb", ptk + "_r"], writes=[ptk], tag=f"tr1 {t},{c}")
                s.act(SCOPY(xnT1[:, c * 128:(c + 1) * 128], pt_[:, 0:128]),
                      reads=[ptk], writes=[f"xnT1_{c}", ptk + "_r"], tag=f"ev1 {t},{c}")
            XK = [f"xnT1_{c}" for c in range(8)]
            pp, ppk = pz[0], "pz0"
            for c in range(8):
                s.te(MM(pp[:, 0:8], xnT1[:, c * 128:(c + 1) * 128], rwsb[:, c * 8:(c + 1) * 8],
                        c == 0, False), reads=XK + RWK + [ppk + "_r"], writes=[ppk],
                     tag=f"rlog {t},{c}")
            s.te(MM(pp[:, 0:8], onesb[0:1, :], rbsb[0:1, :], False, True),
                 reads=["onesb", "rbsb"], writes=[ppk], tag=f"rlogb {t}")
            s.dve(lambda e, a=nmx[:, 0:1], b_=pp[:, 0:8]: e.reduce_max(a, b_, axis=mybir.AxisListType.X, negate=True),
                  reads=[ppk], writes=["nmx"], tag=f"rmax {t}")
            s.act(ACTV(pe8[:, :], pp[:, 0:8], AF.Exp, bias=nmx[:, 0:1]),
                  reads=[ppk, "nmx"], writes=["pe8", ppk + "_r"], tag=f"rexp {t}")
            s.dve(lambda e, a=sden[:, 0:1], b_=pe8[:, :]: e.reduce_sum(a, b_, axis=mybir.AxisListType.X),
                  reads=["pe8"], writes=["sden"], tag=f"rsum {t}")
            s.dve(lambda e, a=sden[:, 0:1], b_=sden[:, 0:1]: e.reciprocal(a, b_),
                  reads=["sden"], writes=["sden"], tag=f"rrcp {t}")
            s.dve(TS(probs[:, :], pe8[:, :], sden[:, 0:1], None, ALU.mult),
                  reads=["pe8", "sden"], writes=["probs"], tag=f"probs {t}")
            s.dve(lambda e, a=tv8[:, :], b_=probs[:, :]: e.max(out=a, in_=b_),
                  reads=["probs"], writes=["tv8"], tag=f"max8 {t}")
            s.dve(lambda e, a=ti8[:, :], b_=tv8[:, :], c_=probs[:, :]: e.max_index(a, b_, c_),
                  reads=["probs", "tv8"], writes=["ti8"], tag=f"mi8 {t}")
            s.dve(TT(r2[:, 0:1], tv8[:, 0:1], tv8[:, 1:2], ALU.add), reads=["tv8"],
                  writes=["r2"], tag=f"r2a {t}")
            s.dve(TT(r2[:, 0:1], r2[:, 0:1], eps9[:, 0:1], ALU.add), reads=["r2", "eps9"],
                  writes=["r2"], tag=f"r2b {t}")
            s.dve(lambda e, a=r2[:, 0:1], b_=r2[:, 0:1]: e.reciprocal(a, b_),
                  reads=["r2"], writes=["r2"], tag=f"r2c {t}")
            rt, rtk = rteb[p], f"rteb{p}"
            s.dve(COPY(rt[:, 0:2], ti8[:, 0:2]), reads=["ti8"], writes=[rtk + "i"],
                  tag=f"rt i {t}")
            s.dve(TS(rt[:, 2:4], tv8[:, 0:2], r2[:, 0:1], None, ALU.mult),
                  reads=["tv8", "r2"], writes=[rtk + "w"], tag=f"rt w {t}")
            s.dma("gpsimd", rt_o[r0:r0 + 128, :], rt[:, :], reads=[rtk + "i", rtk + "w"],
                  semkey="dq:st:rt", tag=f"rt st {t}")



def emit_l1a(s, xn1_i, rt_i, Wq, bq, Wd, bd, Wu, bu, Wo, bo, cosr, sinr, mask_i, identb_in, s1_o, slab_o, erow_o, wrow_o, lrk_o, vscr):
    identb = s.sbuf("identb_sb", [128, 128], BF16)
    s.dma("sync", identb[:, :], identb_in[:, :], writes=["identb"])
    maskc = s.sbuf("maskc_sb", [128, 512], BF16)
    s.dma("sync", maskc[:, :], mask_i[:, :], writes=["maskc"])
    identf = s.sbuf("identf_sb", [128, 128], F32)
    s.dve(COPY(identf[:, :], identb[:, :]), reads=["identb"], writes=["identf"])
    onesb = s.sbuf("onesb", [1, 128], BF16)
    s.pool(MEMSET(onesb[:, :], 1.0), writes=["onesb"])
    onesf = s.sbuf("onesf", [1, 128], F32)
    s.pool(MEMSET(onesf[:, :], 1.0), writes=["onesf"])
    ones8 = s.sbuf("ones8", [8, 1], F32)
    s.pool(MEMSET(ones8[:, :], 1.0), writes=["ones8"])
    nC1 = s.sbuf("nC1", [128, 1], F32)
    s.pool(MEMSET(nC1[:, :], -C1), writes=["nC1"])
    Wqs = s.sbuf("Wqs", [128, 8 * 1024], BF16)
    Wds = s.sbuf("Wds", [128, 8 * 256], BF16)
    Wus = s.sbuf("Wus", [128, 2 * 2048], BF16)
    Wos = s.sbuf("Wos", [64, 16 * 1024], BF16)
    for c in range(8):
        s.dma("gpsimd", Wqs[:, c * 1024:(c + 1) * 1024], Wq[c * 128:(c + 1) * 128, :],
              writes=[f"Wqs{c}"], semkey="dq:wini", tag=f"Wq {c}")
        s.dma("gpsimd", Wds[:, c * 256:(c + 1) * 256], Wd[c * 128:(c + 1) * 128, :],
              writes=[f"Wds{c}"], semkey="dq:wini", tag=f"Wd {c}")
        s.dma("gpsimd", Wos[:, (2 * c) * 1024:(2 * c + 1) * 1024], Wo[c * 128:c * 128 + 64, :],
              writes=[f"Wos{c}"], semkey="dq:wini", tag=f"Wo {c}a")
        s.dma("gpsimd", Wos[:, (2 * c + 1) * 1024:(2 * c + 2) * 1024], Wo[c * 128 + 64:(c + 1) * 128, :],
              writes=[f"Wos{c}b"], semkey="dq:wini", tag=f"Wo {c}b")
    for c in range(2):
        s.dma("gpsimd", Wus[:, c * 2048:(c + 1) * 2048], Wu[c * 128:(c + 1) * 128, :],
              writes=[f"Wus{c}"], semkey="dq:wini", tag=f"Wu {c}")
    bqs = s.sbuf("bqs", [1, 1024], BF16)
    bds = s.sbuf("bds", [1, 256], BF16)
    bus = s.sbuf("bus", [1, 2048], BF16)
    bos = s.sbuf("bos", [1, 1024], BF16)
    for nm, src_, dst_ in (("bqs", bq, bqs), ("bds", bd, bds), ("bus", bu, bus), ("bos", bo, bos)):
        s.dma("gpsimd", dst_[0:1, :], src_[:, :], writes=[nm], semkey="dq:wini", tag=nm)
    WK = ([f"Wqs{c}" for c in range(8)] + [f"Wds{c}" for c in range(8)] +
          [f"Wos{c}" for c in range(8)] + [f"Wos{c}b" for c in range(8)] + ["Wus0", "Wus1", "bqs", "bds", "bus", "bos",
          "onesb", "identb"])

    rtt = [s.sbuf(f"rtt{i}", [128, 4], F32) for i in range(2)]
    rtw = s.sbuf("rtw", [128, 128], F32)
    wjt = [s.sbuf(f"wjt{j}", [1, 128], F32) for j in range(2)]
    idjt = [s.sbuf(f"idjt{j}", [1, 128], BF16) for j in range(2)]
    vmt = s.sbuf("vmt", [1, 128], mybir.dt.uint8)
    c99s = s.sbuf("c99s", [1, 128], BF16)
    s.pool(MEMSET(c99s[:, :], 99.0), writes=["c99s"])

    iems = s.sbuf("iems", [1, 128], BF16)
    iemc = s.sbuf("iemc", [1, 1024], BF16)
    prow = s.psum("prow", [128, 512], F32)
    for t in range(NT):
        ri, rik = rtt[t % 2], f"rtt{t % 2}"
        s.dma("sync", ri[:, :], rt_i[t * 128:(t + 1) * 128, :], writes=[rik], tag=f"rt ld {t}")
        s.dve(lambda e, o=bass.AP(rtw, 0, [[128, 128], [32, 4]]),
              i_=bass.AP(ri, 0, [[4, 128], [1, 4]]): e.tensor_copy(o, i_),
              reads=[rik], writes=["rtw"], tag=f"rt w {t}")
        s.te(TRANS(prow[:, 0:128], rtw[:, :], identf[:, :]),
             reads=["rtw", "identf", "prow_r"], writes=["prow"], tag=f"rt tr {t}")
        for j in range(2):
            s.act(SCOPY(idjt[j][0:1, :], prow[32 * j:32 * j + 1, 0:128]),
                  reads=["prow"], writes=[f"idjt{j}"], tag=f"rt ev i{j} {t}")
            s.act(SCOPY(wjt[j][0:1, :], prow[64 + 32 * j:64 + 32 * j + 1, 0:128]),
                  reads=["prow"], writes=[f"wjt{j}"] + (["prow_r"] if j == 1 else []),
                  tag=f"rt ev w{j} {t}")
            s.dma("gpsimd", wrow_o[j:j + 1, t * 128:(t + 1) * 128], wjt[j][0:1, :],
                  reads=[f"wjt{j}"], semkey="dq:st:wrow", tag=f"wr st {t},{j}")
            s.dve(TS(vmt[0:1, :], wjt[j][0:1, :], 1e-9, None, ALU.is_gt),
                  reads=[f"wjt{j}"], writes=["vmt"], tag=f"vmask {t},{j}")
            s.dve(lambda e, jj=j:
                  e.select(iems[0:1, :], vmt[0:1, :], idjt[jj][0:1, :], c99s[0:1, :]),
                  reads=["vmt", "c99s", f"idjt{j}"], writes=["iems"],
                  tag=f"iem {t},{j}")
            s.dma("gpsimd", erow_o[j:j + 1, t * 128:(t + 1) * 128], iems[0:1, :],
                  reads=["iems"], writes=["erowD"], semkey="dq:st:erow",
                  tag=f"erow st {t},{j}")
    iota8 = s.sbuf("iota8", [8, 1], F32)
    s.pool(lambda e: e.iota(iota8[:, :], [[1, 1]], base=0, channel_multiplier=1,
                            allow_small_or_imprecise_dtypes=True), writes=["iota8"])
    ones2x8 = s.sbuf("ones2x8", [1, 8], BF16)
    s.pool(MEMSET(ones2x8[:, :], 1.0), writes=["ones2x8"])
    sone = s.sbuf("sone", [8, 1], F32)
    s.pool(MEMSET(sone[:, :], 1.0), writes=["sone"])
    ohc = s.sbuf("ohc", [8, 1024], F32)
    incc = s.sbuf("incc", [8, 1024], F32)
    tmpc = s.sbuf("tmpc", [8, 1024], F32)
    carry = s.sbuf("carry", [8, 2], F32)
    lrkc = s.sbuf("lrkc", [1, 1024], F32)
    lrow = s.sbuf("lrow", [1, 128], F32)
    cntc = s.sbuf("cntc", [8, 4], F32)
    CH = 1024

    def build_oh(j, ch, tagp):
        s.dma("sync", iemc[0:1, 0:CH], erow_o[j:j + 1, ch * CH:(ch + 1) * CH],
              reads=["erowD"], writes=["iemc"], tag=f"{tagp} iemld {j},{ch}")
        for sc in range(CH // 512):
            c0 = ch * CH + sc * 512
            s.te(MM(prow[0:8, 0:512], ones2x8[0:1, :], iemc[0:1, sc * 512:(sc + 1) * 512],
                    True, True), reads=["iemc", "ones2x8", "prow_r"], writes=["prow"],
                 tag=f"{tagp} irep {j},{ch},{sc}")
            s.act(SCOPY(ohc[:, sc * 512:(sc + 1) * 512], prow[0:8, 0:512]),
                  reads=["prow"], writes=[f"ohc_{sc}", "prow_r"], tag=f"{tagp} irepe {j},{ch},{sc}")
        s.dve(TS(ohc[:, :], ohc[:, :], iota8[:, 0:1], None, ALU.is_equal),
              reads=[f"ohc_{sc}" for sc in range(CH // 512)] + ["iota8"], writes=["ohc", "s1sc"],
              tag=f"{tagp} oh {j},{ch}")

    for j in range(2):
        for ch in range(NTOK // CH):
            build_oh(j, ch, "p1")
            ini = 0.0 if ch == 0 else carry[:, j:j + 1]
            s.dve(lambda e, o=incc[:, :], d0=sone[:, 0:1].to_broadcast([8, CH]),
                  d1=ohc[:, :], i2=ini: e.tensor_tensor_scan(o, d0, d1, i2, op0=ALU.mult, op1=ALU.add),
                  reads=["ohc", "sone"] + (["carry"] if ch else []), writes=["incc", "s1sc"],
                  tag=f"scan {j},{ch}")
            s.dve(COPY(carry[:, j:j + 1], incc[:, CH - 1:CH]), reads=["incc"],
                  writes=["carry"], tag=f"carry {j},{ch}")
            s.dve(TS(tmpc[:, :], incc[:, :], -1.0, None, ALU.add), reads=["incc"],
                  writes=["tmpca", "s1sc"], tag=f"lrka {j},{ch}")
            s.dve(TT(tmpc[:, :], tmpc[:, :], ohc[:, :], ALU.mult), reads=["tmpca", "ohc"],
                  writes=["tmpc", "s1sc"], tag=f"lrkm {j},{ch}")
            for sc in range(CH // 512):
                s.te(MM(prow[0:1, 0:512], ones8[:, 0:1], tmpc[:, sc * 512:(sc + 1) * 512],
                        True, True), reads=["tmpc", "ones8", "prow_r"], writes=["prow"],
                     tag=f"lrk cs {j},{ch},{sc}")
                s.act(SCOPY(lrkc[0:1, sc * 512:(sc + 1) * 512], prow[0:1, 0:512]),
                      reads=["prow"], writes=[f"lrkc{sc}", "prow_r"],
                      tag=f"lrk ev {j},{ch},{sc}")
            s.dma("gpsimd", lrk_o[j:j + 1, ch * CH:(ch + 1) * CH], lrkc[0:1, 0:CH],
                  reads=[f"lrkc{sc}" for sc in range(CH // 512)], writes=["lrkD"],
                  semkey="dq:st:lrk", tag=f"lrk st {j},{ch}")
        s.dve(COPY(cntc[:, j:j + 1], carry[:, j:j + 1]), reads=["carry"],
              writes=[f"cnt{j}"], tag=f"cnt {j}")
    tot8 = s.sbuf("tot8", [8, 1], F32)
    trow = s.sbuf("trow", [1, 8], F32)
    s.dve(TT(tot8[:, 0:1], cntc[:, 0:1], cntc[:, 1:2], ALU.add), reads=["cnt0", "cnt1"],
          writes=["tot8"], tag="tot8")
    s.te(TRANS(prow[0:1, 0:8], tot8[:, 0:1], identf[0:8, 0:8]),
         reads=["tot8", "identf", "prow_r"], writes=["prow"], tag="tot tr")
    s.act(SCOPY(trow[0:1, :], prow[0:1, 0:8]), reads=["prow"], writes=["trow", "prow_r"],
          tag="tot ev")
    s.dve(lambda e: e.tensor_tensor_scan(trow[0:1, :], sone[0:1, 0:1].to_broadcast([1, 8]),
                                         trow[0:1, :], 0.0, op0=ALU.mult, op1=ALU.add),
          reads=["trow", "sone"], writes=["trow"], tag="tot scan")
    s.te(TRANS(prow[0:8, 0:1], trow[0:1, :], identf[0:1, 0:1]),
         reads=["trow", "identf", "prow_r"], writes=["prow"], tag="tot tr2")
    s.act(SCOPY(cntc[:, 2:3], prow[0:8, 0:1]), reads=["prow"], writes=["base_i", "prow_r"],
          tag="tot ev2")
    s.dve(TT(cntc[:, 2:3], cntc[:, 2:3], tot8[:, 0:1], ALU.subtract),
          reads=["base_i", "tot8"], writes=["base0"], tag="base0")
    s.dve(TT(cntc[:, 3:4], cntc[:, 2:3], cntc[:, 0:1], ALU.add),
          reads=["base0", "cnt0"], writes=["base1"], tag="base1")
    baserep = s.sbuf("baserep", [128, 16], F32)
    iotar = s.sbuf("iotar", [128, 8], F32)
    s.pool(lambda e: e.iota(iotar[:, :], [[1, 8]], base=0, channel_multiplier=0,
                            allow_small_or_imprecise_dtypes=True), writes=["iotar"])
    s.pool(lambda e: e.iota(trashc[:, :], [[1, 1]], base=16384, channel_multiplier=1,
                            allow_small_or_imprecise_dtypes=True), writes=["trashc"])
    for j in range(2):
        s.te(TRANS(prow[0:1, 0:8], cntc[:, 2 + j:3 + j], identf[0:8, 0:8]),
             reads=[f"base{j}", "identf", "prow_r"], writes=["prow"], tag=f"br tr {j}")
        s.act(SCOPY(trow[0:1, :], prow[0:1, 0:8]), reads=["prow"],
              writes=["trow", "prow_r"], tag=f"br ev {j}")
        s.dve(COPY(trow[0:1, :], trow[0:1, :]), reads=["trow"], writes=["trowb"],
              tag=f"br cp {j}")
        s.te(MM(prow[0:128, 0:8], onesf[0:1, :], trow[0:1, :], True, True),
             reads=["trowb", "onesf", "prow_r"], writes=["prow"], tag=f"br rep {j}")
        s.act(SCOPY(baserep[:, j * 8:(j + 1) * 8], prow[0:128, 0:8]), reads=["prow"],
              writes=[f"baserep{j}", "prow_r"], tag=f"br repe {j}")

    xnt = [s.sbuf("xnt0", [128, 1024], F32)] * 2
    xnb = [s.sbuf("xnbA", [128, 1024], BF16)] * 2
    xnT = [s.sbuf("xnTA", [128, 1024], BF16)] * 2
    qf = s.sbuf("qf", [128, 1024], F32)
    kvlb = s.sbuf("kvlb", [128, 256], BF16)
    kvlT = s.sbuf("kvlT", [128, 256], BF16)
    kf = s.sbuf("kf", [128, 1024], F32)
    cost1 = s.sbuf("cost1", [128, 1024], BF16)
    sint1 = s.sbuf("sint1", [128, 1024], BF16)
    swp = s.sbuf("swp", [128, 1024], F32)
    qb = s.sbuf("qb", [128, 1024], BF16)
    kb = s.sbuf("kb", [128, 1024], BF16)
    vb = [s.sbuf("vbA", [128, 1024], BF16)] * 2
    qTt = s.sbuf("qTt", [64, 2048], BF16)
    kTt = s.sbuf("kTt", [64, 2048], BF16)
    vstk = [s.sbuf("vstkA", [128, 16 * 65], BF16)] * 2
    pexp = s.sbuf("pexpa", [128, 512], BF16)
    orw = None
    oTt = s.sbuf("oTt", [64, 2048], BF16)
    oTh = s.sbuf("oTh", [64, 2048], BF16)
    s1b = [s.sbuf("s1b0", [128, 1024], F32)] * 2
    ecol = s.sbuf("ecol", [128, 2], F32)
    lcol = s.sbuf("lcol", [128, 2], F32)
    ohcol = s.sbuf("ohcol", [128, 8], F32)
    bcol = s.sbuf("bcol", [128, 2], F32)
    einvt = s.sbuf("einvt", [128, 2], F32)
    c99col = s.sbuf("c99col", [128, 2], F32)
    vm2c = s.sbuf("vm2c", [128, 2], mybir.dt.uint8)
    s.pool(MEMSET(c99col[:, :], 99.0), writes=["c99col"])
    pcol2 = s.sbuf("pcol2", [128, 2], F32)
    trashc = s.sbuf("trashc", [128, 1], F32)
    posu = s.sbuf("posu", [128, 2], U32)

    pq_ = [s.psum(f"pqa{i}", [128, 512], F32) for i in range(2)]
    pat = s.psum("pat", [128, 512], F32)
    po6 = s.psum("po6", [128, 512], F32)
    pr6 = s.psum("pr6", [128, 512], F32)
    ptb = [s.psum(f"ptba{i}", [128, 1024], BF16) for i in range(2)]

    s.pool(lambda e, a=vstk[0]: e.memset(
        bass.AP(a, 64, [[16 * 65, 128], [65, 16]]), 1.0), writes=["vstkA_ones"])

    for t in range(NT):
        p = t % 2
        b, tc = t // 32, t % 32
        r0 = t * 128
        xi, xik = xnt[p], "xnt0"
        xbb, xbk = xnb[p], "xnbA"
        xT, xTk = xnT[p], "xnTA"
        s.dma("sync", xi[:, :], xn1_i[r0:r0 + 128, :], writes=[xik], tag=f"xn1 ld {t}")
        s.dve(COPY(xbb[:, :], xi[:, :]), reads=[xik], writes=[xbk], tag=f"cast {t}")
        for c in range(8):
            pt_, ptk = ptb[c % 2], f"ptba{c % 2}"
            s.te(TRANS(pt_[:, 0:128], xbb[:, c * 128:(c + 1) * 128], identb[:, :]),
                 reads=[xbk, "identb", ptk + "_r"], writes=[ptk], tag=f"trx {t},{c}")
            s.act(SCOPY(xT[:, c * 128:(c + 1) * 128], pt_[:, 0:128]),
                  reads=[ptk], writes=[xTk + f"_{c}", ptk + "_r"], tag=f"evx {t},{c}")
        XK = [xTk + f"_{c}" for c in range(8)]
        for nb in range(2):
            pp, ppk = pq_[nb], f"pqa{nb}"
            for c in range(8):
                s.te(MM(pp[:, :], xT[:, c * 128:(c + 1) * 128],
                        Wqs[:, c * 1024 + nb * 512: c * 1024 + (nb + 1) * 512], c == 0, False),
                     reads=XK + WK + ([ppk + "_r"] if c == 0 else []), writes=[ppk],
                     tag=f"q {t},{nb},{c}")
            s.te(MM(pp[:, :], onesb[0:1, :], bqs[0:1, nb * 512:(nb + 1) * 512], False, True),
                 reads=WK, writes=[ppk], tag=f"qb {t},{nb}")
            s.dve(COPY(qf[:, nb * 512:(nb + 1) * 512], pp[:, :]), reads=[ppk],
                  writes=[f"qf{nb}", ppk + "_r"], tag=f"q ev {t},{nb}")
        pp, ppk = pq_[0], "pqa0"
        for c in range(8):
            s.te(MM(pp[:, 0:256], xT[:, c * 128:(c + 1) * 128],
                    Wds[:, c * 256:(c + 1) * 256], c == 0, False),
                 reads=XK + WK + (["pqa0_r"] if c == 0 else []), writes=[ppk],
                 tag=f"kvl {t},{c}")
        s.te(MM(pp[:, 0:256], onesb[0:1, :], bds[0:1, :], False, True),
             reads=WK, writes=[ppk], tag=f"kvlb {t}")
        s.dve(COPY(kvlb[:, :], pp[:, 0:256]), reads=[ppk], writes=["kvlb", ppk + "_r"],
              tag=f"kvl ev {t}")
        for c in range(2):
            pt_, ptk = ptb[c % 2], f"ptba{c % 2}"
            s.te(TRANS(pt_[:, 0:128], kvlb[:, c * 128:(c + 1) * 128], identb[:, :]),
                 reads=["kvlb", "identb", ptk + "_r"], writes=[ptk], tag=f"trl {t},{c}")
            s.act(SCOPY(kvlT[:, c * 128:(c + 1) * 128], pt_[:, 0:128]),
                  reads=[ptk], writes=[f"kvlT_{c}", ptk + "_r"], tag=f"evl {t},{c}")
        for nb in range(4):
            pp, ppk = pq_[nb % 2], f"pqa{nb % 2}"
            for c in range(2):
                s.te(MM(pp[:, :], kvlT[:, c * 128:(c + 1) * 128],
                        Wus[:, c * 2048 + nb * 512: c * 2048 + (nb + 1) * 512], c == 0, False),
                     reads=["kvlT_0", "kvlT_1"] + WK + ([ppk + "_r"] if c == 0 else []),
                     writes=[ppk], tag=f"kv {t},{nb},{c}")
            s.te(MM(pp[:, :], onesb[0:1, :], bus[0:1, nb * 512:(nb + 1) * 512], False, True),
                 reads=WK, writes=[ppk], tag=f"kvb {t},{nb}")
            s.dve(lambda e, o=bass.AP(kf, nb * 256, [[1024, 128], [64, 4], [1, 64]]),
                  i_=bass.AP(pp, 0, [[512, 128], [128, 4], [1, 64]]): e.tensor_copy(o, i_),
                  reads=[ppk], writes=[f"kf{nb}"], tag=f"kv evk {t},{nb}")
            s.dve(lambda e, o=bass.AP(vb[p], nb * 256, [[1024, 128], [64, 4], [1, 64]]),
                  i_=bass.AP(pp, 64, [[512, 128], [128, 4], [1, 64]]): e.tensor_copy(o, i_),
                  reads=[ppk], writes=[f"vbp{nb}", ppk + "_r"], tag=f"kv evv {t},{nb}")
        KVK = [f"kf{nb}" for nb in range(4)]
        VBK = [f"vbp{nb}" for nb in range(4)]
        vbb, vbk = vb[p], "vbA"
        s.dma("sync", cost1[:, :], cosr[tc * 128:(tc + 1) * 128, :], writes=["cost1"],
              tag=f"cos {t}")
        s.dma("sync", sint1[:, :], sinr[tc * 128:(tc + 1) * 128, :], writes=["sint1"],
              tag=f"sin {t}")
        for (srcb, srck, dstb, dstk) in ((qf, ["qf0", "qf1"], qb, "qb"),
                                         (kf, KVK, kb, "kb")):
            s.dve(lambda e, o=bass.AP(swp, 0, [[1024, 128], [64, 16], [1, 32]]),
                  i_=bass.AP(srcb, 32, [[1024, 128], [64, 16], [1, 32]]): e.tensor_copy(o, i_),
                  reads=srck, writes=["swp_a"], tag=f"sw1 {t},{dstk}")
            s.dve(lambda e, o=bass.AP(swp, 32, [[1024, 128], [64, 16], [1, 32]]),
                  i_=bass.AP(srcb, 0, [[1024, 128], [64, 16], [1, 32]]): e.tensor_copy(o, i_),
                  reads=srck, writes=["swp_b"], tag=f"sw2 {t},{dstk}")
            s.pool(TT(swp[:, :], swp[:, :], sint1[:, :], ALU.mult),
                   reads=["swp_a", "swp_b", "sint1"], writes=["swp"], tag=f"rsin {t},{dstk}")
            s.dve(TT(srcb[:, :], srcb[:, :], cost1[:, :], ALU.mult),
                  reads=srck + ["cost1"], writes=[srck[0] + "m"] + srck,
                  tag=f"rcos {t},{dstk}")
            s.dve(TT(dstb[:, :], srcb[:, :], swp[:, :], ALU.add),
                  reads=[srck[0] + "m", "swp"], writes=[dstk], tag=f"radd {t},{dstk}")
        s.dma("scalar", vscr[:, :], vbb[:, :], reads=VBK, writes=["vscr"],
              semkey="dq:vscr", tag=f"vscr st {t}")
        vs, vsk = vstk[p], "vstkA"
        for t8 in range(8):
            s.dma("scalar",
                  bass.AP(vs, t8 * 16 * 1040, [[1040, 16], [65, 16], [1, 64]]),
                  bass.AP(vscr, t8 * 1024, [[64, 16], [8192, 16], [1, 64]]),
                  reads=["vscr"], writes=[vsk] if t8 == 7 else [f"vstk_{t8}"],
                  semkey="dq:vstk", tag=f"vstk ld {t},{t8}")
        for g in range(16):
            pt_, ptk = ptb[g % 2], f"ptba{g % 2}"
            s.te(TRANS(pt_[0:64, 0:128], kb[:, g * 64:(g + 1) * 64], identb[:, :]),
                 reads=["kb", "identb", ptk + "_r"], writes=[ptk], tag=f"trk {t},{g}")
            s.dve(lambda e, o=bass.AP(kTt, g, [[2048, 64], [128, 16], [16, 8]]),
                  i_=bass.AP(pt_, 0, [[1024, 64], [8, 16], [1, 8]]): e.tensor_copy(o, i_),
                  reads=[ptk], writes=[f"kTt_{g}", ptk + "_r"], tag=f"evk {t},{g}")
            pt2, pt2k = ptb[(g + 1) % 2], f"ptba{(g + 1) % 2}"
            s.te(TRANS(pt2[0:64, 0:128], qb[:, g * 64:(g + 1) * 64], identb[:, :]),
                 reads=["qb", "identb", pt2k + "_r"], writes=[pt2k], tag=f"trq {t},{g}")
            s.dve(lambda e, o=bass.AP(qTt, g, [[2048, 64], [128, 16], [16, 8]]),
                  i_=bass.AP(pt2, 0, [[1024, 64], [8, 16], [1, 8]]): e.tensor_copy(o, i_),
                  reads=[pt2k], writes=[f"qTt_{g}", pt2k + "_r"], tag=f"evq {t},{g}")
        KTK = [f"kTt_{g}" for g in range(16)]
        QTK = [f"qTt_{g}" for g in range(16)]
        for g4 in range(4):
            for gg in range(4):
                grp = g4 * 4 + gg
                s.te(MM(pat[:, gg * 128:(gg + 1) * 128], kTt[:, grp * 128:(grp + 1) * 128],
                        qTt[:, grp * 128:(grp + 1) * 128], True, True),
                     reads=KTK + QTK + (["pat_r"] if gg == 0 else []), writes=["pat"],
                     tag=f"att {t},{grp}")
            s.act(ACTV(pexp[:, :], pat[:, :], AF.Exp, bias=nC1[:, 0:1], scale=SC1),
                  reads=["pat", "nC1"], writes=["pexpa", "pmsk"], tag=f"aexp {t},{g4}")
            s.dve(TT(pexp[:, :], pexp[:, :], maskc[:, :], ALU.mult),
                  reads=["pexpa", "maskc"], writes=["pmsk"], tag=f"amask {t},{g4}")
            for gg in range(4):
                grp = g4 * 4 + gg
                s.te(MM(po6[0:65, gg * 128:(gg + 1) * 128], vs[:, grp * 65:(grp + 1) * 65],
                        pexp[:, gg * 128:(gg + 1) * 128], True, True),
                     reads=[vsk, "vstkA_ones", "pmsk"] + [f"vstk_{t8}" for t8 in range(8)] + (["po6_r"] if gg == 0 else []),
                     writes=["po6"], tag=f"ao {t},{grp}")
            s.act(SCOPY(qf[0:64, 0:512], po6[0:64, :]), reads=["po6"],
                  writes=["orw", "qf0", "qf1"], tag=f"aev {t},{g4}")
            s.dve(lambda e, a=swp[0:1, 0:512], b_=po6[64:65, :]: e.reciprocal(a, b_),
                  reads=["po6"], writes=["rdena", "swp", "swp_a", "swp_b", "po6_r"],
                  tag=f"arcp {t},{g4}")
            s.te(MM(pr6[0:64, :], onesf[0:1, 0:64], swp[0:1, 0:512], True, True),
                 reads=["onesf", "rdena", "pr6_r"], writes=["pr6"], tag=f"arep {t},{g4}")
            s.dve(TT(oTt[:, g4 * 512:(g4 + 1) * 512], qf[0:64, 0:512], pr6[0:64, :], ALU.mult),
                  reads=["orw", "qf0", "pr6"], writes=[f"oTt_{g4}", "pr6_r"], tag=f"adiv {t},{g4}")
        OTK = [f"oTt_{g4}" for g4 in range(4)]
        for h in range(16):
            s.dve(lambda e, o=oTh[:, h * 128:(h + 1) * 128],
                  i_=bass.AP(oTt, h, [[2048, 64], [128, 16], [16, 8]]): e.tensor_copy(o, i_),
                  reads=OTK, writes=[f"oTh_{h}"], tag=f"ohm {t},{h}")
        OHK2 = [f"oTh_{h}" for h in range(16)]
        s1t, s1k = s1b[p], "s1b0"
        for nb in range(2):
            pp, ppk = pq_[nb], f"pqa{nb}"
            for h in range(16):
                s.te(MM(pp[:, :], oTh[:, h * 128:(h + 1) * 128],
                        Wos[0:64, h * 1024 + nb * 512: h * 1024 + (nb + 1) * 512],
                        h == 0, False),
                     reads=OHK2 + WK + ([ppk + "_r"] if h == 0 else []), writes=[ppk],
                     tag=f"wo1 {t},{nb},{h}")
            s.te(MM(pp[:, :], onesb[0:1, :], bos[0:1, nb * 512:(nb + 1) * 512], False, True),
                 reads=WK, writes=[ppk], tag=f"wo1b {t},{nb}")
            s.dve(COPY(s1t[:, nb * 512:(nb + 1) * 512], pp[:, :]), reads=[ppk],
                  writes=[s1k + f"_{nb}", ppk + "_r"], tag=f"wo1 ev {t},{nb}")
        s.dma("gpsimd", s1_o[r0:r0 + 128, :], s1t[:, :], reads=[s1k + "_0", s1k + "_1"],
              semkey="dq:st:s1", tag=f"s1 st {t}")
        ri2, ri2k = rtt[t % 2], f"rtt{t % 2}"
        s.dma("sync", ri2[:, :], rt_i[t * 128:(t + 1) * 128, :], writes=[ri2k],
              tag=f"rt2 ld {t}")
        s.dve(TS(vm2c[:, 0:2], ri2[:, 2:4], 1e-9, None, ALU.is_gt), reads=[ri2k],
              writes=["vm2"], tag=f"vm2 {t}")
        s.dve(lambda e, rr=ri2: e.select(ecol[:, 0:2], vm2c[:, 0:2], rr[:, 0:2],
                                         c99col[:, 0:2]),
              reads=["vm2", "c99col", ri2k], writes=["ecol", "ecol0"], tag=f"ec {t}")
        for j in range(2):
            s.dma("sync", lrow[0:1, :], lrk_o[j:j + 1, t * 128:(t + 1) * 128],
                  reads=["lrkD"], writes=["lrow"], tag=f"lrow ld {t},{j}")
            s.te(TRANS(prow[0:128, 0:1], lrow[0:1, :], identf[0:1, 0:1]),
                 reads=["lrow", "identf", "prow_r"], writes=["prow"], tag=f"lc tr {t},{j}")
            s.act(SCOPY(lcol[:, j:j + 1], prow[0:128, 0:1]), reads=["prow"],
                  writes=["lcol" if j else "lcol0", "prow_r"], tag=f"lc ev {t},{j}")
        for j in range(2):
            s.dve(TS(ohcol[:, :], iotar[:, :], ecol[:, j:j + 1], None, ALU.is_equal),
                  reads=["ecol", "ecol0", "iotar"], writes=["ohcol"],
                  tag=f"ohc {t},{j}")
            s.dve(TT(ohcol[:, :], ohcol[:, :], baserep[:, j * 8:(j + 1) * 8], ALU.mult),
                  reads=["ohcol", f"baserep{j}"], writes=["ohcol"], tag=f"ohb {t},{j}")
            s.dve(lambda e, a=bcol[:, j:j + 1], b_=ohcol[:, :]:
                  e.reduce_sum(a, b_, axis=mybir.AxisListType.X),
                  reads=["ohcol"], writes=[f"bcol{j}"], tag=f"bsum {t},{j}")
        s.dve(TS(einvt[:, 0:2], ecol[:, 0:2], 98.5, None, ALU.is_gt),
              reads=["ecol", "ecol0"], writes=["einvt"], tag=f"einv {t}")
        s.dve(lambda e: e.scalar_tensor_tensor(pcol2[:, 0:2], einvt[:, 0:2], 1e6,
                                               bcol[:, 0:2], ALU.mult, ALU.add),
              reads=["einvt", "bcol0", "bcol1"], writes=["pcol2"], tag=f"binv {t}")
        s.dve(TT(pcol2[:, 0:2], pcol2[:, 0:2], lcol[:, 0:2], ALU.add),
              reads=["pcol2", "lcol", "lcol0"], writes=["pcol2"], tag=f"posc {t}")
        s.dve(TS(pcol2[:, 0:2], pcol2[:, 0:2], trashc[:, 0:1], None, ALU.min),
              reads=["pcol2", "trashc"], writes=["pcol2"], tag=f"poscl {t}")
        s.dve(COPY(posu[:, :], pcol2[:, 0:2]), reads=["pcol2"], writes=["posu"],
              tag=f"posu {t}")
        for j in range(2):
            s.add("gpsimd", lambda e, o=slab_o[:, :], i_=xbb[:, :],
                  ix=posu[:, j:j + 1]: e.indirect_dma_start(
                      out=o, out_offset=bass.IndirectOffsetOnAxis(ap=ix, axis=0),
                      in_=i_, in_offset=None, oob_is_err=False),
                  reads=[xbk, "posu"], writes=[f"slab{j}"], dma=True, tag=f"slab {t},{j}")
            s.ops[-1]["ch"] = "dq:slab"



def emit_l1b(s, slaball, erow_all, fc1T, fc2T, fb1, fb2, ls_i, ehot, crow16, identb_in, resl, dbgidx):
    identb = s.sbuf("identb_sb", [128, 128], BF16)
    s.dma("sync", identb[:, :], identb_in[:, :], writes=["identb"])
    identf = s.sbuf("identf_sb", [128, 128], F32)
    s.dve(COPY(identf[:, :], identb[:, :]), reads=["identb"], writes=["identf"])
    onesb = s.sbuf("onesb", [1, 128], BF16)
    s.pool(MEMSET(onesb[:, :], 1.0), writes=["onesb"])
    onesf = s.sbuf("onesf", [1, 128], F32)
    s.pool(MEMSET(onesf[:, :], 1.0), writes=["onesf"])
    prow = s.psum("prow", [128, 512], F32)
    w1s = s.sbuf("w1s", [128, 8 * 4096], BF16)
    for c in range(8):
        s.dma("sync", w1s[:, c * 4096:(c + 1) * 4096], fc1T[c * 128:(c + 1) * 128, :],
              writes=[f"w1s{c}"], semkey="dq:w1s", tag=f"w1 {c}")
    W1K = [f"w1s{c}" for c in range(8)]
    ehots = s.sbuf("ehots", [8, 1], F32)
    s.dma("sync", ehots[:, :], ehot[:, :], writes=["ehots"])
    cr16 = s.sbuf("cr16", [1, 16], F32)
    s.dma("sync", cr16[:, :], crow16[:, :], writes=["cr16"])
    fb2sb = s.sbuf("fb2sb", [1, 1024], BF16)
    s.dma("gpsimd", fb2sb[:, :], fb2[:, :], writes=["fb2sb"], semkey="dq:wini")
    lsrow = s.sbuf("lsrow", [1, 1024], F32)
    s.dma("sync", lsrow[:, :], ls_i[:, :], writes=["lsrow"])
    fb1r = s.sbuf("fb1r", [32, 128], F32)
    s.dma("sync", fb1r[:, :], fb1[:, :], writes=["fb1r"])
    fb1c = s.sbuf("fb1c", [128, 32], F32)
    s.te(TRANS(prow[:, 0:32], fb1r[:, :], identf[0:32, 0:32]),
         reads=["fb1r", "identf", "prow_r"], writes=["prow"], tag="fb1 tr")
    s.act(SCOPY(fb1c[:, :], prow[:, 0:32]), reads=["prow"], writes=["fb1c", "prow_r"],
          tag="fb1 ev")
    lsrep = s.sbuf("lsrep", [128, 1024], F32)
    for nb in range(2):
        s.te(MM(prow[:, :], onesf[0:1, :], lsrow[0:1, nb * 512:(nb + 1) * 512], True, True),
             reads=["lsrow", "onesf", "prow_r"], writes=["prow"], tag=f"lsrep {nb}")
        s.act(SCOPY(lsrep[:, nb * 512:(nb + 1) * 512], prow[:, :]), reads=["prow"],
              writes=[f"lsrep{nb}", "prow_r"], tag=f"lsrep ev {nb}")

    cnt0m, cnt1m, CK = build_counts(s, erow_all, prow)

    tot = s.sbuf("tot", [8, 8], F32)
    s.dve(TT(tot[:, :], cnt0m[:, :], cnt1m[:, :], ALU.add), reads=CK, writes=["tot"],
          tag="tot")
    t16 = s.sbuf("t16", [16, 8], F32)
    s.te(TRANS(prow[0:8, 0:8], tot[:, :], identf[0:8, 0:8]),
         reads=["tot", "identf", "prow_r"], writes=["prow"], tag="tot tr")
    s.act(SCOPY(t16[0:8, 0:8], prow[0:8, 0:8]), reads=["prow"], writes=["t16", "prow_r"],
          tag="tot ev")
    sone8 = s.sbuf("sone8", [8, 1], F32)
    s.pool(MEMSET(sone8[:, :], 1.0), writes=["sone8"])
    s.dve(lambda e: e.tensor_tensor_scan(t16[0:8, 0:8], sone8[0:8, 0:1].to_broadcast([8, 8]),
                                         t16[0:8, 0:8], 0.0, op0=ALU.mult, op1=ALU.add),
          reads=["t16", "sone8"], writes=["t16s"], tag="tot scan")
    s.te(TRANS(prow[0:8, 0:8], t16[0:8, 0:8], identf[0:8, 0:8]),
         reads=["t16s", "identf", "prow_r"], writes=["prow"], tag="tot tr2")
    colin = s.sbuf("colin", [8, 8], F32)
    s.act(SCOPY(colin[:, :], prow[0:8, 0:8]), reads=["prow"], writes=["colin", "prow_r"],
          tag="tot ev2")
    colex = s.sbuf("colex", [8, 8], F32)
    s.dve(TT(colex[:, :], colin[:, :], tot[:, :], ALU.subtract), reads=["colin", "tot"],
          writes=["colex"], tag="colex")
    myrows = s.sbuf("myrows", [1, 32], F32)
    for i, (mat, matk) in enumerate([(colex, "colex"), (cnt0m, "cmat0"), (cnt1m, "cmat1")]):
        rk = ["colex"] if i == 0 else CK
        s.te(MM(prow[0:1, 0:8], ehots[:, 0:1], mat[:, :], True, True),
             reads=rk + ["ehots", "prow_r"], writes=["prow"], tag=f"myrow {i}")
        s.act(SCOPY(myrows[0:1, i * 8:(i + 1) * 8], prow[0:1, 0:8]), reads=["prow"],
              writes=[f"myrow{i}", "prow_r"], tag=f"myrow ev {i}")
    brow = s.sbuf("brow", [1, 16], F32)
    s.dve(TT(brow[0:1, 0:8], myrows[0:1, 0:8], cr16[0:1, 0:8], ALU.add),
          reads=["myrow0", "cr16"], writes=["brow_a"], tag="brow a")
    s.dve(TT(brow[0:1, 8:16], brow[0:1, 0:8], myrows[0:1, 8:16], ALU.add),
          reads=["brow_a", "myrow1"], writes=["brow_b"], tag="brow b")
    lrow16 = s.sbuf("lrow16", [1, 16], F32)
    s.dve(COPY(lrow16[0:1, 0:8], myrows[0:1, 8:16]), reads=["myrow1"], writes=["lrow_a"],
          tag="lrow a")
    s.dve(COPY(lrow16[0:1, 8:16], myrows[0:1, 16:24]), reads=["myrow2"], writes=["lrow_b"],
          tag="lrow b")
    srow = s.sbuf("srow", [1, 17], F32)
    s.pool(MEMSET(srow[:, :], 0.0), writes=["srow_z"])
    sone1 = s.sbuf("sone1", [1, 1], F32)
    s.pool(MEMSET(sone1[:, :], 1.0), writes=["sone1"])
    s.dve(lambda e: e.tensor_tensor_scan(srow[0:1, 1:17], sone1[0:1, 0:1].to_broadcast([1, 16]),
                                         lrow16[0:1, :], 0.0, op0=ALU.mult, op1=ALU.add),
          reads=["lrow_a", "lrow_b", "sone1", "srow_z"], writes=["srow"], tag="srow scan")
    vrow = s.sbuf("vrow", [1, 16], F32)
    s.dve(TT(vrow[0:1, :], brow[0:1, :], srow[0:1, 0:16], ALU.subtract),
          reads=["brow_a", "brow_b", "srow", "srow_z"], writes=["vrow"], tag="vrow")
    drow = s.sbuf("drow", [1, 16], F32)
    s.dve(COPY(drow[0:1, 0:1], vrow[0:1, 0:1]), reads=["vrow"], writes=["drow_a"], tag="d0")
    s.dve(TT(drow[0:1, 1:16], vrow[0:1, 1:16], vrow[0:1, 0:15], ALU.subtract),
          reads=["vrow"], writes=["drow_b"], tag="dr")
    srep = s.sbuf("srep", [128, 17], F32)
    drep = s.sbuf("drep", [128, 16], F32)
    s.te(MM(prow[:, 0:17], onesf[0:1, :], srow[0:1, :], True, True),
         reads=["srow", "srow_z", "onesf", "prow_r"], writes=["prow"], tag="srep")
    s.act(SCOPY(srep[:, :], prow[:, 0:17]), reads=["prow"], writes=["srep", "prow_r"],
          tag="srep ev")
    s.te(MM(prow[:, 0:16], onesf[0:1, :], drow[0:1, :], True, True),
         reads=["drow_a", "drow_b", "onesf", "prow_r"], writes=["prow"], tag="drep")
    s.act(SCOPY(drep[:, :], prow[:, 0:16]), reads=["prow"], writes=["drep", "prow_r"],
          tag="drep ev")
    idxm = s.sbuf("idxm", [128, 80], F32)
    tmpm = s.sbuf("tmpm", [128, 80], F32)
    idxu = s.sbuf("idxu", [128, 80], U32)
    s.pool(lambda e: e.iota(idxm[:, :], [[128, 80]], base=0, channel_multiplier=1,
                            allow_small_or_imprecise_dtypes=True),
           writes=["idxm"], tag="idx iota")
    pm = s.sbuf("pm", [128, 80], F32)
    s.dve(COPY(pm[:, :], idxm[:, :]), reads=["idxm"], writes=["pm"], tag="pm")
    for sseg in range(17):
        s.dve(TS(tmpm[:, :], pm[:, :], srep[:, sseg:sseg + 1], None, ALU.is_ge),
              reads=["pm", "srep"], writes=["tmpm"], tag=f"seg ge {sseg}")
        if sseg < 16:
            s.dve(lambda e, d=drep[:, sseg:sseg + 1]:
                  e.scalar_tensor_tensor(idxm[:, :], tmpm[:, :], 1.0, idxm[:, :],
                                         ALU.mult, ALU.bypass) if False else None,
                  reads=[], writes=[], tag="noop")
            s.ops.pop()
            s.dve(lambda e, d=drep[:, sseg:sseg + 1]:
                  e.tensor_scalar(tmpm[:, :], tmpm[:, :], d, None, op0=ALU.mult),
                  reads=["tmpm", "drep"], writes=["tmpm2"], tag=f"seg mul {sseg}")
            s.dve(TT(idxm[:, :], idxm[:, :], tmpm[:, :], ALU.add),
                  reads=["idxm", "tmpm2"], writes=["idxm"], tag=f"seg add {sseg}")
        else:
            s.dve(TS(tmpm[:, :], tmpm[:, :], 1e9, None, ALU.mult),
                  reads=["tmpm"], writes=["tmpm2"], tag="seg big")
            s.dve(TT(idxm[:, :], idxm[:, :], tmpm[:, :], ALU.add),
                  reads=["idxm", "tmpm2"], writes=["idxm"], tag="seg bigadd")
    s.dve(TS(idxm[:, :], idxm[:, :], 16384.0, None, ALU.min), reads=["idxm"],
          writes=["idxm"], tag="idx clamp")
    s.dve(COPY(idxu[:, :], idxm[:, :]), reads=["idxm"], writes=["idxu"], tag="idxu")
    s.dma("gpsimd", dbgidx[:, :], idxm[:, :], reads=["idxm"], semkey="dq:st:dbgi")

    toks = s.sbuf("toks", [128, 4 * 1024], BF16)
    tokT = s.sbuf("tokT", [128, 8 * 512], BF16)
    ybT = s.sbuf("ybT", [128, 32 * 512], BF16)
    w2s = [s.sbuf(f"w2s{i}", [128, 1024], BF16) for i in range(2)]
    yout = s.sbuf("yout", [128, 1024], F32)
    youtb = [s.sbuf(f"youtb{i}", [128, 1024], BF16) for i in range(2)]
    ptb = [s.psum(f"ptbb{i}", [128, 1024], BF16) for i in range(2)]
    pyt = [s.psum(f"pytb{i}", [128, 512], F32) for i in range(2)]
    pz = [s.psum(f"pzb{i}", [128, 512], F32) for i in range(2)]
    s.pool(MEMSET(toks[:, :], 0.0), writes=["toks_init"])

    _STAGE = int(os.environ.get("L1B_STAGE", "4"))
    for grp in range(int(os.environ.get("L1B_GROUPS", "20"))):
        TKK = []
        for ti in range(4):
            g = grp * 4 + ti
            if _STAGE != 12:
                s.add("gpsimd", lambda e, o=toks[:, ti * 1024:(ti + 1) * 1024],
                      ix=idxu[:, g:g + 1]: e.indirect_dma_start(
                          out=o, out_offset=None, in_=slaball[:, :],
                          in_offset=bass.IndirectOffsetOnAxis(ap=ix, axis=0),
                          oob_is_err=False),
                      reads=["idxu", "toks_init", f"tok_r{ti}"], writes=[f"tok{ti}"],
                      dma=True, tag=f"gather {g}")
                s.ops[-1]["ch"] = f"dq:tok{ti}"
            else:
                s.pool(MEMSET(toks[:, ti * 1024:(ti + 1) * 1024], 0.5),
                       reads=[f"tok_r{ti}"], writes=[f"tok{ti}"], tag=f"gmemset {g}")
            TKK.append(f"tok{ti}")
            if _STAGE == 11:
                continue
            if _STAGE < 2:
                s.dma("gpsimd", resl[(grp * 4 + ti) * 128:(grp * 4 + ti + 1) * 128, :],
                      toks[:, ti * 1024:(ti + 1) * 1024], reads=[f"tok{ti}"],
                      writes=[f"tok_r{ti}"], semkey="dq:st:res0", tag=f"res direct {g}")
                continue
            for c in range(8):
                pt_, ptk = ptb[c % 2], f"ptbb{c % 2}"
                s.te(TRANS(pt_[:, 0:128], toks[:, ti * 1024 + c * 128: ti * 1024 + (c + 1) * 128],
                           identb[:, :]),
                     reads=[f"tok{ti}", "identb", ptk + "_r"], writes=[ptk],
                     tag=f"trt {g},{c}")
                s.act(SCOPY(tokT[:, c * 512 + ti * 128: c * 512 + (ti + 1) * 128],
                            pt_[:, 0:128]), reads=[ptk],
                      writes=[f"tokT{c}_{ti}", ptk + "_r"], tag=f"evt {g},{c}")
        TTK = [f"tokT{c}_{ti}" for c in range(8) for ti in range(4)]
        if _STAGE < 3:
            continue
        for yt in range(32):
            pp, ppk = pyt[yt % 2], f"pytb{yt % 2}"
            for c in range(8):
                s.te(MM(pp[:, :], w1s[:, c * 4096 + yt * 128: c * 4096 + (yt + 1) * 128],
                        tokT[:, c * 512:(c + 1) * 512], c == 0, c == 7),
                     reads=TTK + W1K + [ppk + "_r"], writes=[ppk], tag=f"f1 {grp},{yt},{c}")
            s.act(ACTV(ybT[:, yt * 512:(yt + 1) * 512], pp[:, :], AF.Gelu,
                       bias=fb1c[:, yt:yt + 1]),
                  reads=[ppk, "fb1c"], writes=[f"ybT{yt}", ppk + "_r"],
                  tag=f"gelu {grp},{yt}")
        if _STAGE < 4:
            continue
        for half in range(2):
            zb = [pz[0], pz[1], pyt[0], pyt[1]]
            zbk = ["pzb0", "pzb1", "pytb0", "pytb1"]
            for yt in range(32):
                wsl, wslk = w2s[yt % 2], f"w2s{yt % 2}"
                s.dma("sync", wsl[:, :], fc2T[yt * 128:(yt + 1) * 128, :],
                      writes=[wslk], semkey=f"dq:{wslk}", tag=f"w2 {grp},{half},{yt}")
                for q in range(4):
                    ti = half * 2 + q // 2
                    nb = q % 2
                    s.te(MM(zb[q][:, :], ybT[:, yt * 512 + ti * 128: yt * 512 + (ti + 1) * 128],
                            wsl[:, nb * 512:(nb + 1) * 512], yt == 0, False),
                         reads=[f"ybT{yt}", wslk], writes=[zbk[q]],
                         tag=f"f2 {grp},{half},{yt},{q}")
            for q in range(4):
                nb = q % 2
                s.te(MM(zb[q][:, :], onesb[0:1, :], fb2sb[0:1, nb * 512:(nb + 1) * 512],
                        False, True), reads=["onesb", "fb2sb"], writes=[zbk[q]],
                     tag=f"f2b {grp},{half},{q}")
            for tq in range(2):
                ti = half * 2 + tq
                g = grp * 4 + ti
                yo, yok = youtb[tq], f"youtb{tq}"
                for nb in range(2):
                    q = tq * 2 + nb
                    s.dve(TT(yout[:, nb * 512:(nb + 1) * 512], zb[q][:, :],
                             lsrep[:, nb * 512:(nb + 1) * 512], ALU.mult),
                          reads=[zbk[q], "lsrep0", "lsrep1"], writes=[f"yout{nb}"],
                          tag=f"yls {g},{nb}")
                    s.dve(TT(yo[:, nb * 512:(nb + 1) * 512], yout[:, nb * 512:(nb + 1) * 512],
                             toks[:, ti * 1024 + nb * 512: ti * 1024 + (nb + 1) * 512],
                             ALU.add),
                          reads=[f"yout{nb}", f"tok{ti}"], writes=[yok + f"_{nb}", zbk[q] + "_r"],
                          tag=f"yadd {g},{nb}")
                s.dma("gpsimd", resl[g * 128:(g + 1) * 128, :], yo[:, :],
                      reads=[yok + "_0", yok + "_1"], writes=[f"tok_r{ti}"],
                      semkey=f"dq:st:res{tq}", tag=f"res st {g}")



def emit_l1c(s, x1_i, s1_i, rt_i, lrk_i, erow_all, resall, cmask, identb_in, y_o):
    identb = s.sbuf("identb_sb", [128, 128], BF16)
    s.dma("sync", identb[:, :], identb_in[:, :], writes=["identb"])
    identf = s.sbuf("identf_sb", [128, 128], F32)
    s.dve(COPY(identf[:, :], identb[:, :]), reads=["identb"], writes=["identf"])
    onesf = s.sbuf("onesf", [1, 128], F32)
    s.pool(MEMSET(onesf[:, :], 1.0), writes=["onesf"])
    prow = s.psum("prow", [128, 512], F32)
    cnt0m, cnt1m, CK = build_counts(s, erow_all, prow)
    cms = s.sbuf("cms", [8, 1], F32)
    s.dma("sync", cms[:, :], cmask[:, :], writes=["cms"])
    gb = s.sbuf("gb", [8, 4], F32)
    cmrep = s.sbuf("cmrep", [8, 8], F32)
    cmr = s.sbuf("cmr", [1, 8], F32)
    s.te(TRANS(prow[0:1, 0:8], cms[:, 0:1], identf[0:8, 0:8]),
         reads=["cms", "identf", "prow_r"], writes=["prow"], tag="cm tr")
    s.act(SCOPY(cmr[0:1, :], prow[0:1, 0:8]), reads=["prow"], writes=["cmr", "prow_r"],
          tag="cm ev")
    s.te(MM(prow[0:8, 0:8], onesf[0:1, 0:8], cmr[0:1, :], True, True),
         reads=["cmr", "onesf", "prow_r"], writes=["prow"], tag="cm rep")
    s.act(SCOPY(cmrep[:, :], prow[0:8, 0:8]), reads=["prow"], writes=["cmrep", "prow_r"],
          tag="cm repe")
    tmp88 = s.sbuf("tmp88", [8, 8], F32)
    for j in range(2):
        s.dve(TT(tmp88[:, :], [cnt0m, cnt1m][j][:, :], cmrep[:, :], ALU.mult),
              reads=CK + ["cmrep"], writes=["tmp88"], tag=f"gb mul {j}")
        s.dve(lambda e, a=gb[:, j:j + 1], b_=tmp88[:, :]:
              e.reduce_sum(a, b_, axis=mybir.AxisListType.X),
              reads=["tmp88"], writes=[f"gb{j}"], tag=f"gb red {j}")
    s.dve(lambda e, a=gb[:, 2:3], b_=cnt0m[:, :]:
          e.reduce_sum(a, b_, axis=mybir.AxisListType.X),
          reads=CK, writes=["gbT0"], tag="gb T0")
    s.dve(TT(gb[:, 3:4], gb[:, 2:3], gb[:, 1:2], ALU.add), reads=["gbT0", "gb1"],
          writes=["gb3"], tag="gb1f")
    baserep = s.sbuf("baserep", [128, 16], F32)
    trow = s.sbuf("trow", [1, 8], F32)
    for j, (col, ck) in enumerate([(0, "gb0"), (3, "gb3")]):
        s.te(TRANS(prow[0:1, 0:8], gb[:, col:col + 1], identf[0:8, 0:8]),
             reads=[ck, "identf", "prow_r"], writes=["prow"], tag=f"gbr tr {j}")
        s.act(SCOPY(trow[0:1, :], prow[0:1, 0:8]), reads=["prow"], writes=["trow", "prow_r"],
              tag=f"gbr ev {j}")
        s.te(MM(prow[0:128, 0:8], onesf[0:1, :], trow[0:1, :], True, True),
             reads=["trow", "onesf", "prow_r"], writes=["prow"], tag=f"gbr rep {j}")
        s.act(SCOPY(baserep[:, j * 8:(j + 1) * 8], prow[0:128, 0:8]), reads=["prow"],
              writes=[f"baserep{j}", "prow_r"], tag=f"gbr repe {j}")
    iotar = s.sbuf("iotar", [128, 8], F32)
    s.pool(lambda e: e.iota(iotar[:, :], [[1, 8]], base=0, channel_multiplier=0,
                            allow_small_or_imprecise_dtypes=True), writes=["iotar"])


    rtt = [s.sbuf(f"rtt{i}", [128, 4], F32) for i in range(2)]
    x1t = [s.sbuf(f"x1t{i}", [128, 1024], F32) for i in range(2)]
    s1t = [s.sbuf(f"s1t{i}", [128, 1024], F32) for i in range(2)]
    lrow = s.sbuf("lrow", [1, 128], F32)
    lcol = s.sbuf("lcol", [128, 2], F32)
    ecol = s.sbuf("ecol", [128, 2], F32)
    vm2c = s.sbuf("vm2c", [128, 2], mybir.dt.uint8)
    c99col = s.sbuf("c99col", [128, 2], F32)
    s.pool(MEMSET(c99col[:, :], 99.0), writes=["c99col"])
    ohcol = s.sbuf("ohcol", [128, 8], F32)
    bcol = s.sbuf("bcol", [128, 2], F32)
    kmask = s.sbuf("kmask", [128, 2], F32)
    km8 = s.sbuf("km8", [128, 2], mybir.dt.uint8)
    bigcol = s.sbuf("bigcol", [128, 2], F32)
    wcol = s.sbuf("wcol", [128, 2], F32)
    pcol = s.sbuf("pcol", [128, 2], F32)
    ridx = s.sbuf("ridx", [128, 2], F32)
    ridxu = s.sbuf("ridxu", [128, 2], U32)
    gat = [s.sbuf(f"gat{j}", [128, 1024], BF16) for j in range(2)]
    momo = s.sbuf("momo", [128, 1024], F32)
    tmpg = s.sbuf("tmpg", [128, 1024], F32)
    yb = [s.sbuf(f"yb{i}", [128, 1024], F32) for i in range(2)]
    s.pool(MEMSET(bigcol[:, :], 1e9), writes=["bigcol"])
    for j in range(2):
        s.pool(MEMSET(gat[j][:, :], 0.0), writes=[f"gat{j}_init"])

    for t in range(NT):
        p = t % 2
        r0 = t * 128
        ri, rik = rtt[p], f"rtt{p}"
        s.dma("sync", ri[:, :], rt_i[r0:r0 + 128, :], writes=[rik], tag=f"rt ld {t}")
        x1b, x1k = x1t[p], f"x1t{p}"
        s1b, s1k = s1t[p], f"s1t{p}"
        s.dma("sync", x1b[:, :], x1_i[r0:r0 + 128, :], writes=[x1k], tag=f"x1 ld {t}")
        s.dma("sync", s1b[:, :], s1_i[r0:r0 + 128, :], writes=[s1k], tag=f"s1 ld {t}")
        s.dve(TS(vm2c[:, 0:2], ri[:, 2:4], 1e-9, None, ALU.is_gt), reads=[rik],
              writes=["vm2"], tag=f"vm2 {t}")
        s.dve(lambda e, rr=ri: e.select(ecol[:, 0:2], vm2c[:, 0:2], rr[:, 0:2],
                                        c99col[:, 0:2]),
              reads=["vm2", "c99col", rik], writes=["ecol"], tag=f"ec {t}")
        for j in range(2):
            s.dma("sync", lrow[0:1, :], lrk_i[j:j + 1, r0:r0 + 128],
                  writes=["lrow"], tag=f"lrow ld {t},{j}")
            s.te(TRANS(prow[0:128, 0:1], lrow[0:1, :], identf[0:1, 0:1]),
                 reads=["lrow", "identf", "prow_r"], writes=["prow"], tag=f"lc tr {t},{j}")
            s.act(SCOPY(lcol[:, j:j + 1], prow[0:128, 0:1]), reads=["prow"],
                  writes=[f"lcol{j}", "prow_r"], tag=f"lc ev {t},{j}")
            s.dve(TS(ohcol[:, :], iotar[:, :], ecol[:, j:j + 1], None, ALU.is_equal),
                  reads=["ecol", "iotar"], writes=["ohcol"], tag=f"ohc {t},{j}")
            s.dve(TT(ohcol[:, :], ohcol[:, :], baserep[:, j * 8:(j + 1) * 8], ALU.mult),
                  reads=["ohcol", f"baserep{j}"], writes=["ohcol"], tag=f"ohb {t},{j}")
            s.dve(lambda e, a=bcol[:, j:j + 1], b_=ohcol[:, :]:
                  e.reduce_sum(a, b_, axis=mybir.AxisListType.X),
                  reads=["ohcol"], writes=[f"bcol{j}"], tag=f"bsum {t},{j}")
        s.dve(TT(pcol[:, 0:2], bcol[:, 0:2], lcol[:, 0:2], ALU.add),
              reads=["bcol0", "bcol1", "lcol0", "lcol1"], writes=["pcol"], tag=f"pos {t}")
        s.dve(TS(km8[:, 0:2], pcol[:, 0:2], 9830.5, None, ALU.is_lt),
              reads=["pcol"], writes=["km8"], tag=f"km8 {t}")
        s.dve(COPY(kmask[:, 0:2], km8[:, 0:2]), reads=["km8"], writes=["kmask"],
              tag=f"km {t}")
        s.dve(TT(wcol[:, 0:2], ri[:, 2:4], kmask[:, 0:2], ALU.mult),
              reads=[rik, "kmask"], writes=["wcol"], tag=f"wc {t}")
        s.dve(lambda e: e.scalar_tensor_tensor(ridx[:, 0:2], ecol[:, 0:2], 10240.0,
                                               pcol[:, 0:2], ALU.mult, ALU.add),
              reads=["ecol", "pcol"], writes=["ridx"], tag=f"ri {t}")
        s.dve(TS(ridx[:, 0:2], ridx[:, 0:2], 81919.0, None, ALU.min),
              reads=["ridx"], writes=["ridx"], tag=f"ri2 {t}")
        s.dve(COPY(ridxu[:, :], ridx[:, 0:2]), reads=["ridx"], writes=["ridxu"],
              tag=f"riu {t}")
        for j in range(2):
            s.add("gpsimd", lambda e, o=gat[j][:, :], ix=ridxu[:, j:j + 1]:
                  e.indirect_dma_start(out=o, out_offset=None, in_=resall[:, :],
                                       in_offset=bass.IndirectOffsetOnAxis(ap=ix, axis=0),
                                       oob_is_err=False),
                  reads=["ridxu", f"gat{j}_init", f"gat{j}_r"], writes=[f"gat{j}"],
                  dma=True, tag=f"rgat {t},{j}")
            s.ops[-1]["ch"] = f"dq:gat{j}"
        s.dve(TS(momo[:, :], gat[0][:, :], wcol[:, 0:1], None, ALU.mult),
              reads=["gat0", "wcol"], writes=["momo", "gat0_r"], tag=f"mm0 {t}")
        s.dve(TS(tmpg[:, :], gat[1][:, :], wcol[:, 1:2], None, ALU.mult),
              reads=["gat1", "wcol"], writes=["tmpg", "gat1_r"], tag=f"mm1 {t}")
        s.dve(TT(momo[:, :], momo[:, :], tmpg[:, :], ALU.add),
              reads=["momo", "tmpg"], writes=["momo"], tag=f"mms {t}")
        s.act(ACTV(momo[:, :], momo[:, :], AF.Silu), reads=["momo"], writes=["momo"],
              tag=f"sil {t}")
        ybb, ybk = yb[p], f"yb{p}"
        s.dve(TT(ybb[:, :], momo[:, :], s1b[:, :], ALU.mult), reads=["momo", s1k],
              writes=[ybk], tag=f"ys {t}")
        s.dve(TT(ybb[:, :], ybb[:, :], x1b[:, :], ALU.add), reads=[ybk, x1k],
              writes=[ybk], tag=f"yx {t}")
        s.dma("gpsimd", y_o[r0:r0 + 128, :], ybb[:, :], reads=[ybk],
              semkey="dq:st:y", tag=f"y st {t}")



def build_A():
    nc = bass.Bass("TRN2", target_bir_lowering=False)
    T_ = {}
    x_in = nc.dram_tensor("x", [NTOK, D], F32, kind="ExternalInput")
    q1 = nc.dram_tensor("q1", [1024, 4096], I32, kind="ExternalInput")
    q2 = nc.dram_tensor("q2", [4096, 1024], I32, kind="ExternalInput")
    sc1 = nc.dram_tensor("sc1", [8, 4096], F32, kind="ExternalInput")
    sc2 = nc.dram_tensor("sc2", [32, 1024], F32, kind="ExternalInput")
    rw = nc.dram_tensor("rw", [8, 1024], F32, kind="ExternalInput")
    identf_in = nc.dram_tensor("identf", [128, 128], F32, kind="ExternalInput")
    identb_in = nc.dram_tensor("identb", [128, 128], BF16, kind="ExternalInput")
    wdq = nc.dram_tensor("wdq", [1024, 32], F32, kind="ExternalInput")
    wdkv = nc.dram_tensor("wdkv", [1024, 32], F32, kind="ExternalInput")
    wkr = nc.dram_tensor("wkr", [1024, 32], F32, kind="ExternalInput")
    bcat = nc.dram_tensor("bcat", [1, 96], F32, kind="ExternalInput")
    wuq = nc.dram_tensor("wuq", [32, 1024], F32, kind="ExternalInput")
    buq = nc.dram_tensor("buq", [1, 1024], F32, kind="ExternalInput")
    wuk = nc.dram_tensor("wuk", [32, 1024], F32, kind="ExternalInput")
    buk = nc.dram_tensor("buk", [1, 1024], F32, kind="ExternalInput")
    wuv = nc.dram_tensor("wuv", [32, 1024], F32, kind="ExternalInput")
    buv = nc.dram_tensor("buv", [1, 1024], F32, kind="ExternalInput")
    wqr = nc.dram_tensor("wqr", [32, 256], F32, kind="ExternalInput")
    bqr = nc.dram_tensor("bqr", [1, 256], F32, kind="ExternalInput")
    WO = nc.dram_tensor("WO", [1024, 1024], F32, kind="ExternalInput")
    bO = nc.dram_tensor("bO", [1, 1024], F32, kind="ExternalInput")
    gam = nc.dram_tensor("gam", [1, 1024], F32, kind="ExternalInput")
    bet = nc.dram_tensor("bet", [1, 1024], F32, kind="ExternalInput")
    fb1 = nc.dram_tensor("fb1", [32, 128], F32, kind="ExternalInput")
    fb2 = nc.dram_tensor("fb2", [1, 1024], F32, kind="ExternalInput")
    rb_i = nc.dram_tensor("rb", [1, 8], F32, kind="ExternalInput")
    Wq = nc.dram_tensor("Wq", [1024, 1024], F32, kind="ExternalInput")
    bq = nc.dram_tensor("bq", [1, 1024], F32, kind="ExternalInput")
    Wd = nc.dram_tensor("Wd", [1024, 256], F32, kind="ExternalInput")
    bd = nc.dram_tensor("bd", [1, 256], F32, kind="ExternalInput")
    Wu = nc.dram_tensor("Wu", [256, 2048], F32, kind="ExternalInput")
    bu = nc.dram_tensor("bu", [1, 2048], F32, kind="ExternalInput")
    Wo1 = nc.dram_tensor("Wo1", [1024, 1024], F32, kind="ExternalInput")
    bo1 = nc.dram_tensor("bo1", [1, 1024], F32, kind="ExternalInput")
    cosr = nc.dram_tensor("cosr", [4096, 1024], BF16, kind="ExternalInput")
    sinr = nc.dram_tensor("sinr", [4096, 1024], BF16, kind="ExternalInput")
    mask_i = nc.dram_tensor("maskc", [128, 512], BF16, kind="ExternalInput")

    s1_o = nc.dram_tensor("s1", [NTOK, D], F32, kind="ExternalOutput")
    slab_o = nc.dram_tensor("slab", [16512, D], BF16, kind="ExternalOutput")
    erow_o = nc.dram_tensor("erow", [2, NTOK], BF16, kind="ExternalOutput")
    lrk_o = nc.dram_tensor("lrkrow", [2, NTOK], F32, kind="ExternalOutput")
    rt_o = nc.dram_tensor("route", [NTOK, 4], F32, kind="ExternalOutput")
    x1_o = nc.dram_tensor("x1", [NTOK, D], F32, kind="ExternalOutput")

    w1b = nc.dram_tensor("w1b", [1024, 4096], BF16)
    w2b = nc.dram_tensor("w2b", [4096, 1024], BF16)
    rwT = nc.dram_tensor("rwT", [1024, 8], F32)
    xn_s = nc.dram_tensor("xn_s", [NTOK, D], F32)
    mus_s = nc.dram_tensor("mus_s", [NTOK, 2], F32)
    QT1 = nc.dram_tensor("QT1", [2048, 4096], BF16)
    QT2 = nc.dram_tensor("QT2", [512, 4096], BF16)
    KT1 = nc.dram_tensor("KT1", [2048, 4096], BF16)
    KT2 = nc.dram_tensor("KT2", [64, 4096], BF16)
    V_s = nc.dram_tensor("V", [NTOK, D], BF16)
    OT = nc.dram_tensor("OT", [2048, 4096], BF16)
    xn1_s = nc.dram_tensor("xn1", [NTOK, D], F32)
    wrow_s = nc.dram_tensor("wrow", [2, NTOK], F32)
    vscr = nc.dram_tensor("vscr", [128, 1024], BF16)

    s = Sched(nc)
    s.push_scope()
    emit_prep(s, q1, q2, sc1, sc2, rw, identf_in, w1b, w2b, rwT)
    s.barrier(); s.pop_scope(); s.push_scope()
    emit_l0pre(s, x_in, wdq, wdkv, wkr, bcat, wuq, buq, wuk, buk, wuv, buv,
               wqr, bqr, identb_in, xn_s, mus_s, QT1, QT2, KT1, KT2, V_s)
    s.barrier(); s.pop_scope(); s.push_scope()
    emit_l0attn(s, QT1, QT2, KT1, KT2, V_s, OT)
    s.barrier(); s.pop_scope(); s.push_scope()
    emit_l0rest(s, x_in, xn_s, mus_s, OT, WO, bO, gam, bet, w1b, w2b, fb1, fb2,
                rwT, rb_i, identb_in, x1_o, xn1_s, rt_o)
    s.barrier(); s.pop_scope(); s.push_scope()
    emit_l1a(s, xn1_s, rt_o, Wq, bq, Wd, bd, Wu, bu, Wo1, bo1, cosr, sinr, mask_i,
             identb_in, s1_o, slab_o, erow_o, wrow_s, lrk_o, vscr)
    s.barrier(); s.pop_scope()
    s.emit_all()
    return nc


def run_A(inputs):
    if "A" not in _CACHE:
        _CACHE["A"] = build_A()
    cosrep, sinrep, mask4 = host_consts()
    identb_np = np.eye(128, dtype=bf16)
    identf_np = np.eye(128, dtype=np.float32)
    x_full = inputs["L0_attn_h_t"]
    bcat = np.concatenate([inputs["L0_attn_b_DQ"], inputs["L0_attn_b_DKV"],
                           inputs["L0_attn_b_KR"]]).reshape(1, 96)
    in_maps = []
    for c in range(NCORES):
        in_maps.append({
            "x": np.ascontiguousarray(x_full[2 * c:2 * c + 2].reshape(NTOK, D)),
            "q1": inputs["L0_ffn_fc1_q"], "q2": inputs["L0_ffn_fc2_q"],
            "sc1": inputs["L0_ffn_fc1_scales"], "sc2": inputs["L0_ffn_fc2_scales"],
            "rw": inputs["L1_ffn_router_w"], "identf": identf_np, "identb": identb_np,
            "wdq": inputs["L0_attn_W_DQ"], "wdkv": inputs["L0_attn_W_DKV"],
            "wkr": inputs["L0_attn_W_KR"], "bcat": bcat,
            "wuq": inputs["L0_attn_W_UQ"], "buq": inputs["L0_attn_b_UQ"].reshape(1, -1),
            "wuk": inputs["L0_attn_W_UK"], "buk": inputs["L0_attn_b_UK"].reshape(1, -1),
            "wuv": inputs["L0_attn_W_UV"], "buv": inputs["L0_attn_b_UV"].reshape(1, -1),
            "wqr": inputs["L0_attn_W_QR"], "bqr": inputs["L0_attn_b_QR"].reshape(1, -1),
            "WO": inputs["L0_attn_W_O"], "bO": inputs["L0_attn_b_O"].reshape(1, -1),
            "gam": inputs["L0_ffn_ln_gamma"].reshape(1, -1),
            "bet": inputs["L0_ffn_ln_beta"].reshape(1, -1),
            "fb1": inputs["L0_ffn_fc1_bias"].reshape(32, 128),
            "fb2": inputs["L0_ffn_fc2_bias"].reshape(1, -1),
            "rb": inputs["L1_ffn_router_b"].reshape(1, -1),
            "Wq": inputs["L1_attn_Wq"], "bq": inputs["L1_attn_bq"].reshape(1, -1),
            "Wd": inputs["L1_attn_Wdown"], "bd": inputs["L1_attn_bdown"].reshape(1, -1),
            "Wu": inputs["L1_attn_Wup"], "bu": inputs["L1_attn_bup"].reshape(1, -1),
            "Wo1": inputs["L1_attn_Wout"], "bo1": inputs["L1_attn_bout"].reshape(1, -1),
            "cosr": cosrep.astype(bf16), "sinr": sinrep.astype(bf16), "maskc": mask4,
        })
    res = run_bass_kernel_spmd(_CACHE["A"], in_maps, core_ids=list(range(NCORES)))
    return res.results


def build_B():
    nc = bass.Bass("TRN2", target_bir_lowering=False)
    fc1w = nc.dram_tensor("fc1w", [4096, 1024], F32, kind="ExternalInput")
    fc2w = nc.dram_tensor("fc2w", [1024, 4096], F32, kind="ExternalInput")
    slaball = nc.dram_tensor("slaball", [8 * 16512, D], BF16, kind="ExternalInput")
    erow_all = nc.dram_tensor("erow_all", [16, NTOK], BF16, kind="ExternalInput")
    fb1 = nc.dram_tensor("fb1", [32, 128], F32, kind="ExternalInput")
    fb2 = nc.dram_tensor("fb2", [1, 1024], F32, kind="ExternalInput")
    ls_i = nc.dram_tensor("ls", [1, 1024], F32, kind="ExternalInput")
    ehot = nc.dram_tensor("ehot", [8, 1], F32, kind="ExternalInput")
    crow16 = nc.dram_tensor("crow16", [1, 16], F32, kind="ExternalInput")
    identb_in = nc.dram_tensor("identb", [128, 128], BF16, kind="ExternalInput")
    identf_in = nc.dram_tensor("identf", [128, 128], F32, kind="ExternalInput")
    resl = nc.dram_tensor("resl", [10240, D], BF16, kind="ExternalOutput")
    dbgidx = nc.dram_tensor("dbgidx", [128, 80], F32, kind="ExternalOutput")
    fc1T = nc.dram_tensor("fc1T", [1024, 4096], BF16)
    fc2T = nc.dram_tensor("fc2T", [4096, 1024], BF16)

    s = Sched(nc)
    s.push_scope()
    sid = s.sbuf("sid", [128, 128], F32)
    s.dma("sync", sid[:, :], identf_in[:, :], writes=["sid"])
    win = [s.sbuf(f"win{i}", [128, 128], F32) for i in range(4)]
    pt = [s.psum(f"pt{i}", [128, 512], F32) for i in range(4)]
    obuf = [s.sbuf(f"obuf{i}", [128, 4096], BF16) for i in range(2)]

    def transpose_mat(src_, dst, R, C, tagp):
        nr, ncol = R // 128, C // 128
        for cj in range(ncol):
            ob = obuf[cj % 2]
            for ri in range(nr):
                ib = win[ri % 4]
                ibk = f"win{ri % 4}"
                pb = pt[ri % 4]
                pbk = f"pt{ri % 4}"
                s.dma("sync", ib[:, :], src_[ri * 128:(ri + 1) * 128, cj * 128:(cj + 1) * 128],
                      writes=[ibk], tag=f"{tagp} ld {ri},{cj}")
                s.te(TRANS(pb[:, 0:128], ib[:, :], sid[:, :]),
                     reads=[ibk, "sid"], writes=[pbk], tag=f"{tagp} tr {ri},{cj}")
                s.dve(COPY(ob[:, ri * 128:(ri + 1) * 128], pb[:, 0:128]),
                      reads=[pbk], writes=[f"obuf{cj % 2}_{ri}"], tag=f"{tagp} ev {ri},{cj}")
            s.dma("gpsimd", dst[cj * 128:(cj + 1) * 128, :], ob[:, 0:R],
                  reads=[f"obuf{cj % 2}_{ri}" for ri in range(nr)],
                  writes=[f"{tagp}st"], semkey=f"dq:st:obuf{cj % 2}", tag=f"{tagp} st {cj}")

    transpose_mat(fc1w, fc1T, 4096, 1024, "t1")
    transpose_mat(fc2w, fc2T, 1024, 4096, "t2")
    s.barrier(); s.pop_scope(); s.push_scope()
    emit_l1b(s, slaball, erow_all, fc1T, fc2T, fb1, fb2, ls_i, ehot, crow16,
             identb_in, resl, dbgidx)
    s.barrier(); s.pop_scope()
    s.emit_all()
    return nc


def run_B(inputs, A):
    if "B" not in _CACHE:
        _CACHE["B"] = build_B()
    slaball = np.concatenate([np.asarray(A[c]["slab"]) for c in range(NCORES)], axis=0)
    erow_all = np.concatenate([np.asarray(A[c]["erow"]) for c in range(NCORES)], axis=0)
    crow16 = np.tile(np.arange(8, dtype=np.float32) * 16512.0, 2).reshape(1, 16)
    identb_np = np.eye(128, dtype=bf16)
    identf_np = np.eye(128, dtype=np.float32)
    in_maps = []
    for c in range(NCORES):
        eh = np.zeros((8, 1), np.float32)
        eh[c, 0] = 1.0
        in_maps.append({
            "fc1w": np.ascontiguousarray(inputs["L1_ffn_fc1_w"][c]),
            "fc2w": np.ascontiguousarray(inputs["L1_ffn_fc2_w"][c]),
            "slaball": slaball, "erow_all": erow_all,
            "fb1": inputs["L1_ffn_fc1_b"][c].reshape(32, 128),
            "fb2": inputs["L1_ffn_fc2_b"][c].reshape(1, -1),
            "ls": inputs["L1_ffn_layer_scale"][c].reshape(1, -1),
            "ehot": eh, "crow16": crow16, "identb": identb_np, "identf": identf_np,
        })
    res = run_bass_kernel_spmd(_CACHE["B"], in_maps, core_ids=list(range(NCORES)))
    return res.results


def build_C():
    nc = bass.Bass("TRN2", target_bir_lowering=False)
    x1_i = nc.dram_tensor("x1", [NTOK, D], F32, kind="ExternalInput")
    s1_i = nc.dram_tensor("s1", [NTOK, D], F32, kind="ExternalInput")
    rt_i = nc.dram_tensor("route", [NTOK, 4], F32, kind="ExternalInput")
    lrk_i = nc.dram_tensor("lrkrow", [2, NTOK], F32, kind="ExternalInput")
    erow_all = nc.dram_tensor("erow_all", [16, NTOK], BF16, kind="ExternalInput")
    resall = nc.dram_tensor("resall", [8 * 10240, D], BF16, kind="ExternalInput")
    cmask = nc.dram_tensor("cmask", [8, 1], F32, kind="ExternalInput")
    identb_in = nc.dram_tensor("identb", [128, 128], BF16, kind="ExternalInput")
    y_o = nc.dram_tensor("y", [NTOK, D], F32, kind="ExternalOutput")
    s = Sched(nc)
    s.push_scope()
    emit_l1c(s, x1_i, s1_i, rt_i, lrk_i, erow_all, resall, cmask, identb_in, y_o)
    s.barrier(); s.pop_scope()
    s.emit_all()
    return nc


def run_C(inputs, A, B):
    if "C" not in _CACHE:
        _CACHE["C"] = build_C()
    resall = np.concatenate([np.asarray(B[c]["resl"]) for c in range(NCORES)], axis=0)
    erow_all = np.concatenate([np.asarray(A[c]["erow"]) for c in range(NCORES)], axis=0)
    identb_np = np.eye(128, dtype=bf16)
    in_maps = []
    for c in range(NCORES):
        cm = np.zeros((8, 1), np.float32)
        cm[:c, 0] = 1.0
        in_maps.append({
            "x1": np.asarray(A[c]["x1"]), "s1": np.asarray(A[c]["s1"]),
            "route": np.asarray(A[c]["route"]),
            "lrkrow": np.asarray(A[c]["lrkrow"]),
            "erow_all": erow_all, "resall": resall, "cmask": cm,
            "identb": identb_np,
        })
    res = run_bass_kernel_spmd(_CACHE["C"], in_maps, core_ids=list(range(NCORES)))
    return res.results


_CACHE = {}


def kernel(**inputs):
    A = [dict((k, np.asarray(v)) for k, v in r.items()) for r in run_A(inputs)]
    B = [dict((k, np.asarray(v)) for k, v in r.items()) for r in run_B(inputs, A)]
    C = run_C(inputs, A, B)
    y = np.concatenate([np.asarray(C[c]["y"]).reshape(2, 4096, 1024)
                        for c in range(NCORES)], axis=0)
    return np.ascontiguousarray(y, dtype=np.float32)


# version 04
# speedup vs baseline: 1.1190x; 1.1190x over previous
import os
import numpy as np
import ml_dtypes
from contextlib import ExitStack

import concourse.bass as bass
import concourse.mybir as mybir
from concourse.bass_utils import run_bass_kernel_spmd

F32 = mybir.dt.float32
BF16 = mybir.dt.bfloat16
I32 = mybir.dt.int32
U32 = mybir.dt.uint32
AF = mybir.ActivationFunctionType
ALU = mybir.AluOpType
NCORES = 8
BLOC = 2
T = 4096
D = 1024
NTOK = BLOC * T
NT = NTOK // 128

C0 = 8.0
C1 = 12.0
SC0 = 1.0 / float(np.sqrt(160.0))
SC1 = 1.0 / 8.0

bf16 = ml_dtypes.bfloat16


class Sched:
    STREAMS = ("tensor", "scalar", "vector", "gpsimd", "sync")

    def __init__(self, nc):
        self.nc = nc
        self.ops = []
        self.lw = {}
        self.rd = {}
        self.es = ExitStack()
        self.scopes = []
        self._names = set()
        self._nuniq = 0

    def push_scope(self):
        self.scopes.append(ExitStack())

    def pop_scope(self):
        self.scopes.pop().close()

    def barrier(self):
        last_by_ch = {}
        for op in self.ops:
            last_by_ch[op["ch"]] = op["i"]
        deps = sorted(set(last_by_ch.values()))
        for stream in self.STREAMS:
            i = len(self.ops)
            self.ops.append(dict(i=i, stream=stream, ch=stream, emit=lambda e: e.nop(),
                                 deps=deps, dma=False, tag="barrier", need_inc=False))

    def _stack(self):
        return self.scopes[-1] if self.scopes else self.es

    def sbuf(self, name, shape, dtype):
        if name in self._names:
            self._nuniq += 1
            name = f"{name}_u{self._nuniq}"
        self._names.add(name)
        return self._stack().enter_context(self.nc.sbuf_tensor(name, shape, dtype))

    def psum(self, name, shape, dtype):
        if name in self._names:
            self._nuniq += 1
            name = f"{name}_u{self._nuniq}"
        self._names.add(name)
        return self._stack().enter_context(self.nc.psum_tensor(name, shape, dtype))

    def add(self, stream, emit, reads=(), writes=(), dma=False, tag=""):
        i = len(self.ops)
        deps = set()
        for k in reads:
            j = self.lw.get(k)
            if j is not None:
                deps.add(j)
        for k in writes:
            j = self.lw.get(k)
            if j is not None:
                deps.add(j)
            deps.update(self.rd.get(k, ()))
        deps.discard(i)
        ch = stream + ("_q" if dma else "")
        self.ops.append(dict(i=i, stream=stream, ch=ch, emit=emit, deps=sorted(deps),
                             dma=dma, tag=tag, need_inc=False))
        for k in writes:
            self.lw[k] = i
            self.rd[k] = []
        for k in reads:
            self.rd.setdefault(k, []).append(i)
        return i

    def te(self, emit, reads=(), writes=(), tag=""):
        return self.add("tensor", emit, reads, writes, tag=tag)

    def act(self, emit, reads=(), writes=(), tag=""):
        return self.add("scalar", emit, reads, writes, tag=tag)

    def dve(self, emit, reads=(), writes=(), tag=""):
        return self.add("vector", emit, reads, writes, tag=tag)

    def pool(self, emit, reads=(), writes=(), tag=""):
        return self.add("gpsimd", emit, reads, writes, tag=tag)

    def dma(self, queue, out, in_, reads=(), writes=(), tag="", semkey=None, **kw):
        if semkey is None:
            if writes:
                semkey = "dq:" + writes[0]
            elif reads:
                semkey = "dq:st:" + reads[0]
            else:
                semkey = "dq:anon"

        def emit(eng):
            return eng.dma_start(out=out, in_=in_, **kw)
        i = self.add(queue, emit, reads, writes, dma=True, tag=tag)
        self.ops[i]["ch"] = semkey
        return i

    def emit_all(self, check=False):
        nc = self.nc
        for op in self.ops:
            kept = []
            for j in op["deps"]:
                dj = self.ops[j]
                if (dj["stream"] == "tensor" and op["stream"] == "tensor"
                        and op["tag"] != "barrier"):
                    continue
                kept.append(j)
                dj["need_inc"] = True
            op["kdeps"] = kept
        chcnt = {}
        for op in self.ops:
            if op["dma"]:
                op["need_inc"] = True
            if op["need_inc"]:
                ch = op["ch"]
                chcnt[ch] = chcnt.get(ch, 0) + (16 if op["dma"] else 1)
                op["cnt"] = chcnt[ch]
            else:
                op["cnt"] = None
        know = {st: {} for st in self.STREAMS}
        know_at = {}
        for op in self.ops:
            k = know[op["stream"]]
            waits = []
            for j in op["kdeps"]:
                dj = self.ops[j]
                if dj["cnt"] is None:
                    continue
                if k.get(dj["ch"], 0) >= dj["cnt"]:
                    continue
                waits.append((dj["ch"], dj["cnt"]))
                ka = know_at.get(j)
                if ka:
                    for c2, v2 in ka.items():
                        if k.get(c2, 0) < v2:
                            k[c2] = v2
                if k.get(dj["ch"], 0) < dj["cnt"]:
                    k[dj["ch"]] = dj["cnt"]
            op["waits"] = waits
            if op["need_inc"]:
                ka = dict(k)
                ka[op["ch"]] = op["cnt"]
                know_at[op["i"]] = ka
        self.channels = sorted(chcnt)
        self.chcnt = chcnt
        self.n_waits = sum(len(op["waits"]) for op in self.ops)
        with ExitStack() as es:
            sems = {ch: es.enter_context(nc.semaphore("sem_" + ch)) for ch in self.channels}
            block = es.enter_context(nc.Block())

            def run_stream(eng, stream):
                for op in self.ops:
                    if op["stream"] != stream:
                        continue
                    for ch, v in op["waits"]:
                        eng.wait_ge(sems[ch], v)
                    inst = op["emit"](eng)
                    if op["need_inc"]:
                        if inst is None:
                            raise RuntimeError(f"op {op['i']} ({op['tag']}) returned None but needs inc")
                        inst.then_inc(sems[op["ch"]], 16 if op["dma"] else 1)

            @block.tensor
            def _(eng):
                run_stream(eng, "tensor")

            @block.scalar
            def _(eng):
                run_stream(eng, "scalar")

            @block.vector
            def _(eng):
                run_stream(eng, "vector")

            @block.gpsimd
            def _(eng):
                run_stream(eng, "gpsimd")

            @block.sync
            def _(eng):
                run_stream(eng, "sync")
                for ch in self.channels:
                    eng.wait_ge(sems[ch], chcnt[ch])

        self.es.close()


def MM(out, lhsT, rhs, start, stop):
    def f(eng):
        return eng.matmul(out, lhsT, rhs, start=start, stop=stop)
    return f


def TRANS(out, in_, ident):
    def f(eng):
        return eng.transpose(out, in_, ident)
    return f


def COPY(out, in_):
    def f(eng):
        return eng.tensor_copy(out, in_)
    return f


def SCOPY(out, in_):
    def f(eng):
        return eng.copy(out, in_)
    return f


def ACTV(out, in_, func, bias=None, scale=None, accum_out=None):
    kw = {}
    if bias is not None:
        kw["bias"] = bias
    if scale is not None:
        kw["scale"] = scale
    if accum_out is not None:
        kw["accum_out"] = accum_out

    def f(eng):
        return eng.activation(out, in_, func, **kw)
    return f


def TT(out, a, b, op):
    def f(eng):
        return eng.tensor_tensor(out, a, b, op)
    return f


def TS(out, in0, s1, s2, op0, op1=None, accum_out=None):
    kw = dict(op0=op0)
    if op1 is not None:
        kw["op1"] = op1
    if accum_out is not None:
        kw["accum_out"] = accum_out

    def f(eng):
        return eng.tensor_scalar(out, in0, s1, s2, **kw)
    return f


def MEMSET(ap, v):
    def f(eng):
        return eng.memset(ap, v)
    return f




def host_consts():
    pos = np.arange(4096, dtype=np.float64)
    inv = 1.0 / (10000.0 ** (np.arange(0, 64, 2, dtype=np.float64) / 64.0))
    fr = pos[:, None] * inv[None, :]
    cos = np.cos(fr).astype(np.float32)
    sin = np.sin(fr).astype(np.float32)
    cos2 = np.concatenate([cos, cos], 1)
    sinpm = np.concatenate([-sin, sin], 1)
    cosrep = np.tile(cos2, (1, 16)).astype(np.float32)
    sinrep = np.tile(sinpm, (1, 16)).astype(np.float32)
    mask = np.kron(np.eye(8, dtype=np.float32), np.ones((16, 16), np.float32))
    mask4 = np.tile(mask, (1, 4)).astype(bf16)
    return cosrep, sinrep, mask4


def build_counts(s, erow_all, prow, tag="cnt"):
    onesr = s.sbuf(tag + "_onesr", [1, 8], BF16)
    s.pool(MEMSET(onesr[:, :], 1.0), writes=[tag + "onesr"])
    iota8 = s.sbuf(tag + "_iota8", [8, 1], F32)
    s.pool(lambda e: e.iota(iota8[:, :], [[1, 1]], base=0, channel_multiplier=1,
                            allow_small_or_imprecise_dtypes=True), writes=[tag + "iota8"])
    rowb = s.sbuf(tag + "_rowb", [1, 512], BF16)
    ohb = s.sbuf(tag + "_ohb", [8, 512], F32)
    red = s.sbuf(tag + "_red", [8, 1], F32)
    cmat = [s.sbuf(tag + f"_cm{j}", [8, 8], F32) for j in range(2)]
    for cp in range(8):
        for j in range(2):
            r = cp * 2 + j
            dstk = tag + f"cm{j}"
            for ch in range(16):
                s.dma("sync", rowb[0:1, :], erow_all[r:r + 1, ch * 512:(ch + 1) * 512],
                      writes=[tag + "rowb"], tag=f"{tag} ld {r},{ch}")
                s.te(MM(prow[0:8, 0:512], onesr[0:1, :], rowb[0:1, 0:512], True, True),
                     reads=[tag + "rowb", tag + "onesr", "prow_r"], writes=["prow"],
                     tag=f"{tag} rep {r},{ch}")
                s.act(SCOPY(ohb[:, :], prow[0:8, 0:512]), reads=["prow"],
                      writes=[tag + "ohbr", "prow_r"], tag=f"{tag} ev {r},{ch}")
                s.dve(TS(ohb[:, :], ohb[:, :], iota8[:, 0:1], None, ALU.is_equal),
                      reads=[tag + "ohbr", tag + "iota8"], writes=[tag + "ohb"],
                      tag=f"{tag} oh {r},{ch}")
                s.dve(lambda e, a=red[:, 0:1], b_=ohb[:, :]:
                      e.reduce_sum(a, b_, axis=mybir.AxisListType.X),
                      reads=[tag + "ohb"], writes=[tag + "red"], tag=f"{tag} rs {r},{ch}")
                if ch == 0:
                    s.dve(COPY(cmat[j][:, cp:cp + 1], red[:, 0:1]), reads=[tag + "red"],
                          writes=[dstk + f"_{cp}"], tag=f"{tag} st {r},{ch}")
                else:
                    s.dve(TT(cmat[j][:, cp:cp + 1], cmat[j][:, cp:cp + 1], red[:, 0:1],
                             ALU.add), reads=[tag + "red", dstk + f"_{cp}"],
                          writes=[dstk + f"_{cp}"], tag=f"{tag} ac {r},{ch}")
    CK = [tag + f"cm{j}_{cp}" for j in range(2) for cp in range(8)]
    return cmat[0], cmat[1], CK


def emit_prep(s, q1, q2, sc1, sc2, rw, identf, w1b, w2b, rwT):
    sid = s.sbuf("sid", [128, 128], F32)
    s.dma("sync", sid[:, :], identf[:, :], writes=["sid"])

    win = [s.sbuf(f"win{i}", [128, 128], F32) for i in range(4)]
    pt = [s.psum(f"pt{i}", [128, 512], F32) for i in range(4)]
    obuf = [s.sbuf(f"obuf{i}", [128, 4096], BF16) for i in range(2)]

    def transpose_mat(src, dst, R, C, tagp):
        nr, ncol = R // 128, C // 128
        for cj in range(ncol):
            ob = obuf[cj % 2]
            for ri in range(nr):
                ib = win[ri % 4]
                ibk = f"win{ri % 4}"
                pb = pt[ri % 4]
                pbk = f"pt{ri % 4}"
                s.dma("sync", ib[:, :], src[ri * 128:(ri + 1) * 128, cj * 128:(cj + 1) * 128],
                      writes=[ibk], tag=f"{tagp} ld {ri},{cj}")
                s.te(TRANS(pb[:, 0:128], ib[:, :], sid[:, :]),
                     reads=[ibk, "sid"], writes=[pbk], tag=f"{tagp} tr {ri},{cj}")
                s.dve(COPY(ob[:, ri * 128:(ri + 1) * 128], pb[:, 0:128]),
                      reads=[pbk], writes=[f"obuf{cj % 2}_{ri}"], tag=f"{tagp} ev {ri},{cj}")
            s.dma("gpsimd", dst[cj * 128:(cj + 1) * 128, :], ob[:, 0:R],
                  reads=[f"obuf{cj % 2}_{ri}" for ri in range(nr)],
                  writes=[f"{tagp}st"], semkey=f"dq:st:obuf{cj % 2}", tag=f"{tagp} st {cj}")


    rws = s.sbuf("rws", [8, 1024], F32)
    rwo = s.sbuf("rwo", [128, 8 * 8], F32)
    s.dma("sync", rws[:, :], rw[:, :], writes=["rws"])
    for c in range(8):
        pb = pt[c % 4]
        s.te(TRANS(pb[0:128, 0:8], rws[0:8, c * 128:(c + 1) * 128], sid[0:8, 0:8]),
             reads=["rws", "sid"], writes=[f"pt{c % 4}"], tag=f"rwT {c}")
        s.dve(COPY(rwo[:, c * 8:(c + 1) * 8], pb[0:128, 0:8]),
              reads=[f"pt{c % 4}"], writes=[f"rwo{c}"], tag=f"rwT ev {c}")
    for c in range(8):
        s.dma("gpsimd", rwT[c * 128:(c + 1) * 128, :], rwo[:, c * 8:(c + 1) * 8],
              reads=[f"rwo{c}"], writes=["rwT_st"], tag=f"rwT st {c}")

    ones1 = s.sbuf("ones1", [1, 128], BF16)
    s.pool(MEMSET(ones1[:, :], 1.0), writes=["ones1"])
    qin = [s.sbuf(f"qin{i}", [128, 4096], I32) for i in range(2)]
    qf = [s.sbuf(f"qf{i}", [128, 4096], F32) for i in range(2)]
    screp = [s.sbuf(f"screp{i}", [128, 4096], F32) for i in range(2)]
    psc = [s.psum(f"psc{i}", [128, 512], F32) for i in range(2)]
    wob = [s.sbuf(f"wob{i}", [128, 4096], BF16) for i in range(2)]
    scb = [s.sbuf(f"scb{i}", [1, 4096], BF16) for i in range(2)]

    def dequant(qsrc, scsrc, dst, K, N, tagp):
        ng = K // 128
        for g in range(ng):
            qi, qik = qin[g % 2], f"qin{g % 2}"
            qo, qok = qf[g % 2], f"qf{g % 2}"
            sr, srk = screp[g % 2], f"screp{g % 2}"
            wo, wok = wob[g % 2], f"wob{g % 2}"
            sb, sbk = scb[g % 2], f"scb{g % 2}"
            s.dma("sync", qi[:, 0:N], qsrc[g * 128:(g + 1) * 128, :], writes=[qik],
                  tag=f"{tagp} qld {g}")
            s.dma("gpsimd", sb[0:1, 0:N], scsrc[g:g + 1, :], writes=[sbk],
                  tag=f"{tagp} scld {g}")
            for nb in range((N + 511) // 512):
                n0, n1 = nb * 512, min(N, (nb + 1) * 512)
                pb = psc[nb % 2]
                s.te(MM(pb[:, 0:n1 - n0], ones1[0:1, :], sb[0:1, n0:n1], True, True),
                     reads=["ones1", sbk], writes=[f"psc{nb % 2}"],
                     tag=f"{tagp} screp {g},{nb}")
                s.act(SCOPY(sr[:, n0:n1], pb[:, 0:n1 - n0]),
                      reads=[f"psc{nb % 2}"], writes=[srk + f"_{nb}"],
                      tag=f"{tagp} screv {g},{nb}")
            s.dve(TS(qo[:, 0:N], qi[:, 0:N], -128.0, None, ALU.add),
                  reads=[qik], writes=[qok], tag=f"{tagp} qcvt {g}")
            s.dve(TT(wo[:, 0:N], qo[:, 0:N], sr[:, 0:N], ALU.mult),
                  reads=[qok] + [srk + f"_{nb}" for nb in range((N + 511) // 512)],
                  writes=[wok], tag=f"{tagp} mul {g}")
            s.dma("gpsimd", dst[g * 128:(g + 1) * 128, :], wo[:, 0:N],
                  reads=[wok], writes=[f"{tagp}st"], tag=f"{tagp} st {g}")

    dequant(q1, sc1, w1b, 1024, 4096, "d1")
    dequant(q2, sc2, w2b, 4096, 1024, "d2")



def emit_l0pre(s, x_in, wdq, wdkv, wkr, bcat, wuq, buq, wuk, buk, wuv, buv, wqr, bqr, identb_in, xn_o, musig_o, QT1, QT2, KT1, KT2, V_o):
    identb = s.sbuf("identb_sb", [128, 128], BF16)
    s.dma("sync", identb[:, :], identb_in[:, :], writes=["identb"])
    onesb = s.sbuf("onesb", [1, 128], BF16)
    s.pool(MEMSET(onesb[:, :], 1.0), writes=["onesb"])
    epsc = s.sbuf("epsc", [128, 1], F32)
    s.pool(MEMSET(epsc[:, :], 1e-5), writes=["epsc"])
    wcat = s.sbuf("wcat", [128, 8 * 96], BF16)
    for c in range(8):
        s.dma("gpsimd", wcat[:, c * 96:c * 96 + 32], wdq[c * 128:(c + 1) * 128, :],
              writes=[f"wcat{c}a"], semkey="dq:wini", tag=f"wdq {c}")
        s.dma("gpsimd", wcat[:, c * 96 + 32:c * 96 + 64], wdkv[c * 128:(c + 1) * 128, :],
              writes=[f"wcat{c}b"], semkey="dq:wini", tag=f"wdkv {c}")
        s.dma("gpsimd", wcat[:, c * 96 + 64:c * 96 + 96], wkr[c * 128:(c + 1) * 128, :],
              writes=[f"wcat{c}c"], semkey="dq:wini", tag=f"wkr {c}")
    wcb = s.sbuf("wcb", [1, 96], BF16)
    s.dma("gpsimd", wcb[:, :], bcat[:, :], writes=["wcb"], semkey="dq:wini")
    WUQp = s.sbuf("WUQp", [33, 1024], BF16)
    WUKp = s.sbuf("WUKp", [33, 1024], BF16)
    WUVp = s.sbuf("WUVp", [33, 1024], BF16)
    WQRp = s.sbuf("WQRp", [33, 256], BF16)
    for nm, wt, bt, ww in (("WUQp", wuq, buq, WUQp), ("WUKp", wuk, buk, WUKp),
                           ("WUVp", wuv, buv, WUVp), ("WQRp", wqr, bqr, WQRp)):
        s.dma("gpsimd", ww[0:32, :], wt[:, :], writes=[nm], semkey="dq:wini", tag=nm)
        s.dma("gpsimd", ww[32:33, :], bt[:, :], writes=[nm + "b"], semkey="dq:wini", tag=nm + "b")
    WK = ["wcb", "WUQp", "WUQpb", "WUKp", "WUKpb", "WUVp", "WUVpb", "WQRp", "WQRpb",
          "onesb", "identb"] + [f"wcat{c}{x}" for c in range(8) for x in "abc"]

    xin = [s.sbuf(f"xin{i}", [128, D], F32) for i in range(2)]
    xnf = [s.sbuf(f"xnf{i}", [128, D], F32) for i in range(2)]
    xnb = [s.sbuf(f"xnb{i}", [128, D], BF16) for i in range(2)]
    xnT = [s.sbuf(f"xnT{i}", [128, D], BF16) for i in range(2)]
    stt = s.sbuf("stt", [128, 12], F32)
    ag = s.sbuf("ag", [128, 2], F32)
    rstd = s.sbuf("rstd", [128, 1], F32)
    st2 = s.sbuf("st2", [128, 12], F32)
    ag2 = s.sbuf("ag2", [128, 2], F32)
    mus = [s.sbuf(f"mus{i}", [128, 2], F32) for i in range(2)]
    cqb = [s.sbuf(f"cqb{i}", [128, 96], BF16) for i in range(2)]
    cqT = [s.sbuf(f"cqT{i}", [33, 128], BF16) for i in range(2)]
    ckvT = [s.sbuf(f"ckvT{i}", [33, 128], BF16) for i in range(2)]
    krT = [s.sbuf(f"krT{i}", [32, 128], BF16) for i in range(2)]
    qstage = [s.sbuf(f"qstage{i}", [128, 1024], BF16) for i in range(2)]
    kstage = [s.sbuf(f"kstage{i}", [128, 1024], BF16) for i in range(2)]
    qrstage = [s.sbuf(f"qrstage{i}", [32, 1024], BF16) for i in range(2)]
    vstage = [s.sbuf(f"vstage{i}", [128, 1024], BF16) for i in range(2)]
    ptb = [s.psum(f"ptb{i}", [128, 1024], BF16) for i in range(2)]
    p96 = s.psum("p96", [128, 512], F32)
    pq = [s.psum(f"pq{i}", [128, 512], F32) for i in range(2)]
    pqr = s.psum("pqr", [128, 512], F32)
    pv = [s.psum(f"pv{i}", [128, 512], F32) for i in range(2)]

    for i in range(2):
        s.pool(MEMSET(cqT[i][32:33, :], 1.0), writes=[f"cqT{i}_ones"])
        s.pool(MEMSET(ckvT[i][32:33, :], 1.0), writes=[f"ckvT{i}_ones"])

    for t in range(NT):
        p = t % 2
        b, tc = t // 32, t % 32
        xi, xik = xin[p], f"xin{p}"
        xf, xfk = xnf[p], f"xnf{p}"
        xb, xbk = xnb[p], f"xnb{p}"
        xT, xTk = xnT[p], "xnTA"
        r0 = t * 128
        s.dma("sync", xi[:, :], x_in[r0:r0 + 128, :], writes=[xik], tag=f"x ld {t}")
        s.dve(lambda e, a=stt[:, 0:6], b_=xi[:, 0:512]: e.bn_stats(a, b_),
              reads=[xik], writes=["stt_a"], tag=f"bn0a {t}")
        s.dve(lambda e, a=stt[:, 6:12], b_=xi[:, 512:1024]: e.bn_stats(a, b_),
              reads=[xik], writes=["stt_b"], tag=f"bn0b {t}")
        s.dve(lambda e, a=ag[:, :], b_=stt[:, 0:12]: e.bn_aggr(a, b_),
              reads=["stt_a", "stt_b"], writes=["ag"], tag=f"aggr0 {t}")
        s.act(ACTV(rstd[:, 0:1], ag[:, 1:2], AF.Ln, bias=epsc[:, 0:1]), reads=["ag", "epsc"],
              writes=["rstd_ln"], tag=f"ln0 {t}")
        s.act(ACTV(rstd[:, 0:1], rstd[:, 0:1], AF.Exp, scale=-0.5), reads=["rstd_ln"],
              writes=["rstd"], tag=f"exp0 {t}")
        s.dve(TS(xf[:, :], xi[:, :], ag[:, 0:1], rstd[:, 0:1], ALU.subtract, ALU.mult),
              reads=[xik, "ag", "rstd"], writes=[xfk], tag=f"xn {t}")
        s.dma("gpsimd", xn_o[r0:r0 + 128, :], xf[:, :], reads=[xfk],
              semkey="dq:st:xn", tag=f"xn st {t}")
        s.dve(lambda e, a=st2[:, 0:6], b_=xf[:, 0:512]: e.bn_stats(a, b_),
              reads=[xfk], writes=["st2_a"], tag=f"bn2a {t}")
        s.dve(lambda e, a=st2[:, 6:12], b_=xf[:, 512:1024]: e.bn_stats(a, b_),
              reads=[xfk], writes=["st2_b"], tag=f"bn2b {t}")
        s.dve(lambda e, a=ag2[:, :], b_=st2[:, 0:12]: e.bn_aggr(a, b_),
              reads=["st2_a", "st2_b"], writes=["ag2"], tag=f"aggr2 {t}")
        mu, muk = mus[p], f"mus{p}"
        s.act(ACTV(mu[:, 1:2], ag2[:, 1:2], AF.Ln, bias=epsc[:, 0:1]), reads=["ag2", "epsc"],
              writes=[muk + "ln"], tag=f"ln2 {t}")
        s.act(ACTV(mu[:, 1:2], mu[:, 1:2], AF.Exp, scale=-0.5), reads=[muk + "ln"],
              writes=[muk + "r"], tag=f"exp2 {t}")
        s.act(SCOPY(mu[:, 0:1], ag2[:, 0:1]), reads=["ag2"], writes=[muk + "m"],
              tag=f"mu2 {t}")
        s.dma("gpsimd", musig_o[r0:r0 + 128, :], mu[:, :], reads=[muk + "r", muk + "m"],
              semkey="dq:st:mus", tag=f"mus st {t}")
        s.dve(COPY(xb[:, :], xf[:, :]), reads=[xfk], writes=[xbk], tag=f"cast {t}")
        for c in range(8):
            pt_ = ptb[c % 2]
            ptk = f"ptb{c % 2}"
            s.te(TRANS(pt_[:, 0:128], xb[:, c * 128:(c + 1) * 128], identb[:, :]),
                 reads=[xbk, "identb", ptk + "_r"], writes=[ptk], tag=f"trx {t},{c}")
            s.act(SCOPY(xT[:, c * 128:(c + 1) * 128], pt_[:, 0:128]),
                  reads=[ptk], writes=[xTk + f"_{c}", ptk + "_r"], tag=f"evx {t},{c}")
        xTkeys = [xTk + f"_{c}" for c in range(8)]
        for c in range(8):
            s.te(MM(p96[:, 0:96], xT[:, c * 128:(c + 1) * 128], wcat[:, c * 96:(c + 1) * 96],
                    c == 0, False), reads=xTkeys + WK, writes=["p96"], tag=f"p1 {t},{c}")
        s.te(MM(p96[:, 0:96], onesb[0:1, :], wcb[0:1, :], False, True),
             reads=WK, writes=["p96"], tag=f"p1b {t}")
        cq, cqk = cqb[p], f"cqb{p}"
        s.dve(COPY(cq[:, :], p96[:, 0:96]), reads=["p96"], writes=[cqk], tag=f"cq ev {t}")
        c3 = [(cqT[p], f"cqT{p}", 0), (ckvT[p], f"ckvT{p}", 32), (krT[p], f"krT{p}", 64)]
        for (dstT, dstk, off) in c3:
            pt_ = ptb[0]
            s.te(TRANS(pt_[0:32, 0:128], cq[:, off:off + 32], identb[:, :]),
                 reads=[cqk, "identb", "ptb0_r"], writes=["ptb0"], tag=f"trc {t},{off}")
            s.act(SCOPY(dstT[0:32, :], pt_[0:32, 0:128]), reads=["ptb0"],
                  writes=[dstk, "ptb0_r"], tag=f"evc {t},{off}")
        qs, qsk = qstage[p], f"qstage{p}"
        ks, ksk = kstage[p], f"kstage{p}"
        qr_, qrk = qrstage[p], f"qrstage{p}"
        vs, vsk = vstage[p], f"vstage{p}"
        for h in range(8):
            pqh = pq[h % 2]
            pqhk = f"pq{h % 2}"
            s.te(MM(pqh[:, 0:128], WUQp[:, h * 128:(h + 1) * 128], cqT[p][:, :], True, True),
                 reads=[f"cqT{p}", f"cqT{p}_ones", pqhk + "_r"] + WK, writes=[pqhk],
                 tag=f"qc {t},{h}")
            s.act(SCOPY(qs[:, h * 128:(h + 1) * 128], pqh[:, 0:128]), reads=[pqhk],
                  writes=[qsk + f"_{h}", pqhk + "_r"], tag=f"qc ev {t},{h}")
        for h in range(8):
            pqh = pq[h % 2]
            pqhk = f"pq{h % 2}"
            s.te(MM(pqh[:, 0:128], WUKp[:, h * 128:(h + 1) * 128], ckvT[p][:, :], True, True),
                 reads=[f"ckvT{p}", f"ckvT{p}_ones", pqhk + "_r"] + WK, writes=[pqhk],
                 tag=f"kc {t},{h}")
            s.dve(COPY(ks[:, h * 128:(h + 1) * 128], pqh[:, 0:128]), reads=[pqhk],
                  writes=[ksk + f"_{h}", pqhk + "_r"], tag=f"kc ev {t},{h}")
        for h in range(8):
            s.te(MM(pqr[0:32, 0:128], WQRp[:, h * 32:(h + 1) * 32], cqT[p][:, :], True, True),
                 reads=[f"cqT{p}", f"cqT{p}_ones", "pqr_r"] + WK, writes=["pqr"],
                 tag=f"qr {t},{h}")
            s.dve(COPY(qr_[0:32, h * 128:(h + 1) * 128], pqr[0:32, 0:128]), reads=["pqr"],
                  writes=[qrk + f"_{h}", "pqr_r"], tag=f"qr ev {t},{h}")
        for nb in range(2):
            pvb = pv[nb]
            pvbk = f"pv{nb}"
            s.te(MM(pvb[:, :], ckvT[p][:, :], WUVp[:, nb * 512:(nb + 1) * 512], True, True),
                 reads=[f"ckvT{p}", f"ckvT{p}_ones", pvbk + "_r"] + WK, writes=[pvbk],
                 tag=f"v {t},{nb}")
            s.dve(COPY(vs[:, nb * 512:(nb + 1) * 512], pvb[:, :]), reads=[pvbk],
                  writes=[vsk + f"_{nb}", pvbk + "_r"], tag=f"v ev {t},{nb}")
        qkeys = [qsk + f"_{h}" for h in range(8)]
        kkeys = [ksk + f"_{h}" for h in range(8)]
        qrkeys = [qrk + f"_{h}" for h in range(8)]
        vkeys = [vsk + f"_{nb}" for nb in range(2)]
        boff = b * 8 * 128 * 4096
        s.dma("sync",
              bass.AP(QT1, boff + tc * 128, [[4096, 128], [128 * 4096, 8], [1, 128]]),
              bass.AP(qs, 0, [[1024, 128], [128, 8], [1, 128]]),
              reads=qkeys, semkey="dq:st:q", tag=f"QT1 st {t}")
        s.dma("sync",
              bass.AP(KT1, boff + tc * 128, [[4096, 128], [128 * 4096, 8], [1, 128]]),
              bass.AP(ks, 0, [[1024, 128], [128, 8], [1, 128]]),
              reads=kkeys, semkey="dq:st:k", tag=f"KT1 st {t}")
        s.dma("sync",
              bass.AP(QT2, b * 8 * 32 * 4096 + tc * 128, [[4096, 32], [32 * 4096, 8], [1, 128]]),
              bass.AP(qr_, 0, [[1024, 32], [128, 8], [1, 128]]),
              reads=qrkeys, semkey="dq:st:qr", tag=f"QT2 st {t}")
        s.dma("sync",
              bass.AP(KT2, b * 32 * 4096 + tc * 128, [[4096, 32], [1, 128]]),
              bass.AP(krT[p], 0, [[128, 32], [1, 128]]),
              reads=[f"krT{p}"], semkey="dq:st:kr", tag=f"KT2 st {t}")
        s.dma("sync", V_o[r0:r0 + 128, :], vs[:, :], reads=vkeys,
              semkey="dq:st:v", tag=f"V st {t}")



def emit_l0attn(s, QT1, QT2, KT1, KT2, V_i, OT):
    onesc = s.sbuf("onesc", [128, 1], BF16)
    s.pool(MEMSET(onesc[:, :], 1.0), writes=["onesc"])
    onesf = s.sbuf("onesf", [1, 128], F32)
    s.pool(MEMSET(onesf[:, :], 1.0), writes=["onesf"])
    nC0 = s.sbuf("nC0", [128, 1], F32)
    s.pool(MEMSET(nC0[:, :], -C0), writes=["nC0"])

    kt1 = [s.sbuf(f"kt1_{i}", [128, 4096], BF16) for i in range(2)]
    kt2 = [s.sbuf(f"kt2_{i}", [32, 4096], BF16) for i in range(2)]
    vt = [s.sbuf(f"vt{i}", [128, 4096], BF16) for i in range(2)]
    qt1 = [s.sbuf(f"qt1_{i}", [128, 512], BF16) for i in range(2)]
    qt2 = [s.sbuf(f"qt2_{i}", [32, 512], BF16) for i in range(2)]
    pexp = [s.sbuf(f"pexp{i}", [128, 512], BF16) for i in range(4)]
    rden = s.sbuf("rden", [1, 512], F32)
    prs = s.sbuf("prs", [128, 512], F32)
    obuf = [s.sbuf(f"obuf{i}", [128, 512], BF16) for i in range(2)]

    ps = [s.psum(f"ps{i}", [128, 512], F32) for i in range(4)]
    po = s.psum("po", [128, 512], F32)
    pd = s.psum("pd", [128, 512], F32)
    pr = s.psum("pr", [128, 512], F32)

    for u in range(16):
        b, h = u // 8, u % 8
        dp = u % 2
        k1, k1k = kt1[dp], f"kt1_{dp}"
        k2, k2k = kt2[dp], f"kt2_{dp}"
        v_, vk = vt[dp], f"vt{dp}"
        row = (b * 8 + h) * 128
        s.dma("sync", k1[:, :], KT1[row:row + 128, :], writes=[k1k], tag=f"kt1 {u}")
        s.dma("sync", k2[:, :], KT2[b * 32:(b + 1) * 32, :], writes=[k2k], tag=f"kt2 {u}")
        s.dma("sync",
              bass.AP(vt[dp], 0, [[4096, 128], [128, 32], [1, 128]]),
              bass.AP(V_i, b * 4096 * 1024 + h * 128, [[1024, 128], [128 * 1024, 32], [1, 128]]),
              writes=[vk], tag=f"vt {u}")
        for g in range(8):
            qp = g % 2
            q1, q1k = qt1[qp], f"qt1_{qp}"
            q2, q2k = qt2[qp], f"qt2_{qp}"
            qrow = (b * 8 + h) * 128
            s.dma("sync", q1[:, :], QT1[qrow:qrow + 128, g * 512:(g + 1) * 512],
                  writes=[q1k], tag=f"qt1 {u},{g}")
            s.dma("sync", q2[:, :], QT2[(b * 8 + h) * 32:(b * 8 + h) * 32 + 32, g * 512:(g + 1) * 512],
                  writes=[q2k], tag=f"qt2 {u},{g}")
            def emit_S(j):
                sp = j % 4
                psb, psk = ps[sp], f"ps{sp}"
                pe, pek = pexp[sp], f"pexp{sp}"
                s.te(MM(psb[:, :], k1[:, j * 128:(j + 1) * 128], q1[:, :], True, False),
                     reads=[k1k, q1k, psk + "_r"], writes=[psk], tag=f"s1 {u},{g},{j}")
                s.te(MM(psb[:, :], k2[:, j * 128:(j + 1) * 128], q2[:, :], False, True),
                     reads=[k2k, q2k], writes=[psk], tag=f"s2 {u},{g},{j}")
                s.act(ACTV(pe[:, :], psb[:, :], AF.Exp, bias=nC0[:, 0:1], scale=SC0),
                      reads=[psk, "nC0"], writes=[pek, psk + "_r"], tag=f"exp {u},{g},{j}")

            def emit_PV(j):
                sp = j % 4
                pe, pek = pexp[sp], f"pexp{sp}"
                s.te(MM(po[:, :], v_[:, j * 128:(j + 1) * 128], pe[:, :], j == 0, j == 31),
                     reads=[vk, pek, "po_r"] if j == 0 else [vk, pek], writes=["po"],
                     tag=f"pv {u},{g},{j}")
                s.te(MM(pd[0:1, :], onesc[:, 0:1], pe[:, :], j == 0, j == 31),
                     reads=["onesc", pek, "pd_r"] if j == 0 else ["onesc", pek],
                     writes=["pd"], tag=f"den {u},{g},{j}")

            emit_S(0)
            emit_S(1)
            for j in range(32):
                if j + 2 < 32:
                    emit_S(j + 2)
                emit_PV(j)
            s.dve(lambda e, a=rden[0:1, :], b_=pd[0:1, :]: e.reciprocal(a, b_),
                  reads=["pd"], writes=["rden", "pd_r"], tag=f"rcp {u},{g}")
            s.te(MM(pr[:, :], onesf[0:1, :], rden[0:1, :], True, True),
                 reads=["onesf", "rden", "pr_r"], writes=["pr"], tag=f"rep {u},{g}")
            s.act(SCOPY(prs[:, :], pr[:, :]), reads=["pr"], writes=["prs", "pr_r"],
                  tag=f"rev {u},{g}")
            ob, obk = obuf[g % 2], f"obuf{g % 2}"
            s.dve(TT(ob[:, :], po[:, :], prs[:, :], ALU.mult),
                  reads=["po", "prs"], writes=[obk, "po_r"], tag=f"omul {u},{g}")
            s.dma("gpsimd", OT[row:row + 128, g * 512:(g + 1) * 512], ob[:, :],
                  reads=[obk], semkey=f"dq:st:ot{g % 2}", tag=f"ot st {u},{g}")



def emit_l0rest(s, x_in, xn_i, mus_i, OT, WO, bO, gam, bet, w1d, w2d, fb1, fb2, rwT_i, rb_i, identb_in, x1_o, xn1_o, rt_o):
    identb = s.sbuf("identb_sb", [128, 128], BF16)
    s.dma("sync", identb[:, :], identb_in[:, :], writes=["identb"])
    onesb = s.sbuf("onesb", [1, 128], BF16)
    s.pool(MEMSET(onesb[:, :], 1.0), writes=["onesb"])
    epsc = s.sbuf("epsc", [128, 1], F32)
    s.pool(MEMSET(epsc[:, :], 1e-5), writes=["epsc"])
    w1sb = s.sbuf("w1sb", [128, 8 * 4096], BF16)
    for c in range(8):
        s.dma("sync", w1sb[:, c * 4096:(c + 1) * 4096], w1d[c * 128:(c + 1) * 128, :],
              writes=[f"w1sb{c}"], semkey="dq:w1", tag=f"w1 {c}")
    W1K = [f"w1sb{c}" for c in range(8)]
    WOsb = s.sbuf("WOsb", [128, 8 * 1024], BF16)
    for c in range(8):
        s.dma("gpsimd", WOsb[:, c * 1024:(c + 1) * 1024], WO[c * 128:(c + 1) * 128, :],
              writes=[f"WOsb{c}"], semkey="dq:wini", tag=f"WO {c}")
    WOK = [f"WOsb{c}" for c in range(8)]
    bOsb = s.sbuf("bOsb", [1, 1024], BF16)
    s.dma("gpsimd", bOsb[:, :], bO[:, :], writes=["bOsb"], semkey="dq:wini")
    fb2sb = s.sbuf("fb2sb", [1, 1024], BF16)
    s.dma("gpsimd", fb2sb[:, :], fb2[:, :], writes=["fb2sb"], semkey="dq:wini")
    rbsb = s.sbuf("rbsb", [1, 8], BF16)
    s.dma("gpsimd", rbsb[:, :], rb_i[:, :], writes=["rbsb"], semkey="dq:wini")
    rwsb = s.sbuf("rwsb", [128, 64], BF16)
    for c in range(8):
        s.dma("gpsimd", rwsb[:, c * 8:(c + 1) * 8], rwT_i[c * 128:(c + 1) * 128, :],
              writes=[f"rwsb{c}"], semkey="dq:wini", tag=f"rw {c}")
    RWK = [f"rwsb{c}" for c in range(8)]
    fb1r = s.sbuf("fb1r", [32, 128], F32)
    s.dma("sync", fb1r[:, :], fb1[:, :], writes=["fb1r"])
    fb1c = s.sbuf("fb1c", [128, 32], F32)
    identf2 = s.sbuf("identf2", [128, 128], F32)
    s.pool(lambda e: e.memset(identf2[:, :], 0.0), writes=["identf2z"])
    s.dve(COPY(identf2[:, :], identb[:, :]), reads=["identb", "identf2z"], writes=["identf2"])
    grow = s.sbuf("grow", [1, 1024], F32)
    brow = s.sbuf("brow", [1, 1024], F32)
    s.dma("sync", grow[:, :], gam[:, :], writes=["grow"])
    s.dma("sync", brow[:, :], bet[:, :], writes=["brow"])
    onesf = s.sbuf("onesf", [1, 128], F32)
    s.pool(MEMSET(onesf[:, :], 1.0), writes=["onesf"])
    grep = s.sbuf("grep", [128, 1024], F32)
    brep = s.sbuf("brep", [128, 1024], F32)

    pso = [s.psum(f"pso{i}", [128, 512], F32) for i in range(2)]
    ptb = [s.psum(f"ptb{i}", [128, 1024], BF16) for i in range(2)]
    pyt = [s.psum(f"pyt{i}", [128, 512], F32) for i in range(2)]
    pz = [s.psum(f"pz{i}", [128, 512], F32) for i in range(2)]

    for nb in range(2):
        s.te(MM(pso[nb][:, :], onesf[0:1, :], grow[0:1, nb * 512:(nb + 1) * 512], True, True),
             reads=["onesf", "grow", f"pso{nb}_r"], writes=[f"pso{nb}"], tag=f"grep {nb}")
        s.act(SCOPY(grep[:, nb * 512:(nb + 1) * 512], pso[nb][:, :]), reads=[f"pso{nb}"],
              writes=[f"grep{nb}", f"pso{nb}_r"], tag=f"grep ev {nb}")
        s.te(MM(pso[nb][:, :], onesf[0:1, :], brow[0:1, nb * 512:(nb + 1) * 512], True, True),
             reads=["onesf", "brow", f"pso{nb}_r"], writes=[f"pso{nb}"], tag=f"brep {nb}")
        s.act(SCOPY(brep[:, nb * 512:(nb + 1) * 512], pso[nb][:, :]), reads=[f"pso{nb}"],
              writes=[f"brep{nb}", f"pso{nb}_r"], tag=f"brep ev {nb}")
    GBK = ["grep0", "grep1", "brep0", "brep1"]
    s.te(TRANS(pyt[0][:, 0:32], fb1r[:, :], identf2[0:32, 0:32]),
         reads=["fb1r", "identf2", "pyt0_r"], writes=["pyt0"], tag="fb1 tr")
    s.act(SCOPY(fb1c[:, :], pyt[0][:, 0:32]), reads=["pyt0"], writes=["fb1c", "pyt0_r"],
          tag="fb1 ev")

    oin = [s.sbuf(f"oin{i}", [128, 1024], BF16) for i in range(2)]
    sA = s.sbuf("sA", [128, 4 * 1024], BF16)
    xng = s.sbuf("xng", [128, 4 * 1024], F32)
    xg = [s.sbuf(f"xg{i}", [128, 1024], F32) for i in range(2)]
    msg = s.sbuf("msg", [128, 4 * 2], F32)
    hb = [s.sbuf(f"hb{i}", [128, 1024], BF16) for i in range(2)]
    hfp = [s.sbuf("hfp0", [128, 1024], F32)] * 2
    hTg = s.sbuf("hTg", [128, 8 * 512], BF16)
    ybT = s.sbuf("ybT", [128, 32 * 512], BF16)
    w2s = [s.sbuf(f"w2s{i}", [128, 1024], BF16) for i in range(2)]
    gp = [s.sbuf("gp0", [128, 1024], F32)] * 2
    x1t = [s.sbuf("x1t0", [128, 1024], F32)] * 2
    xn1t = [s.sbuf("xn1t0", [128, 1024], F32)] * 2
    xn1b = s.sbuf("xn1b", [128, 1024], BF16)
    xnT1 = s.sbuf("xnT1", [128, 1024], BF16)
    stt = s.sbuf("stt", [128, 12], F32)
    ag = s.sbuf("ag", [128, 2], F32)
    rstd = s.sbuf("rstd", [128, 1], F32)
    pe8 = s.sbuf("pe8", [128, 8], F32)
    nmx = s.sbuf("nmx", [128, 1], F32)
    sden = s.sbuf("sden", [128, 1], F32)
    probs = s.sbuf("probs", [128, 8], F32)
    tv8 = s.sbuf("tv8", [128, 8], F32)
    ti8 = s.sbuf("ti8", [128, 8], U32)
    r2 = s.sbuf("r2", [128, 1], F32)
    rteb = [s.sbuf(f"rteb{i}", [128, 4], F32) for i in range(2)]

    eps9 = s.sbuf("eps9", [128, 1], F32)
    s.pool(MEMSET(eps9[:, :], 1e-9), writes=["eps9"])

    for grp in range(16):
        hTkeys = []
        for ti in range(4):
            t = grp * 4 + ti
            p = t % 2
            b, tc = t // 32, t % 32
            r0 = t * 128
            oi, oik = oin[p], f"oin{p}"
            s.dma("sync",
                  bass.AP(oi, 0, [[1024, 128], [128, 8], [1, 128]]),
                  bass.AP(OT, b * 8 * 128 * 4096 + tc * 128, [[4096, 128], [128 * 4096, 8], [1, 128]]),
                  writes=[oik], tag=f"ot ld {t}")
            for nb in range(2):
                pp, ppk = pso[nb], f"pso{nb}"
                for h in range(8):
                    s.te(MM(pp[:, :], oi[:, h * 128:(h + 1) * 128],
                            WOsb[:, h * 1024 + nb * 512:h * 1024 + (nb + 1) * 512],
                            h == 0, False),
                         reads=[oik, ppk + "_r"] + WOK, writes=[ppk], tag=f"wo {t},{nb},{h}")
                s.te(MM(pp[:, :], onesb[0:1, :], bOsb[0:1, nb * 512:(nb + 1) * 512], False, True),
                     reads=["onesb", "bOsb"], writes=[ppk], tag=f"wob {t},{nb}")
                s.act(SCOPY(sA[:, ti * 1024 + nb * 512: ti * 1024 + (nb + 1) * 512], pp[:, :]),
                      reads=[ppk], writes=[f"sA{ti}_{nb}", ppk + "_r"], tag=f"wo ev {t},{nb}")
            s.dma("sync", xng[:, ti * 1024:(ti + 1) * 1024], xn_i[r0:r0 + 128, :],
                  writes=[f"xng{ti}"], semkey="dq:xng", tag=f"xn ld {t}")
            s.dma("sync", msg[:, ti * 2:(ti + 1) * 2], mus_i[r0:r0 + 128, :],
                  writes=[f"msg{ti}"], semkey="dq:msg", tag=f"mus ld {t}")
            hf, hfk = hfp[p], "hfp0"
            s.dve(TS(hf[:, :], xng[:, ti * 1024:(ti + 1) * 1024], msg[:, ti * 2:ti * 2 + 1],
                     msg[:, ti * 2 + 1:ti * 2 + 2], ALU.subtract, ALU.mult),
                  reads=[f"xng{ti}", f"msg{ti}"], writes=[hfk], tag=f"hln {t}")
            s.dve(TT(hf[:, :], hf[:, :], grep[:, :], ALU.mult), reads=[hfk] + GBK,
                  writes=[hfk], tag=f"hg {t}")
            hbb, hbk = hb[p], f"hb{p}"
            s.dve(TT(hbb[:, :], hf[:, :], brep[:, :], ALU.add), reads=[hfk] + GBK,
                  writes=[hbk], tag=f"hbeta {t}")
            for c in range(8):
                pt_, ptk = ptb[c % 2], f"ptb{c % 2}"
                s.te(TRANS(pt_[:, 0:128], hbb[:, c * 128:(c + 1) * 128], identb[:, :]),
                     reads=[hbk, "identb", ptk + "_r"], writes=[ptk], tag=f"trh {t},{c}")
                s.act(SCOPY(hTg[:, c * 512 + ti * 128: c * 512 + (ti + 1) * 128], pt_[:, 0:128]),
                      reads=[ptk], writes=[f"hTg{c}_{ti}", ptk + "_r"], tag=f"evh {t},{c}")
                hTkeys.append(f"hTg{c}_{ti}")
        for yt in range(32):
            pp, ppk = pyt[yt % 2], f"pyt{yt % 2}"
            for c in range(8):
                s.te(MM(pp[:, :], w1sb[:, c * 4096 + yt * 128: c * 4096 + (yt + 1) * 128],
                        hTg[:, c * 512:(c + 1) * 512], c == 0, c == 7),
                     reads=hTkeys + W1K + [ppk + "_r"], writes=[ppk], tag=f"fc1 {grp},{yt},{c}")
            s.act(ACTV(ybT[:, yt * 512:(yt + 1) * 512], pp[:, :], AF.Silu,
                       bias=fb1c[:, yt:yt + 1]),
                  reads=[ppk, "fb1c"], writes=[f"ybT{yt}", ppk + "_r"], tag=f"silu {grp},{yt}")
        for half in range(2):
          zbank = [pz[0], pz[1], pyt[0], pyt[1]]
          zbk = ["pz0", "pz1", "pyt0", "pyt1"]
          for yt in range(32):
            wsl, wslk = w2s[yt % 2], f"w2s{yt % 2}"
            s.dma("sync", wsl[:, :], w2d[yt * 128:(yt + 1) * 128, :],
                  writes=[wslk], semkey=f"dq:{wslk}", tag=f"w2 ld {grp},{half},{yt}")
            for q in range(4):
                ti = half * 2 + q // 2
                nb = q % 2
                s.te(MM(zbank[q][:, :], ybT[:, yt * 512 + ti * 128: yt * 512 + (ti + 1) * 128],
                        wsl[:, nb * 512:(nb + 1) * 512], yt == 0, False),
                     reads=[f"ybT{yt}", wslk], writes=[zbk[q]],
                     tag=f"fc2 {grp},{half},{yt},{q}")
          for q in range(4):
            nb = q % 2
            s.te(MM(zbank[q][:, :], onesb[0:1, :], fb2sb[0:1, nb * 512:(nb + 1) * 512],
                    False, True), reads=["onesb", "fb2sb"], writes=[zbk[q]],
                 tag=f"fc2b {grp},{half},{q}")
          for tq in range(2):
            ti = half * 2 + tq
            t = grp * 4 + ti
            p = t % 2
            r0 = t * 128
            xgt, xgk = xg[p], f"xg{p}"
            s.dma("sync", xgt[:, :], x_in[r0:r0 + 128, :], writes=[xgk], tag=f"x ld {t}")
            gpt, gpk = gp[p], "gp0"
            for nb in range(2):
                q = tq * 2 + nb
                s.dve(lambda e, o=gpt[:, nb * 512:(nb + 1) * 512], z=zbank[q][:, :],
                      xn_=xng[:, ti * 1024 + nb * 512:ti * 1024 + (nb + 1) * 512]:
                      e.scalar_tensor_tensor(o, z, 0.5, xn_, ALU.mult, ALU.add),
                      reads=[zbk[q], f"xng{ti}"], writes=[gpk + f"_{nb}"],
                      tag=f"gp {t},{nb}")
            s.act(ACTV(gpt[:, :], gpt[:, :], AF.Silu), reads=[gpk + "_0", gpk + "_1"],
                  writes=[gpk], tag=f"g {t}")
            s.dve(TT(gpt[:, :], gpt[:, :], sA[:, ti * 1024:(ti + 1) * 1024], ALU.mult),
                  reads=[gpk, f"sA{ti}_0", f"sA{ti}_1"], writes=[gpk], tag=f"sg {t}")
            x1b, x1k = x1t[p], "x1t0"
            s.dve(TT(x1b[:, :], gpt[:, :], xgt[:, :], ALU.add), reads=[gpk, xgk],
                  writes=[x1k], tag=f"x1 {t}")
            s.dma("gpsimd", x1_o[r0:r0 + 128, :], x1b[:, :], reads=[x1k],
                  semkey="dq:st:x1", tag=f"x1 st {t}")
            s.dve(lambda e, a=stt[:, 0:6], b_=x1b[:, 0:512]: e.bn_stats(a, b_),
                  reads=[x1k], writes=["stt_a"], tag=f"bn1a {t}")
            s.dve(lambda e, a=stt[:, 6:12], b_=x1b[:, 512:1024]: e.bn_stats(a, b_),
                  reads=[x1k], writes=["stt_b"], tag=f"bn1b {t}")
            s.dve(lambda e, a=ag[:, :], b_=stt[:, 0:12]: e.bn_aggr(a, b_),
                  reads=["stt_a", "stt_b"], writes=["ag"], tag=f"aggr1 {t}")
            s.act(ACTV(rstd[:, 0:1], ag[:, 1:2], AF.Ln, bias=epsc[:, 0:1]),
                  reads=["ag", "epsc"], writes=["rstd_ln"], tag=f"ln1 {t}")
            s.act(ACTV(rstd[:, 0:1], rstd[:, 0:1], AF.Exp, scale=-0.5), reads=["rstd_ln"],
                  writes=["rstd"], tag=f"exp1 {t}")
            xnt, xntk = xn1t[p], "xn1t0"
            s.dve(TS(xnt[:, :], x1b[:, :], ag[:, 0:1], rstd[:, 0:1], ALU.subtract, ALU.mult),
                  reads=[x1k, "ag", "rstd"], writes=[xntk], tag=f"xn1 {t}")
            s.dma("gpsimd", xn1_o[r0:r0 + 128, :], xnt[:, :], reads=[xntk],
                  semkey="dq:st:xn1", tag=f"xn1 st {t}")
            s.dve(COPY(xn1b[:, :], xnt[:, :]), reads=[xntk], writes=["xn1b"], tag=f"cast1 {t}")
            for c in range(8):
                pt_, ptk = ptb[c % 2], f"ptb{c % 2}"
                s.te(TRANS(pt_[:, 0:128], xn1b[:, c * 128:(c + 1) * 128], identb[:, :]),
                     reads=["xn1b", "identb", ptk + "_r"], writes=[ptk], tag=f"tr1 {t},{c}")
                s.act(SCOPY(xnT1[:, c * 128:(c + 1) * 128], pt_[:, 0:128]),
                      reads=[ptk], writes=[f"xnT1_{c}", ptk + "_r"], tag=f"ev1 {t},{c}")
            XK = [f"xnT1_{c}" for c in range(8)]
            pp, ppk = pz[0], "pz0"
            for c in range(8):
                s.te(MM(pp[:, 0:8], xnT1[:, c * 128:(c + 1) * 128], rwsb[:, c * 8:(c + 1) * 8],
                        c == 0, False), reads=XK + RWK + [ppk + "_r"], writes=[ppk],
                     tag=f"rlog {t},{c}")
            s.te(MM(pp[:, 0:8], onesb[0:1, :], rbsb[0:1, :], False, True),
                 reads=["onesb", "rbsb"], writes=[ppk], tag=f"rlogb {t}")
            s.dve(lambda e, a=nmx[:, 0:1], b_=pp[:, 0:8]: e.reduce_max(a, b_, axis=mybir.AxisListType.X, negate=True),
                  reads=[ppk], writes=["nmx"], tag=f"rmax {t}")
            s.act(ACTV(pe8[:, :], pp[:, 0:8], AF.Exp, bias=nmx[:, 0:1]),
                  reads=[ppk, "nmx"], writes=["pe8", ppk + "_r"], tag=f"rexp {t}")
            s.dve(lambda e, a=sden[:, 0:1], b_=pe8[:, :]: e.reduce_sum(a, b_, axis=mybir.AxisListType.X),
                  reads=["pe8"], writes=["sden"], tag=f"rsum {t}")
            s.dve(lambda e, a=sden[:, 0:1], b_=sden[:, 0:1]: e.reciprocal(a, b_),
                  reads=["sden"], writes=["sden"], tag=f"rrcp {t}")
            s.dve(TS(probs[:, :], pe8[:, :], sden[:, 0:1], None, ALU.mult),
                  reads=["pe8", "sden"], writes=["probs"], tag=f"probs {t}")
            s.dve(lambda e, a=tv8[:, :], b_=probs[:, :]: e.max(out=a, in_=b_),
                  reads=["probs"], writes=["tv8"], tag=f"max8 {t}")
            s.dve(lambda e, a=ti8[:, :], b_=tv8[:, :], c_=probs[:, :]: e.max_index(a, b_, c_),
                  reads=["probs", "tv8"], writes=["ti8"], tag=f"mi8 {t}")
            s.dve(TT(r2[:, 0:1], tv8[:, 0:1], tv8[:, 1:2], ALU.add), reads=["tv8"],
                  writes=["r2"], tag=f"r2a {t}")
            s.dve(TT(r2[:, 0:1], r2[:, 0:1], eps9[:, 0:1], ALU.add), reads=["r2", "eps9"],
                  writes=["r2"], tag=f"r2b {t}")
            s.dve(lambda e, a=r2[:, 0:1], b_=r2[:, 0:1]: e.reciprocal(a, b_),
                  reads=["r2"], writes=["r2"], tag=f"r2c {t}")
            rt, rtk = rteb[p], f"rteb{p}"
            s.dve(COPY(rt[:, 0:2], ti8[:, 0:2]), reads=["ti8"], writes=[rtk + "i"],
                  tag=f"rt i {t}")
            s.dve(TS(rt[:, 2:4], tv8[:, 0:2], r2[:, 0:1], None, ALU.mult),
                  reads=["tv8", "r2"], writes=[rtk + "w"], tag=f"rt w {t}")
            s.dma("gpsimd", rt_o[r0:r0 + 128, :], rt[:, :], reads=[rtk + "i", rtk + "w"],
                  semkey="dq:st:rt", tag=f"rt st {t}")



def emit_l1a(s, xn1_i, rt_i, Wq, bq, Wd, bd, Wu, bu, Wo, bo, cosr, sinr, mask_i, identb_in, s1_o, slab_o, erow_o, wrow_o, lrk_o, vscr):
    identb = s.sbuf("identb_sb", [128, 128], BF16)
    s.dma("sync", identb[:, :], identb_in[:, :], writes=["identb"])
    maskc = s.sbuf("maskc_sb", [128, 512], BF16)
    s.dma("sync", maskc[:, :], mask_i[:, :], writes=["maskc"])
    identf = s.sbuf("identf_sb", [128, 128], F32)
    s.dve(COPY(identf[:, :], identb[:, :]), reads=["identb"], writes=["identf"])
    onesb = s.sbuf("onesb", [1, 128], BF16)
    s.pool(MEMSET(onesb[:, :], 1.0), writes=["onesb"])
    onesf = s.sbuf("onesf", [1, 128], F32)
    s.pool(MEMSET(onesf[:, :], 1.0), writes=["onesf"])
    ones8 = s.sbuf("ones8", [8, 1], F32)
    s.pool(MEMSET(ones8[:, :], 1.0), writes=["ones8"])
    nC1 = s.sbuf("nC1", [128, 1], F32)
    s.pool(MEMSET(nC1[:, :], -C1), writes=["nC1"])
    Wqs = s.sbuf("Wqs", [128, 8 * 1024], BF16)
    Wds = s.sbuf("Wds", [128, 8 * 256], BF16)
    Wus = s.sbuf("Wus", [128, 2 * 2048], BF16)
    Wos = s.sbuf("Wos", [64, 16 * 1024], BF16)
    for c in range(8):
        s.dma("gpsimd", Wqs[:, c * 1024:(c + 1) * 1024], Wq[c * 128:(c + 1) * 128, :],
              writes=[f"Wqs{c}"], semkey="dq:wini", tag=f"Wq {c}")
        s.dma("gpsimd", Wds[:, c * 256:(c + 1) * 256], Wd[c * 128:(c + 1) * 128, :],
              writes=[f"Wds{c}"], semkey="dq:wini", tag=f"Wd {c}")
        s.dma("gpsimd", Wos[:, (2 * c) * 1024:(2 * c + 1) * 1024], Wo[c * 128:c * 128 + 64, :],
              writes=[f"Wos{c}"], semkey="dq:wini", tag=f"Wo {c}a")
        s.dma("gpsimd", Wos[:, (2 * c + 1) * 1024:(2 * c + 2) * 1024], Wo[c * 128 + 64:(c + 1) * 128, :],
              writes=[f"Wos{c}b"], semkey="dq:wini", tag=f"Wo {c}b")
    for c in range(2):
        s.dma("gpsimd", Wus[:, c * 2048:(c + 1) * 2048], Wu[c * 128:(c + 1) * 128, :],
              writes=[f"Wus{c}"], semkey="dq:wini", tag=f"Wu {c}")
    bqs = s.sbuf("bqs", [1, 1024], BF16)
    bds = s.sbuf("bds", [1, 256], BF16)
    bus = s.sbuf("bus", [1, 2048], BF16)
    bos = s.sbuf("bos", [1, 1024], BF16)
    for nm, src_, dst_ in (("bqs", bq, bqs), ("bds", bd, bds), ("bus", bu, bus), ("bos", bo, bos)):
        s.dma("gpsimd", dst_[0:1, :], src_[:, :], writes=[nm], semkey="dq:wini", tag=nm)
    WK = ([f"Wqs{c}" for c in range(8)] + [f"Wds{c}" for c in range(8)] +
          [f"Wos{c}" for c in range(8)] + [f"Wos{c}b" for c in range(8)] + ["Wus0", "Wus1", "bqs", "bds", "bus", "bos",
          "onesb", "identb"])

    rtt = [s.sbuf(f"rtt{i}", [128, 4], F32) for i in range(2)]
    rtw = s.sbuf("rtw", [128, 128], F32)
    wjt = [s.sbuf(f"wjt{j}", [1, 128], F32) for j in range(2)]
    idjt = [s.sbuf(f"idjt{j}", [1, 128], BF16) for j in range(2)]
    vmt = s.sbuf("vmt", [1, 128], mybir.dt.uint8)
    c99s = s.sbuf("c99s", [1, 128], BF16)
    s.pool(MEMSET(c99s[:, :], 99.0), writes=["c99s"])

    iems = s.sbuf("iems", [1, 128], BF16)
    iemc = s.sbuf("iemc", [1, 1024], BF16)
    prow = s.psum("prow", [128, 512], F32)
    for t in range(NT):
        ri, rik = rtt[t % 2], f"rtt{t % 2}"
        s.dma("sync", ri[:, :], rt_i[t * 128:(t + 1) * 128, :], writes=[rik], tag=f"rt ld {t}")
        s.dve(lambda e, o=bass.AP(rtw, 0, [[128, 128], [32, 4]]),
              i_=bass.AP(ri, 0, [[4, 128], [1, 4]]): e.tensor_copy(o, i_),
              reads=[rik], writes=["rtw"], tag=f"rt w {t}")
        s.te(TRANS(prow[:, 0:128], rtw[:, :], identf[:, :]),
             reads=["rtw", "identf", "prow_r"], writes=["prow"], tag=f"rt tr {t}")
        for j in range(2):
            s.act(SCOPY(idjt[j][0:1, :], prow[32 * j:32 * j + 1, 0:128]),
                  reads=["prow"], writes=[f"idjt{j}"], tag=f"rt ev i{j} {t}")
            s.act(SCOPY(wjt[j][0:1, :], prow[64 + 32 * j:64 + 32 * j + 1, 0:128]),
                  reads=["prow"], writes=[f"wjt{j}"] + (["prow_r"] if j == 1 else []),
                  tag=f"rt ev w{j} {t}")
            s.dma("gpsimd", wrow_o[j:j + 1, t * 128:(t + 1) * 128], wjt[j][0:1, :],
                  reads=[f"wjt{j}"], semkey="dq:st:wrow", tag=f"wr st {t},{j}")
            s.dve(TS(vmt[0:1, :], wjt[j][0:1, :], 1e-9, None, ALU.is_gt),
                  reads=[f"wjt{j}"], writes=["vmt"], tag=f"vmask {t},{j}")
            s.dve(lambda e, jj=j:
                  e.select(iems[0:1, :], vmt[0:1, :], idjt[jj][0:1, :], c99s[0:1, :]),
                  reads=["vmt", "c99s", f"idjt{j}"], writes=["iems"],
                  tag=f"iem {t},{j}")
            s.dma("gpsimd", erow_o[j:j + 1, t * 128:(t + 1) * 128], iems[0:1, :],
                  reads=["iems"], writes=["erowD"], semkey="dq:st:erow",
                  tag=f"erow st {t},{j}")
    iota8 = s.sbuf("iota8", [8, 1], F32)
    s.pool(lambda e: e.iota(iota8[:, :], [[1, 1]], base=0, channel_multiplier=1,
                            allow_small_or_imprecise_dtypes=True), writes=["iota8"])
    ones2x8 = s.sbuf("ones2x8", [1, 8], BF16)
    s.pool(MEMSET(ones2x8[:, :], 1.0), writes=["ones2x8"])
    sone = s.sbuf("sone", [8, 1], F32)
    s.pool(MEMSET(sone[:, :], 1.0), writes=["sone"])
    ohc = s.sbuf("ohc", [8, 1024], F32)
    incc = s.sbuf("incc", [8, 1024], F32)
    tmpc = s.sbuf("tmpc", [8, 1024], F32)
    carry = s.sbuf("carry", [8, 2], F32)
    lrkc = s.sbuf("lrkc", [1, 1024], F32)
    lrow = s.sbuf("lrow", [1, 128], F32)
    cntc = s.sbuf("cntc", [8, 4], F32)
    CH = 1024

    def build_oh(j, ch, tagp):
        s.dma("sync", iemc[0:1, 0:CH], erow_o[j:j + 1, ch * CH:(ch + 1) * CH],
              reads=["erowD"], writes=["iemc"], tag=f"{tagp} iemld {j},{ch}")
        for sc in range(CH // 512):
            c0 = ch * CH + sc * 512
            s.te(MM(prow[0:8, 0:512], ones2x8[0:1, :], iemc[0:1, sc * 512:(sc + 1) * 512],
                    True, True), reads=["iemc", "ones2x8", "prow_r"], writes=["prow"],
                 tag=f"{tagp} irep {j},{ch},{sc}")
            s.act(SCOPY(ohc[:, sc * 512:(sc + 1) * 512], prow[0:8, 0:512]),
                  reads=["prow"], writes=[f"ohc_{sc}", "prow_r"], tag=f"{tagp} irepe {j},{ch},{sc}")
        s.dve(TS(ohc[:, :], ohc[:, :], iota8[:, 0:1], None, ALU.is_equal),
              reads=[f"ohc_{sc}" for sc in range(CH // 512)] + ["iota8"], writes=["ohc", "s1sc"],
              tag=f"{tagp} oh {j},{ch}")

    for j in range(2):
        for ch in range(NTOK // CH):
            build_oh(j, ch, "p1")
            ini = 0.0 if ch == 0 else carry[:, j:j + 1]
            s.dve(lambda e, o=incc[:, :], d0=sone[:, 0:1].to_broadcast([8, CH]),
                  d1=ohc[:, :], i2=ini: e.tensor_tensor_scan(o, d0, d1, i2, op0=ALU.mult, op1=ALU.add),
                  reads=["ohc", "sone"] + (["carry"] if ch else []), writes=["incc", "s1sc"],
                  tag=f"scan {j},{ch}")
            s.dve(COPY(carry[:, j:j + 1], incc[:, CH - 1:CH]), reads=["incc"],
                  writes=["carry"], tag=f"carry {j},{ch}")
            s.dve(TS(tmpc[:, :], incc[:, :], -1.0, None, ALU.add), reads=["incc"],
                  writes=["tmpca", "s1sc"], tag=f"lrka {j},{ch}")
            s.dve(TT(tmpc[:, :], tmpc[:, :], ohc[:, :], ALU.mult), reads=["tmpca", "ohc"],
                  writes=["tmpc", "s1sc"], tag=f"lrkm {j},{ch}")
            for sc in range(CH // 512):
                s.te(MM(prow[0:1, 0:512], ones8[:, 0:1], tmpc[:, sc * 512:(sc + 1) * 512],
                        True, True), reads=["tmpc", "ones8", "prow_r"], writes=["prow"],
                     tag=f"lrk cs {j},{ch},{sc}")
                s.act(SCOPY(lrkc[0:1, sc * 512:(sc + 1) * 512], prow[0:1, 0:512]),
                      reads=["prow"], writes=[f"lrkc{sc}", "prow_r"],
                      tag=f"lrk ev {j},{ch},{sc}")
            s.dma("gpsimd", lrk_o[j:j + 1, ch * CH:(ch + 1) * CH], lrkc[0:1, 0:CH],
                  reads=[f"lrkc{sc}" for sc in range(CH // 512)], writes=["lrkD"],
                  semkey="dq:st:lrk", tag=f"lrk st {j},{ch}")
        s.dve(COPY(cntc[:, j:j + 1], carry[:, j:j + 1]), reads=["carry"],
              writes=[f"cnt{j}"], tag=f"cnt {j}")
    tot8 = s.sbuf("tot8", [8, 1], F32)
    trow = s.sbuf("trow", [1, 8], F32)
    s.dve(TT(tot8[:, 0:1], cntc[:, 0:1], cntc[:, 1:2], ALU.add), reads=["cnt0", "cnt1"],
          writes=["tot8"], tag="tot8")
    s.te(TRANS(prow[0:1, 0:8], tot8[:, 0:1], identf[0:8, 0:8]),
         reads=["tot8", "identf", "prow_r"], writes=["prow"], tag="tot tr")
    s.act(SCOPY(trow[0:1, :], prow[0:1, 0:8]), reads=["prow"], writes=["trow", "prow_r"],
          tag="tot ev")
    s.dve(lambda e: e.tensor_tensor_scan(trow[0:1, :], sone[0:1, 0:1].to_broadcast([1, 8]),
                                         trow[0:1, :], 0.0, op0=ALU.mult, op1=ALU.add),
          reads=["trow", "sone"], writes=["trow"], tag="tot scan")
    s.te(TRANS(prow[0:8, 0:1], trow[0:1, :], identf[0:1, 0:1]),
         reads=["trow", "identf", "prow_r"], writes=["prow"], tag="tot tr2")
    s.act(SCOPY(cntc[:, 2:3], prow[0:8, 0:1]), reads=["prow"], writes=["base_i", "prow_r"],
          tag="tot ev2")
    s.dve(TT(cntc[:, 2:3], cntc[:, 2:3], tot8[:, 0:1], ALU.subtract),
          reads=["base_i", "tot8"], writes=["base0"], tag="base0")
    s.dve(TT(cntc[:, 3:4], cntc[:, 2:3], cntc[:, 0:1], ALU.add),
          reads=["base0", "cnt0"], writes=["base1"], tag="base1")
    baserep = s.sbuf("baserep", [128, 16], F32)
    iotar = s.sbuf("iotar", [128, 8], F32)
    s.pool(lambda e: e.iota(iotar[:, :], [[1, 8]], base=0, channel_multiplier=0,
                            allow_small_or_imprecise_dtypes=True), writes=["iotar"])
    s.pool(lambda e: e.iota(trashc[:, :], [[1, 1]], base=16384, channel_multiplier=1,
                            allow_small_or_imprecise_dtypes=True), writes=["trashc"])
    for j in range(2):
        s.te(TRANS(prow[0:1, 0:8], cntc[:, 2 + j:3 + j], identf[0:8, 0:8]),
             reads=[f"base{j}", "identf", "prow_r"], writes=["prow"], tag=f"br tr {j}")
        s.act(SCOPY(trow[0:1, :], prow[0:1, 0:8]), reads=["prow"],
              writes=["trow", "prow_r"], tag=f"br ev {j}")
        s.dve(COPY(trow[0:1, :], trow[0:1, :]), reads=["trow"], writes=["trowb"],
              tag=f"br cp {j}")
        s.te(MM(prow[0:128, 0:8], onesf[0:1, :], trow[0:1, :], True, True),
             reads=["trowb", "onesf", "prow_r"], writes=["prow"], tag=f"br rep {j}")
        s.act(SCOPY(baserep[:, j * 8:(j + 1) * 8], prow[0:128, 0:8]), reads=["prow"],
              writes=[f"baserep{j}", "prow_r"], tag=f"br repe {j}")

    xnt = [s.sbuf("xnt0", [128, 1024], F32)] * 2
    xnb = [s.sbuf("xnbA", [128, 1024], BF16)] * 2
    xnT = [s.sbuf("xnTA", [128, 1024], BF16)] * 2
    qf = s.sbuf("qf", [128, 1024], F32)
    kvlb = s.sbuf("kvlb", [128, 256], BF16)
    kvlT = s.sbuf("kvlT", [128, 256], BF16)
    kf = s.sbuf("kf", [128, 1024], F32)
    cost1 = s.sbuf("cost1", [128, 1024], BF16)
    sint1 = s.sbuf("sint1", [128, 1024], BF16)
    swp = s.sbuf("swp", [128, 1024], F32)
    qb = s.sbuf("qb", [128, 1024], BF16)
    kb = s.sbuf("kb", [128, 1024], BF16)
    vb = [s.sbuf("vbA", [128, 1024], BF16)] * 2
    qTt = s.sbuf("qTt", [64, 2048], BF16)
    kTt = s.sbuf("kTt", [64, 2048], BF16)
    vstk = [s.sbuf("vstkA", [128, 16 * 65], BF16)] * 2
    pexp = s.sbuf("pexpa", [128, 512], BF16)
    orw = None
    oTt = s.sbuf("oTt", [64, 2048], BF16)
    oTh = s.sbuf("oTh", [64, 2048], BF16)
    s1b = [s.sbuf("s1b0", [128, 1024], F32)] * 2
    ecol = s.sbuf("ecol", [128, 2], F32)
    lcol = s.sbuf("lcol", [128, 2], F32)
    ohcol = s.sbuf("ohcol", [128, 8], F32)
    bcol = s.sbuf("bcol", [128, 2], F32)
    einvt = s.sbuf("einvt", [128, 2], F32)
    c99col = s.sbuf("c99col", [128, 2], F32)
    vm2c = s.sbuf("vm2c", [128, 2], mybir.dt.uint8)
    s.pool(MEMSET(c99col[:, :], 99.0), writes=["c99col"])
    pcol2 = s.sbuf("pcol2", [128, 2], F32)
    trashc = s.sbuf("trashc", [128, 1], F32)
    posu = s.sbuf("posu", [128, 2], U32)

    pq_ = [s.psum(f"pqa{i}", [128, 512], F32) for i in range(2)]
    pat = s.psum("pat", [128, 512], F32)
    po6 = s.psum("po6", [128, 512], F32)
    pr6 = s.psum("pr6", [128, 512], F32)
    ptb = [s.psum(f"ptba{i}", [128, 1024], BF16) for i in range(2)]

    s.pool(lambda e, a=vstk[0]: e.memset(
        bass.AP(a, 64, [[16 * 65, 128], [65, 16]]), 1.0), writes=["vstkA_ones"])

    for t in range(NT):
        p = t % 2
        b, tc = t // 32, t % 32
        r0 = t * 128
        xi, xik = xnt[p], "xnt0"
        xbb, xbk = xnb[p], "xnbA"
        xT, xTk = xnT[p], "xnTA"
        s.dma("sync", xi[:, :], xn1_i[r0:r0 + 128, :], writes=[xik], tag=f"xn1 ld {t}")
        s.dve(COPY(xbb[:, :], xi[:, :]), reads=[xik], writes=[xbk], tag=f"cast {t}")
        for c in range(8):
            pt_, ptk = ptb[c % 2], f"ptba{c % 2}"
            s.te(TRANS(pt_[:, 0:128], xbb[:, c * 128:(c + 1) * 128], identb[:, :]),
                 reads=[xbk, "identb", ptk + "_r"], writes=[ptk], tag=f"trx {t},{c}")
            s.act(SCOPY(xT[:, c * 128:(c + 1) * 128], pt_[:, 0:128]),
                  reads=[ptk], writes=[xTk + f"_{c}", ptk + "_r"], tag=f"evx {t},{c}")
        XK = [xTk + f"_{c}" for c in range(8)]
        for nb in range(2):
            pp, ppk = pq_[nb], f"pqa{nb}"
            for c in range(8):
                s.te(MM(pp[:, :], xT[:, c * 128:(c + 1) * 128],
                        Wqs[:, c * 1024 + nb * 512: c * 1024 + (nb + 1) * 512], c == 0, False),
                     reads=XK + WK + ([ppk + "_r"] if c == 0 else []), writes=[ppk],
                     tag=f"q {t},{nb},{c}")
            s.te(MM(pp[:, :], onesb[0:1, :], bqs[0:1, nb * 512:(nb + 1) * 512], False, True),
                 reads=WK, writes=[ppk], tag=f"qb {t},{nb}")
            s.dve(COPY(qf[:, nb * 512:(nb + 1) * 512], pp[:, :]), reads=[ppk],
                  writes=[f"qf{nb}", ppk + "_r"], tag=f"q ev {t},{nb}")
        pp, ppk = pq_[0], "pqa0"
        for c in range(8):
            s.te(MM(pp[:, 0:256], xT[:, c * 128:(c + 1) * 128],
                    Wds[:, c * 256:(c + 1) * 256], c == 0, False),
                 reads=XK + WK + (["pqa0_r"] if c == 0 else []), writes=[ppk],
                 tag=f"kvl {t},{c}")
        s.te(MM(pp[:, 0:256], onesb[0:1, :], bds[0:1, :], False, True),
             reads=WK, writes=[ppk], tag=f"kvlb {t}")
        s.dve(COPY(kvlb[:, :], pp[:, 0:256]), reads=[ppk], writes=["kvlb", ppk + "_r"],
              tag=f"kvl ev {t}")
        for c in range(2):
            pt_, ptk = ptb[c % 2], f"ptba{c % 2}"
            s.te(TRANS(pt_[:, 0:128], kvlb[:, c * 128:(c + 1) * 128], identb[:, :]),
                 reads=["kvlb", "identb", ptk + "_r"], writes=[ptk], tag=f"trl {t},{c}")
            s.act(SCOPY(kvlT[:, c * 128:(c + 1) * 128], pt_[:, 0:128]),
                  reads=[ptk], writes=[f"kvlT_{c}", ptk + "_r"], tag=f"evl {t},{c}")
        for nb in range(4):
            pp, ppk = pq_[nb % 2], f"pqa{nb % 2}"
            for c in range(2):
                s.te(MM(pp[:, :], kvlT[:, c * 128:(c + 1) * 128],
                        Wus[:, c * 2048 + nb * 512: c * 2048 + (nb + 1) * 512], c == 0, False),
                     reads=["kvlT_0", "kvlT_1"] + WK + ([ppk + "_r"] if c == 0 else []),
                     writes=[ppk], tag=f"kv {t},{nb},{c}")
            s.te(MM(pp[:, :], onesb[0:1, :], bus[0:1, nb * 512:(nb + 1) * 512], False, True),
                 reads=WK, writes=[ppk], tag=f"kvb {t},{nb}")
            s.dve(lambda e, o=bass.AP(kf, nb * 256, [[1024, 128], [64, 4], [1, 64]]),
                  i_=bass.AP(pp, 0, [[512, 128], [128, 4], [1, 64]]): e.tensor_copy(o, i_),
                  reads=[ppk], writes=[f"kf{nb}"], tag=f"kv evk {t},{nb}")
            s.dve(lambda e, o=bass.AP(vb[p], nb * 256, [[1024, 128], [64, 4], [1, 64]]),
                  i_=bass.AP(pp, 64, [[512, 128], [128, 4], [1, 64]]): e.tensor_copy(o, i_),
                  reads=[ppk], writes=[f"vbp{nb}", ppk + "_r"], tag=f"kv evv {t},{nb}")
        KVK = [f"kf{nb}" for nb in range(4)]
        VBK = [f"vbp{nb}" for nb in range(4)]
        vbb, vbk = vb[p], "vbA"
        s.dma("sync", cost1[:, :], cosr[tc * 128:(tc + 1) * 128, :], writes=["cost1"],
              tag=f"cos {t}")
        s.dma("sync", sint1[:, :], sinr[tc * 128:(tc + 1) * 128, :], writes=["sint1"],
              tag=f"sin {t}")
        for (srcb, srck, dstb, dstk) in ((qf, ["qf0", "qf1"], qb, "qb"),
                                         (kf, KVK, kb, "kb")):
            s.dve(lambda e, o=bass.AP(swp, 0, [[1024, 128], [64, 16], [1, 32]]),
                  i_=bass.AP(srcb, 32, [[1024, 128], [64, 16], [1, 32]]): e.tensor_copy(o, i_),
                  reads=srck, writes=["swp_a"], tag=f"sw1 {t},{dstk}")
            s.dve(lambda e, o=bass.AP(swp, 32, [[1024, 128], [64, 16], [1, 32]]),
                  i_=bass.AP(srcb, 0, [[1024, 128], [64, 16], [1, 32]]): e.tensor_copy(o, i_),
                  reads=srck, writes=["swp_b"], tag=f"sw2 {t},{dstk}")
            s.pool(TT(swp[:, :], swp[:, :], sint1[:, :], ALU.mult),
                   reads=["swp_a", "swp_b", "sint1"], writes=["swp"], tag=f"rsin {t},{dstk}")
            s.dve(TT(srcb[:, :], srcb[:, :], cost1[:, :], ALU.mult),
                  reads=srck + ["cost1"], writes=[srck[0] + "m"] + srck,
                  tag=f"rcos {t},{dstk}")
            s.dve(TT(dstb[:, :], srcb[:, :], swp[:, :], ALU.add),
                  reads=[srck[0] + "m", "swp"], writes=[dstk], tag=f"radd {t},{dstk}")
        s.dma("scalar", vscr[:, :], vbb[:, :], reads=VBK, writes=["vscr"],
              semkey="dq:vscr", tag=f"vscr st {t}")
        vs, vsk = vstk[p], "vstkA"
        for t8 in range(8):
            s.dma("scalar",
                  bass.AP(vs, t8 * 16 * 1040, [[1040, 16], [65, 16], [1, 64]]),
                  bass.AP(vscr, t8 * 1024, [[64, 16], [8192, 16], [1, 64]]),
                  reads=["vscr"], writes=[vsk] if t8 == 7 else [f"vstk_{t8}"],
                  semkey="dq:vstk", tag=f"vstk ld {t},{t8}")
        for g in range(16):
            pt_, ptk = ptb[g % 2], f"ptba{g % 2}"
            s.te(TRANS(pt_[0:64, 0:128], kb[:, g * 64:(g + 1) * 64], identb[:, :]),
                 reads=["kb", "identb", ptk + "_r"], writes=[ptk], tag=f"trk {t},{g}")
            s.dve(lambda e, o=bass.AP(kTt, g, [[2048, 64], [128, 16], [16, 8]]),
                  i_=bass.AP(pt_, 0, [[1024, 64], [8, 16], [1, 8]]): e.tensor_copy(o, i_),
                  reads=[ptk], writes=[f"kTt_{g}", ptk + "_r"], tag=f"evk {t},{g}")
            pt2, pt2k = ptb[(g + 1) % 2], f"ptba{(g + 1) % 2}"
            s.te(TRANS(pt2[0:64, 0:128], qb[:, g * 64:(g + 1) * 64], identb[:, :]),
                 reads=["qb", "identb", pt2k + "_r"], writes=[pt2k], tag=f"trq {t},{g}")
            s.dve(lambda e, o=bass.AP(qTt, g, [[2048, 64], [128, 16], [16, 8]]),
                  i_=bass.AP(pt2, 0, [[1024, 64], [8, 16], [1, 8]]): e.tensor_copy(o, i_),
                  reads=[pt2k], writes=[f"qTt_{g}", pt2k + "_r"], tag=f"evq {t},{g}")
        KTK = [f"kTt_{g}" for g in range(16)]
        QTK = [f"qTt_{g}" for g in range(16)]
        for g4 in range(4):
            for gg in range(4):
                grp = g4 * 4 + gg
                s.te(MM(pat[:, gg * 128:(gg + 1) * 128], kTt[:, grp * 128:(grp + 1) * 128],
                        qTt[:, grp * 128:(grp + 1) * 128], True, True),
                     reads=KTK + QTK + (["pat_r"] if gg == 0 else []), writes=["pat"],
                     tag=f"att {t},{grp}")
            s.act(ACTV(pexp[:, :], pat[:, :], AF.Exp, bias=nC1[:, 0:1], scale=SC1),
                  reads=["pat", "nC1"], writes=["pexpa", "pmsk"], tag=f"aexp {t},{g4}")
            s.dve(TT(pexp[:, :], pexp[:, :], maskc[:, :], ALU.mult),
                  reads=["pexpa", "maskc"], writes=["pmsk"], tag=f"amask {t},{g4}")
            for gg in range(4):
                grp = g4 * 4 + gg
                s.te(MM(po6[0:65, gg * 128:(gg + 1) * 128], vs[:, grp * 65:(grp + 1) * 65],
                        pexp[:, gg * 128:(gg + 1) * 128], True, True),
                     reads=[vsk, "vstkA_ones", "pmsk"] + [f"vstk_{t8}" for t8 in range(8)] + (["po6_r"] if gg == 0 else []),
                     writes=["po6"], tag=f"ao {t},{grp}")
            s.act(SCOPY(qf[0:64, 0:512], po6[0:64, :]), reads=["po6"],
                  writes=["orw", "qf0", "qf1"], tag=f"aev {t},{g4}")
            s.dve(lambda e, a=swp[0:1, 0:512], b_=po6[64:65, :]: e.reciprocal(a, b_),
                  reads=["po6"], writes=["rdena", "swp", "swp_a", "swp_b", "po6_r"],
                  tag=f"arcp {t},{g4}")
            s.te(MM(pr6[0:64, :], onesf[0:1, 0:64], swp[0:1, 0:512], True, True),
                 reads=["onesf", "rdena", "pr6_r"], writes=["pr6"], tag=f"arep {t},{g4}")
            s.dve(TT(oTt[:, g4 * 512:(g4 + 1) * 512], qf[0:64, 0:512], pr6[0:64, :], ALU.mult),
                  reads=["orw", "qf0", "pr6"], writes=[f"oTt_{g4}", "pr6_r"], tag=f"adiv {t},{g4}")
        OTK = [f"oTt_{g4}" for g4 in range(4)]
        for h in range(16):
            s.dve(lambda e, o=oTh[:, h * 128:(h + 1) * 128],
                  i_=bass.AP(oTt, h, [[2048, 64], [128, 16], [16, 8]]): e.tensor_copy(o, i_),
                  reads=OTK, writes=[f"oTh_{h}"], tag=f"ohm {t},{h}")
        OHK2 = [f"oTh_{h}" for h in range(16)]
        s1t, s1k = s1b[p], "s1b0"
        for nb in range(2):
            pp, ppk = pq_[nb], f"pqa{nb}"
            for h in range(16):
                s.te(MM(pp[:, :], oTh[:, h * 128:(h + 1) * 128],
                        Wos[0:64, h * 1024 + nb * 512: h * 1024 + (nb + 1) * 512],
                        h == 0, False),
                     reads=OHK2 + WK + ([ppk + "_r"] if h == 0 else []), writes=[ppk],
                     tag=f"wo1 {t},{nb},{h}")
            s.te(MM(pp[:, :], onesb[0:1, :], bos[0:1, nb * 512:(nb + 1) * 512], False, True),
                 reads=WK, writes=[ppk], tag=f"wo1b {t},{nb}")
            s.dve(COPY(s1t[:, nb * 512:(nb + 1) * 512], pp[:, :]), reads=[ppk],
                  writes=[s1k + f"_{nb}", ppk + "_r"], tag=f"wo1 ev {t},{nb}")
        s.dma("gpsimd", s1_o[r0:r0 + 128, :], s1t[:, :], reads=[s1k + "_0", s1k + "_1"],
              semkey="dq:st:s1", tag=f"s1 st {t}")
        ri2, ri2k = rtt[t % 2], f"rtt{t % 2}"
        s.dma("sync", ri2[:, :], rt_i[t * 128:(t + 1) * 128, :], writes=[ri2k],
              tag=f"rt2 ld {t}")
        s.dve(TS(vm2c[:, 0:2], ri2[:, 2:4], 1e-9, None, ALU.is_gt), reads=[ri2k],
              writes=["vm2"], tag=f"vm2 {t}")
        s.dve(lambda e, rr=ri2: e.select(ecol[:, 0:2], vm2c[:, 0:2], rr[:, 0:2],
                                         c99col[:, 0:2]),
              reads=["vm2", "c99col", ri2k], writes=["ecol", "ecol0"], tag=f"ec {t}")
        for j in range(2):
            s.dma("sync", lrow[0:1, :], lrk_o[j:j + 1, t * 128:(t + 1) * 128],
                  reads=["lrkD"], writes=["lrow"], tag=f"lrow ld {t},{j}")
            s.te(TRANS(prow[0:128, 0:1], lrow[0:1, :], identf[0:1, 0:1]),
                 reads=["lrow", "identf", "prow_r"], writes=["prow"], tag=f"lc tr {t},{j}")
            s.act(SCOPY(lcol[:, j:j + 1], prow[0:128, 0:1]), reads=["prow"],
                  writes=["lcol" if j else "lcol0", "prow_r"], tag=f"lc ev {t},{j}")
        for j in range(2):
            s.dve(TS(ohcol[:, :], iotar[:, :], ecol[:, j:j + 1], None, ALU.is_equal),
                  reads=["ecol", "ecol0", "iotar"], writes=["ohcol"],
                  tag=f"ohc {t},{j}")
            s.dve(TT(ohcol[:, :], ohcol[:, :], baserep[:, j * 8:(j + 1) * 8], ALU.mult),
                  reads=["ohcol", f"baserep{j}"], writes=["ohcol"], tag=f"ohb {t},{j}")
            s.dve(lambda e, a=bcol[:, j:j + 1], b_=ohcol[:, :]:
                  e.reduce_sum(a, b_, axis=mybir.AxisListType.X),
                  reads=["ohcol"], writes=[f"bcol{j}"], tag=f"bsum {t},{j}")
        s.dve(TS(einvt[:, 0:2], ecol[:, 0:2], 98.5, None, ALU.is_gt),
              reads=["ecol", "ecol0"], writes=["einvt"], tag=f"einv {t}")
        s.dve(lambda e: e.scalar_tensor_tensor(pcol2[:, 0:2], einvt[:, 0:2], 1e6,
                                               bcol[:, 0:2], ALU.mult, ALU.add),
              reads=["einvt", "bcol0", "bcol1"], writes=["pcol2"], tag=f"binv {t}")
        s.dve(TT(pcol2[:, 0:2], pcol2[:, 0:2], lcol[:, 0:2], ALU.add),
              reads=["pcol2", "lcol", "lcol0"], writes=["pcol2"], tag=f"posc {t}")
        s.dve(TS(pcol2[:, 0:2], pcol2[:, 0:2], trashc[:, 0:1], None, ALU.min),
              reads=["pcol2", "trashc"], writes=["pcol2"], tag=f"poscl {t}")
        s.dve(COPY(posu[:, :], pcol2[:, 0:2]), reads=["pcol2"], writes=["posu"],
              tag=f"posu {t}")
        for j in range(2):
            s.add("gpsimd", lambda e, o=slab_o[:, :], i_=xbb[:, :],
                  ix=posu[:, j:j + 1]: e.indirect_dma_start(
                      out=o, out_offset=bass.IndirectOffsetOnAxis(ap=ix, axis=0),
                      in_=i_, in_offset=None, oob_is_err=False),
                  reads=[xbk, "posu"], writes=[f"slab{j}"], dma=True, tag=f"slab {t},{j}")
            s.ops[-1]["ch"] = "dq:slab"



def emit_l1b(s, slaball, erow_all, fc1T, fc2T, fb1, fb2, ls_i, ehot, crow16, identb_in, resl, dbgidx):
    identb = s.sbuf("identb_sb", [128, 128], BF16)
    s.dma("sync", identb[:, :], identb_in[:, :], writes=["identb"])
    identf = s.sbuf("identf_sb", [128, 128], F32)
    s.dve(COPY(identf[:, :], identb[:, :]), reads=["identb"], writes=["identf"])
    onesb = s.sbuf("onesb", [1, 128], BF16)
    s.pool(MEMSET(onesb[:, :], 1.0), writes=["onesb"])
    onesf = s.sbuf("onesf", [1, 128], F32)
    s.pool(MEMSET(onesf[:, :], 1.0), writes=["onesf"])
    prow = s.psum("prow", [128, 512], F32)
    w1s = s.sbuf("w1s", [128, 8 * 4096], BF16)
    for c in range(8):
        s.dma("sync", w1s[:, c * 4096:(c + 1) * 4096], fc1T[c * 128:(c + 1) * 128, :],
              writes=[f"w1s{c}"], semkey="dq:w1s", tag=f"w1 {c}")
    W1K = [f"w1s{c}" for c in range(8)]
    ehots = s.sbuf("ehots", [8, 1], F32)
    s.dma("sync", ehots[:, :], ehot[:, :], writes=["ehots"])
    cr16 = s.sbuf("cr16", [1, 16], F32)
    s.dma("sync", cr16[:, :], crow16[:, :], writes=["cr16"])
    fb2sb = s.sbuf("fb2sb", [1, 1024], BF16)
    s.dma("gpsimd", fb2sb[:, :], fb2[:, :], writes=["fb2sb"], semkey="dq:wini")
    lsrow = s.sbuf("lsrow", [1, 1024], F32)
    s.dma("sync", lsrow[:, :], ls_i[:, :], writes=["lsrow"])
    fb1r = s.sbuf("fb1r", [32, 128], F32)
    s.dma("sync", fb1r[:, :], fb1[:, :], writes=["fb1r"])
    fb1c = s.sbuf("fb1c", [128, 32], F32)
    s.te(TRANS(prow[:, 0:32], fb1r[:, :], identf[0:32, 0:32]),
         reads=["fb1r", "identf", "prow_r"], writes=["prow"], tag="fb1 tr")
    s.act(SCOPY(fb1c[:, :], prow[:, 0:32]), reads=["prow"], writes=["fb1c", "prow_r"],
          tag="fb1 ev")
    lsrep = s.sbuf("lsrep", [128, 1024], F32)
    for nb in range(2):
        s.te(MM(prow[:, :], onesf[0:1, :], lsrow[0:1, nb * 512:(nb + 1) * 512], True, True),
             reads=["lsrow", "onesf", "prow_r"], writes=["prow"], tag=f"lsrep {nb}")
        s.act(SCOPY(lsrep[:, nb * 512:(nb + 1) * 512], prow[:, :]), reads=["prow"],
              writes=[f"lsrep{nb}", "prow_r"], tag=f"lsrep ev {nb}")

    cnt0m, cnt1m, CK = build_counts(s, erow_all, prow)

    tot = s.sbuf("tot", [8, 8], F32)
    s.dve(TT(tot[:, :], cnt0m[:, :], cnt1m[:, :], ALU.add), reads=CK, writes=["tot"],
          tag="tot")
    t16 = s.sbuf("t16", [16, 8], F32)
    s.te(TRANS(prow[0:8, 0:8], tot[:, :], identf[0:8, 0:8]),
         reads=["tot", "identf", "prow_r"], writes=["prow"], tag="tot tr")
    s.act(SCOPY(t16[0:8, 0:8], prow[0:8, 0:8]), reads=["prow"], writes=["t16", "prow_r"],
          tag="tot ev")
    sone8 = s.sbuf("sone8", [8, 1], F32)
    s.pool(MEMSET(sone8[:, :], 1.0), writes=["sone8"])
    s.dve(lambda e: e.tensor_tensor_scan(t16[0:8, 0:8], sone8[0:8, 0:1].to_broadcast([8, 8]),
                                         t16[0:8, 0:8], 0.0, op0=ALU.mult, op1=ALU.add),
          reads=["t16", "sone8"], writes=["t16s"], tag="tot scan")
    s.te(TRANS(prow[0:8, 0:8], t16[0:8, 0:8], identf[0:8, 0:8]),
         reads=["t16s", "identf", "prow_r"], writes=["prow"], tag="tot tr2")
    colin = s.sbuf("colin", [8, 8], F32)
    s.act(SCOPY(colin[:, :], prow[0:8, 0:8]), reads=["prow"], writes=["colin", "prow_r"],
          tag="tot ev2")
    colex = s.sbuf("colex", [8, 8], F32)
    s.dve(TT(colex[:, :], colin[:, :], tot[:, :], ALU.subtract), reads=["colin", "tot"],
          writes=["colex"], tag="colex")
    myrows = s.sbuf("myrows", [1, 32], F32)
    for i, (mat, matk) in enumerate([(colex, "colex"), (cnt0m, "cmat0"), (cnt1m, "cmat1")]):
        rk = ["colex"] if i == 0 else CK
        s.te(MM(prow[0:1, 0:8], ehots[:, 0:1], mat[:, :], True, True),
             reads=rk + ["ehots", "prow_r"], writes=["prow"], tag=f"myrow {i}")
        s.act(SCOPY(myrows[0:1, i * 8:(i + 1) * 8], prow[0:1, 0:8]), reads=["prow"],
              writes=[f"myrow{i}", "prow_r"], tag=f"myrow ev {i}")
    brow = s.sbuf("brow", [1, 16], F32)
    s.dve(TT(brow[0:1, 0:8], myrows[0:1, 0:8], cr16[0:1, 0:8], ALU.add),
          reads=["myrow0", "cr16"], writes=["brow_a"], tag="brow a")
    s.dve(TT(brow[0:1, 8:16], brow[0:1, 0:8], myrows[0:1, 8:16], ALU.add),
          reads=["brow_a", "myrow1"], writes=["brow_b"], tag="brow b")
    lrow16 = s.sbuf("lrow16", [1, 16], F32)
    s.dve(COPY(lrow16[0:1, 0:8], myrows[0:1, 8:16]), reads=["myrow1"], writes=["lrow_a"],
          tag="lrow a")
    s.dve(COPY(lrow16[0:1, 8:16], myrows[0:1, 16:24]), reads=["myrow2"], writes=["lrow_b"],
          tag="lrow b")
    srow = s.sbuf("srow", [1, 17], F32)
    s.pool(MEMSET(srow[:, :], 0.0), writes=["srow_z"])
    sone1 = s.sbuf("sone1", [1, 1], F32)
    s.pool(MEMSET(sone1[:, :], 1.0), writes=["sone1"])
    s.dve(lambda e: e.tensor_tensor_scan(srow[0:1, 1:17], sone1[0:1, 0:1].to_broadcast([1, 16]),
                                         lrow16[0:1, :], 0.0, op0=ALU.mult, op1=ALU.add),
          reads=["lrow_a", "lrow_b", "sone1", "srow_z"], writes=["srow"], tag="srow scan")
    vrow = s.sbuf("vrow", [1, 16], F32)
    s.dve(TT(vrow[0:1, :], brow[0:1, :], srow[0:1, 0:16], ALU.subtract),
          reads=["brow_a", "brow_b", "srow", "srow_z"], writes=["vrow"], tag="vrow")
    drow = s.sbuf("drow", [1, 16], F32)
    s.dve(COPY(drow[0:1, 0:1], vrow[0:1, 0:1]), reads=["vrow"], writes=["drow_a"], tag="d0")
    s.dve(TT(drow[0:1, 1:16], vrow[0:1, 1:16], vrow[0:1, 0:15], ALU.subtract),
          reads=["vrow"], writes=["drow_b"], tag="dr")
    srep = s.sbuf("srep", [128, 17], F32)
    drep = s.sbuf("drep", [128, 16], F32)
    s.te(MM(prow[:, 0:17], onesf[0:1, :], srow[0:1, :], True, True),
         reads=["srow", "srow_z", "onesf", "prow_r"], writes=["prow"], tag="srep")
    s.act(SCOPY(srep[:, :], prow[:, 0:17]), reads=["prow"], writes=["srep", "prow_r"],
          tag="srep ev")
    s.te(MM(prow[:, 0:16], onesf[0:1, :], drow[0:1, :], True, True),
         reads=["drow_a", "drow_b", "onesf", "prow_r"], writes=["prow"], tag="drep")
    s.act(SCOPY(drep[:, :], prow[:, 0:16]), reads=["prow"], writes=["drep", "prow_r"],
          tag="drep ev")
    idxm = s.sbuf("idxm", [128, 80], F32)
    tmpm = s.sbuf("tmpm", [128, 80], F32)
    idxu = s.sbuf("idxu", [128, 80], U32)
    s.pool(lambda e: e.iota(idxm[:, :], [[128, 80]], base=0, channel_multiplier=1,
                            allow_small_or_imprecise_dtypes=True),
           writes=["idxm"], tag="idx iota")
    pm = s.sbuf("pm", [128, 80], F32)
    s.dve(COPY(pm[:, :], idxm[:, :]), reads=["idxm"], writes=["pm"], tag="pm")
    for sseg in range(17):
        s.dve(TS(tmpm[:, :], pm[:, :], srep[:, sseg:sseg + 1], None, ALU.is_ge),
              reads=["pm", "srep"], writes=["tmpm"], tag=f"seg ge {sseg}")
        if sseg < 16:
            s.dve(lambda e, d=drep[:, sseg:sseg + 1]:
                  e.scalar_tensor_tensor(idxm[:, :], tmpm[:, :], 1.0, idxm[:, :],
                                         ALU.mult, ALU.bypass) if False else None,
                  reads=[], writes=[], tag="noop")
            s.ops.pop()
            s.dve(lambda e, d=drep[:, sseg:sseg + 1]:
                  e.tensor_scalar(tmpm[:, :], tmpm[:, :], d, None, op0=ALU.mult),
                  reads=["tmpm", "drep"], writes=["tmpm2"], tag=f"seg mul {sseg}")
            s.dve(TT(idxm[:, :], idxm[:, :], tmpm[:, :], ALU.add),
                  reads=["idxm", "tmpm2"], writes=["idxm"], tag=f"seg add {sseg}")
        else:
            s.dve(TS(tmpm[:, :], tmpm[:, :], 1e9, None, ALU.mult),
                  reads=["tmpm"], writes=["tmpm2"], tag="seg big")
            s.dve(TT(idxm[:, :], idxm[:, :], tmpm[:, :], ALU.add),
                  reads=["idxm", "tmpm2"], writes=["idxm"], tag="seg bigadd")
    s.dve(TS(idxm[:, :], idxm[:, :], 16384.0, None, ALU.min), reads=["idxm"],
          writes=["idxm"], tag="idx clamp")
    s.dve(COPY(idxu[:, :], idxm[:, :]), reads=["idxm"], writes=["idxu"], tag="idxu")
    s.dma("gpsimd", dbgidx[:, :], idxm[:, :], reads=["idxm"], semkey="dq:st:dbgi")

    toks = s.sbuf("toks", [128, 4 * 1024], BF16)
    tokT = s.sbuf("tokT", [128, 8 * 512], BF16)
    ybT = s.sbuf("ybT", [128, 32 * 512], BF16)
    w2s = [s.sbuf(f"w2s{i}", [128, 1024], BF16) for i in range(2)]
    yout = s.sbuf("yout", [128, 1024], F32)
    youtb = [s.sbuf(f"youtb{i}", [128, 1024], BF16) for i in range(2)]
    ptb = [s.psum(f"ptbb{i}", [128, 1024], BF16) for i in range(2)]
    pyt = [s.psum(f"pytb{i}", [128, 512], F32) for i in range(2)]
    pz = [s.psum(f"pzb{i}", [128, 512], F32) for i in range(2)]
    s.pool(MEMSET(toks[:, :], 0.0), writes=["toks_init"])

    _STAGE = int(os.environ.get("L1B_STAGE", "4"))
    for grp in range(int(os.environ.get("L1B_GROUPS", "20"))):
        TKK = []
        for ti in range(4):
            g = grp * 4 + ti
            if _STAGE != 12:
                s.add("gpsimd", lambda e, o=toks[:, ti * 1024:(ti + 1) * 1024],
                      ix=idxu[:, g:g + 1]: e.indirect_dma_start(
                          out=o, out_offset=None, in_=slaball[:, :],
                          in_offset=bass.IndirectOffsetOnAxis(ap=ix, axis=0),
                          oob_is_err=False),
                      reads=["idxu", "toks_init", f"tok_r{ti}"], writes=[f"tok{ti}"],
                      dma=True, tag=f"gather {g}")
                s.ops[-1]["ch"] = f"dq:tok{ti}"
            else:
                s.pool(MEMSET(toks[:, ti * 1024:(ti + 1) * 1024], 0.5),
                       reads=[f"tok_r{ti}"], writes=[f"tok{ti}"], tag=f"gmemset {g}")
            TKK.append(f"tok{ti}")
            if _STAGE == 11:
                continue
            if _STAGE < 2:
                s.dma("gpsimd", resl[(grp * 4 + ti) * 128:(grp * 4 + ti + 1) * 128, :],
                      toks[:, ti * 1024:(ti + 1) * 1024], reads=[f"tok{ti}"],
                      writes=[f"tok_r{ti}"], semkey="dq:st:res0", tag=f"res direct {g}")
                continue
            for c in range(8):
                pt_, ptk = ptb[c % 2], f"ptbb{c % 2}"
                s.te(TRANS(pt_[:, 0:128], toks[:, ti * 1024 + c * 128: ti * 1024 + (c + 1) * 128],
                           identb[:, :]),
                     reads=[f"tok{ti}", "identb", ptk + "_r"], writes=[ptk],
                     tag=f"trt {g},{c}")
                s.act(SCOPY(tokT[:, c * 512 + ti * 128: c * 512 + (ti + 1) * 128],
                            pt_[:, 0:128]), reads=[ptk],
                      writes=[f"tokT{c}_{ti}", ptk + "_r"], tag=f"evt {g},{c}")
        TTK = [f"tokT{c}_{ti}" for c in range(8) for ti in range(4)]
        if _STAGE < 3:
            continue
        for yt in range(32):
            pp, ppk = pyt[yt % 2], f"pytb{yt % 2}"
            for c in range(8):
                s.te(MM(pp[:, :], w1s[:, c * 4096 + yt * 128: c * 4096 + (yt + 1) * 128],
                        tokT[:, c * 512:(c + 1) * 512], c == 0, c == 7),
                     reads=TTK + W1K + [ppk + "_r"], writes=[ppk], tag=f"f1 {grp},{yt},{c}")
            s.act(ACTV(ybT[:, yt * 512:(yt + 1) * 512], pp[:, :], AF.Gelu,
                       bias=fb1c[:, yt:yt + 1]),
                  reads=[ppk, "fb1c"], writes=[f"ybT{yt}", ppk + "_r"],
                  tag=f"gelu {grp},{yt}")
        if _STAGE < 4:
            continue
        for half in range(2):
            zb = [pz[0], pz[1], pyt[0], pyt[1]]
            zbk = ["pzb0", "pzb1", "pytb0", "pytb1"]
            for yt in range(32):
                wsl, wslk = w2s[yt % 2], f"w2s{yt % 2}"
                s.dma("sync", wsl[:, :], fc2T[yt * 128:(yt + 1) * 128, :],
                      writes=[wslk], semkey=f"dq:{wslk}", tag=f"w2 {grp},{half},{yt}")
                for q in range(4):
                    ti = half * 2 + q // 2
                    nb = q % 2
                    s.te(MM(zb[q][:, :], ybT[:, yt * 512 + ti * 128: yt * 512 + (ti + 1) * 128],
                            wsl[:, nb * 512:(nb + 1) * 512], yt == 0, False),
                         reads=[f"ybT{yt}", wslk], writes=[zbk[q]],
                         tag=f"f2 {grp},{half},{yt},{q}")
            for q in range(4):
                nb = q % 2
                s.te(MM(zb[q][:, :], onesb[0:1, :], fb2sb[0:1, nb * 512:(nb + 1) * 512],
                        False, True), reads=["onesb", "fb2sb"], writes=[zbk[q]],
                     tag=f"f2b {grp},{half},{q}")
            for tq in range(2):
                ti = half * 2 + tq
                g = grp * 4 + ti
                yo, yok = youtb[tq], f"youtb{tq}"
                for nb in range(2):
                    q = tq * 2 + nb
                    s.dve(TT(yout[:, nb * 512:(nb + 1) * 512], zb[q][:, :],
                             lsrep[:, nb * 512:(nb + 1) * 512], ALU.mult),
                          reads=[zbk[q], "lsrep0", "lsrep1"], writes=[f"yout{nb}"],
                          tag=f"yls {g},{nb}")
                    s.dve(TT(yo[:, nb * 512:(nb + 1) * 512], yout[:, nb * 512:(nb + 1) * 512],
                             toks[:, ti * 1024 + nb * 512: ti * 1024 + (nb + 1) * 512],
                             ALU.add),
                          reads=[f"yout{nb}", f"tok{ti}"], writes=[yok + f"_{nb}", zbk[q] + "_r"],
                          tag=f"yadd {g},{nb}")
                s.dma("gpsimd", resl[g * 128:(g + 1) * 128, :], yo[:, :],
                      reads=[yok + "_0", yok + "_1"], writes=[f"tok_r{ti}"],
                      semkey=f"dq:st:res{tq}", tag=f"res st {g}")



def emit_l1c(s, x1_i, s1_i, rt_i, lrk_i, erow_all, resall, cmask, identb_in, y_o):
    identb = s.sbuf("identb_sb", [128, 128], BF16)
    s.dma("sync", identb[:, :], identb_in[:, :], writes=["identb"])
    identf = s.sbuf("identf_sb", [128, 128], F32)
    s.dve(COPY(identf[:, :], identb[:, :]), reads=["identb"], writes=["identf"])
    onesf = s.sbuf("onesf", [1, 128], F32)
    s.pool(MEMSET(onesf[:, :], 1.0), writes=["onesf"])
    prow = s.psum("prow", [128, 512], F32)
    cnt0m, cnt1m, CK = build_counts(s, erow_all, prow)
    cms = s.sbuf("cms", [8, 1], F32)
    s.dma("sync", cms[:, :], cmask[:, :], writes=["cms"])
    gb = s.sbuf("gb", [8, 4], F32)
    cmrep = s.sbuf("cmrep", [8, 8], F32)
    cmr = s.sbuf("cmr", [1, 8], F32)
    s.te(TRANS(prow[0:1, 0:8], cms[:, 0:1], identf[0:8, 0:8]),
         reads=["cms", "identf", "prow_r"], writes=["prow"], tag="cm tr")
    s.act(SCOPY(cmr[0:1, :], prow[0:1, 0:8]), reads=["prow"], writes=["cmr", "prow_r"],
          tag="cm ev")
    s.te(MM(prow[0:8, 0:8], onesf[0:1, 0:8], cmr[0:1, :], True, True),
         reads=["cmr", "onesf", "prow_r"], writes=["prow"], tag="cm rep")
    s.act(SCOPY(cmrep[:, :], prow[0:8, 0:8]), reads=["prow"], writes=["cmrep", "prow_r"],
          tag="cm repe")
    tmp88 = s.sbuf("tmp88", [8, 8], F32)
    for j in range(2):
        s.dve(TT(tmp88[:, :], [cnt0m, cnt1m][j][:, :], cmrep[:, :], ALU.mult),
              reads=CK + ["cmrep"], writes=["tmp88"], tag=f"gb mul {j}")
        s.dve(lambda e, a=gb[:, j:j + 1], b_=tmp88[:, :]:
              e.reduce_sum(a, b_, axis=mybir.AxisListType.X),
              reads=["tmp88"], writes=[f"gb{j}"], tag=f"gb red {j}")
    s.dve(lambda e, a=gb[:, 2:3], b_=cnt0m[:, :]:
          e.reduce_sum(a, b_, axis=mybir.AxisListType.X),
          reads=CK, writes=["gbT0"], tag="gb T0")
    s.dve(TT(gb[:, 3:4], gb[:, 2:3], gb[:, 1:2], ALU.add), reads=["gbT0", "gb1"],
          writes=["gb3"], tag="gb1f")
    baserep = s.sbuf("baserep", [128, 16], F32)
    trow = s.sbuf("trow", [1, 8], F32)
    for j, (col, ck) in enumerate([(0, "gb0"), (3, "gb3")]):
        s.te(TRANS(prow[0:1, 0:8], gb[:, col:col + 1], identf[0:8, 0:8]),
             reads=[ck, "identf", "prow_r"], writes=["prow"], tag=f"gbr tr {j}")
        s.act(SCOPY(trow[0:1, :], prow[0:1, 0:8]), reads=["prow"], writes=["trow", "prow_r"],
              tag=f"gbr ev {j}")
        s.te(MM(prow[0:128, 0:8], onesf[0:1, :], trow[0:1, :], True, True),
             reads=["trow", "onesf", "prow_r"], writes=["prow"], tag=f"gbr rep {j}")
        s.act(SCOPY(baserep[:, j * 8:(j + 1) * 8], prow[0:128, 0:8]), reads=["prow"],
              writes=[f"baserep{j}", "prow_r"], tag=f"gbr repe {j}")
    iotar = s.sbuf("iotar", [128, 8], F32)
    s.pool(lambda e: e.iota(iotar[:, :], [[1, 8]], base=0, channel_multiplier=0,
                            allow_small_or_imprecise_dtypes=True), writes=["iotar"])


    rtt = [s.sbuf(f"rtt{i}", [128, 4], F32) for i in range(2)]
    x1t = [s.sbuf(f"x1t{i}", [128, 1024], F32) for i in range(2)]
    s1t = [s.sbuf(f"s1t{i}", [128, 1024], F32) for i in range(2)]
    lrow = s.sbuf("lrow", [1, 128], F32)
    lcol = s.sbuf("lcol", [128, 2], F32)
    ecol = s.sbuf("ecol", [128, 2], F32)
    vm2c = s.sbuf("vm2c", [128, 2], mybir.dt.uint8)
    c99col = s.sbuf("c99col", [128, 2], F32)
    s.pool(MEMSET(c99col[:, :], 99.0), writes=["c99col"])
    ohcol = s.sbuf("ohcol", [128, 8], F32)
    bcol = s.sbuf("bcol", [128, 2], F32)
    kmask = s.sbuf("kmask", [128, 2], F32)
    km8 = s.sbuf("km8", [128, 2], mybir.dt.uint8)
    bigcol = s.sbuf("bigcol", [128, 2], F32)
    wcol = s.sbuf("wcol", [128, 2], F32)
    pcol = s.sbuf("pcol", [128, 2], F32)
    ridx = s.sbuf("ridx", [128, 2], F32)
    ridxu = s.sbuf("ridxu", [128, 2], U32)
    gat = [s.sbuf(f"gat{j}", [128, 1024], BF16) for j in range(2)]
    momo = s.sbuf("momo", [128, 1024], F32)
    tmpg = s.sbuf("tmpg", [128, 1024], F32)
    yb = [s.sbuf(f"yb{i}", [128, 1024], F32) for i in range(2)]
    s.pool(MEMSET(bigcol[:, :], 1e9), writes=["bigcol"])
    for j in range(2):
        s.pool(MEMSET(gat[j][:, :], 0.0), writes=[f"gat{j}_init"])

    for t in range(NT):
        p = t % 2
        r0 = t * 128
        ri, rik = rtt[p], f"rtt{p}"
        s.dma("sync", ri[:, :], rt_i[r0:r0 + 128, :], writes=[rik], tag=f"rt ld {t}")
        x1b, x1k = x1t[p], f"x1t{p}"
        s1b, s1k = s1t[p], f"s1t{p}"
        s.dma("sync", x1b[:, :], x1_i[r0:r0 + 128, :], writes=[x1k], tag=f"x1 ld {t}")
        s.dma("sync", s1b[:, :], s1_i[r0:r0 + 128, :], writes=[s1k], tag=f"s1 ld {t}")
        s.dve(TS(vm2c[:, 0:2], ri[:, 2:4], 1e-9, None, ALU.is_gt), reads=[rik],
              writes=["vm2"], tag=f"vm2 {t}")
        s.dve(lambda e, rr=ri: e.select(ecol[:, 0:2], vm2c[:, 0:2], rr[:, 0:2],
                                        c99col[:, 0:2]),
              reads=["vm2", "c99col", rik], writes=["ecol"], tag=f"ec {t}")
        for j in range(2):
            s.dma("sync", lrow[0:1, :], lrk_i[j:j + 1, r0:r0 + 128],
                  writes=["lrow"], tag=f"lrow ld {t},{j}")
            s.te(TRANS(prow[0:128, 0:1], lrow[0:1, :], identf[0:1, 0:1]),
                 reads=["lrow", "identf", "prow_r"], writes=["prow"], tag=f"lc tr {t},{j}")
            s.act(SCOPY(lcol[:, j:j + 1], prow[0:128, 0:1]), reads=["prow"],
                  writes=[f"lcol{j}", "prow_r"], tag=f"lc ev {t},{j}")
            s.dve(TS(ohcol[:, :], iotar[:, :], ecol[:, j:j + 1], None, ALU.is_equal),
                  reads=["ecol", "iotar"], writes=["ohcol"], tag=f"ohc {t},{j}")
            s.dve(TT(ohcol[:, :], ohcol[:, :], baserep[:, j * 8:(j + 1) * 8], ALU.mult),
                  reads=["ohcol", f"baserep{j}"], writes=["ohcol"], tag=f"ohb {t},{j}")
            s.dve(lambda e, a=bcol[:, j:j + 1], b_=ohcol[:, :]:
                  e.reduce_sum(a, b_, axis=mybir.AxisListType.X),
                  reads=["ohcol"], writes=[f"bcol{j}"], tag=f"bsum {t},{j}")
        s.dve(TT(pcol[:, 0:2], bcol[:, 0:2], lcol[:, 0:2], ALU.add),
              reads=["bcol0", "bcol1", "lcol0", "lcol1"], writes=["pcol"], tag=f"pos {t}")
        s.dve(TS(km8[:, 0:2], pcol[:, 0:2], 9830.5, None, ALU.is_lt),
              reads=["pcol"], writes=["km8"], tag=f"km8 {t}")
        s.dve(COPY(kmask[:, 0:2], km8[:, 0:2]), reads=["km8"], writes=["kmask"],
              tag=f"km {t}")
        s.dve(TT(wcol[:, 0:2], ri[:, 2:4], kmask[:, 0:2], ALU.mult),
              reads=[rik, "kmask"], writes=["wcol"], tag=f"wc {t}")
        s.dve(lambda e: e.scalar_tensor_tensor(ridx[:, 0:2], ecol[:, 0:2], 10240.0,
                                               pcol[:, 0:2], ALU.mult, ALU.add),
              reads=["ecol", "pcol"], writes=["ridx"], tag=f"ri {t}")
        s.dve(TS(ridx[:, 0:2], ridx[:, 0:2], 81919.0, None, ALU.min),
              reads=["ridx"], writes=["ridx"], tag=f"ri2 {t}")
        s.dve(COPY(ridxu[:, :], ridx[:, 0:2]), reads=["ridx"], writes=["ridxu"],
              tag=f"riu {t}")
        for j in range(2):
            s.add("gpsimd", lambda e, o=gat[j][:, :], ix=ridxu[:, j:j + 1]:
                  e.indirect_dma_start(out=o, out_offset=None, in_=resall[:, :],
                                       in_offset=bass.IndirectOffsetOnAxis(ap=ix, axis=0),
                                       oob_is_err=False),
                  reads=["ridxu", f"gat{j}_init", f"gat{j}_r"], writes=[f"gat{j}"],
                  dma=True, tag=f"rgat {t},{j}")
            s.ops[-1]["ch"] = f"dq:gat{j}"
        s.dve(TS(momo[:, :], gat[0][:, :], wcol[:, 0:1], None, ALU.mult),
              reads=["gat0", "wcol"], writes=["momo", "gat0_r"], tag=f"mm0 {t}")
        s.dve(TS(tmpg[:, :], gat[1][:, :], wcol[:, 1:2], None, ALU.mult),
              reads=["gat1", "wcol"], writes=["tmpg", "gat1_r"], tag=f"mm1 {t}")
        s.dve(TT(momo[:, :], momo[:, :], tmpg[:, :], ALU.add),
              reads=["momo", "tmpg"], writes=["momo"], tag=f"mms {t}")
        s.act(ACTV(momo[:, :], momo[:, :], AF.Silu), reads=["momo"], writes=["momo"],
              tag=f"sil {t}")
        ybb, ybk = yb[p], f"yb{p}"
        s.dve(TT(ybb[:, :], momo[:, :], s1b[:, :], ALU.mult), reads=["momo", s1k],
              writes=[ybk], tag=f"ys {t}")
        s.dve(TT(ybb[:, :], ybb[:, :], x1b[:, :], ALU.add), reads=[ybk, x1k],
              writes=[ybk], tag=f"yx {t}")
        s.dma("gpsimd", y_o[r0:r0 + 128, :], ybb[:, :], reads=[ybk],
              semkey="dq:st:y", tag=f"y st {t}")



def build_A():
    nc = bass.Bass("TRN2", target_bir_lowering=False)
    T_ = {}
    x_in = nc.dram_tensor("x", [NTOK, D], F32, kind="ExternalInput")
    q1 = nc.dram_tensor("q1", [1024, 4096], I32, kind="ExternalInput")
    q2 = nc.dram_tensor("q2", [4096, 1024], I32, kind="ExternalInput")
    sc1 = nc.dram_tensor("sc1", [8, 4096], F32, kind="ExternalInput")
    sc2 = nc.dram_tensor("sc2", [32, 1024], F32, kind="ExternalInput")
    rw = nc.dram_tensor("rw", [8, 1024], F32, kind="ExternalInput")
    identf_in = nc.dram_tensor("identf", [128, 128], F32, kind="ExternalInput")
    identb_in = nc.dram_tensor("identb", [128, 128], BF16, kind="ExternalInput")
    wdq = nc.dram_tensor("wdq", [1024, 32], F32, kind="ExternalInput")
    wdkv = nc.dram_tensor("wdkv", [1024, 32], F32, kind="ExternalInput")
    wkr = nc.dram_tensor("wkr", [1024, 32], F32, kind="ExternalInput")
    bcat = nc.dram_tensor("bcat", [1, 96], F32, kind="ExternalInput")
    wuq = nc.dram_tensor("wuq", [32, 1024], F32, kind="ExternalInput")
    buq = nc.dram_tensor("buq", [1, 1024], F32, kind="ExternalInput")
    wuk = nc.dram_tensor("wuk", [32, 1024], F32, kind="ExternalInput")
    buk = nc.dram_tensor("buk", [1, 1024], F32, kind="ExternalInput")
    wuv = nc.dram_tensor("wuv", [32, 1024], F32, kind="ExternalInput")
    buv = nc.dram_tensor("buv", [1, 1024], F32, kind="ExternalInput")
    wqr = nc.dram_tensor("wqr", [32, 256], F32, kind="ExternalInput")
    bqr = nc.dram_tensor("bqr", [1, 256], F32, kind="ExternalInput")
    WO = nc.dram_tensor("WO", [1024, 1024], F32, kind="ExternalInput")
    bO = nc.dram_tensor("bO", [1, 1024], F32, kind="ExternalInput")
    gam = nc.dram_tensor("gam", [1, 1024], F32, kind="ExternalInput")
    bet = nc.dram_tensor("bet", [1, 1024], F32, kind="ExternalInput")
    fb1 = nc.dram_tensor("fb1", [32, 128], F32, kind="ExternalInput")
    fb2 = nc.dram_tensor("fb2", [1, 1024], F32, kind="ExternalInput")
    rb_i = nc.dram_tensor("rb", [1, 8], F32, kind="ExternalInput")
    Wq = nc.dram_tensor("Wq", [1024, 1024], F32, kind="ExternalInput")
    bq = nc.dram_tensor("bq", [1, 1024], F32, kind="ExternalInput")
    Wd = nc.dram_tensor("Wd", [1024, 256], F32, kind="ExternalInput")
    bd = nc.dram_tensor("bd", [1, 256], F32, kind="ExternalInput")
    Wu = nc.dram_tensor("Wu", [256, 2048], F32, kind="ExternalInput")
    bu = nc.dram_tensor("bu", [1, 2048], F32, kind="ExternalInput")
    Wo1 = nc.dram_tensor("Wo1", [1024, 1024], F32, kind="ExternalInput")
    bo1 = nc.dram_tensor("bo1", [1, 1024], F32, kind="ExternalInput")
    cosr = nc.dram_tensor("cosr", [4096, 1024], BF16, kind="ExternalInput")
    sinr = nc.dram_tensor("sinr", [4096, 1024], BF16, kind="ExternalInput")
    mask_i = nc.dram_tensor("maskc", [128, 512], BF16, kind="ExternalInput")

    s1_o = nc.dram_tensor("s1", [NTOK, D], F32, kind="ExternalOutput")
    slab_o = nc.dram_tensor("slab", [16512, D], BF16, kind="ExternalOutput")
    erow_o = nc.dram_tensor("erow", [2, NTOK], BF16, kind="ExternalOutput")
    lrk_o = nc.dram_tensor("lrkrow", [2, NTOK], F32, kind="ExternalOutput")
    rt_o = nc.dram_tensor("route", [NTOK, 4], F32, kind="ExternalOutput")
    x1_o = nc.dram_tensor("x1", [NTOK, D], F32, kind="ExternalOutput")

    w1b = nc.dram_tensor("w1b", [1024, 4096], BF16)
    w2b = nc.dram_tensor("w2b", [4096, 1024], BF16)
    rwT = nc.dram_tensor("rwT", [1024, 8], F32)
    xn_s = nc.dram_tensor("xn_s", [NTOK, D], F32)
    mus_s = nc.dram_tensor("mus_s", [NTOK, 2], F32)
    QT1 = nc.dram_tensor("QT1", [2048, 4096], BF16)
    QT2 = nc.dram_tensor("QT2", [512, 4096], BF16)
    KT1 = nc.dram_tensor("KT1", [2048, 4096], BF16)
    KT2 = nc.dram_tensor("KT2", [64, 4096], BF16)
    V_s = nc.dram_tensor("V", [NTOK, D], BF16)
    OT = nc.dram_tensor("OT", [2048, 4096], BF16)
    xn1_s = nc.dram_tensor("xn1", [NTOK, D], F32)
    wrow_s = nc.dram_tensor("wrow", [2, NTOK], F32)
    vscr = nc.dram_tensor("vscr", [128, 1024], BF16)

    s = Sched(nc)
    s.push_scope()
    emit_prep(s, q1, q2, sc1, sc2, rw, identf_in, w1b, w2b, rwT)
    s.barrier(); s.pop_scope(); s.push_scope()
    emit_l0pre(s, x_in, wdq, wdkv, wkr, bcat, wuq, buq, wuk, buk, wuv, buv,
               wqr, bqr, identb_in, xn_s, mus_s, QT1, QT2, KT1, KT2, V_s)
    s.barrier(); s.pop_scope(); s.push_scope()
    emit_l0attn(s, QT1, QT2, KT1, KT2, V_s, OT)
    s.barrier(); s.pop_scope(); s.push_scope()
    emit_l0rest(s, x_in, xn_s, mus_s, OT, WO, bO, gam, bet, w1b, w2b, fb1, fb2,
                rwT, rb_i, identb_in, x1_o, xn1_s, rt_o)
    s.barrier(); s.pop_scope(); s.push_scope()
    emit_l1a(s, xn1_s, rt_o, Wq, bq, Wd, bd, Wu, bu, Wo1, bo1, cosr, sinr, mask_i,
             identb_in, s1_o, slab_o, erow_o, wrow_s, lrk_o, vscr)
    s.barrier(); s.pop_scope()
    s.emit_all()
    return nc


def run_A(inputs):
    if "A" not in _CACHE:
        _CACHE["A"] = build_A()
    cosrep, sinrep, mask4 = host_consts()
    identb_np = np.eye(128, dtype=bf16)
    identf_np = np.eye(128, dtype=np.float32)
    x_full = inputs["L0_attn_h_t"]
    bcat = np.concatenate([inputs["L0_attn_b_DQ"], inputs["L0_attn_b_DKV"],
                           inputs["L0_attn_b_KR"]]).reshape(1, 96)
    in_maps = []
    for c in range(NCORES):
        in_maps.append({
            "x": np.ascontiguousarray(x_full[2 * c:2 * c + 2].reshape(NTOK, D)),
            "q1": inputs["L0_ffn_fc1_q"], "q2": inputs["L0_ffn_fc2_q"],
            "sc1": inputs["L0_ffn_fc1_scales"], "sc2": inputs["L0_ffn_fc2_scales"],
            "rw": inputs["L1_ffn_router_w"], "identf": identf_np, "identb": identb_np,
            "wdq": inputs["L0_attn_W_DQ"], "wdkv": inputs["L0_attn_W_DKV"],
            "wkr": inputs["L0_attn_W_KR"], "bcat": bcat,
            "wuq": inputs["L0_attn_W_UQ"], "buq": inputs["L0_attn_b_UQ"].reshape(1, -1),
            "wuk": inputs["L0_attn_W_UK"], "buk": inputs["L0_attn_b_UK"].reshape(1, -1),
            "wuv": inputs["L0_attn_W_UV"], "buv": inputs["L0_attn_b_UV"].reshape(1, -1),
            "wqr": inputs["L0_attn_W_QR"], "bqr": inputs["L0_attn_b_QR"].reshape(1, -1),
            "WO": inputs["L0_attn_W_O"], "bO": inputs["L0_attn_b_O"].reshape(1, -1),
            "gam": inputs["L0_ffn_ln_gamma"].reshape(1, -1),
            "bet": inputs["L0_ffn_ln_beta"].reshape(1, -1),
            "fb1": inputs["L0_ffn_fc1_bias"].reshape(32, 128),
            "fb2": inputs["L0_ffn_fc2_bias"].reshape(1, -1),
            "rb": inputs["L1_ffn_router_b"].reshape(1, -1),
            "Wq": inputs["L1_attn_Wq"], "bq": inputs["L1_attn_bq"].reshape(1, -1),
            "Wd": inputs["L1_attn_Wdown"], "bd": inputs["L1_attn_bdown"].reshape(1, -1),
            "Wu": inputs["L1_attn_Wup"], "bu": inputs["L1_attn_bup"].reshape(1, -1),
            "Wo1": inputs["L1_attn_Wout"], "bo1": inputs["L1_attn_bout"].reshape(1, -1),
            "cosr": cosrep.astype(bf16), "sinr": sinrep.astype(bf16), "maskc": mask4,
        })
    res = run_bass_kernel_spmd(_CACHE["A"], in_maps, core_ids=list(range(NCORES)))
    return res.results


def build_B():
    nc = bass.Bass("TRN2", target_bir_lowering=False)
    fc1w = nc.dram_tensor("fc1w", [4096, 1024], F32, kind="ExternalInput")
    fc2w = nc.dram_tensor("fc2w", [1024, 4096], F32, kind="ExternalInput")
    slaball = nc.dram_tensor("slaball", [8 * 16512, D], BF16, kind="ExternalInput")
    erow_all = nc.dram_tensor("erow_all", [16, NTOK], BF16, kind="ExternalInput")
    fb1 = nc.dram_tensor("fb1", [32, 128], F32, kind="ExternalInput")
    fb2 = nc.dram_tensor("fb2", [1, 1024], F32, kind="ExternalInput")
    ls_i = nc.dram_tensor("ls", [1, 1024], F32, kind="ExternalInput")
    ehot = nc.dram_tensor("ehot", [8, 1], F32, kind="ExternalInput")
    crow16 = nc.dram_tensor("crow16", [1, 16], F32, kind="ExternalInput")
    identb_in = nc.dram_tensor("identb", [128, 128], BF16, kind="ExternalInput")
    identf_in = nc.dram_tensor("identf", [128, 128], F32, kind="ExternalInput")
    resl = nc.dram_tensor("resl", [10240, D], BF16, kind="ExternalOutput")
    dbgidx = nc.dram_tensor("dbgidx", [128, 80], F32, kind="ExternalOutput")
    fc1T = nc.dram_tensor("fc1T", [1024, 4096], BF16)
    fc2T = nc.dram_tensor("fc2T", [4096, 1024], BF16)

    s = Sched(nc)
    s.push_scope()
    sid = s.sbuf("sid", [128, 128], F32)
    s.dma("sync", sid[:, :], identf_in[:, :], writes=["sid"])
    win = [s.sbuf(f"win{i}", [128, 128], F32) for i in range(4)]
    pt = [s.psum(f"pt{i}", [128, 512], F32) for i in range(4)]
    obuf = [s.sbuf(f"obuf{i}", [128, 4096], BF16) for i in range(2)]

    def transpose_mat(src_, dst, R, C, tagp):
        nr, ncol = R // 128, C // 128
        for cj in range(ncol):
            ob = obuf[cj % 2]
            for ri in range(nr):
                ib = win[ri % 4]
                ibk = f"win{ri % 4}"
                pb = pt[ri % 4]
                pbk = f"pt{ri % 4}"
                s.dma("sync", ib[:, :], src_[ri * 128:(ri + 1) * 128, cj * 128:(cj + 1) * 128],
                      writes=[ibk], tag=f"{tagp} ld {ri},{cj}")
                s.te(TRANS(pb[:, 0:128], ib[:, :], sid[:, :]),
                     reads=[ibk, "sid"], writes=[pbk], tag=f"{tagp} tr {ri},{cj}")
                s.dve(COPY(ob[:, ri * 128:(ri + 1) * 128], pb[:, 0:128]),
                      reads=[pbk], writes=[f"obuf{cj % 2}_{ri}"], tag=f"{tagp} ev {ri},{cj}")
            s.dma("gpsimd", dst[cj * 128:(cj + 1) * 128, :], ob[:, 0:R],
                  reads=[f"obuf{cj % 2}_{ri}" for ri in range(nr)],
                  writes=[f"{tagp}st"], semkey=f"dq:st:obuf{cj % 2}", tag=f"{tagp} st {cj}")

    transpose_mat(fc1w, fc1T, 4096, 1024, "t1")
    transpose_mat(fc2w, fc2T, 1024, 4096, "t2")
    s.barrier(); s.pop_scope(); s.push_scope()
    emit_l1b(s, slaball, erow_all, fc1T, fc2T, fb1, fb2, ls_i, ehot, crow16,
             identb_in, resl, dbgidx)
    s.barrier(); s.pop_scope()
    s.emit_all()
    return nc


def run_B(inputs, A):
    if "B" not in _CACHE:
        _CACHE["B"] = build_B()
    slaball = np.concatenate([np.asarray(A[c]["slab"]) for c in range(NCORES)], axis=0)
    erow_all = np.concatenate([np.asarray(A[c]["erow"]) for c in range(NCORES)], axis=0)
    crow16 = np.tile(np.arange(8, dtype=np.float32) * 16512.0, 2).reshape(1, 16)
    identb_np = np.eye(128, dtype=bf16)
    identf_np = np.eye(128, dtype=np.float32)
    in_maps = []
    for c in range(NCORES):
        eh = np.zeros((8, 1), np.float32)
        eh[c, 0] = 1.0
        in_maps.append({
            "fc1w": np.ascontiguousarray(inputs["L1_ffn_fc1_w"][c]),
            "fc2w": np.ascontiguousarray(inputs["L1_ffn_fc2_w"][c]),
            "slaball": slaball, "erow_all": erow_all,
            "fb1": inputs["L1_ffn_fc1_b"][c].reshape(32, 128),
            "fb2": inputs["L1_ffn_fc2_b"][c].reshape(1, -1),
            "ls": inputs["L1_ffn_layer_scale"][c].reshape(1, -1),
            "ehot": eh, "crow16": crow16, "identb": identb_np, "identf": identf_np,
        })
    res = run_bass_kernel_spmd(_CACHE["B"], in_maps, core_ids=list(range(NCORES)))
    return res.results


def build_C():
    nc = bass.Bass("TRN2", target_bir_lowering=False)
    x1_i = nc.dram_tensor("x1", [NTOK, D], F32, kind="ExternalInput")
    s1_i = nc.dram_tensor("s1", [NTOK, D], F32, kind="ExternalInput")
    rt_i = nc.dram_tensor("route", [NTOK, 4], F32, kind="ExternalInput")
    lrk_i = nc.dram_tensor("lrkrow", [2, NTOK], F32, kind="ExternalInput")
    erow_all = nc.dram_tensor("erow_all", [16, NTOK], BF16, kind="ExternalInput")
    resall = nc.dram_tensor("resall", [8 * 10240, D], BF16, kind="ExternalInput")
    cmask = nc.dram_tensor("cmask", [8, 1], F32, kind="ExternalInput")
    identb_in = nc.dram_tensor("identb", [128, 128], BF16, kind="ExternalInput")
    y_o = nc.dram_tensor("y", [NTOK, D], F32, kind="ExternalOutput")
    s = Sched(nc)
    s.push_scope()
    emit_l1c(s, x1_i, s1_i, rt_i, lrk_i, erow_all, resall, cmask, identb_in, y_o)
    s.barrier(); s.pop_scope()
    s.emit_all()
    return nc


def run_C(inputs, A, B):
    if "C" not in _CACHE:
        _CACHE["C"] = build_C()
    resall = np.concatenate([np.asarray(B[c]["resl"]) for c in range(NCORES)], axis=0)
    erow_all = np.concatenate([np.asarray(A[c]["erow"]) for c in range(NCORES)], axis=0)
    identb_np = np.eye(128, dtype=bf16)
    in_maps = []
    for c in range(NCORES):
        cm = np.zeros((8, 1), np.float32)
        cm[:c, 0] = 1.0
        in_maps.append({
            "x1": np.asarray(A[c]["x1"]), "s1": np.asarray(A[c]["s1"]),
            "route": np.asarray(A[c]["route"]),
            "lrkrow": np.asarray(A[c]["lrkrow"]),
            "erow_all": erow_all, "resall": resall, "cmask": cm,
            "identb": identb_np,
        })
    res = run_bass_kernel_spmd(_CACHE["C"], in_maps, core_ids=list(range(NCORES)))
    return res.results


_CACHE = {}


def kernel(**inputs):
    A = [dict((k, np.asarray(v)) for k, v in r.items()) for r in run_A(inputs)]
    B = [dict((k, np.asarray(v)) for k, v in r.items()) for r in run_B(inputs, A)]
    C = run_C(inputs, A, B)
    y = np.concatenate([np.asarray(C[c]["y"]).reshape(2, 4096, 1024)
                        for c in range(NCORES)], axis=0)
    return np.ascontiguousarray(y, dtype=np.float32)


# version 05 (final)
# speedup vs baseline: 1.1947x; 1.0676x over previous
import os
import numpy as np
import ml_dtypes
from contextlib import ExitStack

import concourse.bass as bass
import concourse.mybir as mybir
from concourse.bass_utils import run_bass_kernel_spmd

F32 = mybir.dt.float32
BF16 = mybir.dt.bfloat16
I32 = mybir.dt.int32
U32 = mybir.dt.uint32
AF = mybir.ActivationFunctionType
ALU = mybir.AluOpType
NCORES = 8
BLOC = 2
T = 4096
D = 1024
NTOK = BLOC * T
NT = NTOK // 128

C0 = 8.0
C1 = 12.0
SC0 = 1.0 / float(np.sqrt(160.0))
SC1 = 1.0 / 8.0

bf16 = ml_dtypes.bfloat16


class Sched:
    STREAMS = ("tensor", "scalar", "vector", "gpsimd", "sync")

    def __init__(self, nc):
        self.nc = nc
        self.ops = []
        self.lw = {}
        self.rd = {}
        self.es = ExitStack()
        self.scopes = []
        self._names = set()
        self._nuniq = 0

    def push_scope(self):
        self.scopes.append(ExitStack())

    def pop_scope(self):
        self.scopes.pop().close()

    def barrier(self):
        last_by_ch = {}
        for op in self.ops:
            last_by_ch[op["ch"]] = op["i"]
        deps = sorted(set(last_by_ch.values()))
        for stream in self.STREAMS:
            i = len(self.ops)
            self.ops.append(dict(i=i, stream=stream, ch=stream, emit=lambda e: e.nop(),
                                 deps=deps, dma=False, tag="barrier", need_inc=False))

    def _stack(self):
        return self.scopes[-1] if self.scopes else self.es

    def sbuf(self, name, shape, dtype):
        if name in self._names:
            self._nuniq += 1
            name = f"{name}_u{self._nuniq}"
        self._names.add(name)
        return self._stack().enter_context(self.nc.sbuf_tensor(name, shape, dtype))

    def psum(self, name, shape, dtype):
        if name in self._names:
            self._nuniq += 1
            name = f"{name}_u{self._nuniq}"
        self._names.add(name)
        return self._stack().enter_context(self.nc.psum_tensor(name, shape, dtype))

    def add(self, stream, emit, reads=(), writes=(), dma=False, tag=""):
        i = len(self.ops)
        deps = set()
        for k in reads:
            j = self.lw.get(k)
            if j is not None:
                deps.add(j)
        for k in writes:
            j = self.lw.get(k)
            if j is not None:
                deps.add(j)
            deps.update(self.rd.get(k, ()))
        deps.discard(i)
        ch = stream + ("_q" if dma else "")
        self.ops.append(dict(i=i, stream=stream, ch=ch, emit=emit, deps=sorted(deps),
                             dma=dma, tag=tag, need_inc=False))
        for k in writes:
            self.lw[k] = i
            self.rd[k] = []
        for k in reads:
            self.rd.setdefault(k, []).append(i)
        return i

    def te(self, emit, reads=(), writes=(), tag=""):
        return self.add("tensor", emit, reads, writes, tag=tag)

    def act(self, emit, reads=(), writes=(), tag=""):
        return self.add("scalar", emit, reads, writes, tag=tag)

    def dve(self, emit, reads=(), writes=(), tag=""):
        return self.add("vector", emit, reads, writes, tag=tag)

    def pool(self, emit, reads=(), writes=(), tag=""):
        return self.add("gpsimd", emit, reads, writes, tag=tag)

    def dma(self, queue, out, in_, reads=(), writes=(), tag="", semkey=None, **kw):
        if semkey is None:
            if writes:
                semkey = "dq:" + writes[0]
            elif reads:
                semkey = "dq:st:" + reads[0]
            else:
                semkey = "dq:anon"

        def emit(eng):
            return eng.dma_start(out=out, in_=in_, **kw)
        i = self.add(queue, emit, reads, writes, dma=True, tag=tag)
        self.ops[i]["ch"] = semkey
        return i

    def emit_all(self, check=False):
        nc = self.nc
        for op in self.ops:
            kept = []
            for j in op["deps"]:
                dj = self.ops[j]
                if (dj["stream"] == "tensor" and op["stream"] == "tensor"
                        and op["tag"] != "barrier"):
                    continue
                kept.append(j)
                dj["need_inc"] = True
            op["kdeps"] = kept
        chcnt = {}
        for op in self.ops:
            if op["dma"]:
                op["need_inc"] = True
            if op["need_inc"]:
                ch = op["ch"]
                chcnt[ch] = chcnt.get(ch, 0) + (16 if op["dma"] else 1)
                op["cnt"] = chcnt[ch]
            else:
                op["cnt"] = None
        know = {st: {} for st in self.STREAMS}
        know_at = {}
        for op in self.ops:
            k = know[op["stream"]]
            waits = []
            for j in op["kdeps"]:
                dj = self.ops[j]
                if dj["cnt"] is None:
                    continue
                if k.get(dj["ch"], 0) >= dj["cnt"]:
                    continue
                waits.append((dj["ch"], dj["cnt"]))
                ka = know_at.get(j)
                if ka:
                    for c2, v2 in ka.items():
                        if k.get(c2, 0) < v2:
                            k[c2] = v2
                if k.get(dj["ch"], 0) < dj["cnt"]:
                    k[dj["ch"]] = dj["cnt"]
            op["waits"] = waits
            if op["need_inc"]:
                ka = dict(k)
                ka[op["ch"]] = op["cnt"]
                know_at[op["i"]] = ka
        self.channels = sorted(chcnt)
        self.chcnt = chcnt
        self.n_waits = sum(len(op["waits"]) for op in self.ops)
        with ExitStack() as es:
            sems = {ch: es.enter_context(nc.semaphore("sem_" + ch)) for ch in self.channels}
            block = es.enter_context(nc.Block())

            def run_stream(eng, stream):
                for op in self.ops:
                    if op["stream"] != stream:
                        continue
                    for ch, v in op["waits"]:
                        eng.wait_ge(sems[ch], v)
                    inst = op["emit"](eng)
                    if op["need_inc"]:
                        if inst is None:
                            raise RuntimeError(f"op {op['i']} ({op['tag']}) returned None but needs inc")
                        inst.then_inc(sems[op["ch"]], 16 if op["dma"] else 1)

            @block.tensor
            def _(eng):
                run_stream(eng, "tensor")

            @block.scalar
            def _(eng):
                run_stream(eng, "scalar")

            @block.vector
            def _(eng):
                run_stream(eng, "vector")

            @block.gpsimd
            def _(eng):
                run_stream(eng, "gpsimd")

            @block.sync
            def _(eng):
                run_stream(eng, "sync")
                for ch in self.channels:
                    eng.wait_ge(sems[ch], chcnt[ch])

        self.es.close()


def MM(out, lhsT, rhs, start, stop):
    def f(eng):
        return eng.matmul(out, lhsT, rhs, start=start, stop=stop)
    return f


def TRANS(out, in_, ident):
    def f(eng):
        return eng.transpose(out, in_, ident)
    return f


def COPY(out, in_):
    def f(eng):
        return eng.tensor_copy(out, in_)
    return f


def SCOPY(out, in_):
    def f(eng):
        return eng.copy(out, in_)
    return f


def ACTV(out, in_, func, bias=None, scale=None, accum_out=None):
    kw = {}
    if bias is not None:
        kw["bias"] = bias
    if scale is not None:
        kw["scale"] = scale
    if accum_out is not None:
        kw["accum_out"] = accum_out

    def f(eng):
        return eng.activation(out, in_, func, **kw)
    return f


def TT(out, a, b, op):
    def f(eng):
        return eng.tensor_tensor(out, a, b, op)
    return f


def TS(out, in0, s1, s2, op0, op1=None, accum_out=None):
    kw = dict(op0=op0)
    if op1 is not None:
        kw["op1"] = op1
    if accum_out is not None:
        kw["accum_out"] = accum_out

    def f(eng):
        return eng.tensor_scalar(out, in0, s1, s2, **kw)
    return f


def MEMSET(ap, v):
    def f(eng):
        return eng.memset(ap, v)
    return f




def host_consts():
    pos = np.arange(4096, dtype=np.float64)
    inv = 1.0 / (10000.0 ** (np.arange(0, 64, 2, dtype=np.float64) / 64.0))
    fr = pos[:, None] * inv[None, :]
    cos = np.cos(fr).astype(np.float32)
    sin = np.sin(fr).astype(np.float32)
    cos2 = np.concatenate([cos, cos], 1)
    sinpm = np.concatenate([-sin, sin], 1)
    cosrep = np.tile(cos2, (1, 16)).astype(np.float32)
    sinrep = np.tile(sinpm, (1, 16)).astype(np.float32)
    mask = np.kron(np.eye(8, dtype=np.float32), np.ones((16, 16), np.float32))
    mask4 = np.tile(mask, (1, 4)).astype(bf16)
    return cosrep, sinrep, mask4


def build_counts(s, erow_all, prow, tag="cnt"):
    onesr = s.sbuf(tag + "_onesr", [1, 8], BF16)
    s.pool(MEMSET(onesr[:, :], 1.0), writes=[tag + "onesr"])
    iota8 = s.sbuf(tag + "_iota8", [8, 1], F32)
    s.pool(lambda e: e.iota(iota8[:, :], [[1, 1]], base=0, channel_multiplier=1,
                            allow_small_or_imprecise_dtypes=True), writes=[tag + "iota8"])
    rowb = s.sbuf(tag + "_rowb", [1, 512], BF16)
    ohb = s.sbuf(tag + "_ohb", [8, 512], F32)
    red = s.sbuf(tag + "_red", [8, 1], F32)
    cmat = [s.sbuf(tag + f"_cm{j}", [8, 8], F32) for j in range(2)]
    for cp in range(8):
        for j in range(2):
            r = cp * 2 + j
            dstk = tag + f"cm{j}"
            for ch in range(16):
                s.dma("sync", rowb[0:1, :], erow_all[r:r + 1, ch * 512:(ch + 1) * 512],
                      writes=[tag + "rowb"], tag=f"{tag} ld {r},{ch}")
                s.te(MM(prow[0:8, 0:512], onesr[0:1, :], rowb[0:1, 0:512], True, True),
                     reads=[tag + "rowb", tag + "onesr", "prow_r"], writes=["prow"],
                     tag=f"{tag} rep {r},{ch}")
                s.act(SCOPY(ohb[:, :], prow[0:8, 0:512]), reads=["prow"],
                      writes=[tag + "ohbr", "prow_r"], tag=f"{tag} ev {r},{ch}")
                s.dve(TS(ohb[:, :], ohb[:, :], iota8[:, 0:1], None, ALU.is_equal),
                      reads=[tag + "ohbr", tag + "iota8"], writes=[tag + "ohb"],
                      tag=f"{tag} oh {r},{ch}")
                s.dve(lambda e, a=red[:, 0:1], b_=ohb[:, :]:
                      e.reduce_sum(a, b_, axis=mybir.AxisListType.X),
                      reads=[tag + "ohb"], writes=[tag + "red"], tag=f"{tag} rs {r},{ch}")
                if ch == 0:
                    s.dve(COPY(cmat[j][:, cp:cp + 1], red[:, 0:1]), reads=[tag + "red"],
                          writes=[dstk + f"_{cp}"], tag=f"{tag} st {r},{ch}")
                else:
                    s.dve(TT(cmat[j][:, cp:cp + 1], cmat[j][:, cp:cp + 1], red[:, 0:1],
                             ALU.add), reads=[tag + "red", dstk + f"_{cp}"],
                          writes=[dstk + f"_{cp}"], tag=f"{tag} ac {r},{ch}")
    CK = [tag + f"cm{j}_{cp}" for j in range(2) for cp in range(8)]
    return cmat[0], cmat[1], CK


def emit_prep(s, q1, q2, sc1, sc2, rw, identf, w1b, w2b, rwT):
    sid = s.sbuf("sid", [128, 128], F32)
    s.dma("sync", sid[:, :], identf[:, :], writes=["sid"])

    win = [s.sbuf(f"win{i}", [128, 128], F32) for i in range(4)]
    pt = [s.psum(f"pt{i}", [128, 512], F32) for i in range(4)]
    obuf = [s.sbuf(f"obuf{i}", [128, 4096], BF16) for i in range(2)]

    def transpose_mat(src, dst, R, C, tagp):
        nr, ncol = R // 128, C // 128
        for cj in range(ncol):
            ob = obuf[cj % 2]
            for ri in range(nr):
                ib = win[ri % 4]
                ibk = f"win{ri % 4}"
                pb = pt[ri % 4]
                pbk = f"pt{ri % 4}"
                s.dma("sync", ib[:, :], src[ri * 128:(ri + 1) * 128, cj * 128:(cj + 1) * 128],
                      writes=[ibk], tag=f"{tagp} ld {ri},{cj}")
                s.te(TRANS(pb[:, 0:128], ib[:, :], sid[:, :]),
                     reads=[ibk, "sid"], writes=[pbk], tag=f"{tagp} tr {ri},{cj}")
                s.dve(COPY(ob[:, ri * 128:(ri + 1) * 128], pb[:, 0:128]),
                      reads=[pbk], writes=[f"obuf{cj % 2}_{ri}"], tag=f"{tagp} ev {ri},{cj}")
            s.dma("gpsimd", dst[cj * 128:(cj + 1) * 128, :], ob[:, 0:R],
                  reads=[f"obuf{cj % 2}_{ri}" for ri in range(nr)],
                  writes=[f"{tagp}st"], semkey=f"dq:st:obuf{cj % 2}", tag=f"{tagp} st {cj}")


    rws = s.sbuf("rws", [8, 1024], F32)
    rwo = s.sbuf("rwo", [128, 8 * 8], F32)
    s.dma("sync", rws[:, :], rw[:, :], writes=["rws"])
    for c in range(8):
        pb = pt[c % 4]
        s.te(TRANS(pb[0:128, 0:8], rws[0:8, c * 128:(c + 1) * 128], sid[0:8, 0:8]),
             reads=["rws", "sid"], writes=[f"pt{c % 4}"], tag=f"rwT {c}")
        s.dve(COPY(rwo[:, c * 8:(c + 1) * 8], pb[0:128, 0:8]),
              reads=[f"pt{c % 4}"], writes=[f"rwo{c}"], tag=f"rwT ev {c}")
    for c in range(8):
        s.dma("gpsimd", rwT[c * 128:(c + 1) * 128, :], rwo[:, c * 8:(c + 1) * 8],
              reads=[f"rwo{c}"], writes=["rwT_st"], tag=f"rwT st {c}")

    ones1 = s.sbuf("ones1", [1, 128], BF16)
    s.pool(MEMSET(ones1[:, :], 1.0), writes=["ones1"])
    qin = [s.sbuf(f"qin{i}", [128, 4096], I32) for i in range(2)]
    qf = [s.sbuf(f"qf{i}", [128, 4096], F32) for i in range(2)]
    screp = [s.sbuf(f"screp{i}", [128, 4096], F32) for i in range(2)]
    psc = [s.psum(f"psc{i}", [128, 512], F32) for i in range(2)]
    wob = [s.sbuf(f"wob{i}", [128, 4096], BF16) for i in range(2)]
    scb = [s.sbuf(f"scb{i}", [1, 4096], BF16) for i in range(2)]

    def dequant(qsrc, scsrc, dst, K, N, tagp):
        ng = K // 128
        for g in range(ng):
            qi, qik = qin[g % 2], f"qin{g % 2}"
            qo, qok = qf[g % 2], f"qf{g % 2}"
            sr, srk = screp[g % 2], f"screp{g % 2}"
            wo, wok = wob[g % 2], f"wob{g % 2}"
            sb, sbk = scb[g % 2], f"scb{g % 2}"
            s.dma("sync", qi[:, 0:N], qsrc[g * 128:(g + 1) * 128, :], writes=[qik],
                  tag=f"{tagp} qld {g}")
            s.dma("gpsimd", sb[0:1, 0:N], scsrc[g:g + 1, :], writes=[sbk],
                  tag=f"{tagp} scld {g}")
            for nb in range((N + 511) // 512):
                n0, n1 = nb * 512, min(N, (nb + 1) * 512)
                pb = psc[nb % 2]
                s.te(MM(pb[:, 0:n1 - n0], ones1[0:1, :], sb[0:1, n0:n1], True, True),
                     reads=["ones1", sbk], writes=[f"psc{nb % 2}"],
                     tag=f"{tagp} screp {g},{nb}")
                s.act(SCOPY(sr[:, n0:n1], pb[:, 0:n1 - n0]),
                      reads=[f"psc{nb % 2}"], writes=[srk + f"_{nb}"],
                      tag=f"{tagp} screv {g},{nb}")
            s.dve(TS(qo[:, 0:N], qi[:, 0:N], -128.0, None, ALU.add),
                  reads=[qik], writes=[qok], tag=f"{tagp} qcvt {g}")
            s.dve(TT(wo[:, 0:N], qo[:, 0:N], sr[:, 0:N], ALU.mult),
                  reads=[qok] + [srk + f"_{nb}" for nb in range((N + 511) // 512)],
                  writes=[wok], tag=f"{tagp} mul {g}")
            s.dma("gpsimd", dst[g * 128:(g + 1) * 128, :], wo[:, 0:N],
                  reads=[wok], writes=[f"{tagp}st"], tag=f"{tagp} st {g}")

    dequant(q1, sc1, w1b, 1024, 4096, "d1")
    dequant(q2, sc2, w2b, 4096, 1024, "d2")



def emit_l0pre(s, x_in, wdq, wdkv, wkr, bcat, wuq, buq, wuk, buk, wuv, buv, wqr, bqr, identb_in, xn_o, musig_o, QT1, QT2, KT1, KT2, V_o):
    identb = s.sbuf("identb_sb", [128, 128], BF16)
    s.dma("sync", identb[:, :], identb_in[:, :], writes=["identb"])
    onesb = s.sbuf("onesb", [1, 128], BF16)
    s.pool(MEMSET(onesb[:, :], 1.0), writes=["onesb"])
    epsc = s.sbuf("epsc", [128, 1], F32)
    s.pool(MEMSET(epsc[:, :], 1e-5), writes=["epsc"])
    wcat = s.sbuf("wcat", [128, 8 * 96], BF16)
    for c in range(8):
        s.dma("gpsimd", wcat[:, c * 96:c * 96 + 32], wdq[c * 128:(c + 1) * 128, :],
              writes=[f"wcat{c}a"], semkey="dq:wini", tag=f"wdq {c}")
        s.dma("gpsimd", wcat[:, c * 96 + 32:c * 96 + 64], wdkv[c * 128:(c + 1) * 128, :],
              writes=[f"wcat{c}b"], semkey="dq:wini", tag=f"wdkv {c}")
        s.dma("gpsimd", wcat[:, c * 96 + 64:c * 96 + 96], wkr[c * 128:(c + 1) * 128, :],
              writes=[f"wcat{c}c"], semkey="dq:wini", tag=f"wkr {c}")
    wcb = s.sbuf("wcb", [1, 96], BF16)
    s.dma("gpsimd", wcb[:, :], bcat[:, :], writes=["wcb"], semkey="dq:wini")
    WUQp = s.sbuf("WUQp", [33, 1024], BF16)
    WUKp = s.sbuf("WUKp", [33, 1024], BF16)
    WUVp = s.sbuf("WUVp", [33, 1024], BF16)
    WQRp = s.sbuf("WQRp", [33, 256], BF16)
    for nm, wt, bt, ww in (("WUQp", wuq, buq, WUQp), ("WUKp", wuk, buk, WUKp),
                           ("WUVp", wuv, buv, WUVp), ("WQRp", wqr, bqr, WQRp)):
        s.dma("gpsimd", ww[0:32, :], wt[:, :], writes=[nm], semkey="dq:wini", tag=nm)
        s.dma("gpsimd", ww[32:33, :], bt[:, :], writes=[nm + "b"], semkey="dq:wini", tag=nm + "b")
    WK = ["wcb", "WUQp", "WUQpb", "WUKp", "WUKpb", "WUVp", "WUVpb", "WQRp", "WQRpb",
          "onesb", "identb"] + [f"wcat{c}{x}" for c in range(8) for x in "abc"]

    xin = [s.sbuf(f"xin{i}", [128, D], F32) for i in range(2)]
    xnf = [s.sbuf(f"xnf{i}", [128, D], F32) for i in range(2)]
    xnb = [s.sbuf(f"xnb{i}", [128, D], BF16) for i in range(2)]
    xnT = [s.sbuf(f"xnT{i}", [128, D], BF16) for i in range(2)]
    stt = s.sbuf("stt", [128, 12], F32)
    ag = s.sbuf("ag", [128, 2], F32)
    rstd = s.sbuf("rstd", [128, 1], F32)
    st2 = s.sbuf("st2", [128, 12], F32)
    ag2 = s.sbuf("ag2", [128, 2], F32)
    mus = [s.sbuf(f"mus{i}", [128, 2], F32) for i in range(2)]
    cqb = [s.sbuf(f"cqb{i}", [128, 96], BF16) for i in range(2)]
    cqT = [s.sbuf(f"cqT{i}", [33, 128], BF16) for i in range(2)]
    ckvT = [s.sbuf(f"ckvT{i}", [33, 128], BF16) for i in range(2)]
    krT = [s.sbuf(f"krT{i}", [32, 128], BF16) for i in range(2)]
    qstage = [s.sbuf(f"qstage{i}", [128, 1024], BF16) for i in range(2)]
    kstage = [s.sbuf(f"kstage{i}", [128, 1024], BF16) for i in range(2)]
    qrstage = [s.sbuf(f"qrstage{i}", [32, 1024], BF16) for i in range(2)]
    vstage = [s.sbuf(f"vstage{i}", [128, 1024], BF16) for i in range(2)]
    ptb = [s.psum(f"ptb{i}", [128, 1024], BF16) for i in range(2)]
    p96 = s.psum("p96", [128, 512], F32)
    pq = [s.psum(f"pq{i}", [128, 512], F32) for i in range(2)]
    pqr = s.psum("pqr", [128, 512], F32)
    pv = [s.psum(f"pv{i}", [128, 512], F32) for i in range(2)]

    for i in range(2):
        s.pool(MEMSET(cqT[i][32:33, :], 1.0), writes=[f"cqT{i}_ones"])
        s.pool(MEMSET(ckvT[i][32:33, :], 1.0), writes=[f"ckvT{i}_ones"])

    for t in range(NT):
        p = t % 2
        b, tc = t // 32, t % 32
        xi, xik = xin[p], f"xin{p}"
        xf, xfk = xnf[p], f"xnf{p}"
        xb, xbk = xnb[p], f"xnb{p}"
        xT, xTk = xnT[p], "xnTA"
        r0 = t * 128
        s.dma("sync", xi[:, :], x_in[r0:r0 + 128, :], writes=[xik], tag=f"x ld {t}")
        s.dve(lambda e, a=stt[:, 0:6], b_=xi[:, 0:512]: e.bn_stats(a, b_),
              reads=[xik], writes=["stt_a"], tag=f"bn0a {t}")
        s.dve(lambda e, a=stt[:, 6:12], b_=xi[:, 512:1024]: e.bn_stats(a, b_),
              reads=[xik], writes=["stt_b"], tag=f"bn0b {t}")
        s.dve(lambda e, a=ag[:, :], b_=stt[:, 0:12]: e.bn_aggr(a, b_),
              reads=["stt_a", "stt_b"], writes=["ag"], tag=f"aggr0 {t}")
        s.act(ACTV(rstd[:, 0:1], ag[:, 1:2], AF.Ln, bias=epsc[:, 0:1]), reads=["ag", "epsc"],
              writes=["rstd_ln"], tag=f"ln0 {t}")
        s.act(ACTV(rstd[:, 0:1], rstd[:, 0:1], AF.Exp, scale=-0.5), reads=["rstd_ln"],
              writes=["rstd"], tag=f"exp0 {t}")
        s.dve(TS(xf[:, :], xi[:, :], ag[:, 0:1], rstd[:, 0:1], ALU.subtract, ALU.mult),
              reads=[xik, "ag", "rstd"], writes=[xfk], tag=f"xn {t}")
        s.dma("gpsimd", xn_o[r0:r0 + 128, :], xf[:, :], reads=[xfk],
              semkey="dq:st:xn", tag=f"xn st {t}")
        s.dve(lambda e, a=st2[:, 0:6], b_=xf[:, 0:512]: e.bn_stats(a, b_),
              reads=[xfk], writes=["st2_a"], tag=f"bn2a {t}")
        s.dve(lambda e, a=st2[:, 6:12], b_=xf[:, 512:1024]: e.bn_stats(a, b_),
              reads=[xfk], writes=["st2_b"], tag=f"bn2b {t}")
        s.dve(lambda e, a=ag2[:, :], b_=st2[:, 0:12]: e.bn_aggr(a, b_),
              reads=["st2_a", "st2_b"], writes=["ag2"], tag=f"aggr2 {t}")
        mu, muk = mus[p], f"mus{p}"
        s.act(ACTV(mu[:, 1:2], ag2[:, 1:2], AF.Ln, bias=epsc[:, 0:1]), reads=["ag2", "epsc"],
              writes=[muk + "ln"], tag=f"ln2 {t}")
        s.act(ACTV(mu[:, 1:2], mu[:, 1:2], AF.Exp, scale=-0.5), reads=[muk + "ln"],
              writes=[muk + "r"], tag=f"exp2 {t}")
        s.act(SCOPY(mu[:, 0:1], ag2[:, 0:1]), reads=["ag2"], writes=[muk + "m"],
              tag=f"mu2 {t}")
        s.dma("gpsimd", musig_o[r0:r0 + 128, :], mu[:, :], reads=[muk + "r", muk + "m"],
              semkey="dq:st:mus", tag=f"mus st {t}")
        s.dve(COPY(xb[:, :], xf[:, :]), reads=[xfk], writes=[xbk], tag=f"cast {t}")
        for c in range(8):
            pt_ = ptb[c % 2]
            ptk = f"ptb{c % 2}"
            s.te(TRANS(pt_[:, 0:128], xb[:, c * 128:(c + 1) * 128], identb[:, :]),
                 reads=[xbk, "identb", ptk + "_r"], writes=[ptk], tag=f"trx {t},{c}")
            s.act(SCOPY(xT[:, c * 128:(c + 1) * 128], pt_[:, 0:128]),
                  reads=[ptk], writes=[xTk + f"_{c}", ptk + "_r"], tag=f"evx {t},{c}")
        xTkeys = [xTk + f"_{c}" for c in range(8)]
        for c in range(8):
            s.te(MM(p96[:, 0:96], xT[:, c * 128:(c + 1) * 128], wcat[:, c * 96:(c + 1) * 96],
                    c == 0, False), reads=xTkeys + WK, writes=["p96"], tag=f"p1 {t},{c}")
        s.te(MM(p96[:, 0:96], onesb[0:1, :], wcb[0:1, :], False, True),
             reads=WK, writes=["p96"], tag=f"p1b {t}")
        cq, cqk = cqb[p], f"cqb{p}"
        s.dve(COPY(cq[:, :], p96[:, 0:96]), reads=["p96"], writes=[cqk], tag=f"cq ev {t}")
        c3 = [(cqT[p], f"cqT{p}", 0), (ckvT[p], f"ckvT{p}", 32), (krT[p], f"krT{p}", 64)]
        for (dstT, dstk, off) in c3:
            pt_ = ptb[0]
            s.te(TRANS(pt_[0:32, 0:128], cq[:, off:off + 32], identb[:, :]),
                 reads=[cqk, "identb", "ptb0_r"], writes=["ptb0"], tag=f"trc {t},{off}")
            s.act(SCOPY(dstT[0:32, :], pt_[0:32, 0:128]), reads=["ptb0"],
                  writes=[dstk, "ptb0_r"], tag=f"evc {t},{off}")
        qs, qsk = qstage[p], f"qstage{p}"
        ks, ksk = kstage[p], f"kstage{p}"
        qr_, qrk = qrstage[p], f"qrstage{p}"
        vs, vsk = vstage[p], f"vstage{p}"
        for h in range(8):
            pqh = pq[h % 2]
            pqhk = f"pq{h % 2}"
            s.te(MM(pqh[:, 0:128], WUQp[:, h * 128:(h + 1) * 128], cqT[p][:, :], True, True),
                 reads=[f"cqT{p}", f"cqT{p}_ones", pqhk + "_r"] + WK, writes=[pqhk],
                 tag=f"qc {t},{h}")
            s.act(SCOPY(qs[:, h * 128:(h + 1) * 128], pqh[:, 0:128]), reads=[pqhk],
                  writes=[qsk + f"_{h}", pqhk + "_r"], tag=f"qc ev {t},{h}")
        for h in range(8):
            pqh = pq[h % 2]
            pqhk = f"pq{h % 2}"
            s.te(MM(pqh[:, 0:128], WUKp[:, h * 128:(h + 1) * 128], ckvT[p][:, :], True, True),
                 reads=[f"ckvT{p}", f"ckvT{p}_ones", pqhk + "_r"] + WK, writes=[pqhk],
                 tag=f"kc {t},{h}")
            s.dve(COPY(ks[:, h * 128:(h + 1) * 128], pqh[:, 0:128]), reads=[pqhk],
                  writes=[ksk + f"_{h}", pqhk + "_r"], tag=f"kc ev {t},{h}")
        for h in range(8):
            s.te(MM(pqr[0:32, 0:128], WQRp[:, h * 32:(h + 1) * 32], cqT[p][:, :], True, True),
                 reads=[f"cqT{p}", f"cqT{p}_ones", "pqr_r"] + WK, writes=["pqr"],
                 tag=f"qr {t},{h}")
            s.dve(COPY(qr_[0:32, h * 128:(h + 1) * 128], pqr[0:32, 0:128]), reads=["pqr"],
                  writes=[qrk + f"_{h}", "pqr_r"], tag=f"qr ev {t},{h}")
        for nb in range(2):
            pvb = pv[nb]
            pvbk = f"pv{nb}"
            s.te(MM(pvb[:, :], ckvT[p][:, :], WUVp[:, nb * 512:(nb + 1) * 512], True, True),
                 reads=[f"ckvT{p}", f"ckvT{p}_ones", pvbk + "_r"] + WK, writes=[pvbk],
                 tag=f"v {t},{nb}")
            s.dve(COPY(vs[:, nb * 512:(nb + 1) * 512], pvb[:, :]), reads=[pvbk],
                  writes=[vsk + f"_{nb}", pvbk + "_r"], tag=f"v ev {t},{nb}")
        qkeys = [qsk + f"_{h}" for h in range(8)]
        kkeys = [ksk + f"_{h}" for h in range(8)]
        qrkeys = [qrk + f"_{h}" for h in range(8)]
        vkeys = [vsk + f"_{nb}" for nb in range(2)]
        boff = b * 8 * 128 * 4096
        s.dma("sync",
              bass.AP(QT1, boff + tc * 128, [[4096, 128], [128 * 4096, 8], [1, 128]]),
              bass.AP(qs, 0, [[1024, 128], [128, 8], [1, 128]]),
              reads=qkeys, semkey="dq:st:q", tag=f"QT1 st {t}")
        s.dma("sync",
              bass.AP(KT1, boff + tc * 128, [[4096, 128], [128 * 4096, 8], [1, 128]]),
              bass.AP(ks, 0, [[1024, 128], [128, 8], [1, 128]]),
              reads=kkeys, semkey="dq:st:k", tag=f"KT1 st {t}")
        s.dma("sync",
              bass.AP(QT2, b * 8 * 32 * 4096 + tc * 128, [[4096, 32], [32 * 4096, 8], [1, 128]]),
              bass.AP(qr_, 0, [[1024, 32], [128, 8], [1, 128]]),
              reads=qrkeys, semkey="dq:st:qr", tag=f"QT2 st {t}")
        s.dma("sync",
              bass.AP(KT2, b * 32 * 4096 + tc * 128, [[4096, 32], [1, 128]]),
              bass.AP(krT[p], 0, [[128, 32], [1, 128]]),
              reads=[f"krT{p}"], semkey="dq:st:kr", tag=f"KT2 st {t}")
        s.dma("sync", V_o[r0:r0 + 128, :], vs[:, :], reads=vkeys,
              semkey="dq:st:v", tag=f"V st {t}")



def emit_l0attn(s, QT1, QT2, KT1, KT2, V_i, OT):
    onesc = s.sbuf("onesc", [128, 1], BF16)
    s.pool(MEMSET(onesc[:, :], 1.0), writes=["onesc"])
    onesf = s.sbuf("onesf", [1, 128], F32)
    s.pool(MEMSET(onesf[:, :], 1.0), writes=["onesf"])
    nC0 = s.sbuf("nC0", [128, 1], F32)
    s.pool(MEMSET(nC0[:, :], -C0), writes=["nC0"])
    onesf2 = s.sbuf("onesf2", [128, 1], F32)
    s.pool(MEMSET(onesf2[:, :], 1.0), writes=["onesf2"])

    kt1 = [s.sbuf(f"kt1_{i}", [128, 4096], BF16) for i in range(2)]
    kt2 = [s.sbuf(f"kt2_{i}", [32, 4096], BF16) for i in range(2)]
    vt = [s.sbuf(f"vt{i}", [128, 4096], BF16) for i in range(2)]
    qt1 = [s.sbuf(f"qt1_{i}", [128, 512], BF16) for i in range(2)]
    qt2 = [s.sbuf(f"qt2_{i}", [32, 512], BF16) for i in range(2)]
    pexp = [s.sbuf(f"pexp{i}", [128, 512], BF16) for i in range(4)]
    rden = s.sbuf("rden", [1, 512], F32)
    pacc = s.sbuf("pacc", [128, 512], F32)
    prs = s.sbuf("prs", [128, 512], F32)
    obuf = [s.sbuf(f"obuf{i}", [128, 512], BF16) for i in range(2)]

    ps = [s.psum(f"ps{i}", [128, 512], F32) for i in range(4)]
    po = s.psum("po", [128, 512], F32)
    pd = s.psum("pd", [128, 512], F32)
    pr = s.psum("pr", [128, 512], F32)

    for u in range(16):
        b, h = u // 8, u % 8
        dp = u % 2
        k1, k1k = kt1[dp], f"kt1_{dp}"
        k2, k2k = kt2[dp], f"kt2_{dp}"
        v_, vk = vt[dp], f"vt{dp}"
        row = (b * 8 + h) * 128
        s.dma("sync", k1[:, :], KT1[row:row + 128, :], writes=[k1k], tag=f"kt1 {u}")
        s.dma("sync", k2[:, :], KT2[b * 32:(b + 1) * 32, :], writes=[k2k], tag=f"kt2 {u}")
        s.dma("sync",
              bass.AP(vt[dp], 0, [[4096, 128], [128, 32], [1, 128]]),
              bass.AP(V_i, b * 4096 * 1024 + h * 128, [[1024, 128], [128 * 1024, 32], [1, 128]]),
              writes=[vk], tag=f"vt {u}")
        for g in range(8):
            qp = g % 2
            q1, q1k = qt1[qp], f"qt1_{qp}"
            q2, q2k = qt2[qp], f"qt2_{qp}"
            qrow = (b * 8 + h) * 128
            s.dma("sync", q1[:, :], QT1[qrow:qrow + 128, g * 512:(g + 1) * 512],
                  writes=[q1k], tag=f"qt1 {u},{g}")
            s.dma("sync", q2[:, :], QT2[(b * 8 + h) * 32:(b * 8 + h) * 32 + 32, g * 512:(g + 1) * 512],
                  writes=[q2k], tag=f"qt2 {u},{g}")
            def emit_S(j):
                sp = j % 4
                psb, psk = ps[sp], f"ps{sp}"
                pe, pek = pexp[sp], f"pexp{sp}"
                s.te(MM(psb[:, :], k1[:, j * 128:(j + 1) * 128], q1[:, :], True, False),
                     reads=[k1k, q1k, psk + "_r"], writes=[psk], tag=f"s1 {u},{g},{j}")
                s.te(MM(psb[:, :], k2[:, j * 128:(j + 1) * 128], q2[:, :], False, True),
                     reads=[k2k, q2k], writes=[psk], tag=f"s2 {u},{g},{j}")
                s.act(ACTV(pe[:, :], psb[:, :], AF.Exp, bias=nC0[:, 0:1], scale=SC0),
                      reads=[psk, "nC0"], writes=[pek, psk + "_r"], tag=f"exp {u},{g},{j}")

            def emit_PV(j):
                sp = j % 4
                pe, pek = pexp[sp], f"pexp{sp}"
                s.te(MM(po[:, :], v_[:, j * 128:(j + 1) * 128], pe[:, :], j == 0, j == 31),
                     reads=[vk, pek, "po_r"] if j == 0 else [vk, pek], writes=["po"],
                     tag=f"pv {u},{g},{j}")
                if j == 0:
                    s.dve(COPY(pacc[:, :], pe[:, :]), reads=[pek, "pacc_r"],
                          writes=["pacc"], tag=f"dacc {u},{g},{j}")
                else:
                    s.dve(TT(pacc[:, :], pacc[:, :], pe[:, :], ALU.add),
                          reads=["pacc", pek], writes=["pacc"], tag=f"dacc {u},{g},{j}")

            emit_S(0)
            emit_S(1)
            for j in range(32):
                if j + 2 < 32:
                    emit_S(j + 2)
                emit_PV(j)
            s.te(MM(pd[0:1, :], onesf2[:, 0:1], pacc[:, :], True, True),
                 reads=["onesf2", "pacc", "pd_r"], writes=["pd"], tag=f"dsum {u},{g}")
            s.dve(lambda e, a=rden[0:1, :], b_=pd[0:1, :]: e.reciprocal(a, b_),
                  reads=["pd"], writes=["rden", "pd_r", "pacc_r"], tag=f"rcp {u},{g}")
            s.te(MM(pr[:, :], onesf[0:1, :], rden[0:1, :], True, True),
                 reads=["onesf", "rden", "pr_r"], writes=["pr"], tag=f"rep {u},{g}")
            s.act(SCOPY(prs[:, :], pr[:, :]), reads=["pr"], writes=["prs", "pr_r"],
                  tag=f"rev {u},{g}")
            ob, obk = obuf[g % 2], f"obuf{g % 2}"
            s.dve(TT(ob[:, :], po[:, :], prs[:, :], ALU.mult),
                  reads=["po", "prs"], writes=[obk, "po_r"], tag=f"omul {u},{g}")
            s.dma("gpsimd", OT[row:row + 128, g * 512:(g + 1) * 512], ob[:, :],
                  reads=[obk], semkey=f"dq:st:ot{g % 2}", tag=f"ot st {u},{g}")



def emit_l0rest(s, x_in, xn_i, mus_i, OT, WO, bO, gam, bet, w1d, w2d, fb1, fb2, rwT_i, rb_i, identb_in, x1_o, xn1_o, rt_o):
    identb = s.sbuf("identb_sb", [128, 128], BF16)
    s.dma("sync", identb[:, :], identb_in[:, :], writes=["identb"])
    onesb = s.sbuf("onesb", [1, 128], BF16)
    s.pool(MEMSET(onesb[:, :], 1.0), writes=["onesb"])
    epsc = s.sbuf("epsc", [128, 1], F32)
    s.pool(MEMSET(epsc[:, :], 1e-5), writes=["epsc"])
    w1sb = s.sbuf("w1sb", [128, 8 * 4096], BF16)
    for c in range(8):
        s.dma("sync", w1sb[:, c * 4096:(c + 1) * 4096], w1d[c * 128:(c + 1) * 128, :],
              writes=[f"w1sb{c}"], semkey="dq:w1", tag=f"w1 {c}")
    W1K = [f"w1sb{c}" for c in range(8)]
    WOsb = s.sbuf("WOsb", [128, 8 * 1024], BF16)
    for c in range(8):
        s.dma("gpsimd", WOsb[:, c * 1024:(c + 1) * 1024], WO[c * 128:(c + 1) * 128, :],
              writes=[f"WOsb{c}"], semkey="dq:wini", tag=f"WO {c}")
    WOK = [f"WOsb{c}" for c in range(8)]
    bOsb = s.sbuf("bOsb", [1, 1024], BF16)
    s.dma("gpsimd", bOsb[:, :], bO[:, :], writes=["bOsb"], semkey="dq:wini")
    fb2sb = s.sbuf("fb2sb", [1, 1024], BF16)
    s.dma("gpsimd", fb2sb[:, :], fb2[:, :], writes=["fb2sb"], semkey="dq:wini")
    rbsb = s.sbuf("rbsb", [1, 8], BF16)
    s.dma("gpsimd", rbsb[:, :], rb_i[:, :], writes=["rbsb"], semkey="dq:wini")
    rwsb = s.sbuf("rwsb", [128, 64], BF16)
    for c in range(8):
        s.dma("gpsimd", rwsb[:, c * 8:(c + 1) * 8], rwT_i[c * 128:(c + 1) * 128, :],
              writes=[f"rwsb{c}"], semkey="dq:wini", tag=f"rw {c}")
    RWK = [f"rwsb{c}" for c in range(8)]
    fb1r = s.sbuf("fb1r", [32, 128], F32)
    s.dma("sync", fb1r[:, :], fb1[:, :], writes=["fb1r"])
    fb1c = s.sbuf("fb1c", [128, 32], F32)
    identf2 = s.sbuf("identf2", [128, 128], F32)
    s.pool(lambda e: e.memset(identf2[:, :], 0.0), writes=["identf2z"])
    s.dve(COPY(identf2[:, :], identb[:, :]), reads=["identb", "identf2z"], writes=["identf2"])
    grow = s.sbuf("grow", [1, 1024], F32)
    brow = s.sbuf("brow", [1, 1024], F32)
    s.dma("sync", grow[:, :], gam[:, :], writes=["grow"])
    s.dma("sync", brow[:, :], bet[:, :], writes=["brow"])
    onesf = s.sbuf("onesf", [1, 128], F32)
    s.pool(MEMSET(onesf[:, :], 1.0), writes=["onesf"])
    grep = s.sbuf("grep", [128, 1024], F32)
    brep = s.sbuf("brep", [128, 1024], F32)

    pso = [s.psum(f"pso{i}", [128, 512], F32) for i in range(2)]
    ptb = [s.psum(f"ptb{i}", [128, 1024], BF16) for i in range(2)]
    pyt = [s.psum(f"pyt{i}", [128, 512], F32) for i in range(2)]
    pz = [s.psum(f"pz{i}", [128, 512], F32) for i in range(2)]

    for nb in range(2):
        s.te(MM(pso[nb][:, :], onesf[0:1, :], grow[0:1, nb * 512:(nb + 1) * 512], True, True),
             reads=["onesf", "grow", f"pso{nb}_r"], writes=[f"pso{nb}"], tag=f"grep {nb}")
        s.act(SCOPY(grep[:, nb * 512:(nb + 1) * 512], pso[nb][:, :]), reads=[f"pso{nb}"],
              writes=[f"grep{nb}", f"pso{nb}_r"], tag=f"grep ev {nb}")
        s.te(MM(pso[nb][:, :], onesf[0:1, :], brow[0:1, nb * 512:(nb + 1) * 512], True, True),
             reads=["onesf", "brow", f"pso{nb}_r"], writes=[f"pso{nb}"], tag=f"brep {nb}")
        s.act(SCOPY(brep[:, nb * 512:(nb + 1) * 512], pso[nb][:, :]), reads=[f"pso{nb}"],
              writes=[f"brep{nb}", f"pso{nb}_r"], tag=f"brep ev {nb}")
    GBK = ["grep0", "grep1", "brep0", "brep1"]
    s.te(TRANS(pyt[0][:, 0:32], fb1r[:, :], identf2[0:32, 0:32]),
         reads=["fb1r", "identf2", "pyt0_r"], writes=["pyt0"], tag="fb1 tr")
    s.act(SCOPY(fb1c[:, :], pyt[0][:, 0:32]), reads=["pyt0"], writes=["fb1c", "pyt0_r"],
          tag="fb1 ev")

    oin = [s.sbuf(f"oin{i}", [128, 1024], BF16) for i in range(2)]
    sA = s.sbuf("sA", [128, 4 * 1024], BF16)
    xng = s.sbuf("xng", [128, 4 * 1024], F32)
    xg = [s.sbuf(f"xg{i}", [128, 1024], F32) for i in range(2)]
    msg = s.sbuf("msg", [128, 4 * 2], F32)
    hb = [s.sbuf(f"hb{i}", [128, 1024], BF16) for i in range(2)]
    hfp = [s.sbuf("hfp0", [128, 1024], F32)] * 2
    hTg = s.sbuf("hTg", [128, 8 * 512], BF16)
    ybT = s.sbuf("ybT", [128, 32 * 512], BF16)
    w2s = [s.sbuf(f"w2s{i}", [128, 1024], BF16) for i in range(2)]
    gp = [s.sbuf("gp0", [128, 1024], F32)] * 2
    x1t = [s.sbuf("x1t0", [128, 1024], F32)] * 2
    xn1t = [s.sbuf("xn1t0", [128, 1024], F32)] * 2
    xn1b = s.sbuf("xn1b", [128, 1024], BF16)
    xnT1 = s.sbuf("xnT1", [128, 1024], BF16)
    stt = s.sbuf("stt", [128, 12], F32)
    ag = s.sbuf("ag", [128, 2], F32)
    rstd = s.sbuf("rstd", [128, 1], F32)
    pe8 = s.sbuf("pe8", [128, 8], F32)
    nmx = s.sbuf("nmx", [128, 1], F32)
    sden = s.sbuf("sden", [128, 1], F32)
    probs = s.sbuf("probs", [128, 8], F32)
    tv8 = s.sbuf("tv8", [128, 8], F32)
    ti8 = s.sbuf("ti8", [128, 8], U32)
    r2 = s.sbuf("r2", [128, 1], F32)
    rteb = [s.sbuf(f"rteb{i}", [128, 4], F32) for i in range(2)]

    eps9 = s.sbuf("eps9", [128, 1], F32)
    s.pool(MEMSET(eps9[:, :], 1e-9), writes=["eps9"])

    for grp in range(16):
        hTkeys = []
        for ti in range(4):
            t = grp * 4 + ti
            p = t % 2
            b, tc = t // 32, t % 32
            r0 = t * 128
            oi, oik = oin[p], f"oin{p}"
            s.dma("sync",
                  bass.AP(oi, 0, [[1024, 128], [128, 8], [1, 128]]),
                  bass.AP(OT, b * 8 * 128 * 4096 + tc * 128, [[4096, 128], [128 * 4096, 8], [1, 128]]),
                  writes=[oik], tag=f"ot ld {t}")
            for nb in range(2):
                pp, ppk = pso[nb], f"pso{nb}"
                for h in range(8):
                    s.te(MM(pp[:, :], oi[:, h * 128:(h + 1) * 128],
                            WOsb[:, h * 1024 + nb * 512:h * 1024 + (nb + 1) * 512],
                            h == 0, False),
                         reads=[oik, ppk + "_r"] + WOK, writes=[ppk], tag=f"wo {t},{nb},{h}")
                s.te(MM(pp[:, :], onesb[0:1, :], bOsb[0:1, nb * 512:(nb + 1) * 512], False, True),
                     reads=["onesb", "bOsb"], writes=[ppk], tag=f"wob {t},{nb}")
                s.act(SCOPY(sA[:, ti * 1024 + nb * 512: ti * 1024 + (nb + 1) * 512], pp[:, :]),
                      reads=[ppk], writes=[f"sA{ti}_{nb}", ppk + "_r"], tag=f"wo ev {t},{nb}")
            s.dma("sync", xng[:, ti * 1024:(ti + 1) * 1024], xn_i[r0:r0 + 128, :],
                  writes=[f"xng{ti}"], semkey="dq:xng", tag=f"xn ld {t}")
            s.dma("sync", msg[:, ti * 2:(ti + 1) * 2], mus_i[r0:r0 + 128, :],
                  writes=[f"msg{ti}"], semkey="dq:msg", tag=f"mus ld {t}")
            hf, hfk = hfp[p], "hfp0"
            s.dve(TS(hf[:, :], xng[:, ti * 1024:(ti + 1) * 1024], msg[:, ti * 2:ti * 2 + 1],
                     msg[:, ti * 2 + 1:ti * 2 + 2], ALU.subtract, ALU.mult),
                  reads=[f"xng{ti}", f"msg{ti}"], writes=[hfk], tag=f"hln {t}")
            s.dve(TT(hf[:, :], hf[:, :], grep[:, :], ALU.mult), reads=[hfk] + GBK,
                  writes=[hfk], tag=f"hg {t}")
            hbb, hbk = hb[p], f"hb{p}"
            s.dve(TT(hbb[:, :], hf[:, :], brep[:, :], ALU.add), reads=[hfk] + GBK,
                  writes=[hbk], tag=f"hbeta {t}")
            for c in range(8):
                pt_, ptk = ptb[c % 2], f"ptb{c % 2}"
                s.te(TRANS(pt_[:, 0:128], hbb[:, c * 128:(c + 1) * 128], identb[:, :]),
                     reads=[hbk, "identb", ptk + "_r"], writes=[ptk], tag=f"trh {t},{c}")
                s.act(SCOPY(hTg[:, c * 512 + ti * 128: c * 512 + (ti + 1) * 128], pt_[:, 0:128]),
                      reads=[ptk], writes=[f"hTg{c}_{ti}", ptk + "_r"], tag=f"evh {t},{c}")
                hTkeys.append(f"hTg{c}_{ti}")
        for yt in range(32):
            pp, ppk = pyt[yt % 2], f"pyt{yt % 2}"
            for c in range(8):
                s.te(MM(pp[:, :], w1sb[:, c * 4096 + yt * 128: c * 4096 + (yt + 1) * 128],
                        hTg[:, c * 512:(c + 1) * 512], c == 0, c == 7),
                     reads=hTkeys + W1K + [ppk + "_r"], writes=[ppk], tag=f"fc1 {grp},{yt},{c}")
            s.act(ACTV(ybT[:, yt * 512:(yt + 1) * 512], pp[:, :], AF.Silu,
                       bias=fb1c[:, yt:yt + 1]),
                  reads=[ppk, "fb1c"], writes=[f"ybT{yt}", ppk + "_r"], tag=f"silu {grp},{yt}")
        for half in range(2):
          zbank = [pz[0], pz[1], pyt[0], pyt[1]]
          zbk = ["pz0", "pz1", "pyt0", "pyt1"]
          for yt in range(32):
            wsl, wslk = w2s[yt % 2], f"w2s{yt % 2}"
            s.dma("sync", wsl[:, :], w2d[yt * 128:(yt + 1) * 128, :],
                  writes=[wslk], semkey=f"dq:{wslk}", tag=f"w2 ld {grp},{half},{yt}")
            for q in range(4):
                ti = half * 2 + q // 2
                nb = q % 2
                s.te(MM(zbank[q][:, :], ybT[:, yt * 512 + ti * 128: yt * 512 + (ti + 1) * 128],
                        wsl[:, nb * 512:(nb + 1) * 512], yt == 0, False),
                     reads=[f"ybT{yt}", wslk], writes=[zbk[q]],
                     tag=f"fc2 {grp},{half},{yt},{q}")
          for q in range(4):
            nb = q % 2
            s.te(MM(zbank[q][:, :], onesb[0:1, :], fb2sb[0:1, nb * 512:(nb + 1) * 512],
                    False, True), reads=["onesb", "fb2sb"], writes=[zbk[q]],
                 tag=f"fc2b {grp},{half},{q}")
          for tq in range(2):
            ti = half * 2 + tq
            t = grp * 4 + ti
            p = t % 2
            r0 = t * 128
            xgt, xgk = xg[p], f"xg{p}"
            s.dma("sync", xgt[:, :], x_in[r0:r0 + 128, :], writes=[xgk], tag=f"x ld {t}")
            gpt, gpk = gp[p], "gp0"
            for nb in range(2):
                q = tq * 2 + nb
                s.dve(lambda e, o=gpt[:, nb * 512:(nb + 1) * 512], z=zbank[q][:, :],
                      xn_=xng[:, ti * 1024 + nb * 512:ti * 1024 + (nb + 1) * 512]:
                      e.scalar_tensor_tensor(o, z, 0.5, xn_, ALU.mult, ALU.add),
                      reads=[zbk[q], f"xng{ti}"], writes=[gpk + f"_{nb}"],
                      tag=f"gp {t},{nb}")
            s.act(ACTV(gpt[:, :], gpt[:, :], AF.Silu), reads=[gpk + "_0", gpk + "_1"],
                  writes=[gpk], tag=f"g {t}")
            s.dve(TT(gpt[:, :], gpt[:, :], sA[:, ti * 1024:(ti + 1) * 1024], ALU.mult),
                  reads=[gpk, f"sA{ti}_0", f"sA{ti}_1"], writes=[gpk], tag=f"sg {t}")
            x1b, x1k = x1t[p], "x1t0"
            s.dve(TT(x1b[:, :], gpt[:, :], xgt[:, :], ALU.add), reads=[gpk, xgk],
                  writes=[x1k], tag=f"x1 {t}")
            s.dma("gpsimd", x1_o[r0:r0 + 128, :], x1b[:, :], reads=[x1k],
                  semkey="dq:st:x1", tag=f"x1 st {t}")
            s.dve(lambda e, a=stt[:, 0:6], b_=x1b[:, 0:512]: e.bn_stats(a, b_),
                  reads=[x1k], writes=["stt_a"], tag=f"bn1a {t}")
            s.dve(lambda e, a=stt[:, 6:12], b_=x1b[:, 512:1024]: e.bn_stats(a, b_),
                  reads=[x1k], writes=["stt_b"], tag=f"bn1b {t}")
            s.dve(lambda e, a=ag[:, :], b_=stt[:, 0:12]: e.bn_aggr(a, b_),
                  reads=["stt_a", "stt_b"], writes=["ag"], tag=f"aggr1 {t}")
            s.act(ACTV(rstd[:, 0:1], ag[:, 1:2], AF.Ln, bias=epsc[:, 0:1]),
                  reads=["ag", "epsc"], writes=["rstd_ln"], tag=f"ln1 {t}")
            s.act(ACTV(rstd[:, 0:1], rstd[:, 0:1], AF.Exp, scale=-0.5), reads=["rstd_ln"],
                  writes=["rstd"], tag=f"exp1 {t}")
            xnt, xntk = xn1t[p], "xn1t0"
            s.dve(TS(xnt[:, :], x1b[:, :], ag[:, 0:1], rstd[:, 0:1], ALU.subtract, ALU.mult),
                  reads=[x1k, "ag", "rstd"], writes=[xntk], tag=f"xn1 {t}")
            s.dma("gpsimd", xn1_o[r0:r0 + 128, :], xnt[:, :], reads=[xntk],
                  semkey="dq:st:xn1", tag=f"xn1 st {t}")
            s.dve(COPY(xn1b[:, :], xnt[:, :]), reads=[xntk], writes=["xn1b"], tag=f"cast1 {t}")
            for c in range(8):
                pt_, ptk = ptb[c % 2], f"ptb{c % 2}"
                s.te(TRANS(pt_[:, 0:128], xn1b[:, c * 128:(c + 1) * 128], identb[:, :]),
                     reads=["xn1b", "identb", ptk + "_r"], writes=[ptk], tag=f"tr1 {t},{c}")
                s.act(SCOPY(xnT1[:, c * 128:(c + 1) * 128], pt_[:, 0:128]),
                      reads=[ptk], writes=[f"xnT1_{c}", ptk + "_r"], tag=f"ev1 {t},{c}")
            XK = [f"xnT1_{c}" for c in range(8)]
            pp, ppk = pz[0], "pz0"
            for c in range(8):
                s.te(MM(pp[:, 0:8], xnT1[:, c * 128:(c + 1) * 128], rwsb[:, c * 8:(c + 1) * 8],
                        c == 0, False), reads=XK + RWK + [ppk + "_r"], writes=[ppk],
                     tag=f"rlog {t},{c}")
            s.te(MM(pp[:, 0:8], onesb[0:1, :], rbsb[0:1, :], False, True),
                 reads=["onesb", "rbsb"], writes=[ppk], tag=f"rlogb {t}")
            s.dve(lambda e, a=nmx[:, 0:1], b_=pp[:, 0:8]: e.reduce_max(a, b_, axis=mybir.AxisListType.X, negate=True),
                  reads=[ppk], writes=["nmx"], tag=f"rmax {t}")
            s.act(ACTV(pe8[:, :], pp[:, 0:8], AF.Exp, bias=nmx[:, 0:1]),
                  reads=[ppk, "nmx"], writes=["pe8", ppk + "_r"], tag=f"rexp {t}")
            s.dve(lambda e, a=sden[:, 0:1], b_=pe8[:, :]: e.reduce_sum(a, b_, axis=mybir.AxisListType.X),
                  reads=["pe8"], writes=["sden"], tag=f"rsum {t}")
            s.dve(lambda e, a=sden[:, 0:1], b_=sden[:, 0:1]: e.reciprocal(a, b_),
                  reads=["sden"], writes=["sden"], tag=f"rrcp {t}")
            s.dve(TS(probs[:, :], pe8[:, :], sden[:, 0:1], None, ALU.mult),
                  reads=["pe8", "sden"], writes=["probs"], tag=f"probs {t}")
            s.dve(lambda e, a=tv8[:, :], b_=probs[:, :]: e.max(out=a, in_=b_),
                  reads=["probs"], writes=["tv8"], tag=f"max8 {t}")
            s.dve(lambda e, a=ti8[:, :], b_=tv8[:, :], c_=probs[:, :]: e.max_index(a, b_, c_),
                  reads=["probs", "tv8"], writes=["ti8"], tag=f"mi8 {t}")
            s.dve(TT(r2[:, 0:1], tv8[:, 0:1], tv8[:, 1:2], ALU.add), reads=["tv8"],
                  writes=["r2"], tag=f"r2a {t}")
            s.dve(TT(r2[:, 0:1], r2[:, 0:1], eps9[:, 0:1], ALU.add), reads=["r2", "eps9"],
                  writes=["r2"], tag=f"r2b {t}")
            s.dve(lambda e, a=r2[:, 0:1], b_=r2[:, 0:1]: e.reciprocal(a, b_),
                  reads=["r2"], writes=["r2"], tag=f"r2c {t}")
            rt, rtk = rteb[p], f"rteb{p}"
            s.dve(COPY(rt[:, 0:2], ti8[:, 0:2]), reads=["ti8"], writes=[rtk + "i"],
                  tag=f"rt i {t}")
            s.dve(TS(rt[:, 2:4], tv8[:, 0:2], r2[:, 0:1], None, ALU.mult),
                  reads=["tv8", "r2"], writes=[rtk + "w"], tag=f"rt w {t}")
            s.dma("gpsimd", rt_o[r0:r0 + 128, :], rt[:, :], reads=[rtk + "i", rtk + "w"],
                  semkey="dq:st:rt", tag=f"rt st {t}")



def emit_l1a(s, xn1_i, rt_i, Wq, bq, Wd, bd, Wu, bu, Wo, bo, cosr, sinr, mask_i, identb_in, s1_o, slab_o, erow_o, wrow_o, lrk_o, vscr):
    identb = s.sbuf("identb_sb", [128, 128], BF16)
    s.dma("sync", identb[:, :], identb_in[:, :], writes=["identb"])
    maskc = s.sbuf("maskc_sb", [128, 512], BF16)
    s.dma("sync", maskc[:, :], mask_i[:, :], writes=["maskc"])
    identf = s.sbuf("identf_sb", [128, 128], F32)
    s.dve(COPY(identf[:, :], identb[:, :]), reads=["identb"], writes=["identf"])
    onesb = s.sbuf("onesb", [1, 128], BF16)
    s.pool(MEMSET(onesb[:, :], 1.0), writes=["onesb"])
    onesf = s.sbuf("onesf", [1, 128], F32)
    s.pool(MEMSET(onesf[:, :], 1.0), writes=["onesf"])
    ones8 = s.sbuf("ones8", [8, 1], F32)
    s.pool(MEMSET(ones8[:, :], 1.0), writes=["ones8"])
    nC1 = s.sbuf("nC1", [128, 1], F32)
    s.pool(MEMSET(nC1[:, :], -C1), writes=["nC1"])
    Wqs = s.sbuf("Wqs", [128, 8 * 1024], BF16)
    Wds = s.sbuf("Wds", [128, 8 * 256], BF16)
    Wus = s.sbuf("Wus", [128, 2 * 2048], BF16)
    Wos = s.sbuf("Wos", [64, 16 * 1024], BF16)
    for c in range(8):
        s.dma("gpsimd", Wqs[:, c * 1024:(c + 1) * 1024], Wq[c * 128:(c + 1) * 128, :],
              writes=[f"Wqs{c}"], semkey="dq:wini", tag=f"Wq {c}")
        s.dma("gpsimd", Wds[:, c * 256:(c + 1) * 256], Wd[c * 128:(c + 1) * 128, :],
              writes=[f"Wds{c}"], semkey="dq:wini", tag=f"Wd {c}")
        s.dma("gpsimd", Wos[:, (2 * c) * 1024:(2 * c + 1) * 1024], Wo[c * 128:c * 128 + 64, :],
              writes=[f"Wos{c}"], semkey="dq:wini", tag=f"Wo {c}a")
        s.dma("gpsimd", Wos[:, (2 * c + 1) * 1024:(2 * c + 2) * 1024], Wo[c * 128 + 64:(c + 1) * 128, :],
              writes=[f"Wos{c}b"], semkey="dq:wini", tag=f"Wo {c}b")
    for c in range(2):
        s.dma("gpsimd", Wus[:, c * 2048:(c + 1) * 2048], Wu[c * 128:(c + 1) * 128, :],
              writes=[f"Wus{c}"], semkey="dq:wini", tag=f"Wu {c}")
    bqs = s.sbuf("bqs", [1, 1024], BF16)
    bds = s.sbuf("bds", [1, 256], BF16)
    bus = s.sbuf("bus", [1, 2048], BF16)
    bos = s.sbuf("bos", [1, 1024], BF16)
    for nm, src_, dst_ in (("bqs", bq, bqs), ("bds", bd, bds), ("bus", bu, bus), ("bos", bo, bos)):
        s.dma("gpsimd", dst_[0:1, :], src_[:, :], writes=[nm], semkey="dq:wini", tag=nm)
    WK = ([f"Wqs{c}" for c in range(8)] + [f"Wds{c}" for c in range(8)] +
          [f"Wos{c}" for c in range(8)] + [f"Wos{c}b" for c in range(8)] + ["Wus0", "Wus1", "bqs", "bds", "bus", "bos",
          "onesb", "identb"])

    rtt = [s.sbuf(f"rtt{i}", [128, 4], F32) for i in range(2)]
    rtw = s.sbuf("rtw", [128, 128], F32)
    wjt = [s.sbuf(f"wjt{j}", [1, 128], F32) for j in range(2)]
    idjt = [s.sbuf(f"idjt{j}", [1, 128], BF16) for j in range(2)]
    vmt = s.sbuf("vmt", [1, 128], mybir.dt.uint8)
    c99s = s.sbuf("c99s", [1, 128], BF16)
    s.pool(MEMSET(c99s[:, :], 99.0), writes=["c99s"])

    iems = s.sbuf("iems", [1, 128], BF16)
    iemc = s.sbuf("iemc", [1, 1024], BF16)
    prow = s.psum("prow", [128, 512], F32)
    for t in range(NT):
        ri, rik = rtt[t % 2], f"rtt{t % 2}"
        s.dma("sync", ri[:, :], rt_i[t * 128:(t + 1) * 128, :], writes=[rik], tag=f"rt ld {t}")
        s.dve(lambda e, o=bass.AP(rtw, 0, [[128, 128], [32, 4]]),
              i_=bass.AP(ri, 0, [[4, 128], [1, 4]]): e.tensor_copy(o, i_),
              reads=[rik], writes=["rtw"], tag=f"rt w {t}")
        s.te(TRANS(prow[:, 0:128], rtw[:, :], identf[:, :]),
             reads=["rtw", "identf", "prow_r"], writes=["prow"], tag=f"rt tr {t}")
        for j in range(2):
            s.act(SCOPY(idjt[j][0:1, :], prow[32 * j:32 * j + 1, 0:128]),
                  reads=["prow"], writes=[f"idjt{j}"], tag=f"rt ev i{j} {t}")
            s.act(SCOPY(wjt[j][0:1, :], prow[64 + 32 * j:64 + 32 * j + 1, 0:128]),
                  reads=["prow"], writes=[f"wjt{j}"] + (["prow_r"] if j == 1 else []),
                  tag=f"rt ev w{j} {t}")
            s.dma("gpsimd", wrow_o[j:j + 1, t * 128:(t + 1) * 128], wjt[j][0:1, :],
                  reads=[f"wjt{j}"], semkey="dq:st:wrow", tag=f"wr st {t},{j}")
            s.dve(TS(vmt[0:1, :], wjt[j][0:1, :], 1e-9, None, ALU.is_gt),
                  reads=[f"wjt{j}"], writes=["vmt"], tag=f"vmask {t},{j}")
            s.dve(lambda e, jj=j:
                  e.select(iems[0:1, :], vmt[0:1, :], idjt[jj][0:1, :], c99s[0:1, :]),
                  reads=["vmt", "c99s", f"idjt{j}"], writes=["iems"],
                  tag=f"iem {t},{j}")
            s.dma("gpsimd", erow_o[j:j + 1, t * 128:(t + 1) * 128], iems[0:1, :],
                  reads=["iems"], writes=["erowD"], semkey="dq:st:erow",
                  tag=f"erow st {t},{j}")
    iota8 = s.sbuf("iota8", [8, 1], F32)
    s.pool(lambda e: e.iota(iota8[:, :], [[1, 1]], base=0, channel_multiplier=1,
                            allow_small_or_imprecise_dtypes=True), writes=["iota8"])
    ones2x8 = s.sbuf("ones2x8", [1, 8], BF16)
    s.pool(MEMSET(ones2x8[:, :], 1.0), writes=["ones2x8"])
    sone = s.sbuf("sone", [8, 1], F32)
    s.pool(MEMSET(sone[:, :], 1.0), writes=["sone"])
    ohc = s.sbuf("ohc", [8, 1024], F32)
    incc = s.sbuf("incc", [8, 1024], F32)
    tmpc = s.sbuf("tmpc", [8, 1024], F32)
    carry = s.sbuf("carry", [8, 2], F32)
    lrkc = s.sbuf("lrkc", [1, 1024], F32)
    lrow = s.sbuf("lrow", [1, 128], F32)
    cntc = s.sbuf("cntc", [8, 4], F32)
    CH = 1024

    def build_oh(j, ch, tagp):
        s.dma("sync", iemc[0:1, 0:CH], erow_o[j:j + 1, ch * CH:(ch + 1) * CH],
              reads=["erowD"], writes=["iemc"], tag=f"{tagp} iemld {j},{ch}")
        for sc in range(CH // 512):
            c0 = ch * CH + sc * 512
            s.te(MM(prow[0:8, 0:512], ones2x8[0:1, :], iemc[0:1, sc * 512:(sc + 1) * 512],
                    True, True), reads=["iemc", "ones2x8", "prow_r"], writes=["prow"],
                 tag=f"{tagp} irep {j},{ch},{sc}")
            s.act(SCOPY(ohc[:, sc * 512:(sc + 1) * 512], prow[0:8, 0:512]),
                  reads=["prow"], writes=[f"ohc_{sc}", "prow_r"], tag=f"{tagp} irepe {j},{ch},{sc}")
        s.dve(TS(ohc[:, :], ohc[:, :], iota8[:, 0:1], None, ALU.is_equal),
              reads=[f"ohc_{sc}" for sc in range(CH // 512)] + ["iota8"], writes=["ohc", "s1sc"],
              tag=f"{tagp} oh {j},{ch}")

    for j in range(2):
        for ch in range(NTOK // CH):
            build_oh(j, ch, "p1")
            ini = 0.0 if ch == 0 else carry[:, j:j + 1]
            s.dve(lambda e, o=incc[:, :], d0=sone[:, 0:1].to_broadcast([8, CH]),
                  d1=ohc[:, :], i2=ini: e.tensor_tensor_scan(o, d0, d1, i2, op0=ALU.mult, op1=ALU.add),
                  reads=["ohc", "sone"] + (["carry"] if ch else []), writes=["incc", "s1sc"],
                  tag=f"scan {j},{ch}")
            s.dve(COPY(carry[:, j:j + 1], incc[:, CH - 1:CH]), reads=["incc"],
                  writes=["carry"], tag=f"carry {j},{ch}")
            s.dve(TS(tmpc[:, :], incc[:, :], -1.0, None, ALU.add), reads=["incc"],
                  writes=["tmpca", "s1sc"], tag=f"lrka {j},{ch}")
            s.dve(TT(tmpc[:, :], tmpc[:, :], ohc[:, :], ALU.mult), reads=["tmpca", "ohc"],
                  writes=["tmpc", "s1sc"], tag=f"lrkm {j},{ch}")
            for sc in range(CH // 512):
                s.te(MM(prow[0:1, 0:512], ones8[:, 0:1], tmpc[:, sc * 512:(sc + 1) * 512],
                        True, True), reads=["tmpc", "ones8", "prow_r"], writes=["prow"],
                     tag=f"lrk cs {j},{ch},{sc}")
                s.act(SCOPY(lrkc[0:1, sc * 512:(sc + 1) * 512], prow[0:1, 0:512]),
                      reads=["prow"], writes=[f"lrkc{sc}", "prow_r"],
                      tag=f"lrk ev {j},{ch},{sc}")
            s.dma("gpsimd", lrk_o[j:j + 1, ch * CH:(ch + 1) * CH], lrkc[0:1, 0:CH],
                  reads=[f"lrkc{sc}" for sc in range(CH // 512)], writes=["lrkD"],
                  semkey="dq:st:lrk", tag=f"lrk st {j},{ch}")
        s.dve(COPY(cntc[:, j:j + 1], carry[:, j:j + 1]), reads=["carry"],
              writes=[f"cnt{j}"], tag=f"cnt {j}")
    tot8 = s.sbuf("tot8", [8, 1], F32)
    trow = s.sbuf("trow", [1, 8], F32)
    s.dve(TT(tot8[:, 0:1], cntc[:, 0:1], cntc[:, 1:2], ALU.add), reads=["cnt0", "cnt1"],
          writes=["tot8"], tag="tot8")
    s.te(TRANS(prow[0:1, 0:8], tot8[:, 0:1], identf[0:8, 0:8]),
         reads=["tot8", "identf", "prow_r"], writes=["prow"], tag="tot tr")
    s.act(SCOPY(trow[0:1, :], prow[0:1, 0:8]), reads=["prow"], writes=["trow", "prow_r"],
          tag="tot ev")
    s.dve(lambda e: e.tensor_tensor_scan(trow[0:1, :], sone[0:1, 0:1].to_broadcast([1, 8]),
                                         trow[0:1, :], 0.0, op0=ALU.mult, op1=ALU.add),
          reads=["trow", "sone"], writes=["trow"], tag="tot scan")
    s.te(TRANS(prow[0:8, 0:1], trow[0:1, :], identf[0:1, 0:1]),
         reads=["trow", "identf", "prow_r"], writes=["prow"], tag="tot tr2")
    s.act(SCOPY(cntc[:, 2:3], prow[0:8, 0:1]), reads=["prow"], writes=["base_i", "prow_r"],
          tag="tot ev2")
    s.dve(TT(cntc[:, 2:3], cntc[:, 2:3], tot8[:, 0:1], ALU.subtract),
          reads=["base_i", "tot8"], writes=["base0"], tag="base0")
    s.dve(TT(cntc[:, 3:4], cntc[:, 2:3], cntc[:, 0:1], ALU.add),
          reads=["base0", "cnt0"], writes=["base1"], tag="base1")
    baserep = s.sbuf("baserep", [128, 16], F32)
    iotar = s.sbuf("iotar", [128, 8], F32)
    s.pool(lambda e: e.iota(iotar[:, :], [[1, 8]], base=0, channel_multiplier=0,
                            allow_small_or_imprecise_dtypes=True), writes=["iotar"])
    s.pool(lambda e: e.iota(trashc[:, :], [[1, 1]], base=16384, channel_multiplier=1,
                            allow_small_or_imprecise_dtypes=True), writes=["trashc"])
    for j in range(2):
        s.te(TRANS(prow[0:1, 0:8], cntc[:, 2 + j:3 + j], identf[0:8, 0:8]),
             reads=[f"base{j}", "identf", "prow_r"], writes=["prow"], tag=f"br tr {j}")
        s.act(SCOPY(trow[0:1, :], prow[0:1, 0:8]), reads=["prow"],
              writes=["trow", "prow_r"], tag=f"br ev {j}")
        s.dve(COPY(trow[0:1, :], trow[0:1, :]), reads=["trow"], writes=["trowb"],
              tag=f"br cp {j}")
        s.te(MM(prow[0:128, 0:8], onesf[0:1, :], trow[0:1, :], True, True),
             reads=["trowb", "onesf", "prow_r"], writes=["prow"], tag=f"br rep {j}")
        s.act(SCOPY(baserep[:, j * 8:(j + 1) * 8], prow[0:128, 0:8]), reads=["prow"],
              writes=[f"baserep{j}", "prow_r"], tag=f"br repe {j}")

    xnt = [s.sbuf("xnt0", [128, 1024], F32)] * 2
    xnb = [s.sbuf("xnbA", [128, 1024], BF16)] * 2
    xnT = [s.sbuf("xnTA", [128, 1024], BF16)] * 2
    qf = s.sbuf("qf", [128, 1024], F32)
    kvlb = s.sbuf("kvlb", [128, 256], BF16)
    kvlT = s.sbuf("kvlT", [128, 256], BF16)
    kf = s.sbuf("kf", [128, 1024], F32)
    cost1 = s.sbuf("cost1", [128, 1024], BF16)
    sint1 = s.sbuf("sint1", [128, 1024], BF16)
    swp = s.sbuf("swp", [128, 1024], F32)
    qb = s.sbuf("qb", [128, 1024], BF16)
    kb = s.sbuf("kb", [128, 1024], BF16)
    vb = [s.sbuf("vbA", [128, 1024], BF16)] * 2
    qTt = s.sbuf("qTt", [64, 2048], BF16)
    kTt = s.sbuf("kTt", [64, 2048], BF16)
    vstk = [s.sbuf("vstkA", [128, 16 * 65], BF16)] * 2
    pexp = s.sbuf("pexpa", [128, 512], BF16)
    orw = None
    oTt = s.sbuf("oTt", [64, 2048], BF16)
    oTh = s.sbuf("oTh", [64, 2048], BF16)
    s1b = [s.sbuf("s1b0", [128, 1024], F32)] * 2
    ecol = s.sbuf("ecol", [128, 2], F32)
    lcol = s.sbuf("lcol", [128, 2], F32)
    ohcol = s.sbuf("ohcol", [128, 8], F32)
    bcol = s.sbuf("bcol", [128, 2], F32)
    einvt = s.sbuf("einvt", [128, 2], F32)
    c99col = s.sbuf("c99col", [128, 2], F32)
    vm2c = s.sbuf("vm2c", [128, 2], mybir.dt.uint8)
    s.pool(MEMSET(c99col[:, :], 99.0), writes=["c99col"])
    pcol2 = s.sbuf("pcol2", [128, 2], F32)
    trashc = s.sbuf("trashc", [128, 1], F32)
    posu = s.sbuf("posu", [128, 2], U32)

    pq_ = [s.psum(f"pqa{i}", [128, 512], F32) for i in range(2)]
    pat = s.psum("pat", [128, 512], F32)
    po6 = s.psum("po6", [128, 512], F32)
    pr6 = s.psum("pr6", [128, 512], F32)
    ptb = [s.psum(f"ptba{i}", [128, 1024], BF16) for i in range(2)]

    s.pool(lambda e, a=vstk[0]: e.memset(
        bass.AP(a, 64, [[16 * 65, 128], [65, 16]]), 1.0), writes=["vstkA_ones"])

    for t in range(NT):
        p = t % 2
        b, tc = t // 32, t % 32
        r0 = t * 128
        xi, xik = xnt[p], "xnt0"
        xbb, xbk = xnb[p], "xnbA"
        xT, xTk = xnT[p], "xnTA"
        s.dma("sync", xi[:, :], xn1_i[r0:r0 + 128, :], writes=[xik], tag=f"xn1 ld {t}")
        s.dve(COPY(xbb[:, :], xi[:, :]), reads=[xik], writes=[xbk], tag=f"cast {t}")
        for c in range(8):
            pt_, ptk = ptb[c % 2], f"ptba{c % 2}"
            s.te(TRANS(pt_[:, 0:128], xbb[:, c * 128:(c + 1) * 128], identb[:, :]),
                 reads=[xbk, "identb", ptk + "_r"], writes=[ptk], tag=f"trx {t},{c}")
            s.act(SCOPY(xT[:, c * 128:(c + 1) * 128], pt_[:, 0:128]),
                  reads=[ptk], writes=[xTk + f"_{c}", ptk + "_r"], tag=f"evx {t},{c}")
        XK = [xTk + f"_{c}" for c in range(8)]
        for nb in range(2):
            pp, ppk = pq_[nb], f"pqa{nb}"
            for c in range(8):
                s.te(MM(pp[:, :], xT[:, c * 128:(c + 1) * 128],
                        Wqs[:, c * 1024 + nb * 512: c * 1024 + (nb + 1) * 512], c == 0, False),
                     reads=XK + WK + ([ppk + "_r"] if c == 0 else []), writes=[ppk],
                     tag=f"q {t},{nb},{c}")
            s.te(MM(pp[:, :], onesb[0:1, :], bqs[0:1, nb * 512:(nb + 1) * 512], False, True),
                 reads=WK, writes=[ppk], tag=f"qb {t},{nb}")
            s.dve(COPY(qf[:, nb * 512:(nb + 1) * 512], pp[:, :]), reads=[ppk],
                  writes=[f"qf{nb}", ppk + "_r"], tag=f"q ev {t},{nb}")
        pp, ppk = pq_[0], "pqa0"
        for c in range(8):
            s.te(MM(pp[:, 0:256], xT[:, c * 128:(c + 1) * 128],
                    Wds[:, c * 256:(c + 1) * 256], c == 0, False),
                 reads=XK + WK + (["pqa0_r"] if c == 0 else []), writes=[ppk],
                 tag=f"kvl {t},{c}")
        s.te(MM(pp[:, 0:256], onesb[0:1, :], bds[0:1, :], False, True),
             reads=WK, writes=[ppk], tag=f"kvlb {t}")
        s.dve(COPY(kvlb[:, :], pp[:, 0:256]), reads=[ppk], writes=["kvlb", ppk + "_r"],
              tag=f"kvl ev {t}")
        for c in range(2):
            pt_, ptk = ptb[c % 2], f"ptba{c % 2}"
            s.te(TRANS(pt_[:, 0:128], kvlb[:, c * 128:(c + 1) * 128], identb[:, :]),
                 reads=["kvlb", "identb", ptk + "_r"], writes=[ptk], tag=f"trl {t},{c}")
            s.act(SCOPY(kvlT[:, c * 128:(c + 1) * 128], pt_[:, 0:128]),
                  reads=[ptk], writes=[f"kvlT_{c}", ptk + "_r"], tag=f"evl {t},{c}")
        for nb in range(4):
            pp, ppk = pq_[nb % 2], f"pqa{nb % 2}"
            for c in range(2):
                s.te(MM(pp[:, :], kvlT[:, c * 128:(c + 1) * 128],
                        Wus[:, c * 2048 + nb * 512: c * 2048 + (nb + 1) * 512], c == 0, False),
                     reads=["kvlT_0", "kvlT_1"] + WK + ([ppk + "_r"] if c == 0 else []),
                     writes=[ppk], tag=f"kv {t},{nb},{c}")
            s.te(MM(pp[:, :], onesb[0:1, :], bus[0:1, nb * 512:(nb + 1) * 512], False, True),
                 reads=WK, writes=[ppk], tag=f"kvb {t},{nb}")
            s.dve(lambda e, o=bass.AP(kf, nb * 256, [[1024, 128], [64, 4], [1, 64]]),
                  i_=bass.AP(pp, 0, [[512, 128], [128, 4], [1, 64]]): e.tensor_copy(o, i_),
                  reads=[ppk], writes=[f"kf{nb}"], tag=f"kv evk {t},{nb}")
            s.dve(lambda e, o=bass.AP(vb[p], nb * 256, [[1024, 128], [64, 4], [1, 64]]),
                  i_=bass.AP(pp, 64, [[512, 128], [128, 4], [1, 64]]): e.tensor_copy(o, i_),
                  reads=[ppk], writes=[f"vbp{nb}", ppk + "_r"], tag=f"kv evv {t},{nb}")
        KVK = [f"kf{nb}" for nb in range(4)]
        VBK = [f"vbp{nb}" for nb in range(4)]
        vbb, vbk = vb[p], "vbA"
        s.dma("sync", cost1[:, :], cosr[tc * 128:(tc + 1) * 128, :], writes=["cost1"],
              tag=f"cos {t}")
        s.dma("sync", sint1[:, :], sinr[tc * 128:(tc + 1) * 128, :], writes=["sint1"],
              tag=f"sin {t}")
        for (srcb, srck, dstb, dstk) in ((qf, ["qf0", "qf1"], qb, "qb"),
                                         (kf, KVK, kb, "kb")):
            s.dve(lambda e, o=bass.AP(swp, 0, [[1024, 128], [64, 16], [1, 32]]),
                  i_=bass.AP(srcb, 32, [[1024, 128], [64, 16], [1, 32]]): e.tensor_copy(o, i_),
                  reads=srck, writes=["swp_a"], tag=f"sw1 {t},{dstk}")
            s.dve(lambda e, o=bass.AP(swp, 32, [[1024, 128], [64, 16], [1, 32]]),
                  i_=bass.AP(srcb, 0, [[1024, 128], [64, 16], [1, 32]]): e.tensor_copy(o, i_),
                  reads=srck, writes=["swp_b"], tag=f"sw2 {t},{dstk}")
            s.pool(TT(swp[:, :], swp[:, :], sint1[:, :], ALU.mult),
                   reads=["swp_a", "swp_b", "sint1"], writes=["swp"], tag=f"rsin {t},{dstk}")
            s.dve(TT(srcb[:, :], srcb[:, :], cost1[:, :], ALU.mult),
                  reads=srck + ["cost1"], writes=[srck[0] + "m"] + srck,
                  tag=f"rcos {t},{dstk}")
            s.dve(TT(dstb[:, :], srcb[:, :], swp[:, :], ALU.add),
                  reads=[srck[0] + "m", "swp"], writes=[dstk], tag=f"radd {t},{dstk}")
        s.dma("scalar", vscr[:, :], vbb[:, :], reads=VBK, writes=["vscr"],
              semkey="dq:vscr", tag=f"vscr st {t}")
        vs, vsk = vstk[p], "vstkA"
        for t8 in range(8):
            s.dma("scalar",
                  bass.AP(vs, t8 * 16 * 1040, [[1040, 16], [65, 16], [1, 64]]),
                  bass.AP(vscr, t8 * 1024, [[64, 16], [8192, 16], [1, 64]]),
                  reads=["vscr"], writes=[vsk] if t8 == 7 else [f"vstk_{t8}"],
                  semkey="dq:vstk", tag=f"vstk ld {t},{t8}")
        for g in range(16):
            pt_, ptk = ptb[g % 2], f"ptba{g % 2}"
            s.te(TRANS(pt_[0:64, 0:128], kb[:, g * 64:(g + 1) * 64], identb[:, :]),
                 reads=["kb", "identb", ptk + "_r"], writes=[ptk], tag=f"trk {t},{g}")
            s.dve(lambda e, o=bass.AP(kTt, g, [[2048, 64], [128, 16], [16, 8]]),
                  i_=bass.AP(pt_, 0, [[1024, 64], [8, 16], [1, 8]]): e.tensor_copy(o, i_),
                  reads=[ptk], writes=[f"kTt_{g}", ptk + "_r"], tag=f"evk {t},{g}")
            pt2, pt2k = ptb[(g + 1) % 2], f"ptba{(g + 1) % 2}"
            s.te(TRANS(pt2[0:64, 0:128], qb[:, g * 64:(g + 1) * 64], identb[:, :]),
                 reads=["qb", "identb", pt2k + "_r"], writes=[pt2k], tag=f"trq {t},{g}")
            s.dve(lambda e, o=bass.AP(qTt, g, [[2048, 64], [128, 16], [16, 8]]),
                  i_=bass.AP(pt2, 0, [[1024, 64], [8, 16], [1, 8]]): e.tensor_copy(o, i_),
                  reads=[pt2k], writes=[f"qTt_{g}", pt2k + "_r"], tag=f"evq {t},{g}")
        KTK = [f"kTt_{g}" for g in range(16)]
        QTK = [f"qTt_{g}" for g in range(16)]
        for g4 in range(4):
            for gg in range(4):
                grp = g4 * 4 + gg
                s.te(MM(pat[:, gg * 128:(gg + 1) * 128], kTt[:, grp * 128:(grp + 1) * 128],
                        qTt[:, grp * 128:(grp + 1) * 128], True, True),
                     reads=KTK + QTK + (["pat_r"] if gg == 0 else []), writes=["pat"],
                     tag=f"att {t},{grp}")
            s.act(ACTV(pexp[:, :], pat[:, :], AF.Exp, bias=nC1[:, 0:1], scale=SC1),
                  reads=["pat", "nC1"], writes=["pexpa", "pmsk"], tag=f"aexp {t},{g4}")
            s.dve(TT(pexp[:, :], pexp[:, :], maskc[:, :], ALU.mult),
                  reads=["pexpa", "maskc"], writes=["pmsk"], tag=f"amask {t},{g4}")
            for gg in range(4):
                grp = g4 * 4 + gg
                s.te(MM(po6[0:65, gg * 128:(gg + 1) * 128], vs[:, grp * 65:(grp + 1) * 65],
                        pexp[:, gg * 128:(gg + 1) * 128], True, True),
                     reads=[vsk, "vstkA_ones", "pmsk"] + [f"vstk_{t8}" for t8 in range(8)] + (["po6_r"] if gg == 0 else []),
                     writes=["po6"], tag=f"ao {t},{grp}")
            s.act(SCOPY(qf[0:64, 0:512], po6[0:64, :]), reads=["po6"],
                  writes=["orw", "qf0", "qf1"], tag=f"aev {t},{g4}")
            s.dve(lambda e, a=swp[0:1, 0:512], b_=po6[64:65, :]: e.reciprocal(a, b_),
                  reads=["po6"], writes=["rdena", "swp", "swp_a", "swp_b", "po6_r"],
                  tag=f"arcp {t},{g4}")
            s.te(MM(pr6[0:64, :], onesf[0:1, 0:64], swp[0:1, 0:512], True, True),
                 reads=["onesf", "rdena", "pr6_r"], writes=["pr6"], tag=f"arep {t},{g4}")
            s.dve(TT(oTt[:, g4 * 512:(g4 + 1) * 512], qf[0:64, 0:512], pr6[0:64, :], ALU.mult),
                  reads=["orw", "qf0", "pr6"], writes=[f"oTt_{g4}", "pr6_r"], tag=f"adiv {t},{g4}")
        OTK = [f"oTt_{g4}" for g4 in range(4)]
        for h in range(16):
            s.dve(lambda e, o=oTh[:, h * 128:(h + 1) * 128],
                  i_=bass.AP(oTt, h, [[2048, 64], [128, 16], [16, 8]]): e.tensor_copy(o, i_),
                  reads=OTK, writes=[f"oTh_{h}"], tag=f"ohm {t},{h}")
        OHK2 = [f"oTh_{h}" for h in range(16)]
        s1t, s1k = s1b[p], "s1b0"
        for nb in range(2):
            pp, ppk = pq_[nb], f"pqa{nb}"
            for h in range(16):
                s.te(MM(pp[:, :], oTh[:, h * 128:(h + 1) * 128],
                        Wos[0:64, h * 1024 + nb * 512: h * 1024 + (nb + 1) * 512],
                        h == 0, False),
                     reads=OHK2 + WK + ([ppk + "_r"] if h == 0 else []), writes=[ppk],
                     tag=f"wo1 {t},{nb},{h}")
            s.te(MM(pp[:, :], onesb[0:1, :], bos[0:1, nb * 512:(nb + 1) * 512], False, True),
                 reads=WK, writes=[ppk], tag=f"wo1b {t},{nb}")
            s.dve(COPY(s1t[:, nb * 512:(nb + 1) * 512], pp[:, :]), reads=[ppk],
                  writes=[s1k + f"_{nb}", ppk + "_r"], tag=f"wo1 ev {t},{nb}")
        s.dma("gpsimd", s1_o[r0:r0 + 128, :], s1t[:, :], reads=[s1k + "_0", s1k + "_1"],
              semkey="dq:st:s1", tag=f"s1 st {t}")
        ri2, ri2k = rtt[t % 2], f"rtt{t % 2}"
        s.dma("sync", ri2[:, :], rt_i[t * 128:(t + 1) * 128, :], writes=[ri2k],
              tag=f"rt2 ld {t}")
        s.dve(TS(vm2c[:, 0:2], ri2[:, 2:4], 1e-9, None, ALU.is_gt), reads=[ri2k],
              writes=["vm2"], tag=f"vm2 {t}")
        s.dve(lambda e, rr=ri2: e.select(ecol[:, 0:2], vm2c[:, 0:2], rr[:, 0:2],
                                         c99col[:, 0:2]),
              reads=["vm2", "c99col", ri2k], writes=["ecol", "ecol0"], tag=f"ec {t}")
        for j in range(2):
            s.dma("sync", lrow[0:1, :], lrk_o[j:j + 1, t * 128:(t + 1) * 128],
                  reads=["lrkD"], writes=["lrow"], tag=f"lrow ld {t},{j}")
            s.te(TRANS(prow[0:128, 0:1], lrow[0:1, :], identf[0:1, 0:1]),
                 reads=["lrow", "identf", "prow_r"], writes=["prow"], tag=f"lc tr {t},{j}")
            s.act(SCOPY(lcol[:, j:j + 1], prow[0:128, 0:1]), reads=["prow"],
                  writes=["lcol" if j else "lcol0", "prow_r"], tag=f"lc ev {t},{j}")
        for j in range(2):
            s.dve(TS(ohcol[:, :], iotar[:, :], ecol[:, j:j + 1], None, ALU.is_equal),
                  reads=["ecol", "ecol0", "iotar"], writes=["ohcol"],
                  tag=f"ohc {t},{j}")
            s.dve(TT(ohcol[:, :], ohcol[:, :], baserep[:, j * 8:(j + 1) * 8], ALU.mult),
                  reads=["ohcol", f"baserep{j}"], writes=["ohcol"], tag=f"ohb {t},{j}")
            s.dve(lambda e, a=bcol[:, j:j + 1], b_=ohcol[:, :]:
                  e.reduce_sum(a, b_, axis=mybir.AxisListType.X),
                  reads=["ohcol"], writes=[f"bcol{j}"], tag=f"bsum {t},{j}")
        s.dve(TS(einvt[:, 0:2], ecol[:, 0:2], 98.5, None, ALU.is_gt),
              reads=["ecol", "ecol0"], writes=["einvt"], tag=f"einv {t}")
        s.dve(lambda e: e.scalar_tensor_tensor(pcol2[:, 0:2], einvt[:, 0:2], 1e6,
                                               bcol[:, 0:2], ALU.mult, ALU.add),
              reads=["einvt", "bcol0", "bcol1"], writes=["pcol2"], tag=f"binv {t}")
        s.dve(TT(pcol2[:, 0:2], pcol2[:, 0:2], lcol[:, 0:2], ALU.add),
              reads=["pcol2", "lcol", "lcol0"], writes=["pcol2"], tag=f"posc {t}")
        s.dve(TS(pcol2[:, 0:2], pcol2[:, 0:2], trashc[:, 0:1], None, ALU.min),
              reads=["pcol2", "trashc"], writes=["pcol2"], tag=f"poscl {t}")
        s.dve(COPY(posu[:, :], pcol2[:, 0:2]), reads=["pcol2"], writes=["posu"],
              tag=f"posu {t}")
        for j in range(2):
            s.add("gpsimd", lambda e, o=slab_o[:, :], i_=xbb[:, :],
                  ix=posu[:, j:j + 1]: e.indirect_dma_start(
                      out=o, out_offset=bass.IndirectOffsetOnAxis(ap=ix, axis=0),
                      in_=i_, in_offset=None, oob_is_err=False),
                  reads=[xbk, "posu"], writes=[f"slab{j}"], dma=True, tag=f"slab {t},{j}")
            s.ops[-1]["ch"] = "dq:slab"



def emit_l1b(s, slaball, erow_all, fc1T, fc2T, fb1, fb2, ls_i, ehot, crow16, identb_in, resl, dbgidx):
    identb = s.sbuf("identb_sb", [128, 128], BF16)
    s.dma("sync", identb[:, :], identb_in[:, :], writes=["identb"])
    identf = s.sbuf("identf_sb", [128, 128], F32)
    s.dve(COPY(identf[:, :], identb[:, :]), reads=["identb"], writes=["identf"])
    onesb = s.sbuf("onesb", [1, 128], BF16)
    s.pool(MEMSET(onesb[:, :], 1.0), writes=["onesb"])
    onesf = s.sbuf("onesf", [1, 128], F32)
    s.pool(MEMSET(onesf[:, :], 1.0), writes=["onesf"])
    prow = s.psum("prow", [128, 512], F32)
    w1s = s.sbuf("w1s", [128, 8 * 4096], BF16)
    for c in range(8):
        s.dma("sync", w1s[:, c * 4096:(c + 1) * 4096], fc1T[c * 128:(c + 1) * 128, :],
              writes=[f"w1s{c}"], semkey="dq:w1s", tag=f"w1 {c}")
    W1K = [f"w1s{c}" for c in range(8)]
    ehots = s.sbuf("ehots", [8, 1], F32)
    s.dma("sync", ehots[:, :], ehot[:, :], writes=["ehots"])
    cr16 = s.sbuf("cr16", [1, 16], F32)
    s.dma("sync", cr16[:, :], crow16[:, :], writes=["cr16"])
    fb2sb = s.sbuf("fb2sb", [1, 1024], BF16)
    s.dma("gpsimd", fb2sb[:, :], fb2[:, :], writes=["fb2sb"], semkey="dq:wini")
    lsrow = s.sbuf("lsrow", [1, 1024], F32)
    s.dma("sync", lsrow[:, :], ls_i[:, :], writes=["lsrow"])
    fb1r = s.sbuf("fb1r", [32, 128], F32)
    s.dma("sync", fb1r[:, :], fb1[:, :], writes=["fb1r"])
    fb1c = s.sbuf("fb1c", [128, 32], F32)
    s.te(TRANS(prow[:, 0:32], fb1r[:, :], identf[0:32, 0:32]),
         reads=["fb1r", "identf", "prow_r"], writes=["prow"], tag="fb1 tr")
    s.act(SCOPY(fb1c[:, :], prow[:, 0:32]), reads=["prow"], writes=["fb1c", "prow_r"],
          tag="fb1 ev")
    lsrep = s.sbuf("lsrep", [128, 1024], F32)
    for nb in range(2):
        s.te(MM(prow[:, :], onesf[0:1, :], lsrow[0:1, nb * 512:(nb + 1) * 512], True, True),
             reads=["lsrow", "onesf", "prow_r"], writes=["prow"], tag=f"lsrep {nb}")
        s.act(SCOPY(lsrep[:, nb * 512:(nb + 1) * 512], prow[:, :]), reads=["prow"],
              writes=[f"lsrep{nb}", "prow_r"], tag=f"lsrep ev {nb}")

    cnt0m, cnt1m, CK = build_counts(s, erow_all, prow)

    tot = s.sbuf("tot", [8, 8], F32)
    s.dve(TT(tot[:, :], cnt0m[:, :], cnt1m[:, :], ALU.add), reads=CK, writes=["tot"],
          tag="tot")
    t16 = s.sbuf("t16", [16, 8], F32)
    s.te(TRANS(prow[0:8, 0:8], tot[:, :], identf[0:8, 0:8]),
         reads=["tot", "identf", "prow_r"], writes=["prow"], tag="tot tr")
    s.act(SCOPY(t16[0:8, 0:8], prow[0:8, 0:8]), reads=["prow"], writes=["t16", "prow_r"],
          tag="tot ev")
    sone8 = s.sbuf("sone8", [8, 1], F32)
    s.pool(MEMSET(sone8[:, :], 1.0), writes=["sone8"])
    s.dve(lambda e: e.tensor_tensor_scan(t16[0:8, 0:8], sone8[0:8, 0:1].to_broadcast([8, 8]),
                                         t16[0:8, 0:8], 0.0, op0=ALU.mult, op1=ALU.add),
          reads=["t16", "sone8"], writes=["t16s"], tag="tot scan")
    s.te(TRANS(prow[0:8, 0:8], t16[0:8, 0:8], identf[0:8, 0:8]),
         reads=["t16s", "identf", "prow_r"], writes=["prow"], tag="tot tr2")
    colin = s.sbuf("colin", [8, 8], F32)
    s.act(SCOPY(colin[:, :], prow[0:8, 0:8]), reads=["prow"], writes=["colin", "prow_r"],
          tag="tot ev2")
    colex = s.sbuf("colex", [8, 8], F32)
    s.dve(TT(colex[:, :], colin[:, :], tot[:, :], ALU.subtract), reads=["colin", "tot"],
          writes=["colex"], tag="colex")
    myrows = s.sbuf("myrows", [1, 32], F32)
    for i, (mat, matk) in enumerate([(colex, "colex"), (cnt0m, "cmat0"), (cnt1m, "cmat1")]):
        rk = ["colex"] if i == 0 else CK
        s.te(MM(prow[0:1, 0:8], ehots[:, 0:1], mat[:, :], True, True),
             reads=rk + ["ehots", "prow_r"], writes=["prow"], tag=f"myrow {i}")
        s.act(SCOPY(myrows[0:1, i * 8:(i + 1) * 8], prow[0:1, 0:8]), reads=["prow"],
              writes=[f"myrow{i}", "prow_r"], tag=f"myrow ev {i}")
    brow = s.sbuf("brow", [1, 16], F32)
    s.dve(TT(brow[0:1, 0:8], myrows[0:1, 0:8], cr16[0:1, 0:8], ALU.add),
          reads=["myrow0", "cr16"], writes=["brow_a"], tag="brow a")
    s.dve(TT(brow[0:1, 8:16], brow[0:1, 0:8], myrows[0:1, 8:16], ALU.add),
          reads=["brow_a", "myrow1"], writes=["brow_b"], tag="brow b")
    lrow16 = s.sbuf("lrow16", [1, 16], F32)
    s.dve(COPY(lrow16[0:1, 0:8], myrows[0:1, 8:16]), reads=["myrow1"], writes=["lrow_a"],
          tag="lrow a")
    s.dve(COPY(lrow16[0:1, 8:16], myrows[0:1, 16:24]), reads=["myrow2"], writes=["lrow_b"],
          tag="lrow b")
    srow = s.sbuf("srow", [1, 17], F32)
    s.pool(MEMSET(srow[:, :], 0.0), writes=["srow_z"])
    sone1 = s.sbuf("sone1", [1, 1], F32)
    s.pool(MEMSET(sone1[:, :], 1.0), writes=["sone1"])
    s.dve(lambda e: e.tensor_tensor_scan(srow[0:1, 1:17], sone1[0:1, 0:1].to_broadcast([1, 16]),
                                         lrow16[0:1, :], 0.0, op0=ALU.mult, op1=ALU.add),
          reads=["lrow_a", "lrow_b", "sone1", "srow_z"], writes=["srow"], tag="srow scan")
    vrow = s.sbuf("vrow", [1, 16], F32)
    s.dve(TT(vrow[0:1, :], brow[0:1, :], srow[0:1, 0:16], ALU.subtract),
          reads=["brow_a", "brow_b", "srow", "srow_z"], writes=["vrow"], tag="vrow")
    drow = s.sbuf("drow", [1, 16], F32)
    s.dve(COPY(drow[0:1, 0:1], vrow[0:1, 0:1]), reads=["vrow"], writes=["drow_a"], tag="d0")
    s.dve(TT(drow[0:1, 1:16], vrow[0:1, 1:16], vrow[0:1, 0:15], ALU.subtract),
          reads=["vrow"], writes=["drow_b"], tag="dr")
    srep = s.sbuf("srep", [128, 17], F32)
    drep = s.sbuf("drep", [128, 16], F32)
    s.te(MM(prow[:, 0:17], onesf[0:1, :], srow[0:1, :], True, True),
         reads=["srow", "srow_z", "onesf", "prow_r"], writes=["prow"], tag="srep")
    s.act(SCOPY(srep[:, :], prow[:, 0:17]), reads=["prow"], writes=["srep", "prow_r"],
          tag="srep ev")
    s.te(MM(prow[:, 0:16], onesf[0:1, :], drow[0:1, :], True, True),
         reads=["drow_a", "drow_b", "onesf", "prow_r"], writes=["prow"], tag="drep")
    s.act(SCOPY(drep[:, :], prow[:, 0:16]), reads=["prow"], writes=["drep", "prow_r"],
          tag="drep ev")
    idxm = s.sbuf("idxm", [128, 80], F32)
    tmpm = s.sbuf("tmpm", [128, 80], F32)
    idxu = s.sbuf("idxu", [128, 80], U32)
    s.pool(lambda e: e.iota(idxm[:, :], [[128, 80]], base=0, channel_multiplier=1,
                            allow_small_or_imprecise_dtypes=True),
           writes=["idxm"], tag="idx iota")
    pm = s.sbuf("pm", [128, 80], F32)
    s.dve(COPY(pm[:, :], idxm[:, :]), reads=["idxm"], writes=["pm"], tag="pm")
    for sseg in range(17):
        s.dve(TS(tmpm[:, :], pm[:, :], srep[:, sseg:sseg + 1], None, ALU.is_ge),
              reads=["pm", "srep"], writes=["tmpm"], tag=f"seg ge {sseg}")
        if sseg < 16:
            s.dve(lambda e, d=drep[:, sseg:sseg + 1]:
                  e.scalar_tensor_tensor(idxm[:, :], tmpm[:, :], 1.0, idxm[:, :],
                                         ALU.mult, ALU.bypass) if False else None,
                  reads=[], writes=[], tag="noop")
            s.ops.pop()
            s.dve(lambda e, d=drep[:, sseg:sseg + 1]:
                  e.tensor_scalar(tmpm[:, :], tmpm[:, :], d, None, op0=ALU.mult),
                  reads=["tmpm", "drep"], writes=["tmpm2"], tag=f"seg mul {sseg}")
            s.dve(TT(idxm[:, :], idxm[:, :], tmpm[:, :], ALU.add),
                  reads=["idxm", "tmpm2"], writes=["idxm"], tag=f"seg add {sseg}")
        else:
            s.dve(TS(tmpm[:, :], tmpm[:, :], 1e9, None, ALU.mult),
                  reads=["tmpm"], writes=["tmpm2"], tag="seg big")
            s.dve(TT(idxm[:, :], idxm[:, :], tmpm[:, :], ALU.add),
                  reads=["idxm", "tmpm2"], writes=["idxm"], tag="seg bigadd")
    s.dve(TS(idxm[:, :], idxm[:, :], 16384.0, None, ALU.min), reads=["idxm"],
          writes=["idxm"], tag="idx clamp")
    s.dve(COPY(idxu[:, :], idxm[:, :]), reads=["idxm"], writes=["idxu"], tag="idxu")
    s.dma("gpsimd", dbgidx[:, :], idxm[:, :], reads=["idxm"], semkey="dq:st:dbgi")

    toks = s.sbuf("toks", [128, 4 * 1024], BF16)
    tokT = s.sbuf("tokT", [128, 8 * 512], BF16)
    ybT = s.sbuf("ybT", [128, 32 * 512], BF16)
    w2s = [s.sbuf(f"w2s{i}", [128, 1024], BF16) for i in range(2)]
    yout = s.sbuf("yout", [128, 1024], F32)
    youtb = [s.sbuf(f"youtb{i}", [128, 1024], BF16) for i in range(2)]
    ptb = [s.psum(f"ptbb{i}", [128, 1024], BF16) for i in range(2)]
    pyt = [s.psum(f"pytb{i}", [128, 512], F32) for i in range(2)]
    pz = [s.psum(f"pzb{i}", [128, 512], F32) for i in range(2)]
    s.pool(MEMSET(toks[:, :], 0.0), writes=["toks_init"])

    _STAGE = int(os.environ.get("L1B_STAGE", "4"))
    for grp in range(int(os.environ.get("L1B_GROUPS", "20"))):
        TKK = []
        for ti in range(4):
            g = grp * 4 + ti
            if _STAGE != 12:
                s.add("gpsimd", lambda e, o=toks[:, ti * 1024:(ti + 1) * 1024],
                      ix=idxu[:, g:g + 1]: e.indirect_dma_start(
                          out=o, out_offset=None, in_=slaball[:, :],
                          in_offset=bass.IndirectOffsetOnAxis(ap=ix, axis=0),
                          oob_is_err=False),
                      reads=["idxu", "toks_init", f"tok_r{ti}"], writes=[f"tok{ti}"],
                      dma=True, tag=f"gather {g}")
                s.ops[-1]["ch"] = f"dq:tok{ti}"
            else:
                s.pool(MEMSET(toks[:, ti * 1024:(ti + 1) * 1024], 0.5),
                       reads=[f"tok_r{ti}"], writes=[f"tok{ti}"], tag=f"gmemset {g}")
            TKK.append(f"tok{ti}")
            if _STAGE == 11:
                continue
            if _STAGE < 2:
                s.dma("gpsimd", resl[(grp * 4 + ti) * 128:(grp * 4 + ti + 1) * 128, :],
                      toks[:, ti * 1024:(ti + 1) * 1024], reads=[f"tok{ti}"],
                      writes=[f"tok_r{ti}"], semkey="dq:st:res0", tag=f"res direct {g}")
                continue
            for c in range(8):
                pt_, ptk = ptb[c % 2], f"ptbb{c % 2}"
                s.te(TRANS(pt_[:, 0:128], toks[:, ti * 1024 + c * 128: ti * 1024 + (c + 1) * 128],
                           identb[:, :]),
                     reads=[f"tok{ti}", "identb", ptk + "_r"], writes=[ptk],
                     tag=f"trt {g},{c}")
                s.act(SCOPY(tokT[:, c * 512 + ti * 128: c * 512 + (ti + 1) * 128],
                            pt_[:, 0:128]), reads=[ptk],
                      writes=[f"tokT{c}_{ti}", ptk + "_r"], tag=f"evt {g},{c}")
        TTK = [f"tokT{c}_{ti}" for c in range(8) for ti in range(4)]
        if _STAGE < 3:
            continue
        for yt in range(32):
            pp, ppk = pyt[yt % 2], f"pytb{yt % 2}"
            for c in range(8):
                s.te(MM(pp[:, :], w1s[:, c * 4096 + yt * 128: c * 4096 + (yt + 1) * 128],
                        tokT[:, c * 512:(c + 1) * 512], c == 0, c == 7),
                     reads=TTK + W1K + [ppk + "_r"], writes=[ppk], tag=f"f1 {grp},{yt},{c}")
            s.act(ACTV(ybT[:, yt * 512:(yt + 1) * 512], pp[:, :], AF.Gelu,
                       bias=fb1c[:, yt:yt + 1]),
                  reads=[ppk, "fb1c"], writes=[f"ybT{yt}", ppk + "_r"],
                  tag=f"gelu {grp},{yt}")
        if _STAGE < 4:
            continue
        for half in range(2):
            zb = [pz[0], pz[1], pyt[0], pyt[1]]
            zbk = ["pzb0", "pzb1", "pytb0", "pytb1"]
            for yt in range(32):
                wsl, wslk = w2s[yt % 2], f"w2s{yt % 2}"
                s.dma("sync", wsl[:, :], fc2T[yt * 128:(yt + 1) * 128, :],
                      writes=[wslk], semkey=f"dq:{wslk}", tag=f"w2 {grp},{half},{yt}")
                for q in range(4):
                    ti = half * 2 + q // 2
                    nb = q % 2
                    s.te(MM(zb[q][:, :], ybT[:, yt * 512 + ti * 128: yt * 512 + (ti + 1) * 128],
                            wsl[:, nb * 512:(nb + 1) * 512], yt == 0, False),
                         reads=[f"ybT{yt}", wslk], writes=[zbk[q]],
                         tag=f"f2 {grp},{half},{yt},{q}")
            for q in range(4):
                nb = q % 2
                s.te(MM(zb[q][:, :], onesb[0:1, :], fb2sb[0:1, nb * 512:(nb + 1) * 512],
                        False, True), reads=["onesb", "fb2sb"], writes=[zbk[q]],
                     tag=f"f2b {grp},{half},{q}")
            for tq in range(2):
                ti = half * 2 + tq
                g = grp * 4 + ti
                yo, yok = youtb[tq], f"youtb{tq}"
                for nb in range(2):
                    q = tq * 2 + nb
                    s.dve(TT(yout[:, nb * 512:(nb + 1) * 512], zb[q][:, :],
                             lsrep[:, nb * 512:(nb + 1) * 512], ALU.mult),
                          reads=[zbk[q], "lsrep0", "lsrep1"], writes=[f"yout{nb}"],
                          tag=f"yls {g},{nb}")
                    s.dve(TT(yo[:, nb * 512:(nb + 1) * 512], yout[:, nb * 512:(nb + 1) * 512],
                             toks[:, ti * 1024 + nb * 512: ti * 1024 + (nb + 1) * 512],
                             ALU.add),
                          reads=[f"yout{nb}", f"tok{ti}"], writes=[yok + f"_{nb}", zbk[q] + "_r"],
                          tag=f"yadd {g},{nb}")
                s.dma("gpsimd", resl[g * 128:(g + 1) * 128, :], yo[:, :],
                      reads=[yok + "_0", yok + "_1"], writes=[f"tok_r{ti}"],
                      semkey=f"dq:st:res{tq}", tag=f"res st {g}")



def emit_l1c(s, x1_i, s1_i, rt_i, lrk_i, erow_all, resall, cmask, identb_in, y_o):
    identb = s.sbuf("identb_sb", [128, 128], BF16)
    s.dma("sync", identb[:, :], identb_in[:, :], writes=["identb"])
    identf = s.sbuf("identf_sb", [128, 128], F32)
    s.dve(COPY(identf[:, :], identb[:, :]), reads=["identb"], writes=["identf"])
    onesf = s.sbuf("onesf", [1, 128], F32)
    s.pool(MEMSET(onesf[:, :], 1.0), writes=["onesf"])
    prow = s.psum("prow", [128, 512], F32)
    cnt0m, cnt1m, CK = build_counts(s, erow_all, prow)
    cms = s.sbuf("cms", [8, 1], F32)
    s.dma("sync", cms[:, :], cmask[:, :], writes=["cms"])
    gb = s.sbuf("gb", [8, 4], F32)
    cmrep = s.sbuf("cmrep", [8, 8], F32)
    cmr = s.sbuf("cmr", [1, 8], F32)
    s.te(TRANS(prow[0:1, 0:8], cms[:, 0:1], identf[0:8, 0:8]),
         reads=["cms", "identf", "prow_r"], writes=["prow"], tag="cm tr")
    s.act(SCOPY(cmr[0:1, :], prow[0:1, 0:8]), reads=["prow"], writes=["cmr", "prow_r"],
          tag="cm ev")
    s.te(MM(prow[0:8, 0:8], onesf[0:1, 0:8], cmr[0:1, :], True, True),
         reads=["cmr", "onesf", "prow_r"], writes=["prow"], tag="cm rep")
    s.act(SCOPY(cmrep[:, :], prow[0:8, 0:8]), reads=["prow"], writes=["cmrep", "prow_r"],
          tag="cm repe")
    tmp88 = s.sbuf("tmp88", [8, 8], F32)
    for j in range(2):
        s.dve(TT(tmp88[:, :], [cnt0m, cnt1m][j][:, :], cmrep[:, :], ALU.mult),
              reads=CK + ["cmrep"], writes=["tmp88"], tag=f"gb mul {j}")
        s.dve(lambda e, a=gb[:, j:j + 1], b_=tmp88[:, :]:
              e.reduce_sum(a, b_, axis=mybir.AxisListType.X),
              reads=["tmp88"], writes=[f"gb{j}"], tag=f"gb red {j}")
    s.dve(lambda e, a=gb[:, 2:3], b_=cnt0m[:, :]:
          e.reduce_sum(a, b_, axis=mybir.AxisListType.X),
          reads=CK, writes=["gbT0"], tag="gb T0")
    s.dve(TT(gb[:, 3:4], gb[:, 2:3], gb[:, 1:2], ALU.add), reads=["gbT0", "gb1"],
          writes=["gb3"], tag="gb1f")
    baserep = s.sbuf("baserep", [128, 16], F32)
    trow = s.sbuf("trow", [1, 8], F32)
    for j, (col, ck) in enumerate([(0, "gb0"), (3, "gb3")]):
        s.te(TRANS(prow[0:1, 0:8], gb[:, col:col + 1], identf[0:8, 0:8]),
             reads=[ck, "identf", "prow_r"], writes=["prow"], tag=f"gbr tr {j}")
        s.act(SCOPY(trow[0:1, :], prow[0:1, 0:8]), reads=["prow"], writes=["trow", "prow_r"],
              tag=f"gbr ev {j}")
        s.te(MM(prow[0:128, 0:8], onesf[0:1, :], trow[0:1, :], True, True),
             reads=["trow", "onesf", "prow_r"], writes=["prow"], tag=f"gbr rep {j}")
        s.act(SCOPY(baserep[:, j * 8:(j + 1) * 8], prow[0:128, 0:8]), reads=["prow"],
              writes=[f"baserep{j}", "prow_r"], tag=f"gbr repe {j}")
    iotar = s.sbuf("iotar", [128, 8], F32)
    s.pool(lambda e: e.iota(iotar[:, :], [[1, 8]], base=0, channel_multiplier=0,
                            allow_small_or_imprecise_dtypes=True), writes=["iotar"])


    rtt = [s.sbuf(f"rtt{i}", [128, 4], F32) for i in range(2)]
    x1t = [s.sbuf(f"x1t{i}", [128, 1024], F32) for i in range(2)]
    s1t = [s.sbuf(f"s1t{i}", [128, 1024], F32) for i in range(2)]
    lrow = s.sbuf("lrow", [1, 128], F32)
    lcol = s.sbuf("lcol", [128, 2], F32)
    ecol = s.sbuf("ecol", [128, 2], F32)
    vm2c = s.sbuf("vm2c", [128, 2], mybir.dt.uint8)
    c99col = s.sbuf("c99col", [128, 2], F32)
    s.pool(MEMSET(c99col[:, :], 99.0), writes=["c99col"])
    ohcol = s.sbuf("ohcol", [128, 8], F32)
    bcol = s.sbuf("bcol", [128, 2], F32)
    kmask = s.sbuf("kmask", [128, 2], F32)
    km8 = s.sbuf("km8", [128, 2], mybir.dt.uint8)
    bigcol = s.sbuf("bigcol", [128, 2], F32)
    wcol = s.sbuf("wcol", [128, 2], F32)
    pcol = s.sbuf("pcol", [128, 2], F32)
    ridx = s.sbuf("ridx", [128, 2], F32)
    ridxu = s.sbuf("ridxu", [128, 2], U32)
    gat = [s.sbuf(f"gat{j}", [128, 1024], BF16) for j in range(2)]
    momo = s.sbuf("momo", [128, 1024], F32)
    tmpg = s.sbuf("tmpg", [128, 1024], F32)
    yb = [s.sbuf(f"yb{i}", [128, 1024], F32) for i in range(2)]
    s.pool(MEMSET(bigcol[:, :], 1e9), writes=["bigcol"])
    for j in range(2):
        s.pool(MEMSET(gat[j][:, :], 0.0), writes=[f"gat{j}_init"])

    for t in range(NT):
        p = t % 2
        r0 = t * 128
        ri, rik = rtt[p], f"rtt{p}"
        s.dma("sync", ri[:, :], rt_i[r0:r0 + 128, :], writes=[rik], tag=f"rt ld {t}")
        x1b, x1k = x1t[p], f"x1t{p}"
        s1b, s1k = s1t[p], f"s1t{p}"
        s.dma("sync", x1b[:, :], x1_i[r0:r0 + 128, :], writes=[x1k], tag=f"x1 ld {t}")
        s.dma("sync", s1b[:, :], s1_i[r0:r0 + 128, :], writes=[s1k], tag=f"s1 ld {t}")
        s.dve(TS(vm2c[:, 0:2], ri[:, 2:4], 1e-9, None, ALU.is_gt), reads=[rik],
              writes=["vm2"], tag=f"vm2 {t}")
        s.dve(lambda e, rr=ri: e.select(ecol[:, 0:2], vm2c[:, 0:2], rr[:, 0:2],
                                        c99col[:, 0:2]),
              reads=["vm2", "c99col", rik], writes=["ecol"], tag=f"ec {t}")
        for j in range(2):
            s.dma("sync", lrow[0:1, :], lrk_i[j:j + 1, r0:r0 + 128],
                  writes=["lrow"], tag=f"lrow ld {t},{j}")
            s.te(TRANS(prow[0:128, 0:1], lrow[0:1, :], identf[0:1, 0:1]),
                 reads=["lrow", "identf", "prow_r"], writes=["prow"], tag=f"lc tr {t},{j}")
            s.act(SCOPY(lcol[:, j:j + 1], prow[0:128, 0:1]), reads=["prow"],
                  writes=[f"lcol{j}", "prow_r"], tag=f"lc ev {t},{j}")
            s.dve(TS(ohcol[:, :], iotar[:, :], ecol[:, j:j + 1], None, ALU.is_equal),
                  reads=["ecol", "iotar"], writes=["ohcol"], tag=f"ohc {t},{j}")
            s.dve(TT(ohcol[:, :], ohcol[:, :], baserep[:, j * 8:(j + 1) * 8], ALU.mult),
                  reads=["ohcol", f"baserep{j}"], writes=["ohcol"], tag=f"ohb {t},{j}")
            s.dve(lambda e, a=bcol[:, j:j + 1], b_=ohcol[:, :]:
                  e.reduce_sum(a, b_, axis=mybir.AxisListType.X),
                  reads=["ohcol"], writes=[f"bcol{j}"], tag=f"bsum {t},{j}")
        s.dve(TT(pcol[:, 0:2], bcol[:, 0:2], lcol[:, 0:2], ALU.add),
              reads=["bcol0", "bcol1", "lcol0", "lcol1"], writes=["pcol"], tag=f"pos {t}")
        s.dve(TS(km8[:, 0:2], pcol[:, 0:2], 9830.5, None, ALU.is_lt),
              reads=["pcol"], writes=["km8"], tag=f"km8 {t}")
        s.dve(COPY(kmask[:, 0:2], km8[:, 0:2]), reads=["km8"], writes=["kmask"],
              tag=f"km {t}")
        s.dve(TT(wcol[:, 0:2], ri[:, 2:4], kmask[:, 0:2], ALU.mult),
              reads=[rik, "kmask"], writes=["wcol"], tag=f"wc {t}")
        s.dve(lambda e: e.scalar_tensor_tensor(ridx[:, 0:2], ecol[:, 0:2], 10240.0,
                                               pcol[:, 0:2], ALU.mult, ALU.add),
              reads=["ecol", "pcol"], writes=["ridx"], tag=f"ri {t}")
        s.dve(TS(ridx[:, 0:2], ridx[:, 0:2], 81919.0, None, ALU.min),
              reads=["ridx"], writes=["ridx"], tag=f"ri2 {t}")
        s.dve(COPY(ridxu[:, :], ridx[:, 0:2]), reads=["ridx"], writes=["ridxu"],
              tag=f"riu {t}")
        for j in range(2):
            s.add("gpsimd", lambda e, o=gat[j][:, :], ix=ridxu[:, j:j + 1]:
                  e.indirect_dma_start(out=o, out_offset=None, in_=resall[:, :],
                                       in_offset=bass.IndirectOffsetOnAxis(ap=ix, axis=0),
                                       oob_is_err=False),
                  reads=["ridxu", f"gat{j}_init", f"gat{j}_r"], writes=[f"gat{j}"],
                  dma=True, tag=f"rgat {t},{j}")
            s.ops[-1]["ch"] = f"dq:gat{j}"
        s.dve(TS(momo[:, :], gat[0][:, :], wcol[:, 0:1], None, ALU.mult),
              reads=["gat0", "wcol"], writes=["momo", "gat0_r"], tag=f"mm0 {t}")
        s.dve(TS(tmpg[:, :], gat[1][:, :], wcol[:, 1:2], None, ALU.mult),
              reads=["gat1", "wcol"], writes=["tmpg", "gat1_r"], tag=f"mm1 {t}")
        s.dve(TT(momo[:, :], momo[:, :], tmpg[:, :], ALU.add),
              reads=["momo", "tmpg"], writes=["momo"], tag=f"mms {t}")
        s.act(ACTV(momo[:, :], momo[:, :], AF.Silu), reads=["momo"], writes=["momo"],
              tag=f"sil {t}")
        ybb, ybk = yb[p], f"yb{p}"
        s.dve(TT(ybb[:, :], momo[:, :], s1b[:, :], ALU.mult), reads=["momo", s1k],
              writes=[ybk], tag=f"ys {t}")
        s.dve(TT(ybb[:, :], ybb[:, :], x1b[:, :], ALU.add), reads=[ybk, x1k],
              writes=[ybk], tag=f"yx {t}")
        s.dma("gpsimd", y_o[r0:r0 + 128, :], ybb[:, :], reads=[ybk],
              semkey="dq:st:y", tag=f"y st {t}")



def build_A():
    nc = bass.Bass("TRN2", target_bir_lowering=False)
    T_ = {}
    x_in = nc.dram_tensor("x", [NTOK, D], F32, kind="ExternalInput")
    q1 = nc.dram_tensor("q1", [1024, 4096], I32, kind="ExternalInput")
    q2 = nc.dram_tensor("q2", [4096, 1024], I32, kind="ExternalInput")
    sc1 = nc.dram_tensor("sc1", [8, 4096], F32, kind="ExternalInput")
    sc2 = nc.dram_tensor("sc2", [32, 1024], F32, kind="ExternalInput")
    rw = nc.dram_tensor("rw", [8, 1024], F32, kind="ExternalInput")
    identf_in = nc.dram_tensor("identf", [128, 128], F32, kind="ExternalInput")
    identb_in = nc.dram_tensor("identb", [128, 128], BF16, kind="ExternalInput")
    wdq = nc.dram_tensor("wdq", [1024, 32], F32, kind="ExternalInput")
    wdkv = nc.dram_tensor("wdkv", [1024, 32], F32, kind="ExternalInput")
    wkr = nc.dram_tensor("wkr", [1024, 32], F32, kind="ExternalInput")
    bcat = nc.dram_tensor("bcat", [1, 96], F32, kind="ExternalInput")
    wuq = nc.dram_tensor("wuq", [32, 1024], F32, kind="ExternalInput")
    buq = nc.dram_tensor("buq", [1, 1024], F32, kind="ExternalInput")
    wuk = nc.dram_tensor("wuk", [32, 1024], F32, kind="ExternalInput")
    buk = nc.dram_tensor("buk", [1, 1024], F32, kind="ExternalInput")
    wuv = nc.dram_tensor("wuv", [32, 1024], F32, kind="ExternalInput")
    buv = nc.dram_tensor("buv", [1, 1024], F32, kind="ExternalInput")
    wqr = nc.dram_tensor("wqr", [32, 256], F32, kind="ExternalInput")
    bqr = nc.dram_tensor("bqr", [1, 256], F32, kind="ExternalInput")
    WO = nc.dram_tensor("WO", [1024, 1024], F32, kind="ExternalInput")
    bO = nc.dram_tensor("bO", [1, 1024], F32, kind="ExternalInput")
    gam = nc.dram_tensor("gam", [1, 1024], F32, kind="ExternalInput")
    bet = nc.dram_tensor("bet", [1, 1024], F32, kind="ExternalInput")
    fb1 = nc.dram_tensor("fb1", [32, 128], F32, kind="ExternalInput")
    fb2 = nc.dram_tensor("fb2", [1, 1024], F32, kind="ExternalInput")
    rb_i = nc.dram_tensor("rb", [1, 8], F32, kind="ExternalInput")
    Wq = nc.dram_tensor("Wq", [1024, 1024], F32, kind="ExternalInput")
    bq = nc.dram_tensor("bq", [1, 1024], F32, kind="ExternalInput")
    Wd = nc.dram_tensor("Wd", [1024, 256], F32, kind="ExternalInput")
    bd = nc.dram_tensor("bd", [1, 256], F32, kind="ExternalInput")
    Wu = nc.dram_tensor("Wu", [256, 2048], F32, kind="ExternalInput")
    bu = nc.dram_tensor("bu", [1, 2048], F32, kind="ExternalInput")
    Wo1 = nc.dram_tensor("Wo1", [1024, 1024], F32, kind="ExternalInput")
    bo1 = nc.dram_tensor("bo1", [1, 1024], F32, kind="ExternalInput")
    cosr = nc.dram_tensor("cosr", [4096, 1024], BF16, kind="ExternalInput")
    sinr = nc.dram_tensor("sinr", [4096, 1024], BF16, kind="ExternalInput")
    mask_i = nc.dram_tensor("maskc", [128, 512], BF16, kind="ExternalInput")

    s1_o = nc.dram_tensor("s1", [NTOK, D], F32, kind="ExternalOutput")
    slab_o = nc.dram_tensor("slab", [16512, D], BF16, kind="ExternalOutput")
    erow_o = nc.dram_tensor("erow", [2, NTOK], BF16, kind="ExternalOutput")
    lrk_o = nc.dram_tensor("lrkrow", [2, NTOK], F32, kind="ExternalOutput")
    rt_o = nc.dram_tensor("route", [NTOK, 4], F32, kind="ExternalOutput")
    x1_o = nc.dram_tensor("x1", [NTOK, D], F32, kind="ExternalOutput")

    w1b = nc.dram_tensor("w1b", [1024, 4096], BF16)
    w2b = nc.dram_tensor("w2b", [4096, 1024], BF16)
    rwT = nc.dram_tensor("rwT", [1024, 8], F32)
    xn_s = nc.dram_tensor("xn_s", [NTOK, D], F32)
    mus_s = nc.dram_tensor("mus_s", [NTOK, 2], F32)
    QT1 = nc.dram_tensor("QT1", [2048, 4096], BF16)
    QT2 = nc.dram_tensor("QT2", [512, 4096], BF16)
    KT1 = nc.dram_tensor("KT1", [2048, 4096], BF16)
    KT2 = nc.dram_tensor("KT2", [64, 4096], BF16)
    V_s = nc.dram_tensor("V", [NTOK, D], BF16)
    OT = nc.dram_tensor("OT", [2048, 4096], BF16)
    xn1_s = nc.dram_tensor("xn1", [NTOK, D], F32)
    wrow_s = nc.dram_tensor("wrow", [2, NTOK], F32)
    vscr = nc.dram_tensor("vscr", [128, 1024], BF16)

    s = Sched(nc)
    s.push_scope()
    emit_prep(s, q1, q2, sc1, sc2, rw, identf_in, w1b, w2b, rwT)
    s.barrier(); s.pop_scope(); s.push_scope()
    emit_l0pre(s, x_in, wdq, wdkv, wkr, bcat, wuq, buq, wuk, buk, wuv, buv,
               wqr, bqr, identb_in, xn_s, mus_s, QT1, QT2, KT1, KT2, V_s)
    s.barrier(); s.pop_scope(); s.push_scope()
    emit_l0attn(s, QT1, QT2, KT1, KT2, V_s, OT)
    s.barrier(); s.pop_scope(); s.push_scope()
    emit_l0rest(s, x_in, xn_s, mus_s, OT, WO, bO, gam, bet, w1b, w2b, fb1, fb2,
                rwT, rb_i, identb_in, x1_o, xn1_s, rt_o)
    s.barrier(); s.pop_scope(); s.push_scope()
    emit_l1a(s, xn1_s, rt_o, Wq, bq, Wd, bd, Wu, bu, Wo1, bo1, cosr, sinr, mask_i,
             identb_in, s1_o, slab_o, erow_o, wrow_s, lrk_o, vscr)
    s.barrier(); s.pop_scope()
    s.emit_all()
    return nc


def run_A(inputs):
    if "A" not in _CACHE:
        _CACHE["A"] = build_A()
    cosrep, sinrep, mask4 = host_consts()
    identb_np = np.eye(128, dtype=bf16)
    identf_np = np.eye(128, dtype=np.float32)
    x_full = inputs["L0_attn_h_t"]
    bcat = np.concatenate([inputs["L0_attn_b_DQ"], inputs["L0_attn_b_DKV"],
                           inputs["L0_attn_b_KR"]]).reshape(1, 96)
    in_maps = []
    for c in range(NCORES):
        in_maps.append({
            "x": np.ascontiguousarray(x_full[2 * c:2 * c + 2].reshape(NTOK, D)),
            "q1": inputs["L0_ffn_fc1_q"], "q2": inputs["L0_ffn_fc2_q"],
            "sc1": inputs["L0_ffn_fc1_scales"], "sc2": inputs["L0_ffn_fc2_scales"],
            "rw": inputs["L1_ffn_router_w"], "identf": identf_np, "identb": identb_np,
            "wdq": inputs["L0_attn_W_DQ"], "wdkv": inputs["L0_attn_W_DKV"],
            "wkr": inputs["L0_attn_W_KR"], "bcat": bcat,
            "wuq": inputs["L0_attn_W_UQ"], "buq": inputs["L0_attn_b_UQ"].reshape(1, -1),
            "wuk": inputs["L0_attn_W_UK"], "buk": inputs["L0_attn_b_UK"].reshape(1, -1),
            "wuv": inputs["L0_attn_W_UV"], "buv": inputs["L0_attn_b_UV"].reshape(1, -1),
            "wqr": inputs["L0_attn_W_QR"], "bqr": inputs["L0_attn_b_QR"].reshape(1, -1),
            "WO": inputs["L0_attn_W_O"], "bO": inputs["L0_attn_b_O"].reshape(1, -1),
            "gam": inputs["L0_ffn_ln_gamma"].reshape(1, -1),
            "bet": inputs["L0_ffn_ln_beta"].reshape(1, -1),
            "fb1": inputs["L0_ffn_fc1_bias"].reshape(32, 128),
            "fb2": inputs["L0_ffn_fc2_bias"].reshape(1, -1),
            "rb": inputs["L1_ffn_router_b"].reshape(1, -1),
            "Wq": inputs["L1_attn_Wq"], "bq": inputs["L1_attn_bq"].reshape(1, -1),
            "Wd": inputs["L1_attn_Wdown"], "bd": inputs["L1_attn_bdown"].reshape(1, -1),
            "Wu": inputs["L1_attn_Wup"], "bu": inputs["L1_attn_bup"].reshape(1, -1),
            "Wo1": inputs["L1_attn_Wout"], "bo1": inputs["L1_attn_bout"].reshape(1, -1),
            "cosr": cosrep.astype(bf16), "sinr": sinrep.astype(bf16), "maskc": mask4,
        })
    res = run_bass_kernel_spmd(_CACHE["A"], in_maps, core_ids=list(range(NCORES)))
    return res.results


def build_B():
    nc = bass.Bass("TRN2", target_bir_lowering=False)
    fc1w = nc.dram_tensor("fc1w", [4096, 1024], F32, kind="ExternalInput")
    fc2w = nc.dram_tensor("fc2w", [1024, 4096], F32, kind="ExternalInput")
    slaball = nc.dram_tensor("slaball", [8 * 16512, D], BF16, kind="ExternalInput")
    erow_all = nc.dram_tensor("erow_all", [16, NTOK], BF16, kind="ExternalInput")
    fb1 = nc.dram_tensor("fb1", [32, 128], F32, kind="ExternalInput")
    fb2 = nc.dram_tensor("fb2", [1, 1024], F32, kind="ExternalInput")
    ls_i = nc.dram_tensor("ls", [1, 1024], F32, kind="ExternalInput")
    ehot = nc.dram_tensor("ehot", [8, 1], F32, kind="ExternalInput")
    crow16 = nc.dram_tensor("crow16", [1, 16], F32, kind="ExternalInput")
    identb_in = nc.dram_tensor("identb", [128, 128], BF16, kind="ExternalInput")
    identf_in = nc.dram_tensor("identf", [128, 128], F32, kind="ExternalInput")
    resl = nc.dram_tensor("resl", [10240, D], BF16, kind="ExternalOutput")
    dbgidx = nc.dram_tensor("dbgidx", [128, 80], F32, kind="ExternalOutput")
    fc1T = nc.dram_tensor("fc1T", [1024, 4096], BF16)
    fc2T = nc.dram_tensor("fc2T", [4096, 1024], BF16)

    s = Sched(nc)
    s.push_scope()
    sid = s.sbuf("sid", [128, 128], F32)
    s.dma("sync", sid[:, :], identf_in[:, :], writes=["sid"])
    win = [s.sbuf(f"win{i}", [128, 128], F32) for i in range(4)]
    pt = [s.psum(f"pt{i}", [128, 512], F32) for i in range(4)]
    obuf = [s.sbuf(f"obuf{i}", [128, 4096], BF16) for i in range(2)]

    def transpose_mat(src_, dst, R, C, tagp):
        nr, ncol = R // 128, C // 128
        for cj in range(ncol):
            ob = obuf[cj % 2]
            for ri in range(nr):
                ib = win[ri % 4]
                ibk = f"win{ri % 4}"
                pb = pt[ri % 4]
                pbk = f"pt{ri % 4}"
                s.dma("sync", ib[:, :], src_[ri * 128:(ri + 1) * 128, cj * 128:(cj + 1) * 128],
                      writes=[ibk], tag=f"{tagp} ld {ri},{cj}")
                s.te(TRANS(pb[:, 0:128], ib[:, :], sid[:, :]),
                     reads=[ibk, "sid"], writes=[pbk], tag=f"{tagp} tr {ri},{cj}")
                s.dve(COPY(ob[:, ri * 128:(ri + 1) * 128], pb[:, 0:128]),
                      reads=[pbk], writes=[f"obuf{cj % 2}_{ri}"], tag=f"{tagp} ev {ri},{cj}")
            s.dma("gpsimd", dst[cj * 128:(cj + 1) * 128, :], ob[:, 0:R],
                  reads=[f"obuf{cj % 2}_{ri}" for ri in range(nr)],
                  writes=[f"{tagp}st"], semkey=f"dq:st:obuf{cj % 2}", tag=f"{tagp} st {cj}")

    transpose_mat(fc1w, fc1T, 4096, 1024, "t1")
    transpose_mat(fc2w, fc2T, 1024, 4096, "t2")
    s.barrier(); s.pop_scope(); s.push_scope()
    emit_l1b(s, slaball, erow_all, fc1T, fc2T, fb1, fb2, ls_i, ehot, crow16,
             identb_in, resl, dbgidx)
    s.barrier(); s.pop_scope()
    s.emit_all()
    return nc


def run_B(inputs, A):
    if "B" not in _CACHE:
        _CACHE["B"] = build_B()
    slaball = np.concatenate([np.asarray(A[c]["slab"]) for c in range(NCORES)], axis=0)
    erow_all = np.concatenate([np.asarray(A[c]["erow"]) for c in range(NCORES)], axis=0)
    crow16 = np.tile(np.arange(8, dtype=np.float32) * 16512.0, 2).reshape(1, 16)
    identb_np = np.eye(128, dtype=bf16)
    identf_np = np.eye(128, dtype=np.float32)
    in_maps = []
    for c in range(NCORES):
        eh = np.zeros((8, 1), np.float32)
        eh[c, 0] = 1.0
        in_maps.append({
            "fc1w": np.ascontiguousarray(inputs["L1_ffn_fc1_w"][c]),
            "fc2w": np.ascontiguousarray(inputs["L1_ffn_fc2_w"][c]),
            "slaball": slaball, "erow_all": erow_all,
            "fb1": inputs["L1_ffn_fc1_b"][c].reshape(32, 128),
            "fb2": inputs["L1_ffn_fc2_b"][c].reshape(1, -1),
            "ls": inputs["L1_ffn_layer_scale"][c].reshape(1, -1),
            "ehot": eh, "crow16": crow16, "identb": identb_np, "identf": identf_np,
        })
    res = run_bass_kernel_spmd(_CACHE["B"], in_maps, core_ids=list(range(NCORES)))
    return res.results


def build_C():
    nc = bass.Bass("TRN2", target_bir_lowering=False)
    x1_i = nc.dram_tensor("x1", [NTOK, D], F32, kind="ExternalInput")
    s1_i = nc.dram_tensor("s1", [NTOK, D], F32, kind="ExternalInput")
    rt_i = nc.dram_tensor("route", [NTOK, 4], F32, kind="ExternalInput")
    lrk_i = nc.dram_tensor("lrkrow", [2, NTOK], F32, kind="ExternalInput")
    erow_all = nc.dram_tensor("erow_all", [16, NTOK], BF16, kind="ExternalInput")
    resall = nc.dram_tensor("resall", [8 * 10240, D], BF16, kind="ExternalInput")
    cmask = nc.dram_tensor("cmask", [8, 1], F32, kind="ExternalInput")
    identb_in = nc.dram_tensor("identb", [128, 128], BF16, kind="ExternalInput")
    y_o = nc.dram_tensor("y", [NTOK, D], F32, kind="ExternalOutput")
    s = Sched(nc)
    s.push_scope()
    emit_l1c(s, x1_i, s1_i, rt_i, lrk_i, erow_all, resall, cmask, identb_in, y_o)
    s.barrier(); s.pop_scope()
    s.emit_all()
    return nc


def run_C(inputs, A, B):
    if "C" not in _CACHE:
        _CACHE["C"] = build_C()
    resall = np.concatenate([np.asarray(B[c]["resl"]) for c in range(NCORES)], axis=0)
    erow_all = np.concatenate([np.asarray(A[c]["erow"]) for c in range(NCORES)], axis=0)
    identb_np = np.eye(128, dtype=bf16)
    in_maps = []
    for c in range(NCORES):
        cm = np.zeros((8, 1), np.float32)
        cm[:c, 0] = 1.0
        in_maps.append({
            "x1": np.asarray(A[c]["x1"]), "s1": np.asarray(A[c]["s1"]),
            "route": np.asarray(A[c]["route"]),
            "lrkrow": np.asarray(A[c]["lrkrow"]),
            "erow_all": erow_all, "resall": resall, "cmask": cm,
            "identb": identb_np,
        })
    res = run_bass_kernel_spmd(_CACHE["C"], in_maps, core_ids=list(range(NCORES)))
    return res.results


_CACHE = {}


def kernel(**inputs):
    A = [dict((k, np.asarray(v)) for k, v in r.items()) for r in run_A(inputs)]
    B = [dict((k, np.asarray(v)) for k, v in r.items()) for r in run_B(inputs, A)]
    C = run_C(inputs, A, B)
    y = np.concatenate([np.asarray(C[c]["y"]).reshape(2, 4096, 1024)
                        for c in range(NCORES)], axis=0)
    return np.ascontiguousarray(y, dtype=np.float32)
